# version 04
# baseline (speedup 1.0000x reference)
import contextlib
import math

import numpy as np
import ml_dtypes

import concourse.bass as bass
import concourse.mybir as mybir
from concourse.bass_utils import run_bass_kernel_spmd

F32 = mybir.dt.float32
F16 = mybir.dt.float16
BF16 = mybir.dt.bfloat16
I32 = mybir.dt.int32
ALU = mybir.AluOpType
AF = mybir.ActivationFunctionType
AX = mybir.AxisListType.X

D = 1024
S = 4096
H = 128
FC = D // 128
NE = 10
DE = 512
NSUB = 4
SUB = S // NSUB

TWO_PI = float(2.0 * math.pi)
INV_SQRT_H = 1.0 / math.sqrt(H)
EXP_SHIFT = -float(math.sqrt(H))
ROUND_OFS = 128.0


class Sem:
    def __init__(self, h, is_dma=False):
        self.h = h
        self.is_dma = is_dma
        self.n = 0

    def bump(self, k):
        self.n += k
        return self.n


class Prog:
    ENGS = ("sync", "vector", "scalar", "tensor", "gpsimd")

    def __init__(self):
        self.q = {e: [] for e in self.ENGS}

    def op(self, eng, build, sem=None, k=1):
        if sem is not None:
            v = sem.bump(k)
            if sem.is_dma:
                self.q[eng].append(
                    lambda e, b=build, s=sem, kk=k: b(e).then_inc(s.h, kk))
            else:
                def _f(e, b=build, s=sem, kk=k):
                    b(e)
                    e.drain(fusable=False).then_inc(s.h, kk)
                self.q[eng].append(_f)
            return v
        self.q[eng].append(build)
        return None

    def wait(self, eng, sem, val):
        if val:
            self.q[eng].append(lambda e, s=sem, v=val: e.wait_ge(s.h, v))

    def flush(self, block):
        for eng in self.ENGS:
            ops = self.q[eng]
            if not ops:
                continue

            def body(engine, ops=ops):
                for f in ops:
                    f(engine)

            getattr(block, eng)(body)
        self.q = {e: [] for e in self.ENGS}


def build_nc(dbg=False):
    nc = bass.Bass("TRN2", target_bir_lowering=False,
                   detect_race_conditions=False)

    x_t = nc.dram_tensor("x_t", [D, S], F32, kind="ExternalInput")
    wqT = nc.dram_tensor("wqT", [D, H], F32, kind="ExternalInput")
    wkT = nc.dram_tensor("wkT", [D, H], F32, kind="ExternalInput")
    wvT = nc.dram_tensor("wvT", [D, H], F32, kind="ExternalInput")
    woT = nc.dram_tensor("woT", [H, D], F32, kind="ExternalInput")
    wavel = nc.dram_tensor("wavel", [H], F32, kind="ExternalInput")
    pbias = nc.dram_tensor("pbias", [H], F32, kind="ExternalInput")
    bq_d = nc.dram_tensor("bq", [H], F32, kind="ExternalInput")
    bk_d = nc.dram_tensor("bk", [H], F32, kind="ExternalInput")
    bv_d = nc.dram_tensor("bv", [H], F32, kind="ExternalInput")
    bo_d = nc.dram_tensor("bo", [D], F32, kind="ExternalInput")
    router = nc.dram_tensor("router", [D, 8], F32, kind="ExternalInput")
    gup = nc.dram_tensor("gup", [NE, 2, 128, FC * DE], F32,
                         kind="ExternalInput")
    dwn = nc.dram_tensor("dwn", [FC, 128, NE * 4 * 128], F32,
                         kind="ExternalInput")
    eye_in = nc.dram_tensor("eye_in", [128, 128], BF16, kind="ExternalInput")
    sel_in = nc.dram_tensor("sel_in", [8, 8 * 128], BF16, kind="ExternalInput")

    y = nc.dram_tensor("y", [D, S], F32, kind="ExternalOutput")

    x2_hbm = nc.dram_tensor("x2_hbm", [D, S], F32,
                            kind="ExternalOutput" if dbg else "Internal")
    if dbg:
        dbg_t = {n: nc.dram_tensor("dbg_" + n, [128, S], F16,
                                   kind="ExternalOutput")
                 for n in ("cosq", "sinq", "cosk", "sink")}
        dbg_t["vf"] = nc.dram_tensor("dbg_vf", [128, S], BF16,
                                     kind="ExternalOutput")
        dbg_us = {n: nc.dram_tensor("dbg_" + n, [128, S], F16,
                                    kind="ExternalOutput")
                  for n in ("usq", "ucq")}
        dbg_coef = nc.dram_tensor("dbg_coef", [8, S], BF16,
                                  kind="ExternalOutput")
        dbg_htil = nc.dram_tensor("dbg_htil", [128, 2 * SUB * 4], BF16,
                                  kind="ExternalOutput")
        dbg_ln2b = nc.dram_tensor("dbg_ln2b", [128, FC * SUB], BF16,
                                  kind="ExternalOutput")
        dbg_rt = nc.dram_tensor("dbg_rt", [128, 64], F32,
                                kind="ExternalOutput")
        dbg_rt2 = nc.dram_tensor("dbg_rt2", [128, 40 * 8], F32,
                                 kind="ExternalOutput")
    gup_sc = nc.dram_tensor("gup_sc", [NE, 2, 128, FC * DE], BF16)
    dwn_sc = nc.dram_tensor("dwn_sc", [FC, 128, NE * 4 * 128], BF16)

    es = contextlib.ExitStack()

    def SB(name, shape, dt):
        return es.enter_context(nc.sbuf_tensor(name, shape, dt))

    def SEM(name, is_dma=False):
        return Sem(es.enter_context(nc.semaphore(name)), is_dma)

    eyeb = SB("eyeb", [128, 128], BF16)
    ones_b = SB("ones_b", [128, 128], BF16)
    ones_f = SB("ones_f", [128, 128], F32)
    c2pi = SB("c2pi", [128, 1], F32)
    ceps = SB("ceps", [128, 1], F32)
    cnhalf = SB("cnhalf", [128, 1], F32)
    cnone = SB("cnone", [128, 1], F32)
    cinvsq = SB("cinvsq", [128, 1], F32)
    cshift = SB("cshift", [128, 1], F32)
    invw = SB("invw", [128, 1], F32)
    s1c = SB("s1c", [128, 1], F32)
    s2q = SB("s2q", [128, 1], F32)
    s2k = SB("s2k", [128, 1], F32)
    sqc = SB("sqc", [128, 1], F32)
    skc = SB("skc", [128, 1], F32)
    svc = SB("svc", [128, 1], F32)
    bvb = SB("bvb", [128, 1], BF16)
    bqf = SB("bqf", [128, 1], F32)
    bkf = SB("bkf", [128, 1], F32)
    bvf = SB("bvf", [128, 1], F32)
    wlf = SB("wlf", [128, 1], F32)
    pbf = SB("pbf", [128, 1], F32)
    bo8 = SB("bo8", [128, FC], F32)
    boP = SB("boP", [128, FC], F32)
    tcol = SB("tcol", [128, 1], F32)
    wqTb = SB("wqTb", [128, FC * 128], BF16)
    wkTb = SB("wkTb", [128, FC * 128], BF16)
    wvTb = SB("wvTb", [128, FC * 128], BF16)
    woTb = SB("woTb", [128, FC * 128], BF16)
    r_sb = SB("r_sb", [128, FC * 8], F32)
    r_hi = SB("r_hi", [128, FC * 8], BF16)
    r_lo = SB("r_lo", [128, FC * 8], BF16)
    sr_rep = SB("sr_rep", [128, 8], F32)

    dW = SEM("dW", is_dma=True)
    dG = SEM("dG", is_dma=True)
    yst = SEM("yst", is_dma=True)
    sT = SEM("sT")
    sV = SEM("sV")
    sA = SEM("sA")

    p = Prog()

    def vdrain():
        p.op("vector", lambda e: e.drain(fusable=False))

    esAB = contextlib.ExitStack()

    def SBab(name, shape, dt):
        return esAB.enter_context(nc.sbuf_tensor(name, shape, dt))

    cosq = SBab("cosq", [128, S], F16)
    sinq = SBab("sinq", [128, S], F16)
    cosk = SBab("cosk", [128, S], F16)
    sink = SBab("sink", [128, S], F16)
    vf = SBab("vf", [128, S], BF16)

    esA = contextlib.ExitStack()

    def SBa(name, shape, dt):
        return esA.enter_context(nc.sbuf_tensor(name, shape, dt))

    def PSa(name, shape, dt=F32):
        return esA.enter_context(nc.psum_tensor(name, shape, dt))

    CH = 512
    NCH = S // CH
    xfA = [SBa(f"xfA{i}", [128, FC * CH], F32) for i in range(2)]
    xbA = [SBa(f"xbA{i}", [128, FC * CH], BF16) for i in range(2)]
    xsqA = [SBa(f"xsqA{i}", [128, FC * CH], BF16) for i in range(2)]
    stat1 = SBa("stat1", [128, CH], F32)
    stat2 = SBa("stat2", [128, CH], F32)
    rrep = SBa("rrep", [128, CH], F32)
    nmr = SBa("nmr", [128, CH], F32)
    ga = SBa("ga", [128, CH], F32)
    gb = SBa("gb", [128, CH], F32)
    gc = SBa("gc", [128, CH], F32)
    gi = SBa("gi", [128, CH], I32)
    usq = SBa("usq", [128, S], F16)
    ucq = SBa("ucq", [128, S], F16)
    usk = SBa("usk", [128, S], F16)
    uck = SBa("uck", [128, S], F16)
    wstage = SBa("wstage", [128, FC * 128], F32)

    psMean = PSa("psMean", [128, CH])
    psSq = PSa("psSq", [128, CH])
    psQ = [PSa(f"psQ{i}", [128, CH]) for i in range(2)]
    psK = [PSa(f"psK{i}", [128, CH]) for i in range(2)]
    psV = [PSa(f"psV{i}", [128, CH]) for i in range(2)]

    p.op("sync", lambda e: e.dma_start(out=eyeb[:], in_=eye_in[:]), dW, 16)
    for col, src in ((wlf, wavel), (pbf, pbias), (bqf, bq_d), (bkf, bk_d),
                     (bvf, bv_d)):
        p.op("sync", lambda e, c=col, s=src: e.dma_start(
            out=c[:], in_=bass.AP(s, 0, [[1, 128], [1, 1]])), dW, 16)
    for dc in range(FC):
        p.op("sync", lambda e, dc=dc: e.dma_start(
            out=bo8[:, dc:dc + 1],
            in_=bass.AP(bo_d, dc * 128, [[1, 128], [1, 1]])), dW, 16)
    for kc in range(FC):
        p.op("sync", lambda e, kc=kc: e.dma_start(
            out=r_sb[:, kc * 8:(kc + 1) * 8],
            in_=router[kc * 128:(kc + 1) * 128, :]), dW, 16)
    small_loads = dW.n

    p.op("vector", lambda e: e.memset(ones_b[:], 1.0))
    p.op("vector", lambda e: e.memset(ones_f[:], 1.0))
    p.op("vector", lambda e: e.memset(c2pi[:], TWO_PI))
    p.op("vector", lambda e: e.memset(ceps[:], 1e-5))
    p.op("vector", lambda e: e.memset(cnhalf[:], -0.5))
    p.op("vector", lambda e: e.memset(cnone[:], -1.0))
    p.op("vector", lambda e: e.memset(cinvsq[:], INV_SQRT_H))
    p.op("vector", lambda e: e.memset(cshift[:], EXP_SHIFT))
    p.wait("vector", dW, small_loads)
    p.op("vector", lambda e: e.tensor_scalar(tcol[:], wlf[:], 1e-8, None,
                                             ALU.add))
    vdrain()
    p.op("vector", lambda e: e.reciprocal(invw[:], tcol[:]))
    vdrain()
    p.op("vector", lambda e: e.tensor_scalar(s1c[:], invw[:], 1.0 / TWO_PI,
                                             None, ALU.mult))
    for bcol, s2 in ((bqf, s2q), (bkf, s2k)):
        p.op("vector", lambda e, b=bcol, s=s2: e.tensor_mul(s[:], b[:],
                                                            invw[:]))
        vdrain()
        p.op("vector", lambda e, s=s2: e.tensor_add(s[:], s[:], pbf[:]))
        vdrain()
        p.op("vector", lambda e, s=s2: e.tensor_scalar(
            s[:], s[:], 1.0 / TWO_PI, ROUND_OFS, ALU.mult, ALU.add))
    p.op("vector", lambda e: e.tensor_copy(bvb[:], bvf[:]), sV, 1)

    for wi, (wsrc, wdst) in enumerate(((wqT, wqTb), (wkT, wkTb), (wvT, wvTb),
                                       (woT, woTb))):
        p.wait("sync", sV, sV.n)
        for kc in range(FC):
            if wi < 3:
                src = wsrc[kc * 128:(kc + 1) * 128, :]
            else:
                src = wsrc[:, kc * 128:(kc + 1) * 128]
            p.op("sync", lambda e, s=src, kc=kc: e.dma_start(
                out=wstage[:, kc * 128:(kc + 1) * 128], in_=s), dW, 16)
        p.wait("vector", dW, dW.n)
        p.op("vector", lambda e, d=wdst: e.tensor_copy(d[:], wstage[:]), sV, 1)
    wcast_v = sV.n

    p.wait("tensor", sV, wcast_v)
    for wsb, col in ((wqTb, sqc), (wkTb, skc), (wvTb, svc)):
        for kc in range(FC):
            p.op("tensor", lambda e, w=wsb, kc=kc: e.matmul(
                psMean[:, 0:1], w[:, kc * 128:(kc + 1) * 128], ones_b[:, 0:1],
                start=(kc == 0), stop=(kc == FC - 1)),
                sT if kc == FC - 1 else None, 1)
        p.wait("vector", sT, sT.n)
        p.op("vector", lambda e, c=col: e.tensor_copy(c[:], psMean[:, 0:1]),
             sV, 1)
        p.wait("tensor", sV, sV.n)
    for dc in range(FC):
        p.op("tensor", lambda e, dc=dc: e.matmul(
            psSq[:, dc:dc + 1], woTb[:, dc * 128:(dc + 1) * 128], bvb[:, 0:1],
            start=True, stop=True), sT if dc == FC - 1 else None, 1)
    p.wait("vector", sT, sT.n)
    p.op("vector", lambda e: e.tensor_add(boP[:], psSq[:, 0:FC], bo8[:]),
         sV, 1)
    p.op("vector", lambda e: e.tensor_copy(r_hi[:], r_sb[:]))
    vdrain()
    p.op("vector", lambda e: e.tensor_sub(r_lo[:], r_sb[:], r_hi[:]), sV, 1)
    p.wait("tensor", sV, sV.n)
    for pas, rp in ((0, r_hi), (1, r_lo)):
        for kc in range(FC):
            p.op("tensor", lambda e, kc=kc, rp=rp, pas=pas: e.matmul(
                psMean[:, 0:8], ones_b[:], rp[:, kc * 8:(kc + 1) * 8],
                start=(kc == 0 and pas == 0),
                stop=(kc == FC - 1 and pas == 1)),
                sT if (kc == FC - 1 and pas == 1) else None, 1)
    p.wait("vector", sT, sT.n)
    p.op("vector", lambda e: e.tensor_copy(sr_rep[:], psMean[:, 0:8]), sV, 1)
    prol_v = sV.n

    x_loads = [0] * NCH
    cast_v = [0] * NCH
    cons_v = [0] * NCH
    qkv_t = [0] * NCH
    stats_used_v = 0

    for c in range(NCH):
        par = c % 2
        if c >= 2:
            p.wait("sync", sV, cons_v[c - 2])
        for fc in range(FC):
            p.op("sync", lambda e, c=c, fc=fc, par=par: e.dma_start(
                out=xfA[par][:, fc * CH:(fc + 1) * CH],
                in_=x_t[fc * 128:(fc + 1) * 128, c * CH:(c + 1) * CH]),
                dW, 16)
        x_loads[c] = dW.n

        p.wait("vector", dW, x_loads[c])
        if c >= 2:
            p.wait("vector", sT, qkv_t[c - 2])
        p.op("vector", lambda e, par=par: e.tensor_copy(xbA[par][:],
                                                        xfA[par][:]))
        p.op("vector", lambda e, par=par: e.tensor_mul(
            xsqA[par][:], xbA[par][:], xbA[par][:]), sV, 1)
        cast_v[c] = sV.n

        p.wait("tensor", sV, cast_v[c])
        p.wait("tensor", sV, stats_used_v if c > 0 else prol_v)
        for kc in range(FC):
            p.op("tensor", lambda e, par=par, kc=kc: e.matmul(
                psMean[:], ones_b[:], xbA[par][:, kc * CH:(kc + 1) * CH],
                start=(kc == 0), stop=(kc == FC - 1)))
        for kc in range(FC):
            p.op("tensor", lambda e, par=par, kc=kc: e.matmul(
                psSq[:], ones_b[:], xsqA[par][:, kc * CH:(kc + 1) * CH],
                start=(kc == 0), stop=(kc == FC - 1)),
                sT if kc == FC - 1 else None, 1)
        stats_t = sT.n

        if c >= 2:
            p.wait("tensor", sV, cons_v[c - 2])
        for wsb, dst, last in ((wqTb, psQ[par], False), (wkTb, psK[par], False),
                               (wvTb, psV[par], True)):
            for kc in range(FC):
                p.op("tensor", lambda e, w=wsb, d=dst, par=par, kc=kc:
                     e.matmul(
                         d[:], w[:, kc * 128:(kc + 1) * 128],
                         xbA[par][:, kc * CH:(kc + 1) * CH],
                         start=(kc == 0), stop=(kc == FC - 1)),
                     sT if (kc == FC - 1 and last) else None, 1)
        qkv_t[c] = sT.n

        p.wait("vector", sT, stats_t)
        p.op("vector", lambda e: e.tensor_scalar(stat1[:], psMean[:], 1.0 / D,
                                                 None, ALU.mult))
        p.op("vector", lambda e: e.tensor_scalar(stat2[:], psSq[:], 1.0 / D,
                                                 None, ALU.mult))
        p.op("vector", lambda e: e.scalar_tensor_tensor(
            ga[:], stat1[:], -1.0, stat1[:], ALU.mult, ALU.mult))
        p.op("vector", lambda e: e.scalar_tensor_tensor(
            gb[:], stat2[:], 1e-5, ga[:], ALU.add, ALU.add), sV, 1)
        stats_used_v = sV.n
        p.wait("scalar", sV, stats_used_v)
        p.op("scalar", lambda e: e.activation(ga[:], gb[:], AF.Ln), sA, 1)
        p.wait("vector", sA, sA.n)
        p.op("vector", lambda e: e.tensor_scalar(
            gb[:], ga[:], -0.5, None, ALU.mult), sV, 1)
        p.wait("scalar", sV, sV.n)
        p.op("scalar", lambda e: e.activation(rrep[:], gb[:], AF.Exp), sA, 1)
        p.wait("vector", sA, sA.n)
        p.op("vector", lambda e: e.scalar_tensor_tensor(
            nmr[:], stat1[:], -1.0, rrep[:], ALU.mult, ALU.mult))

        p.wait("vector", sT, qkv_t[c])
        sl = slice(c * CH, (c + 1) * CH)
        for psX, s2x, sxc, us_f, uc_f in ((psQ[par], s2q, sqc, usq, ucq),
                                          (psK[par], s2k, skc, usk, uck)):
            p.op("vector", lambda e, psX=psX: e.tensor_mul(ga[:], psX[:],
                                                           rrep[:]))
            p.op("vector", lambda e, sxc=sxc: e.scalar_tensor_tensor(
                gb[:], nmr[:], sxc[:], ga[:], ALU.mult, ALU.add))
            p.op("vector", lambda e, s2x=s2x: e.tensor_scalar(
                ga[:], gb[:], s1c[:], s2x[:], ALU.mult, ALU.add))
            p.op("vector", lambda e: e.tensor_copy(gi[:], ga[:]))
            p.op("vector", lambda e: e.tensor_copy(gb[:], gi[:]))
            p.op("vector", lambda e: e.tensor_sub(ga[:], ga[:], gb[:]))
            p.op("vector", lambda e: e.tensor_scalar(gb[:], ga[:], 0.5, None,
                                                     ALU.is_gt))
            p.op("vector", lambda e: e.scalar_tensor_tensor(
                gc[:], gb[:], -1.0, ga[:], ALU.mult, ALU.add))
            p.op("vector", lambda e, u=us_f, sl=sl: e.tensor_scalar(
                u[:, sl], gc[:], TWO_PI, None, ALU.mult))
            p.op("vector", lambda e: e.tensor_scalar(
                ga[:], gc[:], 0.25, None, ALU.add))
            p.op("vector", lambda e: e.tensor_scalar(gb[:], ga[:], 0.5, None,
                                                     ALU.is_gt))
            p.op("vector", lambda e: e.scalar_tensor_tensor(
                gc[:], gb[:], -1.0, ga[:], ALU.mult, ALU.add))
            p.op("vector", lambda e, u=uc_f, sl=sl: e.tensor_scalar(
                u[:, sl], gc[:], TWO_PI, None, ALU.mult), sV, 1)
        p.op("vector", lambda e, par=par: e.tensor_mul(ga[:], psV[par][:],
                                                       rrep[:]))
        p.op("vector", lambda e, sl=sl: e.scalar_tensor_tensor(
            vf[:, sl], nmr[:], svc[:], ga[:], ALU.mult, ALU.add), sV, 1)
        cons_v[c] = sV.n

    p.wait("scalar", sV, cons_v[NCH - 1])
    for src, dst in ((usq, sinq), (ucq, cosq), (usk, sink), (uck, cosk)):
        p.op("scalar", lambda e, s=src, d=dst: e.activation(
            d[:], s[:], AF.Sin), sA, 1)
    p.wait("vector", sA, sA.n)
    for t in (cosq, sinq):
        p.op("vector", lambda e, t=t: e.tensor_scalar(
            t[:], t[:], INV_SQRT_H, None, ALU.mult), sV, 1)
    sinscale_v = sV.n
    trig_a = sA.n
    if dbg:
        p.wait("gpsimd", sA, trig_a)
        p.wait("gpsimd", sV, sinscale_v)
        for n, t in (("cosq", cosq), ("sinq", sinq), ("cosk", cosk),
                     ("sink", sink), ("vf", vf)):
            p.op("gpsimd", lambda e, n=n, t=t: e.dma_start(
                out=dbg_t[n][:], in_=t[:]), dG, 16)
        for n, t in (("usq", usq), ("ucq", ucq)):
            p.op("gpsimd", lambda e, n=n, t=t: e.dma_start(
                out=dbg_us[n][:], in_=t[:]), dG, 16)
        p.wait("gpsimd", dG, dG.n)

    with nc.Block() as blk:
        p.flush(blk)
    esA.close()

    esB = contextlib.ExitStack()
    v_tm = esB.enter_context(nc.sbuf_tensor("v_tm", [128, S], BF16))

    esB0 = contextlib.ExitStack()
    psTrV = [esB0.enter_context(nc.psum_tensor(f"psTrV{i}", [128, 128], BF16))
             for i in range(2)]
    tr_cons = [0, 0]
    for kt in range(S // 128):
        par = kt % 2
        if kt >= 2:
            p.wait("tensor", sV, tr_cons[par])
        p.op("tensor", lambda e, kt=kt, par=par: e.transpose(
            psTrV[par][:], vf[:, kt * 128:(kt + 1) * 128], eyeb[:]), sT, 1)
        p.wait("vector", sT, sT.n)
        p.op("vector", lambda e, kt=kt, par=par: e.tensor_copy(
            v_tm[:, kt * 128:(kt + 1) * 128], psTrV[par][:]), sV, 1)
        tr_cons[par] = sV.n
    with nc.Block() as blk:
        p.flush(blk)
    esB0.close()

    esB1 = contextlib.ExitStack()

    def SBb(name, shape, dt):
        return esB1.enter_context(nc.sbuf_tensor(name, shape, dt))

    def PSb(name, shape, dt=F32):
        return esB1.enter_context(nc.psum_tensor(name, shape, dt))

    QC = 512
    NQ = S // QC
    NKT = S // 128
    xfB = [SBb(f"xfB{i}", [128, FC * QC], F32) for i in range(2)]
    ptb = [SBb(f"ptb{i}", [128, 1024], BF16) for i in range(2)]
    attnT = [SBb(f"attnT{i}", [128, QC], BF16) for i in range(2)]
    lnS = [SBb(f"lnS{i}", [128, QC], F32) for i in range(2)]
    rcp = [SBb(f"rcp{i}", [128, QC], F32) for i in range(2)]
    psSc = [PSb(f"psSc{i}", [128, 1024]) for i in range(2)]
    psPV = [PSb(f"psPV{i}", [128, QC]) for i in range(2)]
    psSm = [PSb(f"psSm{i}", [128, QC]) for i in range(2)]

    xB_loads = [0] * NQ
    x2_st = [0] * NQ
    wo_ev_all = [0] * NQ
    prev_exp = {0: 0, 1: 0}
    prev_spv = {0: 0, 1: 0}
    attn_overwrite_t = {0: 0, 1: 0}

    for q in range(NQ):
        qpar = q % 2
        qs = slice(q * QC, (q + 1) * QC)
        if q >= 2:
            p.wait("gpsimd", dG, x2_st[q - 2])
        for fc in range(FC):
            p.op("gpsimd", lambda e, q=q, fc=fc, qpar=qpar: e.dma_start(
                out=xfB[qpar][:, fc * QC:(fc + 1) * QC],
                in_=x_t[fc * 128:(fc + 1) * 128, q * QC:(q + 1) * QC]),
                dG, 16)
        xB_loads[q] = dG.n

        if q >= 1:
            p.wait("tensor", sV, wo_ev_all[q - 1])
        exp_a = [0] * (NKT // 2)
        spv_t = [0] * (NKT // 2)
        for pr in range(NKT // 2):
            par = pr % 2
            p.wait("tensor", sA, exp_a[pr - 2] if pr >= 2 else prev_exp[par])
            for half in range(2):
                kt = pr * 2 + half
                ks = slice(kt * 128, (kt + 1) * 128)
                hs = slice(half * 512, (half + 1) * 512)
                p.op("tensor", lambda e, ks=ks, hs=hs, par=par, qs=qs:
                     e.matmul(psSc[par][:, hs], cosk[:, ks], cosq[:, qs],
                              start=True, stop=False))
                p.op("tensor", lambda e, ks=ks, hs=hs, par=par, qs=qs:
                     e.matmul(psSc[par][:, hs], sink[:, ks], sinq[:, qs],
                              start=False, stop=True),
                     sT if half == 1 else None, 1)
            sc_t = sT.n
            p.wait("scalar", sT, sc_t)
            p.wait("scalar", sT, spv_t[pr - 2] if pr >= 2 else prev_spv[par])
            p.op("scalar", lambda e, par=par: e.activation(
                ptb[par][:], psSc[par][:], AF.Exp), sA, 1)
            exp_a[pr] = sA.n
            p.wait("tensor", sA, exp_a[pr])
            for half in range(2):
                kt = pr * 2 + half
                hs = slice(half * 512, (half + 1) * 512)
                p.op("tensor", lambda e, par=par, hs=hs, qpar=qpar, kt=kt:
                     e.matmul(psSm[qpar][:], ones_b[:], ptb[par][:, hs],
                              start=(kt == 0), stop=(kt == NKT - 1)))
                p.op("tensor", lambda e, par=par, hs=hs, qpar=qpar, kt=kt:
                     e.matmul(psPV[qpar][:], v_tm[:, kt * 128:(kt + 1) * 128],
                              ptb[par][:, hs],
                              start=(kt == 0), stop=(kt == NKT - 1)),
                     sT if kt == NKT - 1 else None, 1)
            spv_t[pr] = sT.n
        prev_exp = {0: exp_a[NKT // 2 - 2], 1: exp_a[NKT // 2 - 1]}
        prev_spv = {0: spv_t[NKT // 2 - 2], 1: spv_t[NKT // 2 - 1]}
        pv_t = sT.n

        p.wait("scalar", sT, pv_t)
        p.op("scalar", lambda e, qpar=qpar: e.activation(
            lnS[qpar][:], psSm[qpar][:], AF.Ln), sA, 1)
        p.wait("vector", sA, sA.n)
        p.op("vector", lambda e, qpar=qpar: e.tensor_scalar(
            lnS[qpar][:], lnS[qpar][:], -1.0, None, ALU.mult), sV, 1)
        p.wait("scalar", sV, sV.n)
        p.op("scalar", lambda e, qpar=qpar: e.activation(
            rcp[qpar][:], lnS[qpar][:], AF.Exp), sA, 1)
        rcp_a = sA.n
        p.wait("vector", sA, rcp_a)
        p.wait("vector", sT, attn_overwrite_t[qpar])
        p.op("vector", lambda e, qpar=qpar: e.tensor_mul(
            attnT[qpar][:], psPV[qpar][:], rcp[qpar][:]), sV, 1)
        attn_v = sV.n

        p.wait("tensor", sV, attn_v)
        p.wait("vector", dG, xB_loads[q])
        wo_ev = [0] * FC
        for dc in range(FC):
            wo_ps = psPV[1 - qpar] if dc % 2 == 0 else psSm[qpar]
            if dc >= 2:
                p.wait("tensor", sV, wo_ev[dc - 2])
            p.op("tensor", lambda e, dc=dc, ps_=wo_ps, qpar=qpar: e.matmul(
                ps_[:], woTb[:, dc * 128:(dc + 1) * 128], attnT[qpar][:],
                start=True, stop=True), sT, 1)
            mm_t = sT.n
            p.wait("vector", sT, mm_t)
            p.op("vector", lambda e, dc=dc, ps_=wo_ps, qpar=qpar:
                 e.scalar_tensor_tensor(
                     xfB[qpar][:, dc * QC:(dc + 1) * QC], ps_[:],
                     boP[:, dc:dc + 1],
                     xfB[qpar][:, dc * QC:(dc + 1) * QC],
                     ALU.add, ALU.add), sV, 1)
            wo_ev[dc] = sV.n
            if dc == FC - 1:
                attn_overwrite_t[qpar] = mm_t
        wo_ev_all[q] = sV.n
        p.wait("gpsimd", sV, wo_ev_all[q])
        for fc in range(FC):
            p.op("gpsimd", lambda e, q=q, fc=fc, qpar=qpar: e.dma_start(
                out=x2_hbm[fc * 128:(fc + 1) * 128, q * QC:(q + 1) * QC],
                in_=xfB[qpar][:, fc * QC:(fc + 1) * QC]), dG, 16)
        x2_st[q] = dG.n

    with nc.Block() as blk:
        p.flush(blk)
    esB1.close()
    esB.close()
    esAB.close()

    esC = contextlib.ExitStack()

    def SBc(name, shape, dt):
        return esC.enter_context(nc.sbuf_tensor(name, shape, dt))

    def PSc(name, shape, dt=F32):
        return esC.enter_context(nc.psum_tensor(name, shape, dt))

    CC = 256
    NCC = SUB // CC
    htil = SBc("htil", [128, NE * SUB * 4], BF16)
    ln2b = SBc("ln2b", [128, FC * SUB], BF16)
    x2f = [SBc(f"x2f{i}", [128, FC * CC], F32) for i in range(2)]
    xb2 = SBc("xb2", [128, FC * CC], BF16)
    x2lo = SBc("x2lo", [128, FC * CC], BF16)
    xsq2 = SBc("xsq2", [128, FC * CC], BF16)
    st1C = SBc("st1C", [128, CC], F32)
    st2C = SBc("st2C", [128, CC], F32)
    rrepC = SBc("rrepC", [128, CC], F32)
    nmrC = SBc("nmrC", [128, CC], F32)
    gaC = SBc("gaC", [128, CC], F32)
    gbC = SBc("gbC", [128, CC], F32)
    coefT = SBc("coefT", [8, SUB], BF16)
    sel8 = SBc("sel8", [8, 8 * 128], BF16)
    c8m = SBc("c8m", [128, 8], F32)
    c8q = SBc("c8q", [128, 8], F32)
    c8a = SBc("c8a", [128, 8], F32)
    c8b = SBc("c8b", [128, 8], F32)
    c8c = SBc("c8c", [128, 8], F32)
    rstd8 = SBc("rstd8", [128, 8], F32)
    nmr8 = SBc("nmr8", [128, 8], F32)
    r64a = SBc("r64a", [128, 64], F32)
    r64b = SBc("r64b", [128, 64], F32)
    coefb64 = SBc("coefb64", [128, 64], BF16)
    wslot = [SBc(f"wslot{i}", [128, FC * DE], BF16) for i in range(3)]
    wstgC = SBc("wstgC", [128, 2048], F32)
    dslot = [SBc(f"dslot{i}", [128, 20 * 128], BF16) for i in range(2)]
    gsall = SBc("gsall", [128, 8 * 512], BF16)
    t2b = [SBc(f"t2b{i}", [128, 512], F32) for i in range(2)]
    x2r = [SBc(f"x2r{i}", [128, 512], F32) for i in range(2)]


    psG = [PSc(f"psG{i}", [128, 512]) for i in range(2)]
    psU = [PSc(f"psU{i}", [128, 512]) for i in range(2)]
    psD = [PSc(f"psD{i}", [128, 512]) for i in range(2)]
    psM = PSc("psM", [128, 512])
    psTr = PSc("psTr", [8, 128], BF16)

    wstg_free_v = 0
    slot_free_t = [0, 0, 0]
    slot_scr_dG = [0, 0, 0]
    dslot_free_t = [0, 0]
    dslot_scr_dG = [0, 0]
    x2f_cons = [0, 0]
    x2f_cons_v = [0, 0]
    x2r_free = [0, 0]
    ln2b_free_t = 0
    psM_free_v = 0
    psD_free_v = 0
    gu_prev_silu = {0: 0, 1: 0}
    up_prev_ev = {0: 0, 1: 0}
    prev_up_ev_all = {i: 0 for i in range(8)}

    def fetch_piece(dst_sb, src_f32, src_bf, sub, length, slot_free_t,
                    scr_done_dG, src_off=0):
        nonlocal wstg_free_v
        if sub == 0:
            nh = (length + 2047) // 2048
            for hh in range(nh):
                a = hh * 2048
                b = min(length, a + 2048)
                p.wait("sync", sV, wstg_free_v)
                p.op("sync", lambda e, s=src_f32, a=a, b=b, o=src_off:
                     e.dma_start(out=wstgC[:, 0:b - a],
                                 in_=s[:, o + a:o + b]), dW, 16)
                ld = dW.n
                p.wait("vector", dW, ld)
                if hh == 0:
                    p.wait("vector", sT, slot_free_t)
                    p.wait("vector", dG, scr_done_dG)
                p.op("vector", lambda e, d=dst_sb, a=a, b=b: e.tensor_copy(
                    d[:, a:b], wstgC[:, 0:b - a]), sV, 1)
                wstg_free_v = sV.n
            p.wait("gpsimd", sV, wstg_free_v)
            p.op("gpsimd", lambda e, d=dst_sb, s=src_bf, L=length,
                 o=src_off: e.dma_start(out=s[:, o:o + L], in_=d[:, 0:L]),
                 dG, 16)
            return sV, wstg_free_v, dG.n
        p.wait("sync", sT, slot_free_t)
        p.wait("sync", dG, scr_done_dG)
        p.op("sync", lambda e, d=dst_sb, s=src_bf, L=length, o=src_off:
             e.dma_start(out=d[:, 0:L], in_=s[:, o:o + L]), dW, 16)
        return dW, dW.n, scr_done_dG

    p.op("sync", lambda e: e.dma_start(out=sel8[:], in_=sel_in[:]), dW, 16)
    p.wait("tensor", dW, dW.n)
    p.op("vector", lambda e: e.tensor_copy(r_hi[:], r_sb[:]))
    vdrain()
    p.op("vector", lambda e: e.tensor_sub(r_lo[:], r_sb[:], r_hi[:]), sV, 1)
    p.wait("tensor", sV, sV.n)

    for sub in range(NSUB):
        for h in range(NCC):
            ci = sub * NCC + h
            par = h % 2
            t0 = sub * SUB + h * CC
            p.wait("gpsimd", sT, x2f_cons[par])
            p.wait("gpsimd", sV, x2f_cons_v[par])
            for fc in range(FC):
                p.op("gpsimd", lambda e, fc=fc, t0=t0, par=par: e.dma_start(
                    out=x2f[par][:, fc * CC:(fc + 1) * CC],
                    in_=x2_hbm[fc * 128:(fc + 1) * 128, t0:t0 + CC]), dG, 16)
            x2_ld = dG.n
            p.wait("vector", dG, x2_ld)
            p.wait("vector", sT, chunk_last_t if ci > 0 else 0)
            p.op("vector", lambda e, par=par: e.tensor_copy(xb2[:],
                                                            x2f[par][:]))
            p.op("vector", lambda e, par=par: e.tensor_sub(
                x2lo[:], x2f[par][:], xb2[:]))
            p.op("vector", lambda e: e.tensor_mul(xsq2[:], xb2[:], xb2[:]),
                 sV, 1)
            cast2_v = sV.n
            p.wait("tensor", sV, cast2_v)
            p.wait("tensor", sV, psD_free_v)
            for kc in range(FC):
                p.op("tensor", lambda e, kc=kc: e.matmul(
                    psD[0][:, 0:CC], ones_b[:], xb2[:, kc * CC:(kc + 1) * CC],
                    start=(kc == 0), stop=(kc == FC - 1)))
            for kc in range(FC):
                p.op("tensor", lambda e, kc=kc: e.matmul(
                    psD[1][:, 0:CC], ones_b[:], xsq2[:, kc * CC:(kc + 1) * CC],
                    start=(kc == 0), stop=(kc == FC - 1)),
                    sT if kc == FC - 1 else None, 1)
            stats_mm_done = sT.n
            p.wait("vector", sT, stats_mm_done)
            p.op("vector", lambda e: e.tensor_scalar(
                st1C[:], psD[0][:, 0:CC], 1.0 / D, None, ALU.mult))
            p.op("vector", lambda e: e.tensor_scalar(
                st2C[:], psD[1][:, 0:CC], 1.0 / D, None, ALU.mult))
            p.op("vector", lambda e: e.scalar_tensor_tensor(
                gaC[:], st1C[:], -1.0, st1C[:], ALU.mult, ALU.mult))
            p.op("vector", lambda e: e.scalar_tensor_tensor(
                gbC[:], st2C[:], 1e-5, gaC[:], ALU.add, ALU.add), sV, 1)
            psD_free_v = sV.n
            p.wait("scalar", sV, psD_free_v)
            p.op("scalar", lambda e: e.activation(gaC[:], gbC[:], AF.Ln),
                 sA, 1)
            p.wait("vector", sA, sA.n)
            p.op("vector", lambda e: e.tensor_scalar(
                gbC[:], gaC[:], -0.5, None, ALU.mult), sV, 1)
            p.wait("scalar", sV, sV.n)
            p.op("scalar", lambda e: e.activation(rrepC[:], gbC[:], AF.Exp),
                 sA, 1)
            p.wait("vector", sA, sA.n)
            p.op("vector", lambda e: e.scalar_tensor_tensor(
                nmrC[:], st1C[:], -1.0, rrepC[:], ALU.mult, ALU.mult))
            p.wait("vector", sT, ln2b_free_t if (sub > 0 and h == 0) else 0)
            for fc in range(FC):
                p.op("vector", lambda e, fc=fc, par=par: e.tensor_mul(
                    gaC[:], x2f[par][:, fc * CC:(fc + 1) * CC], rrepC[:]))
                p.op("vector", lambda e, fc=fc, h=h: e.tensor_add(
                    ln2b[:, fc * SUB + h * CC:fc * SUB + (h + 1) * CC],
                    gaC[:], nmrC[:]),
                    sV if fc == FC - 1 else None, 1)
            ln2b_chunk_v = sV.n

            for tt in range(CC // 128):
                ti = h * 2 + tt
                tsl = slice(tt * 128, (tt + 1) * 128)
                if ti == 0:
                    p.wait("tensor", sV, psM_free_v)
                for kc in range(FC):
                    p.op("tensor", lambda e, kc=kc, tsl=tsl, ti=ti: e.matmul(
                        psM[:, 64 + ti:65 + ti],
                        xb2[:, kc * CC:(kc + 1) * CC][:, tsl],
                        ones_b[:, 0:1],
                        start=(kc == 0), stop=(kc == FC - 1)))
                for kc in range(FC):
                    p.op("tensor", lambda e, kc=kc, tsl=tsl, ti=ti: e.matmul(
                        psM[:, 72 + ti:73 + ti],
                        xsq2[:, kc * CC:(kc + 1) * CC][:, tsl],
                        ones_b[:, 0:1],
                        start=(kc == 0), stop=(kc == FC - 1)))
                for pas, xs, rp in ((0, xb2, r_hi), (1, xb2, r_lo),
                                    (2, x2lo, r_hi)):
                    for kc in range(FC):
                        p.op("tensor", lambda e, kc=kc, tsl=tsl, xs=xs,
                             rp=rp, pas=pas, ti=ti: e.matmul(
                                 psM[:, ti * 8:(ti + 1) * 8],
                                 xs[:, kc * CC:(kc + 1) * CC][:, tsl],
                                 rp[:, kc * 8:(kc + 1) * 8],
                                 start=(kc == 0 and pas == 0),
                                 stop=(kc == FC - 1 and pas == 2)),
                             sT if (kc == FC - 1 and pas == 2) else None, 1)
            x2f_cons[par] = sT.n
            x2f_cons_v[par] = ln2b_chunk_v
            chunk_last_t = sT.n

        def d3(h_):
            return bass.AP(h_, 0, [[64, 128], [8, 8], [1, 8]])

        def segb(h_):
            return bass.AP(h_, 0, [[8, 128], [1, 8], [0, 8]])

        def tileb(h_):
            return bass.AP(h_, 0, [[8, 128], [0, 8], [1, 8]])

        p.wait("vector", sT, chunk_last_t)
        p.op("vector", lambda e: e.tensor_scalar(
            c8m[:], psM[:, 64:72], 1.0 / D, None, ALU.mult))
        p.op("vector", lambda e: e.tensor_scalar(
            c8q[:], psM[:, 72:80], 1.0 / D, None, ALU.mult))
        vdrain()
        p.op("vector", lambda e: e.scalar_tensor_tensor(
            c8a[:], c8m[:], -1.0, c8m[:], ALU.mult, ALU.mult))
        vdrain()
        p.op("vector", lambda e: e.scalar_tensor_tensor(
            c8b[:], c8q[:], 1e-5, c8a[:], ALU.add, ALU.add), sV, 1)
        p.wait("scalar", sV, sV.n)
        p.op("scalar", lambda e: e.activation(c8a[:], c8b[:], AF.Ln), sA, 1)
        p.wait("vector", sA, sA.n)
        p.op("vector", lambda e: e.tensor_scalar(
            c8b[:], c8a[:], -0.5, None, ALU.mult), sV, 1)
        p.wait("scalar", sV, sV.n)
        p.op("scalar", lambda e: e.activation(rstd8[:], c8b[:], AF.Exp),
             sA, 1)
        p.wait("vector", sA, sA.n)
        p.op("vector", lambda e: e.scalar_tensor_tensor(
            nmr8[:], c8m[:], -1.0, rstd8[:], ALU.mult, ALU.mult))
        p.op("vector", lambda e: e.tensor_copy(r64a[:], psM[:, 0:64]), sV, 1)
        psM_free_v = sV.n
        vdrain()
        p.op("vector", lambda e: e.tensor_tensor(
            d3(r64a), d3(r64a), segb(rstd8), ALU.mult))
        p.op("vector", lambda e: e.tensor_tensor(
            d3(r64b), tileb(sr_rep), segb(nmr8), ALU.mult))
        vdrain()
        p.op("vector", lambda e: e.tensor_add(r64a[:], r64a[:], r64b[:]))
        vdrain()
        p.op("vector", lambda e: e.reduce_max(c8a[:], d3(r64a), axis=AX))
        vdrain()
        p.op("vector", lambda e: e.tensor_tensor(
            d3(r64b), d3(r64a), segb(c8a), ALU.subtract), sV, 1)
        p.wait("scalar", sV, sV.n)
        p.op("scalar", lambda e: e.activation(r64a[:], r64b[:], AF.Exp),
             sA, 1)
        p.wait("vector", sA, sA.n)
        p.op("vector", lambda e: e.reduce_sum(c8b[:], d3(r64a), axis=AX))
        vdrain()
        p.op("vector", lambda e: e.reciprocal(c8c[:], c8b[:]))
        vdrain()
        p.op("vector", lambda e: e.tensor_tensor(
            d3(r64b), d3(r64a), segb(c8c), ALU.mult))
        vdrain()
        p.op("vector", lambda e: e.reduce_max(c8a[:], d3(r64b), axis=AX))
        vdrain()
        p.op("vector", lambda e: e.tensor_tensor(
            d3(r64a), d3(r64b), segb(c8a), ALU.is_equal))
        vdrain()
        p.op("vector", lambda e: e.scalar_tensor_tensor(
            r64a[:], r64a[:], -2.0, r64b[:], ALU.mult, ALU.add))
        vdrain()
        p.op("vector", lambda e: e.reduce_max(c8b[:], d3(r64a), axis=AX))
        vdrain()
        p.op("vector", lambda e: e.tensor_tensor(
            d3(r64a), d3(r64b), segb(c8b), ALU.is_ge))
        vdrain()
        p.op("vector", lambda e: e.tensor_mul(r64a[:], r64a[:], r64b[:]))
        vdrain()
        p.op("vector", lambda e: e.tensor_copy(coefb64[:], r64a[:]), sV, 1)
        coef_cast_v = sV.n
        p.wait("tensor", sV, coef_cast_v)
        for ti in range(8):
            p.op("tensor", lambda e, ti=ti: e.transpose(
                psTr[:], coefb64[:, ti * 8:(ti + 1) * 8], eyeb[:]), sT, 1)
            p.wait("vector", sT, sT.n)
            p.op("vector", lambda e, ti=ti: e.tensor_copy(
                coefT[:, ti * 128:(ti + 1) * 128], psTr[:]), sV, 1)
            if ti < 7:
                p.wait("tensor", sV, sV.n)
        coef_ready_v = sV.n
        if dbg:
            p.wait("gpsimd", sV, coef_ready_v)
            p.op("gpsimd", lambda e, sub=sub: e.dma_start(
                out=dbg_coef[:, sub * SUB:(sub + 1) * SUB], in_=coefT[:]),
                dG, 16)
            if sub == 0:
                p.op("gpsimd", lambda e: e.dma_start(
                    out=dbg_ln2b[:], in_=ln2b[:]), dG, 16)

        for pe in range(NE):
            gslot = (2 * pe) % 3
            uslot = (2 * pe + 1) % 3
            gsem, gval, slot_scr_dG[gslot] = fetch_piece(
                wslot[gslot], gup[pe, 0], gup_sc[pe, 0], sub, FC * DE,
                slot_free_t[gslot], slot_scr_dG[gslot])
            usem, uval, slot_scr_dG[uslot] = fetch_piece(
                wslot[uslot], gup[pe, 1], gup_sc[pe, 1], sub, FC * DE,
                slot_free_t[uslot], slot_scr_dG[uslot])
            p.wait("tensor", gsem, gval)
            p.wait("tensor", sV, coef_ready_v)
            silu_a = {}
            for dcq in range(8):
                qc, dc = dcq // 4, dcq % 4
                par = dcq % 2
                p.wait("tensor", sA,
                       silu_a[dcq - 2] if dcq >= 2 else gu_prev_silu[par])
                for kc in range(FC):
                    p.op("tensor", lambda e, kc=kc, dc=dc, qc=qc, par=par,
                         gslot=gslot: e.matmul(
                             psG[par][:],
                             wslot[gslot][:, kc * DE + dc * 128:
                                          kc * DE + (dc + 1) * 128],
                             ln2b[:, kc * SUB + qc * 512:
                                  kc * SUB + (qc + 1) * 512],
                             start=(kc == 0), stop=(kc == FC - 1)),
                         sT if kc == FC - 1 else None, 1)
                g_t = sT.n
                p.wait("scalar", sT, g_t)
                p.wait("scalar", sV, prev_up_ev_all[dcq])
                p.op("scalar", lambda e, dcq=dcq, par=par: e.activation(
                    gsall[:, dcq * 512:(dcq + 1) * 512], psG[par][:],
                    AF.Silu), sA, 1)
                silu_a[dcq] = sA.n
            slot_free_t[gslot] = sT.n
            gu_prev_silu = {0: silu_a[6], 1: silu_a[7]}

            p.wait("tensor", usem, uval)
            up_ev = {}
            crep_t = 0
            for dcq in range(8):
                qc, dc = dcq // 4, dcq % 4
                par = dcq % 2
                if pe < 8 and dc == 0:
                    p.wait("tensor", sV, psM_free_v)
                    p.op("tensor", lambda e, pe=pe, qc=qc: e.matmul(
                        psM[:], sel8[:, pe * 128:(pe + 1) * 128],
                        coefT[:, qc * 512:(qc + 1) * 512],
                        start=True, stop=True), sT, 1)
                    crep_t = sT.n
                p.wait("tensor", sV,
                       up_ev[dcq - 2] if dcq >= 2 else up_prev_ev[par])
                for kc in range(FC):
                    p.op("tensor", lambda e, kc=kc, dc=dc, qc=qc, par=par,
                         uslot=uslot: e.matmul(
                             psU[par][:],
                             wslot[uslot][:, kc * DE + dc * 128:
                                          kc * DE + (dc + 1) * 128],
                             ln2b[:, kc * SUB + qc * 512:
                                  kc * SUB + (qc + 1) * 512],
                             start=(kc == 0), stop=(kc == FC - 1)),
                         sT if kc == FC - 1 else None, 1)
                u_t = sT.n
                p.wait("vector", sT, u_t)
                p.wait("vector", sA, silu_a[dcq])
                hpos = pe * 4 * SUB + dc * SUB + qc * 512
                if pe < 8:
                    p.op("vector", lambda e, dcq=dcq, par=par: e.tensor_mul(
                        t2b[par][:], psU[par][:],
                        gsall[:, dcq * 512:(dcq + 1) * 512]))
                    p.op("vector", lambda e, hpos=hpos, par=par:
                         e.tensor_mul(htil[:, hpos:hpos + 512], t2b[par][:],
                                      psM[:]), sV, 1)
                else:
                    p.op("vector", lambda e, dcq=dcq, par=par, hpos=hpos:
                         e.tensor_mul(htil[:, hpos:hpos + 512], psU[par][:],
                                      gsall[:, dcq * 512:(dcq + 1) * 512]),
                         sV, 1)
                up_ev[dcq] = sV.n
                if pe < 8 and dc == 3:
                    psM_free_v = sV.n
            slot_free_t[uslot] = sT.n
            up_prev_ev = {0: up_ev[6], 1: up_ev[7]}
            prev_up_ev_all = dict(up_ev)
        htil_v = sV.n

        if dbg and sub == 0:
            p.wait("gpsimd", sV, htil_v)
            p.op("gpsimd", lambda e: e.dma_start(
                out=dbg_htil[:], in_=htil[:, 0:2 * SUB * 4]), dG, 16)
        p.wait("tensor", sV, htil_v)
        dev_v = {}
        for dc in range(FC):
            half_rdy = []
            for hh in range(2):
                dsem, dval, dslot_scr_dG[hh] = fetch_piece(
                    dslot[hh], dwn[dc], dwn_sc[dc], sub, 20 * 128,
                    dslot_free_t[hh], dslot_scr_dG[hh],
                    src_off=hh * 20 * 128)
                half_rdy.append((dsem, dval))
            p.wait("tensor", half_rdy[0][0], half_rdy[0][1])
            p.wait("tensor", half_rdy[1][0], half_rdy[1][1])
            for qc in range(2):
                idx = dc * 2 + qc
                par = idx % 2
                p.wait("gpsimd", yst, x2r_free[par])
                p.op("gpsimd", lambda e, dc=dc, qc=qc, sub=sub, par=par:
                     e.dma_start(
                         out=x2r[par][:],
                         in_=x2_hbm[dc * 128:(dc + 1) * 128,
                                    sub * SUB + qc * 512:
                                    sub * SUB + (qc + 1) * 512]), dG, 16)
                x2r_ld = dG.n
                if idx >= 2:
                    p.wait("tensor", sV, dev_v[idx - 2])
                for pe in range(NE):
                    for cc in range(4):
                        blk = pe * 4 + cc
                        hh = blk // 20
                        lblk = blk % 20
                        first = (pe == 0 and cc == 0)
                        last = (pe == NE - 1 and cc == 3)
                        p.op("tensor",
                             lambda e, lblk=lblk, hh=hh, pe=pe, cc=cc, qc=qc,
                             par=par, first=first, last=last: e.matmul(
                                 psD[par][:],
                                 dslot[hh][:, lblk * 128:(lblk + 1) * 128],
                                 htil[:, pe * 4 * SUB + cc * SUB + qc * 512:
                                      pe * 4 * SUB + cc * SUB +
                                      (qc + 1) * 512],
                                 start=first, stop=last),
                             sT if last else None, 1)
                if qc == 1:
                    dslot_free_t[0] = sT.n
                    dslot_free_t[1] = sT.n
                dmm_t = sT.n
                p.wait("vector", sT, dmm_t)
                p.wait("vector", dG, x2r_ld)
                p.op("vector", lambda e, par=par: e.tensor_add(
                    x2r[par][:], psD[par][:], x2r[par][:]), sV, 1)
                dev_v[idx] = sV.n
                p.wait("gpsimd", sV, dev_v[idx])
                p.op("gpsimd", lambda e, dc=dc, qc=qc, sub=sub, par=par:
                     e.dma_start(
                         out=y[dc * 128:(dc + 1) * 128,
                               sub * SUB + qc * 512:
                               sub * SUB + (qc + 1) * 512],
                         in_=x2r[par][:]), yst, 16)
                x2r_free[par] = yst.n
        psD_free_v = dev_v[2 * FC - 1]
        ln2b_free_t = sT.n
        stats_mm_done = sT.n

    p.wait("gpsimd", yst, yst.n)
    p.wait("gpsimd", dG, dG.n)

    with nc.Block() as blk:
        p.flush(blk)
    esC.close()
    es.close()
    return nc


_NC_CACHE = {}


def _get_nc():
    if "nc" not in _NC_CACHE:
        _NC_CACHE["nc"] = build_nc()
    return _NC_CACHE["nc"]


def _mk_sel():
    s = np.zeros((8, 8 * 128), np.float32)
    for e in range(8):
        s[e, e * 128:(e + 1) * 128] = 1.0
    return s.astype(ml_dtypes.bfloat16)


def _prep_inputs(inputs):
    bf = ml_dtypes.bfloat16
    x = np.asarray(inputs["L0_attn_x"])

    def tile_gu(w):
        return np.ascontiguousarray(
            w.reshape(8, 128, DE).transpose(1, 0, 2).reshape(128, 8 * DE))

    eg = np.asarray(inputs["L0_ffn_experts_gate"])
    eu = np.asarray(inputs["L0_ffn_experts_up"])
    ed = np.asarray(inputs["L0_ffn_experts_down"])
    sg = np.asarray(inputs["L0_ffn_shared_gate"])
    su = np.asarray(inputs["L0_ffn_shared_up"])
    sd = np.asarray(inputs["L0_ffn_shared_down"])
    gup = np.zeros((NE, 2, 128, FC * DE), np.float32)
    for e in range(8):
        gup[e, 0] = tile_gu(eg[e])
        gup[e, 1] = tile_gu(eu[e])
    gup[8, 0] = tile_gu(sg[:, 0:DE])
    gup[8, 1] = tile_gu(su[:, 0:DE])
    gup[9, 0] = tile_gu(sg[:, DE:2 * DE])
    gup[9, 1] = tile_gu(su[:, DE:2 * DE])

    downs = [ed[e] for e in range(8)] + [sd[0:DE, :], sd[DE:2 * DE, :]]
    dwn = np.zeros((FC, 128, NE * 4 * 128), np.float32)
    for dc in range(FC):
        blocks = []
        for pe in range(NE):
            w = downs[pe]
            for cc in range(4):
                blocks.append(w[cc * 128:(cc + 1) * 128,
                                dc * 128:(dc + 1) * 128])
        dwn[dc] = np.concatenate(blocks, axis=1)

    common = dict(
        wqT=np.ascontiguousarray(np.asarray(inputs["L0_attn_Wq"]).T),
        wkT=np.ascontiguousarray(np.asarray(inputs["L0_attn_Wk"]).T),
        wvT=np.ascontiguousarray(np.asarray(inputs["L0_attn_Wv"]).T),
        woT=np.ascontiguousarray(np.asarray(inputs["L0_attn_Wo"]).T),
        wavel=np.ascontiguousarray(np.asarray(inputs["L0_attn_wavelengths"])),
        pbias=np.ascontiguousarray(np.asarray(inputs["L0_attn_phase_bias"])),
        bq=np.ascontiguousarray(np.asarray(inputs["L0_attn_bq"])),
        bk=np.ascontiguousarray(np.asarray(inputs["L0_attn_bk"])),
        bv=np.ascontiguousarray(np.asarray(inputs["L0_attn_bv"])),
        bo=np.ascontiguousarray(np.asarray(inputs["L0_attn_bo"])),
        router=np.ascontiguousarray(np.asarray(inputs["L0_ffn_router"])),
        gup=gup, dwn=dwn,
        eye_in=np.eye(128, dtype=np.float32).astype(bf),
        sel_in=_mk_sel(),
    )
    per_core = []
    for b in range(8):
        m = dict(common)
        m["x_t"] = np.ascontiguousarray(np.asarray(x[b]).T)
        per_core.append(m)
    return per_core


def kernel(**inputs):
    nc = _get_nc()
    in_maps = _prep_inputs(inputs)
    res = run_bass_kernel_spmd(nc, in_maps, core_ids=list(range(8)))
    out = np.empty((8, S, D), np.float32)
    for b in range(8):
        out[b] = np.asarray(res.results[b]["y"]).T
    return out


# version 05
# speedup vs baseline: 1.0375x; 1.0375x over previous
import contextlib
import math

import numpy as np
import ml_dtypes

import concourse.bass as bass
import concourse.mybir as mybir
from concourse.bass_utils import run_bass_kernel_spmd

F32 = mybir.dt.float32
F16 = mybir.dt.float16
BF16 = mybir.dt.bfloat16
I32 = mybir.dt.int32
ALU = mybir.AluOpType
AF = mybir.ActivationFunctionType
AX = mybir.AxisListType.X

D = 1024
S = 4096
H = 128
FC = D // 128
NE = 10
DE = 512
NSUB = 4
SUB = S // NSUB

TWO_PI = float(2.0 * math.pi)
INV_SQRT_H = 1.0 / math.sqrt(H)
EXP_SHIFT = -float(math.sqrt(H))
ROUND_OFS = 128.0


class Sem:
    def __init__(self, h, is_dma=False):
        self.h = h
        self.is_dma = is_dma
        self.n = 0

    def bump(self, k):
        self.n += k
        return self.n


class Prog:
    ENGS = ("sync", "vector", "scalar", "tensor", "gpsimd")

    def __init__(self):
        self.q = {e: [] for e in self.ENGS}

    def op(self, eng, build, sem=None, k=1):
        if sem is not None:
            v = sem.bump(k)
            if sem.is_dma:
                self.q[eng].append(
                    lambda e, b=build, s=sem, kk=k: b(e).then_inc(s.h, kk))
            else:
                def _f(e, b=build, s=sem, kk=k):
                    b(e)
                    e.drain(fusable=False).then_inc(s.h, kk)
                self.q[eng].append(_f)
            return v
        self.q[eng].append(build)
        return None

    def wait(self, eng, sem, val):
        if val:
            self.q[eng].append(lambda e, s=sem, v=val: e.wait_ge(s.h, v))

    def flush(self, block):
        for eng in self.ENGS:
            ops = self.q[eng]
            if not ops:
                continue

            def body(engine, ops=ops):
                for f in ops:
                    f(engine)

            getattr(block, eng)(body)
        self.q = {e: [] for e in self.ENGS}


def build_nc(dbg=False):
    nc = bass.Bass("TRN2", target_bir_lowering=False,
                   detect_race_conditions=False)

    x_t = nc.dram_tensor("x_t", [D, S], F32, kind="ExternalInput")
    wqT = nc.dram_tensor("wqT", [D, H], F32, kind="ExternalInput")
    wkT = nc.dram_tensor("wkT", [D, H], F32, kind="ExternalInput")
    wvT = nc.dram_tensor("wvT", [D, H], F32, kind="ExternalInput")
    woT = nc.dram_tensor("woT", [H, D], F32, kind="ExternalInput")
    wavel = nc.dram_tensor("wavel", [H], F32, kind="ExternalInput")
    pbias = nc.dram_tensor("pbias", [H], F32, kind="ExternalInput")
    bq_d = nc.dram_tensor("bq", [H], F32, kind="ExternalInput")
    bk_d = nc.dram_tensor("bk", [H], F32, kind="ExternalInput")
    bv_d = nc.dram_tensor("bv", [H], F32, kind="ExternalInput")
    bo_d = nc.dram_tensor("bo", [D], F32, kind="ExternalInput")
    router = nc.dram_tensor("router", [D, 8], F32, kind="ExternalInput")
    gup = nc.dram_tensor("gup", [NE, 2, 128, FC * DE], F32,
                         kind="ExternalInput")
    dwn = nc.dram_tensor("dwn", [FC, 128, NE * 4 * 128], F32,
                         kind="ExternalInput")
    eye_in = nc.dram_tensor("eye_in", [128, 128], BF16, kind="ExternalInput")
    sel_in = nc.dram_tensor("sel_in", [8, 8 * 128], BF16, kind="ExternalInput")

    y = nc.dram_tensor("y", [D, S], F32, kind="ExternalOutput")

    x2_hbm = nc.dram_tensor("x2_hbm", [D, S], F32,
                            kind="ExternalOutput" if dbg else "Internal")
    if dbg:
        dbg_t = {n: nc.dram_tensor("dbg_" + n, [128, S], F16,
                                   kind="ExternalOutput")
                 for n in ("cosq", "sinq", "cosk", "sink")}
        dbg_t["vf"] = nc.dram_tensor("dbg_vf", [128, S], BF16,
                                     kind="ExternalOutput")
        dbg_us = {n: nc.dram_tensor("dbg_" + n, [128, S], F16,
                                    kind="ExternalOutput")
                  for n in ("usq", "ucq")}
        dbg_coef = nc.dram_tensor("dbg_coef", [8, S], BF16,
                                  kind="ExternalOutput")
        dbg_htil = nc.dram_tensor("dbg_htil", [128, 2 * SUB * 4], BF16,
                                  kind="ExternalOutput")
        dbg_ln2b = nc.dram_tensor("dbg_ln2b", [128, FC * SUB], BF16,
                                  kind="ExternalOutput")
        dbg_rt = nc.dram_tensor("dbg_rt", [128, 64], F32,
                                kind="ExternalOutput")
        dbg_rt2 = nc.dram_tensor("dbg_rt2", [128, 40 * 8], F32,
                                 kind="ExternalOutput")
    gup_sc = nc.dram_tensor("gup_sc", [NE, 2, 128, FC * DE], BF16)
    dwn_sc = nc.dram_tensor("dwn_sc", [FC, 128, NE * 4 * 128], BF16)

    es = contextlib.ExitStack()

    def SB(name, shape, dt):
        return es.enter_context(nc.sbuf_tensor(name, shape, dt))

    def SEM(name, is_dma=False):
        return Sem(es.enter_context(nc.semaphore(name)), is_dma)

    eyeb = SB("eyeb", [128, 128], BF16)
    ones_b = SB("ones_b", [128, 128], BF16)
    ones_f = SB("ones_f", [128, 128], F32)
    c2pi = SB("c2pi", [128, 1], F32)
    ceps = SB("ceps", [128, 1], F32)
    cnhalf = SB("cnhalf", [128, 1], F32)
    cnone = SB("cnone", [128, 1], F32)
    cinvsq = SB("cinvsq", [128, 1], F32)
    cshift = SB("cshift", [128, 1], F32)
    invw = SB("invw", [128, 1], F32)
    s1c = SB("s1c", [128, 1], F32)
    s2q = SB("s2q", [128, 1], F32)
    s2k = SB("s2k", [128, 1], F32)
    sqc = SB("sqc", [128, 1], F32)
    skc = SB("skc", [128, 1], F32)
    svc = SB("svc", [128, 1], F32)
    bvb = SB("bvb", [128, 1], BF16)
    bqf = SB("bqf", [128, 1], F32)
    bkf = SB("bkf", [128, 1], F32)
    bvf = SB("bvf", [128, 1], F32)
    wlf = SB("wlf", [128, 1], F32)
    pbf = SB("pbf", [128, 1], F32)
    bo8 = SB("bo8", [128, FC], F32)
    boP = SB("boP", [128, FC], F32)
    tcol = SB("tcol", [128, 1], F32)
    wqTb = SB("wqTb", [128, FC * 128], BF16)
    wkTb = SB("wkTb", [128, FC * 128], BF16)
    wvTb = SB("wvTb", [128, FC * 128], BF16)
    woTb = SB("woTb", [128, FC * 128], BF16)
    r_sb = SB("r_sb", [128, FC * 8], F32)
    r_hi = SB("r_hi", [128, FC * 8], BF16)
    r_lo = SB("r_lo", [128, FC * 8], BF16)
    sr_rep = SB("sr_rep", [128, 8], F32)

    dW = SEM("dW", is_dma=True)
    dG = SEM("dG", is_dma=True)
    yst = SEM("yst", is_dma=True)
    sT = SEM("sT")
    sV = SEM("sV")
    sA = SEM("sA")

    p = Prog()

    def vdrain():
        p.op("vector", lambda e: e.drain(fusable=False))

    esAB = contextlib.ExitStack()

    def SBab(name, shape, dt):
        return esAB.enter_context(nc.sbuf_tensor(name, shape, dt))

    cosq = SBab("cosq", [128, S], F16)
    sinq = SBab("sinq", [128, S], F16)
    cosk = SBab("cosk", [128, S], F16)
    sink = SBab("sink", [128, S], F16)
    vf = SBab("vf", [128, S], BF16)

    esA = contextlib.ExitStack()

    def SBa(name, shape, dt):
        return esA.enter_context(nc.sbuf_tensor(name, shape, dt))

    def PSa(name, shape, dt=F32):
        return esA.enter_context(nc.psum_tensor(name, shape, dt))

    CH = 512
    NCH = S // CH
    xfA = [SBa(f"xfA{i}", [128, FC * CH], F32) for i in range(2)]
    xbA = [SBa(f"xbA{i}", [128, FC * CH], BF16) for i in range(2)]
    xsqA = [SBa(f"xsqA{i}", [128, FC * CH], BF16) for i in range(2)]
    stat1 = SBa("stat1", [128, CH], F32)
    stat2 = SBa("stat2", [128, CH], F32)
    rrep = SBa("rrep", [128, CH], F32)
    nmr = SBa("nmr", [128, CH], F32)
    ga = SBa("ga", [128, CH], F32)
    gb = SBa("gb", [128, CH], F32)
    gc = SBa("gc", [128, CH], F32)
    gi = SBa("gi", [128, CH], I32)
    usq = SBa("usq", [128, S], F16)
    ucq = SBa("ucq", [128, S], F16)
    usk = SBa("usk", [128, S], F16)
    uck = SBa("uck", [128, S], F16)
    wstage = SBa("wstage", [128, FC * 128], F32)

    psMean = PSa("psMean", [128, CH])
    psSq = PSa("psSq", [128, CH])
    psQ = [PSa(f"psQ{i}", [128, CH]) for i in range(2)]
    psK = [PSa(f"psK{i}", [128, CH]) for i in range(2)]
    psV = [PSa(f"psV{i}", [128, CH]) for i in range(2)]

    p.op("sync", lambda e: e.dma_start(out=eyeb[:], in_=eye_in[:]), dW, 16)
    for col, src in ((wlf, wavel), (pbf, pbias), (bqf, bq_d), (bkf, bk_d),
                     (bvf, bv_d)):
        p.op("sync", lambda e, c=col, s=src: e.dma_start(
            out=c[:], in_=bass.AP(s, 0, [[1, 128], [1, 1]])), dW, 16)
    for dc in range(FC):
        p.op("sync", lambda e, dc=dc: e.dma_start(
            out=bo8[:, dc:dc + 1],
            in_=bass.AP(bo_d, dc * 128, [[1, 128], [1, 1]])), dW, 16)
    for kc in range(FC):
        p.op("sync", lambda e, kc=kc: e.dma_start(
            out=r_sb[:, kc * 8:(kc + 1) * 8],
            in_=router[kc * 128:(kc + 1) * 128, :]), dW, 16)
    small_loads = dW.n

    p.op("vector", lambda e: e.memset(ones_b[:], 1.0))
    p.op("vector", lambda e: e.memset(ones_f[:], 1.0))
    p.op("vector", lambda e: e.memset(c2pi[:], TWO_PI))
    p.op("vector", lambda e: e.memset(ceps[:], 1e-5))
    p.op("vector", lambda e: e.memset(cnhalf[:], -0.5))
    p.op("vector", lambda e: e.memset(cnone[:], -1.0))
    p.op("vector", lambda e: e.memset(cinvsq[:], INV_SQRT_H))
    p.op("vector", lambda e: e.memset(cshift[:], EXP_SHIFT))
    p.wait("vector", dW, small_loads)
    p.op("vector", lambda e: e.tensor_scalar(tcol[:], wlf[:], 1e-8, None,
                                             ALU.add))
    vdrain()
    p.op("vector", lambda e: e.reciprocal(invw[:], tcol[:]))
    vdrain()
    p.op("vector", lambda e: e.tensor_scalar(s1c[:], invw[:], 1.0 / TWO_PI,
                                             None, ALU.mult))
    for bcol, s2 in ((bqf, s2q), (bkf, s2k)):
        p.op("vector", lambda e, b=bcol, s=s2: e.tensor_mul(s[:], b[:],
                                                            invw[:]))
        vdrain()
        p.op("vector", lambda e, s=s2: e.tensor_add(s[:], s[:], pbf[:]))
        vdrain()
        p.op("vector", lambda e, s=s2: e.tensor_scalar(
            s[:], s[:], 1.0 / TWO_PI, ROUND_OFS, ALU.mult, ALU.add))
    p.op("vector", lambda e: e.tensor_copy(bvb[:], bvf[:]), sV, 1)

    for wi, (wsrc, wdst) in enumerate(((wqT, wqTb), (wkT, wkTb), (wvT, wvTb),
                                       (woT, woTb))):
        p.wait("sync", sV, sV.n)
        for kc in range(FC):
            if wi < 3:
                src = wsrc[kc * 128:(kc + 1) * 128, :]
            else:
                src = wsrc[:, kc * 128:(kc + 1) * 128]
            p.op("sync", lambda e, s=src, kc=kc: e.dma_start(
                out=wstage[:, kc * 128:(kc + 1) * 128], in_=s), dW, 16)
        p.wait("vector", dW, dW.n)
        p.op("vector", lambda e, d=wdst: e.tensor_copy(d[:], wstage[:]), sV, 1)
    wcast_v = sV.n

    p.wait("tensor", sV, wcast_v)
    for wsb, col in ((wqTb, sqc), (wkTb, skc), (wvTb, svc)):
        for kc in range(FC):
            p.op("tensor", lambda e, w=wsb, kc=kc: e.matmul(
                psMean[:, 0:1], w[:, kc * 128:(kc + 1) * 128], ones_b[:, 0:1],
                start=(kc == 0), stop=(kc == FC - 1)),
                sT if kc == FC - 1 else None, 1)
        p.wait("vector", sT, sT.n)
        p.op("vector", lambda e, c=col: e.tensor_copy(c[:], psMean[:, 0:1]),
             sV, 1)
        p.wait("tensor", sV, sV.n)
    for dc in range(FC):
        p.op("tensor", lambda e, dc=dc: e.matmul(
            psSq[:, dc:dc + 1], woTb[:, dc * 128:(dc + 1) * 128], bvb[:, 0:1],
            start=True, stop=True), sT if dc == FC - 1 else None, 1)
    p.wait("vector", sT, sT.n)
    p.op("vector", lambda e: e.tensor_add(boP[:], psSq[:, 0:FC], bo8[:]),
         sV, 1)
    p.op("vector", lambda e: e.tensor_copy(r_hi[:], r_sb[:]))
    vdrain()
    p.op("vector", lambda e: e.tensor_sub(r_lo[:], r_sb[:], r_hi[:]), sV, 1)
    p.wait("tensor", sV, sV.n)
    for pas, rp in ((0, r_hi), (1, r_lo)):
        for kc in range(FC):
            p.op("tensor", lambda e, kc=kc, rp=rp, pas=pas: e.matmul(
                psMean[:, 0:8], ones_b[:], rp[:, kc * 8:(kc + 1) * 8],
                start=(kc == 0 and pas == 0),
                stop=(kc == FC - 1 and pas == 1)),
                sT if (kc == FC - 1 and pas == 1) else None, 1)
    p.wait("vector", sT, sT.n)
    p.op("vector", lambda e: e.tensor_copy(sr_rep[:], psMean[:, 0:8]), sV, 1)
    prol_v = sV.n

    x_loads = [0] * NCH
    cast_v = [0] * NCH
    cons_v = [0] * NCH
    qkv_t = [0] * NCH
    stats_used_v = 0

    for c in range(NCH):
        par = c % 2
        if c >= 2:
            p.wait("sync", sV, cons_v[c - 2])
        for fc in range(FC):
            p.op("sync", lambda e, c=c, fc=fc, par=par: e.dma_start(
                out=xfA[par][:, fc * CH:(fc + 1) * CH],
                in_=x_t[fc * 128:(fc + 1) * 128, c * CH:(c + 1) * CH]),
                dW, 16)
        x_loads[c] = dW.n

        p.wait("vector", dW, x_loads[c])
        if c >= 2:
            p.wait("vector", sT, qkv_t[c - 2])
        p.op("vector", lambda e, par=par: e.tensor_copy(xbA[par][:],
                                                        xfA[par][:]))
        p.op("vector", lambda e, par=par: e.tensor_mul(
            xsqA[par][:], xbA[par][:], xbA[par][:]), sV, 1)
        cast_v[c] = sV.n

        p.wait("tensor", sV, cast_v[c])
        p.wait("tensor", sV, stats_used_v if c > 0 else prol_v)
        for kc in range(FC):
            p.op("tensor", lambda e, par=par, kc=kc: e.matmul(
                psMean[:], ones_b[:], xbA[par][:, kc * CH:(kc + 1) * CH],
                start=(kc == 0), stop=(kc == FC - 1)))
        for kc in range(FC):
            p.op("tensor", lambda e, par=par, kc=kc: e.matmul(
                psSq[:], ones_b[:], xsqA[par][:, kc * CH:(kc + 1) * CH],
                start=(kc == 0), stop=(kc == FC - 1)),
                sT if kc == FC - 1 else None, 1)
        stats_t = sT.n

        if c >= 2:
            p.wait("tensor", sV, cons_v[c - 2])
        for wsb, dst, last in ((wqTb, psQ[par], False), (wkTb, psK[par], False),
                               (wvTb, psV[par], True)):
            for kc in range(FC):
                p.op("tensor", lambda e, w=wsb, d=dst, par=par, kc=kc:
                     e.matmul(
                         d[:], w[:, kc * 128:(kc + 1) * 128],
                         xbA[par][:, kc * CH:(kc + 1) * CH],
                         start=(kc == 0), stop=(kc == FC - 1)),
                     sT if (kc == FC - 1 and last) else None, 1)
        qkv_t[c] = sT.n

        p.wait("vector", sT, stats_t)
        p.op("vector", lambda e: e.tensor_scalar(stat1[:], psMean[:], 1.0 / D,
                                                 None, ALU.mult))
        p.op("vector", lambda e: e.tensor_scalar(stat2[:], psSq[:], 1.0 / D,
                                                 None, ALU.mult))
        p.op("vector", lambda e: e.scalar_tensor_tensor(
            ga[:], stat1[:], -1.0, stat1[:], ALU.mult, ALU.mult))
        p.op("vector", lambda e: e.scalar_tensor_tensor(
            gb[:], stat2[:], 1e-5, ga[:], ALU.add, ALU.add), sV, 1)
        stats_used_v = sV.n
        p.wait("scalar", sV, stats_used_v)
        p.op("scalar", lambda e: e.activation(ga[:], gb[:], AF.Ln), sA, 1)
        p.wait("vector", sA, sA.n)
        p.op("vector", lambda e: e.tensor_scalar(
            gb[:], ga[:], -0.5, None, ALU.mult), sV, 1)
        p.wait("scalar", sV, sV.n)
        p.op("scalar", lambda e: e.activation(rrep[:], gb[:], AF.Exp), sA, 1)
        p.wait("vector", sA, sA.n)
        p.op("vector", lambda e: e.scalar_tensor_tensor(
            nmr[:], stat1[:], -1.0, rrep[:], ALU.mult, ALU.mult))

        p.wait("vector", sT, qkv_t[c])
        sl = slice(c * CH, (c + 1) * CH)
        for psX, s2x, sxc, us_f, uc_f in ((psQ[par], s2q, sqc, usq, ucq),
                                          (psK[par], s2k, skc, usk, uck)):
            p.op("vector", lambda e, psX=psX: e.tensor_mul(ga[:], psX[:],
                                                           rrep[:]))
            p.op("vector", lambda e, sxc=sxc: e.scalar_tensor_tensor(
                gb[:], nmr[:], sxc[:], ga[:], ALU.mult, ALU.add))
            p.op("vector", lambda e, s2x=s2x: e.tensor_scalar(
                ga[:], gb[:], s1c[:], s2x[:], ALU.mult, ALU.add))
            p.op("vector", lambda e: e.tensor_copy(gi[:], ga[:]))
            p.op("vector", lambda e: e.tensor_copy(gb[:], gi[:]))
            p.op("vector", lambda e: e.tensor_sub(ga[:], ga[:], gb[:]))
            p.op("vector", lambda e: e.tensor_scalar(gb[:], ga[:], 0.5, None,
                                                     ALU.is_gt))
            p.op("vector", lambda e: e.scalar_tensor_tensor(
                gc[:], gb[:], -1.0, ga[:], ALU.mult, ALU.add))
            p.op("vector", lambda e, u=us_f, sl=sl: e.tensor_scalar(
                u[:, sl], gc[:], TWO_PI, None, ALU.mult))
            p.op("vector", lambda e: e.tensor_scalar(
                ga[:], gc[:], 0.25, None, ALU.add))
            p.op("vector", lambda e: e.tensor_scalar(gb[:], ga[:], 0.5, None,
                                                     ALU.is_gt))
            p.op("vector", lambda e: e.scalar_tensor_tensor(
                gc[:], gb[:], -1.0, ga[:], ALU.mult, ALU.add))
            p.op("vector", lambda e, u=uc_f, sl=sl: e.tensor_scalar(
                u[:, sl], gc[:], TWO_PI, None, ALU.mult), sV, 1)
        p.op("vector", lambda e, par=par: e.tensor_mul(ga[:], psV[par][:],
                                                       rrep[:]))
        p.op("vector", lambda e, sl=sl: e.scalar_tensor_tensor(
            vf[:, sl], nmr[:], svc[:], ga[:], ALU.mult, ALU.add), sV, 1)
        cons_v[c] = sV.n

    p.wait("scalar", sV, cons_v[NCH - 1])
    for src, dst in ((usq, sinq), (ucq, cosq), (usk, sink), (uck, cosk)):
        p.op("scalar", lambda e, s=src, d=dst: e.activation(
            d[:], s[:], AF.Sin), sA, 1)
    p.wait("vector", sA, sA.n)
    for t in (cosq, sinq):
        p.op("vector", lambda e, t=t: e.tensor_scalar(
            t[:], t[:], INV_SQRT_H, None, ALU.mult), sV, 1)
    sinscale_v = sV.n
    trig_a = sA.n
    if dbg:
        p.wait("gpsimd", sA, trig_a)
        p.wait("gpsimd", sV, sinscale_v)
        for n, t in (("cosq", cosq), ("sinq", sinq), ("cosk", cosk),
                     ("sink", sink), ("vf", vf)):
            p.op("gpsimd", lambda e, n=n, t=t: e.dma_start(
                out=dbg_t[n][:], in_=t[:]), dG, 16)
        for n, t in (("usq", usq), ("ucq", ucq)):
            p.op("gpsimd", lambda e, n=n, t=t: e.dma_start(
                out=dbg_us[n][:], in_=t[:]), dG, 16)
        p.wait("gpsimd", dG, dG.n)

    with nc.Block() as blk:
        p.flush(blk)
    esA.close()

    esB = contextlib.ExitStack()
    v_tm = esB.enter_context(nc.sbuf_tensor("v_tm", [128, S], BF16))

    esB0 = contextlib.ExitStack()
    psTrV = [esB0.enter_context(nc.psum_tensor(f"psTrV{i}", [128, 128], BF16))
             for i in range(2)]
    tr_cons = [0, 0]
    for kt in range(S // 128):
        par = kt % 2
        if kt >= 2:
            p.wait("tensor", sV, tr_cons[par])
        p.op("tensor", lambda e, kt=kt, par=par: e.transpose(
            psTrV[par][:], vf[:, kt * 128:(kt + 1) * 128], eyeb[:]), sT, 1)
        p.wait("vector", sT, sT.n)
        p.op("vector", lambda e, kt=kt, par=par: e.tensor_copy(
            v_tm[:, kt * 128:(kt + 1) * 128], psTrV[par][:]), sV, 1)
        tr_cons[par] = sV.n
    with nc.Block() as blk:
        p.flush(blk)
    esB0.close()

    esB1 = contextlib.ExitStack()

    def SBb(name, shape, dt):
        return esB1.enter_context(nc.sbuf_tensor(name, shape, dt))

    def PSb(name, shape, dt=F32):
        return esB1.enter_context(nc.psum_tensor(name, shape, dt))

    QC = 512
    NQ = S // QC
    NKT = S // 128
    xfB = [SBb(f"xfB{i}", [128, FC * QC], F32) for i in range(2)]
    ptb = [SBb(f"ptb{i}", [128, 1024], BF16) for i in range(2)]
    attnT = [SBb(f"attnT{i}", [128, QC], BF16) for i in range(2)]
    lnS = [SBb(f"lnS{i}", [128, QC], F32) for i in range(2)]
    rcp = [SBb(f"rcp{i}", [128, QC], F32) for i in range(2)]
    psSc = [PSb(f"psSc{i}", [128, 1024]) for i in range(2)]
    psPV = [PSb(f"psPV{i}", [128, QC]) for i in range(2)]
    psSm = [PSb(f"psSm{i}", [128, QC]) for i in range(2)]

    xB_loads = [0] * NQ
    x2_st = [0] * NQ
    wo_ev_all = [0] * NQ
    prev_exp = {0: 0, 1: 0}
    prev_spv = {0: 0, 1: 0}
    attn_overwrite_t = {0: 0, 1: 0}

    for q in range(NQ):
        qpar = q % 2
        qs = slice(q * QC, (q + 1) * QC)
        if q >= 2:
            p.wait("gpsimd", dG, x2_st[q - 2])
        for fc in range(FC):
            p.op("gpsimd", lambda e, q=q, fc=fc, qpar=qpar: e.dma_start(
                out=xfB[qpar][:, fc * QC:(fc + 1) * QC],
                in_=x_t[fc * 128:(fc + 1) * 128, q * QC:(q + 1) * QC]),
                dG, 16)
        xB_loads[q] = dG.n

        if q >= 1:
            p.wait("tensor", sV, wo_ev_all[q - 1])
        exp_a = [0] * (NKT // 2)
        spv_t = [0] * (NKT // 2)
        for pr in range(NKT // 2):
            par = pr % 2
            p.wait("tensor", sA, exp_a[pr - 2] if pr >= 2 else prev_exp[par])
            for half in range(2):
                kt = pr * 2 + half
                ks = slice(kt * 128, (kt + 1) * 128)
                hs = slice(half * 512, (half + 1) * 512)
                p.op("tensor", lambda e, ks=ks, hs=hs, par=par, qs=qs:
                     e.matmul(psSc[par][:, hs], cosk[:, ks], cosq[:, qs],
                              start=True, stop=False))
                p.op("tensor", lambda e, ks=ks, hs=hs, par=par, qs=qs:
                     e.matmul(psSc[par][:, hs], sink[:, ks], sinq[:, qs],
                              start=False, stop=True),
                     sT if half == 1 else None, 1)
            sc_t = sT.n
            p.wait("scalar", sT, sc_t)
            p.wait("scalar", sT, spv_t[pr - 2] if pr >= 2 else prev_spv[par])
            p.op("scalar", lambda e, par=par: e.activation(
                ptb[par][:], psSc[par][:], AF.Exp), sA, 1)
            exp_a[pr] = sA.n
            p.wait("tensor", sA, exp_a[pr])
            for half in range(2):
                kt = pr * 2 + half
                hs = slice(half * 512, (half + 1) * 512)
                p.op("tensor", lambda e, par=par, hs=hs, qpar=qpar, kt=kt:
                     e.matmul(psSm[qpar][:], ones_b[:], ptb[par][:, hs],
                              start=(kt == 0), stop=(kt == NKT - 1)))
                p.op("tensor", lambda e, par=par, hs=hs, qpar=qpar, kt=kt:
                     e.matmul(psPV[qpar][:], v_tm[:, kt * 128:(kt + 1) * 128],
                              ptb[par][:, hs],
                              start=(kt == 0), stop=(kt == NKT - 1)),
                     sT if kt == NKT - 1 else None, 1)
            spv_t[pr] = sT.n
        prev_exp = {0: exp_a[NKT // 2 - 2], 1: exp_a[NKT // 2 - 1]}
        prev_spv = {0: spv_t[NKT // 2 - 2], 1: spv_t[NKT // 2 - 1]}
        pv_t = sT.n

        p.wait("scalar", sT, pv_t)
        p.op("scalar", lambda e, qpar=qpar: e.activation(
            lnS[qpar][:], psSm[qpar][:], AF.Ln), sA, 1)
        p.wait("vector", sA, sA.n)
        p.op("vector", lambda e, qpar=qpar: e.tensor_scalar(
            lnS[qpar][:], lnS[qpar][:], -1.0, None, ALU.mult), sV, 1)
        p.wait("scalar", sV, sV.n)
        p.op("scalar", lambda e, qpar=qpar: e.activation(
            rcp[qpar][:], lnS[qpar][:], AF.Exp), sA, 1)
        rcp_a = sA.n
        p.wait("vector", sA, rcp_a)
        p.wait("vector", sT, attn_overwrite_t[qpar])
        p.op("vector", lambda e, qpar=qpar: e.tensor_mul(
            attnT[qpar][:], psPV[qpar][:], rcp[qpar][:]), sV, 1)
        attn_v = sV.n

        p.wait("tensor", sV, attn_v)
        p.wait("vector", dG, xB_loads[q])
        wo_ev = [0] * FC
        for dc in range(FC):
            wo_ps = psPV[1 - qpar] if dc % 2 == 0 else psSm[qpar]
            if dc >= 2:
                p.wait("tensor", sV, wo_ev[dc - 2])
            p.op("tensor", lambda e, dc=dc, ps_=wo_ps, qpar=qpar: e.matmul(
                ps_[:], woTb[:, dc * 128:(dc + 1) * 128], attnT[qpar][:],
                start=True, stop=True), sT, 1)
            mm_t = sT.n
            p.wait("vector", sT, mm_t)
            p.op("vector", lambda e, dc=dc, ps_=wo_ps, qpar=qpar:
                 e.scalar_tensor_tensor(
                     xfB[qpar][:, dc * QC:(dc + 1) * QC], ps_[:],
                     boP[:, dc:dc + 1],
                     xfB[qpar][:, dc * QC:(dc + 1) * QC],
                     ALU.add, ALU.add), sV, 1)
            wo_ev[dc] = sV.n
            if dc == FC - 1:
                attn_overwrite_t[qpar] = mm_t
        wo_ev_all[q] = sV.n
        p.wait("gpsimd", sV, wo_ev_all[q])
        for fc in range(FC):
            p.op("gpsimd", lambda e, q=q, fc=fc, qpar=qpar: e.dma_start(
                out=x2_hbm[fc * 128:(fc + 1) * 128, q * QC:(q + 1) * QC],
                in_=xfB[qpar][:, fc * QC:(fc + 1) * QC]), dG, 16)
        x2_st[q] = dG.n

    with nc.Block() as blk:
        p.flush(blk)
    esB1.close()
    esB.close()
    esAB.close()

    esC = contextlib.ExitStack()

    def SBc(name, shape, dt):
        return esC.enter_context(nc.sbuf_tensor(name, shape, dt))

    def PSc(name, shape, dt=F32):
        return esC.enter_context(nc.psum_tensor(name, shape, dt))

    CC = 256
    NCC = SUB // CC
    htil = SBc("htil", [128, NE * SUB * 4], BF16)
    ln2b = SBc("ln2b", [128, FC * SUB], BF16)
    x2f = [SBc(f"x2f{i}", [128, FC * CC], F32) for i in range(2)]
    xb2 = SBc("xb2", [128, FC * CC], BF16)
    x2lo = SBc("x2lo", [128, FC * CC], BF16)
    xsq2 = SBc("xsq2", [128, FC * CC], BF16)
    st1C = SBc("st1C", [128, CC], F32)
    st2C = SBc("st2C", [128, CC], F32)
    rrepC = SBc("rrepC", [128, CC], F32)
    nmrC = SBc("nmrC", [128, CC], F32)
    gaC = SBc("gaC", [128, CC], F32)
    gbC = SBc("gbC", [128, CC], F32)
    coefT = SBc("coefT", [8, SUB], BF16)
    sel8 = SBc("sel8", [8, 8 * 128], BF16)
    c8m = SBc("c8m", [128, 8], F32)
    c8q = SBc("c8q", [128, 8], F32)
    c8a = SBc("c8a", [128, 8], F32)
    c8b = SBc("c8b", [128, 8], F32)
    c8c = SBc("c8c", [128, 8], F32)
    rstd8 = SBc("rstd8", [128, 8], F32)
    nmr8 = SBc("nmr8", [128, 8], F32)
    r64a = SBc("r64a", [128, 64], F32)
    r64b = SBc("r64b", [128, 64], F32)
    coefb64 = SBc("coefb64", [128, 64], BF16)
    wslot = [SBc(f"wslot{i}", [128, FC * DE], BF16) for i in range(2)]
    wstgC = SBc("wstgC", [128, 1536], F32)
    dslot = [SBc(f"dslot{i}", [128, NE * 4 * 128], BF16) for i in range(2)]
    gsall = SBc("gsall", [128, 8 * 512], BF16)
    t2b = [SBc("t2b0", [128, 512], F32)] * 2
    x2r = [SBc(f"x2r{i}", [128, 512], F32) for i in range(2)]


    psG = [PSc(f"psG{i}", [128, 512]) for i in range(2)]
    psU = [PSc(f"psU{i}", [128, 512]) for i in range(2)]
    psD = [PSc(f"psD{i}", [128, 512]) for i in range(2)]
    psM = PSc("psM", [128, 512])
    psTr = PSc("psTr", [8, 128], BF16)

    wstg_free_v = 0
    slot_free_t = [0, 0, 0]
    slot_scr_dG = [0, 0, 0]
    dslot_free_t = [0, 0]
    dslot_scr_dG = [0, 0]
    x2f_cons = [0, 0]
    x2f_cons_v = [0, 0]
    x2r_free = [0, 0]
    ln2b_free_t = 0
    psM_free_v = 0
    psD_free_v = 0
    gu_prev_silu = {0: 0, 1: 0}
    up_prev_ev = {0: 0, 1: 0}
    prev_up_ev_all = {i: 0 for i in range(8)}

    def fetch_piece(dst_sb, src_f32, src_bf, sub, length, slot_free_t,
                    scr_done_dG, src_off=0):
        nonlocal wstg_free_v
        if sub == 0:
            nh = (length + 1535) // 1536
            for hh in range(nh):
                a = hh * 1536
                b = min(length, a + 1536)
                p.wait("sync", sV, wstg_free_v)
                p.op("sync", lambda e, s=src_f32, a=a, b=b, o=src_off:
                     e.dma_start(out=wstgC[:, 0:b - a],
                                 in_=s[:, o + a:o + b]), dW, 16)
                ld = dW.n
                p.wait("vector", dW, ld)
                if hh == 0:
                    p.wait("vector", sT, slot_free_t)
                    p.wait("vector", dG, scr_done_dG)
                p.op("vector", lambda e, d=dst_sb, a=a, b=b: e.tensor_copy(
                    d[:, a:b], wstgC[:, 0:b - a]), sV, 1)
                wstg_free_v = sV.n
            p.wait("gpsimd", sV, wstg_free_v)
            p.op("gpsimd", lambda e, d=dst_sb, s=src_bf, L=length,
                 o=src_off: e.dma_start(out=s[:, o:o + L], in_=d[:, 0:L]),
                 dG, 16)
            return sV, wstg_free_v, dG.n
        p.wait("sync", sT, slot_free_t)
        p.wait("sync", dG, scr_done_dG)
        p.op("sync", lambda e, d=dst_sb, s=src_bf, L=length, o=src_off:
             e.dma_start(out=d[:, 0:L], in_=s[:, o:o + L]), dW, 16)
        return dW, dW.n, scr_done_dG

    p.op("sync", lambda e: e.dma_start(out=sel8[:], in_=sel_in[:]), dW, 16)
    p.wait("tensor", dW, dW.n)
    p.op("vector", lambda e: e.tensor_copy(r_hi[:], r_sb[:]))
    vdrain()
    p.op("vector", lambda e: e.tensor_sub(r_lo[:], r_sb[:], r_hi[:]), sV, 1)
    p.wait("tensor", sV, sV.n)

    for sub in range(NSUB):
        for h in range(NCC):
            ci = sub * NCC + h
            par = h % 2
            t0 = sub * SUB + h * CC
            p.wait("gpsimd", sT, x2f_cons[par])
            p.wait("gpsimd", sV, x2f_cons_v[par])
            for fc in range(FC):
                p.op("gpsimd", lambda e, fc=fc, t0=t0, par=par: e.dma_start(
                    out=x2f[par][:, fc * CC:(fc + 1) * CC],
                    in_=x2_hbm[fc * 128:(fc + 1) * 128, t0:t0 + CC]), dG, 16)
            x2_ld = dG.n
            p.wait("vector", dG, x2_ld)
            p.wait("vector", sT, chunk_last_t if ci > 0 else 0)
            p.op("vector", lambda e, par=par: e.tensor_copy(xb2[:],
                                                            x2f[par][:]))
            p.op("vector", lambda e, par=par: e.tensor_sub(
                x2lo[:], x2f[par][:], xb2[:]))
            p.op("vector", lambda e: e.tensor_mul(xsq2[:], xb2[:], xb2[:]),
                 sV, 1)
            cast2_v = sV.n
            p.wait("tensor", sV, cast2_v)
            p.wait("tensor", sV, psD_free_v)
            for kc in range(FC):
                p.op("tensor", lambda e, kc=kc: e.matmul(
                    psD[0][:, 0:CC], ones_b[:], xb2[:, kc * CC:(kc + 1) * CC],
                    start=(kc == 0), stop=(kc == FC - 1)))
            for kc in range(FC):
                p.op("tensor", lambda e, kc=kc: e.matmul(
                    psD[1][:, 0:CC], ones_b[:], xsq2[:, kc * CC:(kc + 1) * CC],
                    start=(kc == 0), stop=(kc == FC - 1)),
                    sT if kc == FC - 1 else None, 1)
            stats_mm_done = sT.n
            p.wait("vector", sT, stats_mm_done)
            p.op("vector", lambda e: e.tensor_scalar(
                st1C[:], psD[0][:, 0:CC], 1.0 / D, None, ALU.mult))
            p.op("vector", lambda e: e.tensor_scalar(
                st2C[:], psD[1][:, 0:CC], 1.0 / D, None, ALU.mult))
            p.op("vector", lambda e: e.scalar_tensor_tensor(
                gaC[:], st1C[:], -1.0, st1C[:], ALU.mult, ALU.mult))
            p.op("vector", lambda e: e.scalar_tensor_tensor(
                gbC[:], st2C[:], 1e-5, gaC[:], ALU.add, ALU.add), sV, 1)
            psD_free_v = sV.n
            p.wait("scalar", sV, psD_free_v)
            p.op("scalar", lambda e: e.activation(gaC[:], gbC[:], AF.Ln),
                 sA, 1)
            p.wait("vector", sA, sA.n)
            p.op("vector", lambda e: e.tensor_scalar(
                gbC[:], gaC[:], -0.5, None, ALU.mult), sV, 1)
            p.wait("scalar", sV, sV.n)
            p.op("scalar", lambda e: e.activation(rrepC[:], gbC[:], AF.Exp),
                 sA, 1)
            p.wait("vector", sA, sA.n)
            p.op("vector", lambda e: e.scalar_tensor_tensor(
                nmrC[:], st1C[:], -1.0, rrepC[:], ALU.mult, ALU.mult))
            p.wait("vector", sT, ln2b_free_t if (sub > 0 and h == 0) else 0)
            for fc in range(FC):
                p.op("vector", lambda e, fc=fc, par=par: e.tensor_mul(
                    gaC[:], x2f[par][:, fc * CC:(fc + 1) * CC], rrepC[:]))
                p.op("vector", lambda e, fc=fc, h=h: e.tensor_add(
                    ln2b[:, fc * SUB + h * CC:fc * SUB + (h + 1) * CC],
                    gaC[:], nmrC[:]),
                    sV if fc == FC - 1 else None, 1)
            ln2b_chunk_v = sV.n

            for tt in range(CC // 128):
                ti = h * 2 + tt
                tsl = slice(tt * 128, (tt + 1) * 128)
                if ti == 0:
                    p.wait("tensor", sV, psM_free_v)
                for kc in range(FC):
                    p.op("tensor", lambda e, kc=kc, tsl=tsl, ti=ti: e.matmul(
                        psM[:, 64 + ti:65 + ti],
                        xb2[:, kc * CC:(kc + 1) * CC][:, tsl],
                        ones_b[:, 0:1],
                        start=(kc == 0), stop=(kc == FC - 1)))
                for kc in range(FC):
                    p.op("tensor", lambda e, kc=kc, tsl=tsl, ti=ti: e.matmul(
                        psM[:, 72 + ti:73 + ti],
                        xsq2[:, kc * CC:(kc + 1) * CC][:, tsl],
                        ones_b[:, 0:1],
                        start=(kc == 0), stop=(kc == FC - 1)))
                for pas, xs, rp in ((0, xb2, r_hi), (1, xb2, r_lo),
                                    (2, x2lo, r_hi)):
                    for kc in range(FC):
                        p.op("tensor", lambda e, kc=kc, tsl=tsl, xs=xs,
                             rp=rp, pas=pas, ti=ti: e.matmul(
                                 psM[:, ti * 8:(ti + 1) * 8],
                                 xs[:, kc * CC:(kc + 1) * CC][:, tsl],
                                 rp[:, kc * 8:(kc + 1) * 8],
                                 start=(kc == 0 and pas == 0),
                                 stop=(kc == FC - 1 and pas == 2)),
                             sT if (kc == FC - 1 and pas == 2) else None, 1)
            x2f_cons[par] = sT.n
            x2f_cons_v[par] = ln2b_chunk_v
            chunk_last_t = sT.n

        def d3(h_):
            return bass.AP(h_, 0, [[64, 128], [8, 8], [1, 8]])

        def segb(h_):
            return bass.AP(h_, 0, [[8, 128], [1, 8], [0, 8]])

        def tileb(h_):
            return bass.AP(h_, 0, [[8, 128], [0, 8], [1, 8]])

        p.wait("vector", sT, chunk_last_t)
        p.op("vector", lambda e: e.tensor_scalar(
            c8m[:], psM[:, 64:72], 1.0 / D, None, ALU.mult))
        p.op("vector", lambda e: e.tensor_scalar(
            c8q[:], psM[:, 72:80], 1.0 / D, None, ALU.mult))
        vdrain()
        p.op("vector", lambda e: e.scalar_tensor_tensor(
            c8a[:], c8m[:], -1.0, c8m[:], ALU.mult, ALU.mult))
        vdrain()
        p.op("vector", lambda e: e.scalar_tensor_tensor(
            c8b[:], c8q[:], 1e-5, c8a[:], ALU.add, ALU.add), sV, 1)
        p.wait("scalar", sV, sV.n)
        p.op("scalar", lambda e: e.activation(c8a[:], c8b[:], AF.Ln), sA, 1)
        p.wait("vector", sA, sA.n)
        p.op("vector", lambda e: e.tensor_scalar(
            c8b[:], c8a[:], -0.5, None, ALU.mult), sV, 1)
        p.wait("scalar", sV, sV.n)
        p.op("scalar", lambda e: e.activation(rstd8[:], c8b[:], AF.Exp),
             sA, 1)
        p.wait("vector", sA, sA.n)
        p.op("vector", lambda e: e.scalar_tensor_tensor(
            nmr8[:], c8m[:], -1.0, rstd8[:], ALU.mult, ALU.mult))
        p.op("vector", lambda e: e.tensor_copy(r64a[:], psM[:, 0:64]), sV, 1)
        psM_free_v = sV.n
        vdrain()
        p.op("vector", lambda e: e.tensor_tensor(
            d3(r64a), d3(r64a), segb(rstd8), ALU.mult))
        p.op("vector", lambda e: e.tensor_tensor(
            d3(r64b), tileb(sr_rep), segb(nmr8), ALU.mult))
        vdrain()
        p.op("vector", lambda e: e.tensor_add(r64a[:], r64a[:], r64b[:]))
        vdrain()
        p.op("vector", lambda e: e.reduce_max(c8a[:], d3(r64a), axis=AX))
        vdrain()
        p.op("vector", lambda e: e.tensor_tensor(
            d3(r64b), d3(r64a), segb(c8a), ALU.subtract), sV, 1)
        p.wait("scalar", sV, sV.n)
        p.op("scalar", lambda e: e.activation(r64a[:], r64b[:], AF.Exp),
             sA, 1)
        p.wait("vector", sA, sA.n)
        p.op("vector", lambda e: e.reduce_sum(c8b[:], d3(r64a), axis=AX))
        vdrain()
        p.op("vector", lambda e: e.reciprocal(c8c[:], c8b[:]))
        vdrain()
        p.op("vector", lambda e: e.tensor_tensor(
            d3(r64b), d3(r64a), segb(c8c), ALU.mult))
        vdrain()
        p.op("vector", lambda e: e.reduce_max(c8a[:], d3(r64b), axis=AX))
        vdrain()
        p.op("vector", lambda e: e.tensor_tensor(
            d3(r64a), d3(r64b), segb(c8a), ALU.is_equal))
        vdrain()
        p.op("vector", lambda e: e.scalar_tensor_tensor(
            r64a[:], r64a[:], -2.0, r64b[:], ALU.mult, ALU.add))
        vdrain()
        p.op("vector", lambda e: e.reduce_max(c8b[:], d3(r64a), axis=AX))
        vdrain()
        p.op("vector", lambda e: e.tensor_tensor(
            d3(r64a), d3(r64b), segb(c8b), ALU.is_ge))
        vdrain()
        p.op("vector", lambda e: e.tensor_mul(r64a[:], r64a[:], r64b[:]))
        vdrain()
        p.op("vector", lambda e: e.tensor_copy(coefb64[:], r64a[:]), sV, 1)
        coef_cast_v = sV.n
        p.wait("tensor", sV, coef_cast_v)
        for ti in range(8):
            p.op("tensor", lambda e, ti=ti: e.transpose(
                psTr[:], coefb64[:, ti * 8:(ti + 1) * 8], eyeb[:]), sT, 1)
            p.wait("vector", sT, sT.n)
            p.op("vector", lambda e, ti=ti: e.tensor_copy(
                coefT[:, ti * 128:(ti + 1) * 128], psTr[:]), sV, 1)
            if ti < 7:
                p.wait("tensor", sV, sV.n)
        coef_ready_v = sV.n
        if dbg:
            p.wait("gpsimd", sV, coef_ready_v)
            p.op("gpsimd", lambda e, sub=sub: e.dma_start(
                out=dbg_coef[:, sub * SUB:(sub + 1) * SUB], in_=coefT[:]),
                dG, 16)
            if sub == 0:
                p.op("gpsimd", lambda e: e.dma_start(
                    out=dbg_ln2b[:], in_=ln2b[:]), dG, 16)

        for pe in range(NE):
            gslot = 0
            uslot = 1
            gsem, gval, slot_scr_dG[gslot] = fetch_piece(
                wslot[gslot], gup[pe, 0], gup_sc[pe, 0], sub, FC * DE,
                slot_free_t[gslot], slot_scr_dG[gslot])
            usem, uval, slot_scr_dG[uslot] = fetch_piece(
                wslot[uslot], gup[pe, 1], gup_sc[pe, 1], sub, FC * DE,
                slot_free_t[uslot], slot_scr_dG[uslot])
            p.wait("tensor", gsem, gval)
            p.wait("tensor", sV, coef_ready_v)
            silu_a = {}
            for dcq in range(8):
                qc, dc = dcq // 4, dcq % 4
                par = dcq % 2
                p.wait("tensor", sA,
                       silu_a[dcq - 2] if dcq >= 2 else gu_prev_silu[par])
                for kc in range(FC):
                    p.op("tensor", lambda e, kc=kc, dc=dc, qc=qc, par=par,
                         gslot=gslot: e.matmul(
                             psG[par][:],
                             wslot[gslot][:, kc * DE + dc * 128:
                                          kc * DE + (dc + 1) * 128],
                             ln2b[:, kc * SUB + qc * 512:
                                  kc * SUB + (qc + 1) * 512],
                             start=(kc == 0), stop=(kc == FC - 1)),
                         sT if kc == FC - 1 else None, 1)
                g_t = sT.n
                p.wait("scalar", sT, g_t)
                p.wait("scalar", sV, prev_up_ev_all[dcq])
                p.op("scalar", lambda e, dcq=dcq, par=par: e.activation(
                    gsall[:, dcq * 512:(dcq + 1) * 512], psG[par][:],
                    AF.Silu), sA, 1)
                silu_a[dcq] = sA.n
            slot_free_t[gslot] = sT.n
            gu_prev_silu = {0: silu_a[6], 1: silu_a[7]}

            p.wait("tensor", usem, uval)
            up_ev = {}
            crep_t = 0
            for dcq in range(8):
                qc, dc = dcq // 4, dcq % 4
                par = dcq % 2
                if pe < 8 and dc == 0:
                    p.wait("tensor", sV, psM_free_v)
                    p.op("tensor", lambda e, pe=pe, qc=qc: e.matmul(
                        psM[:], sel8[:, pe * 128:(pe + 1) * 128],
                        coefT[:, qc * 512:(qc + 1) * 512],
                        start=True, stop=True), sT, 1)
                    crep_t = sT.n
                p.wait("tensor", sV,
                       up_ev[dcq - 2] if dcq >= 2 else up_prev_ev[par])
                for kc in range(FC):
                    p.op("tensor", lambda e, kc=kc, dc=dc, qc=qc, par=par,
                         uslot=uslot: e.matmul(
                             psU[par][:],
                             wslot[uslot][:, kc * DE + dc * 128:
                                          kc * DE + (dc + 1) * 128],
                             ln2b[:, kc * SUB + qc * 512:
                                  kc * SUB + (qc + 1) * 512],
                             start=(kc == 0), stop=(kc == FC - 1)),
                         sT if kc == FC - 1 else None, 1)
                u_t = sT.n
                p.wait("vector", sT, u_t)
                p.wait("vector", sA, silu_a[dcq])
                hpos = pe * 4 * SUB + dc * SUB + qc * 512
                if pe < 8:
                    p.op("vector", lambda e, dcq=dcq, par=par: e.tensor_mul(
                        t2b[par][:], psU[par][:],
                        gsall[:, dcq * 512:(dcq + 1) * 512]))
                    p.op("vector", lambda e, hpos=hpos, par=par:
                         e.tensor_mul(htil[:, hpos:hpos + 512], t2b[par][:],
                                      psM[:]), sV, 1)
                else:
                    p.op("vector", lambda e, dcq=dcq, par=par, hpos=hpos:
                         e.tensor_mul(htil[:, hpos:hpos + 512], psU[par][:],
                                      gsall[:, dcq * 512:(dcq + 1) * 512]),
                         sV, 1)
                up_ev[dcq] = sV.n
                if pe < 8 and dc == 3:
                    psM_free_v = sV.n
            slot_free_t[uslot] = sT.n
            up_prev_ev = {0: up_ev[6], 1: up_ev[7]}
            prev_up_ev_all = dict(up_ev)
        htil_v = sV.n

        if dbg and sub == 0:
            p.wait("gpsimd", sV, htil_v)
            p.op("gpsimd", lambda e: e.dma_start(
                out=dbg_htil[:], in_=htil[:, 0:2 * SUB * 4]), dG, 16)
        p.wait("tensor", sV, htil_v)
        dev_v = {}
        for dc in range(FC):
            dpar = dc % 2
            dsem, dval, dslot_scr_dG[dpar] = fetch_piece(
                dslot[dpar], dwn[dc], dwn_sc[dc], sub, NE * 4 * 128,
                dslot_free_t[dpar], dslot_scr_dG[dpar])
            p.wait("tensor", dsem, dval)
            for qc in range(2):
                idx = dc * 2 + qc
                par = idx % 2
                p.wait("gpsimd", yst, x2r_free[par])
                p.op("gpsimd", lambda e, dc=dc, qc=qc, sub=sub, par=par:
                     e.dma_start(
                         out=x2r[par][:],
                         in_=x2_hbm[dc * 128:(dc + 1) * 128,
                                    sub * SUB + qc * 512:
                                    sub * SUB + (qc + 1) * 512]), dG, 16)
                x2r_ld = dG.n
                if idx >= 2:
                    p.wait("tensor", sV, dev_v[idx - 2])
                for pe in range(NE):
                    for cc in range(4):
                        blk = pe * 4 + cc
                        first = (pe == 0 and cc == 0)
                        last = (pe == NE - 1 and cc == 3)
                        p.op("tensor",
                             lambda e, blk=blk, pe=pe, cc=cc, qc=qc, par=par,
                             dpar=dpar, first=first, last=last: e.matmul(
                                 psD[par][:],
                                 dslot[dpar][:, blk * 128:(blk + 1) * 128],
                                 htil[:, pe * 4 * SUB + cc * SUB + qc * 512:
                                      pe * 4 * SUB + cc * SUB +
                                      (qc + 1) * 512],
                                 start=first, stop=last),
                             sT if last else None, 1)
                if qc == 1:
                    dslot_free_t[dpar] = sT.n
                dmm_t = sT.n
                p.wait("vector", sT, dmm_t)
                p.wait("vector", dG, x2r_ld)
                p.op("vector", lambda e, par=par: e.tensor_add(
                    x2r[par][:], psD[par][:], x2r[par][:]), sV, 1)
                dev_v[idx] = sV.n
                p.wait("gpsimd", sV, dev_v[idx])
                p.op("gpsimd", lambda e, dc=dc, qc=qc, sub=sub, par=par:
                     e.dma_start(
                         out=y[dc * 128:(dc + 1) * 128,
                               sub * SUB + qc * 512:
                               sub * SUB + (qc + 1) * 512],
                         in_=x2r[par][:]), yst, 16)
                x2r_free[par] = yst.n
        psD_free_v = dev_v[2 * FC - 1]
        ln2b_free_t = sT.n
        stats_mm_done = sT.n

    p.wait("gpsimd", yst, yst.n)
    p.wait("gpsimd", dG, dG.n)

    with nc.Block() as blk:
        p.flush(blk)
    esC.close()
    es.close()
    return nc


_NC_CACHE = {}


def _get_nc():
    if "nc" not in _NC_CACHE:
        _NC_CACHE["nc"] = build_nc()
    return _NC_CACHE["nc"]


def _mk_sel():
    s = np.zeros((8, 8 * 128), np.float32)
    for e in range(8):
        s[e, e * 128:(e + 1) * 128] = 1.0
    return s.astype(ml_dtypes.bfloat16)


def _prep_inputs(inputs):
    bf = ml_dtypes.bfloat16
    x = np.asarray(inputs["L0_attn_x"])

    def tile_gu(w):
        return np.ascontiguousarray(
            w.reshape(8, 128, DE).transpose(1, 0, 2).reshape(128, 8 * DE))

    eg = np.asarray(inputs["L0_ffn_experts_gate"])
    eu = np.asarray(inputs["L0_ffn_experts_up"])
    ed = np.asarray(inputs["L0_ffn_experts_down"])
    sg = np.asarray(inputs["L0_ffn_shared_gate"])
    su = np.asarray(inputs["L0_ffn_shared_up"])
    sd = np.asarray(inputs["L0_ffn_shared_down"])
    gup = np.zeros((NE, 2, 128, FC * DE), np.float32)
    for e in range(8):
        gup[e, 0] = tile_gu(eg[e])
        gup[e, 1] = tile_gu(eu[e])
    gup[8, 0] = tile_gu(sg[:, 0:DE])
    gup[8, 1] = tile_gu(su[:, 0:DE])
    gup[9, 0] = tile_gu(sg[:, DE:2 * DE])
    gup[9, 1] = tile_gu(su[:, DE:2 * DE])

    downs = [ed[e] for e in range(8)] + [sd[0:DE, :], sd[DE:2 * DE, :]]
    dwn = np.zeros((FC, 128, NE * 4 * 128), np.float32)
    for dc in range(FC):
        blocks = []
        for pe in range(NE):
            w = downs[pe]
            for cc in range(4):
                blocks.append(w[cc * 128:(cc + 1) * 128,
                                dc * 128:(dc + 1) * 128])
        dwn[dc] = np.concatenate(blocks, axis=1)

    common = dict(
        wqT=np.ascontiguousarray(np.asarray(inputs["L0_attn_Wq"]).T),
        wkT=np.ascontiguousarray(np.asarray(inputs["L0_attn_Wk"]).T),
        wvT=np.ascontiguousarray(np.asarray(inputs["L0_attn_Wv"]).T),
        woT=np.ascontiguousarray(np.asarray(inputs["L0_attn_Wo"]).T),
        wavel=np.ascontiguousarray(np.asarray(inputs["L0_attn_wavelengths"])),
        pbias=np.ascontiguousarray(np.asarray(inputs["L0_attn_phase_bias"])),
        bq=np.ascontiguousarray(np.asarray(inputs["L0_attn_bq"])),
        bk=np.ascontiguousarray(np.asarray(inputs["L0_attn_bk"])),
        bv=np.ascontiguousarray(np.asarray(inputs["L0_attn_bv"])),
        bo=np.ascontiguousarray(np.asarray(inputs["L0_attn_bo"])),
        router=np.ascontiguousarray(np.asarray(inputs["L0_ffn_router"])),
        gup=gup, dwn=dwn,
        eye_in=np.eye(128, dtype=np.float32).astype(bf),
        sel_in=_mk_sel(),
    )
    per_core = []
    for b in range(8):
        m = dict(common)
        m["x_t"] = np.ascontiguousarray(np.asarray(x[b]).T)
        per_core.append(m)
    return per_core


def kernel(**inputs):
    nc = _get_nc()
    in_maps = _prep_inputs(inputs)
    res = run_bass_kernel_spmd(nc, in_maps, core_ids=list(range(8)))
    out = np.empty((8, S, D), np.float32)
    for b in range(8):
        out[b] = np.asarray(res.results[b]["y"]).T
    return out


# version 06
# speedup vs baseline: 1.1164x; 1.0761x over previous
import contextlib
import math

import numpy as np
import ml_dtypes

import concourse.bass as bass
import concourse.mybir as mybir
from concourse.bass_utils import run_bass_kernel_spmd

F32 = mybir.dt.float32
F16 = mybir.dt.float16
BF16 = mybir.dt.bfloat16
I32 = mybir.dt.int32
ALU = mybir.AluOpType
AF = mybir.ActivationFunctionType
AX = mybir.AxisListType.X

D = 1024
S = 4096
H = 128
FC = D // 128
NE = 10
DE = 512
NSUB = 4
SUB = S // NSUB

TWO_PI = float(2.0 * math.pi)
INV_SQRT_H = 1.0 / math.sqrt(H)
EXP_SHIFT = -float(math.sqrt(H))
ROUND_OFS = 128.0


class Sem:
    def __init__(self, h, is_dma=False):
        self.h = h
        self.is_dma = is_dma
        self.n = 0

    def bump(self, k):
        self.n += k
        return self.n


class Prog:
    ENGS = ("sync", "vector", "scalar", "tensor", "gpsimd")

    def __init__(self):
        self.q = {e: [] for e in self.ENGS}

    def op(self, eng, build, sem=None, k=1, fast=False):
        if sem is not None:
            v = sem.bump(k)
            if sem.is_dma or fast:
                self.q[eng].append(
                    lambda e, b=build, s=sem, kk=k: b(e).then_inc(s.h, kk))
            else:
                def _f(e, b=build, s=sem, kk=k):
                    b(e)
                    e.drain(fusable=False).then_inc(s.h, kk)
                self.q[eng].append(_f)
            return v
        self.q[eng].append(build)
        return None

    def wait(self, eng, sem, val):
        if val:
            self.q[eng].append(lambda e, s=sem, v=val: e.wait_ge(s.h, v))

    def flush(self, block):
        for eng in self.ENGS:
            ops = self.q[eng]
            if not ops:
                continue

            def body(engine, ops=ops):
                for f in ops:
                    f(engine)

            getattr(block, eng)(body)
        self.q = {e: [] for e in self.ENGS}


def build_nc(dbg=False):
    nc = bass.Bass("TRN2", target_bir_lowering=False,
                   detect_race_conditions=False)

    x_t = nc.dram_tensor("x_t", [D, S], F32, kind="ExternalInput")
    wqT = nc.dram_tensor("wqT", [D, H], F32, kind="ExternalInput")
    wkT = nc.dram_tensor("wkT", [D, H], F32, kind="ExternalInput")
    wvT = nc.dram_tensor("wvT", [D, H], F32, kind="ExternalInput")
    woT = nc.dram_tensor("woT", [H, D], F32, kind="ExternalInput")
    wavel = nc.dram_tensor("wavel", [H], F32, kind="ExternalInput")
    pbias = nc.dram_tensor("pbias", [H], F32, kind="ExternalInput")
    bq_d = nc.dram_tensor("bq", [H], F32, kind="ExternalInput")
    bk_d = nc.dram_tensor("bk", [H], F32, kind="ExternalInput")
    bv_d = nc.dram_tensor("bv", [H], F32, kind="ExternalInput")
    bo_d = nc.dram_tensor("bo", [D], F32, kind="ExternalInput")
    router = nc.dram_tensor("router", [D, 8], F32, kind="ExternalInput")
    gup = nc.dram_tensor("gup", [NE, 2, 128, FC * DE], F32,
                         kind="ExternalInput")
    dwn = nc.dram_tensor("dwn", [FC, 128, NE * 4 * 128], F32,
                         kind="ExternalInput")
    eye_in = nc.dram_tensor("eye_in", [128, 128], BF16, kind="ExternalInput")
    sel_in = nc.dram_tensor("sel_in", [8, 8 * 128], BF16, kind="ExternalInput")

    y = nc.dram_tensor("y", [D, S], F32, kind="ExternalOutput")

    x2_hbm = nc.dram_tensor("x2_hbm", [D, S], F32,
                            kind="ExternalOutput" if dbg else "Internal")
    if dbg:
        dbg_t = {n: nc.dram_tensor("dbg_" + n, [128, S], F16,
                                   kind="ExternalOutput")
                 for n in ("cosq", "sinq", "cosk", "sink")}
        dbg_t["vf"] = nc.dram_tensor("dbg_vf", [128, S], BF16,
                                     kind="ExternalOutput")
        dbg_us = {n: nc.dram_tensor("dbg_" + n, [128, S], F16,
                                    kind="ExternalOutput")
                  for n in ("usq", "ucq")}
        dbg_coef = nc.dram_tensor("dbg_coef", [8, S], BF16,
                                  kind="ExternalOutput")
        dbg_htil = nc.dram_tensor("dbg_htil", [128, 2 * SUB * 4], BF16,
                                  kind="ExternalOutput")
        dbg_ln2b = nc.dram_tensor("dbg_ln2b", [128, FC * SUB], BF16,
                                  kind="ExternalOutput")
        dbg_rt = nc.dram_tensor("dbg_rt", [128, 64], F32,
                                kind="ExternalOutput")
        dbg_rt2 = nc.dram_tensor("dbg_rt2", [128, 40 * 8], F32,
                                 kind="ExternalOutput")
    gup_sc = nc.dram_tensor("gup_sc", [NE, 2, 128, FC * DE], BF16)
    dwn_sc = nc.dram_tensor("dwn_sc", [FC, 128, NE * 4 * 128], BF16)

    es = contextlib.ExitStack()

    def SB(name, shape, dt):
        return es.enter_context(nc.sbuf_tensor(name, shape, dt))

    def SEM(name, is_dma=False):
        return Sem(es.enter_context(nc.semaphore(name)), is_dma)

    eyeb = SB("eyeb", [128, 128], BF16)
    ones_b = SB("ones_b", [128, 128], BF16)
    ones_f = SB("ones_f", [128, 128], F32)
    c2pi = SB("c2pi", [128, 1], F32)
    ceps = SB("ceps", [128, 1], F32)
    cnhalf = SB("cnhalf", [128, 1], F32)
    cnone = SB("cnone", [128, 1], F32)
    cinvsq = SB("cinvsq", [128, 1], F32)
    cshift = SB("cshift", [128, 1], F32)
    invw = SB("invw", [128, 1], F32)
    s1c = SB("s1c", [128, 1], F32)
    s2q = SB("s2q", [128, 1], F32)
    s2k = SB("s2k", [128, 1], F32)
    sqc = SB("sqc", [128, 1], F32)
    skc = SB("skc", [128, 1], F32)
    svc = SB("svc", [128, 1], F32)
    bvb = SB("bvb", [128, 1], BF16)
    bqf = SB("bqf", [128, 1], F32)
    bkf = SB("bkf", [128, 1], F32)
    bvf = SB("bvf", [128, 1], F32)
    wlf = SB("wlf", [128, 1], F32)
    pbf = SB("pbf", [128, 1], F32)
    bo8 = SB("bo8", [128, FC], F32)
    boP = SB("boP", [128, FC], F32)
    tcol = SB("tcol", [128, 1], F32)
    wqTb = SB("wqTb", [128, FC * 128], BF16)
    wkTb = SB("wkTb", [128, FC * 128], BF16)
    wvTb = SB("wvTb", [128, FC * 128], BF16)
    woTb = SB("woTb", [128, FC * 128], BF16)
    r_sb = SB("r_sb", [128, FC * 8], F32)
    r_hi = SB("r_hi", [128, FC * 8], BF16)
    r_lo = SB("r_lo", [128, FC * 8], BF16)
    sr_rep = SB("sr_rep", [128, 8], F32)

    dW = SEM("dW", is_dma=True)
    dG = SEM("dG", is_dma=True)
    yst = SEM("yst", is_dma=True)
    sT = SEM("sT")
    sV = SEM("sV")
    sA = SEM("sA")

    p = Prog()

    def vdrain():
        p.op("vector", lambda e: e.drain(fusable=False))

    esAB = contextlib.ExitStack()

    def SBab(name, shape, dt):
        return esAB.enter_context(nc.sbuf_tensor(name, shape, dt))

    cosq = SBab("cosq", [128, S], F16)
    sinq = SBab("sinq", [128, S], F16)
    cosk = SBab("cosk", [128, S], F16)
    sink = SBab("sink", [128, S], F16)
    vf = SBab("vf", [128, S], BF16)

    esA = contextlib.ExitStack()

    def SBa(name, shape, dt):
        return esA.enter_context(nc.sbuf_tensor(name, shape, dt))

    def PSa(name, shape, dt=F32):
        return esA.enter_context(nc.psum_tensor(name, shape, dt))

    CH = 512
    NCH = S // CH
    xfA = [SBa(f"xfA{i}", [128, FC * CH], F32) for i in range(2)]
    xbA = [SBa(f"xbA{i}", [128, FC * CH], BF16) for i in range(2)]
    xsqA = [SBa(f"xsqA{i}", [128, FC * CH], BF16) for i in range(2)]
    stat1 = SBa("stat1", [128, CH], F32)
    stat2 = SBa("stat2", [128, CH], F32)
    rrep = SBa("rrep", [128, CH], F32)
    nmr = SBa("nmr", [128, CH], F32)
    ga = SBa("ga", [128, CH], F32)
    gb = SBa("gb", [128, CH], F32)
    gc = SBa("gc", [128, CH], F32)
    gi = SBa("gi", [128, CH], I32)
    usq = SBa("usq", [128, S], F16)
    ucq = SBa("ucq", [128, S], F16)
    usk = SBa("usk", [128, S], F16)
    uck = SBa("uck", [128, S], F16)
    wstage = SBa("wstage", [128, FC * 128], F32)

    psMean = PSa("psMean", [128, CH])
    psSq = PSa("psSq", [128, CH])
    psQ = [PSa(f"psQ{i}", [128, CH]) for i in range(2)]
    psK = [PSa(f"psK{i}", [128, CH]) for i in range(2)]
    psV = [PSa(f"psV{i}", [128, CH]) for i in range(2)]

    p.op("sync", lambda e: e.dma_start(out=eyeb[:], in_=eye_in[:]), dW, 16)
    for col, src in ((wlf, wavel), (pbf, pbias), (bqf, bq_d), (bkf, bk_d),
                     (bvf, bv_d)):
        p.op("sync", lambda e, c=col, s=src: e.dma_start(
            out=c[:], in_=bass.AP(s, 0, [[1, 128], [1, 1]])), dW, 16)
    for dc in range(FC):
        p.op("sync", lambda e, dc=dc: e.dma_start(
            out=bo8[:, dc:dc + 1],
            in_=bass.AP(bo_d, dc * 128, [[1, 128], [1, 1]])), dW, 16)
    for kc in range(FC):
        p.op("sync", lambda e, kc=kc: e.dma_start(
            out=r_sb[:, kc * 8:(kc + 1) * 8],
            in_=router[kc * 128:(kc + 1) * 128, :]), dW, 16)
    small_loads = dW.n

    p.op("vector", lambda e: e.memset(ones_b[:], 1.0))
    p.op("vector", lambda e: e.memset(ones_f[:], 1.0))
    p.op("vector", lambda e: e.memset(c2pi[:], TWO_PI))
    p.op("vector", lambda e: e.memset(ceps[:], 1e-5))
    p.op("vector", lambda e: e.memset(cnhalf[:], -0.5))
    p.op("vector", lambda e: e.memset(cnone[:], -1.0))
    p.op("vector", lambda e: e.memset(cinvsq[:], INV_SQRT_H))
    p.op("vector", lambda e: e.memset(cshift[:], EXP_SHIFT))
    p.wait("vector", dW, small_loads)
    p.op("vector", lambda e: e.tensor_scalar(tcol[:], wlf[:], 1e-8, None,
                                             ALU.add))
    vdrain()
    p.op("vector", lambda e: e.reciprocal(invw[:], tcol[:]))
    vdrain()
    p.op("vector", lambda e: e.tensor_scalar(s1c[:], invw[:], 1.0 / TWO_PI,
                                             None, ALU.mult))
    for bcol, s2 in ((bqf, s2q), (bkf, s2k)):
        p.op("vector", lambda e, b=bcol, s=s2: e.tensor_mul(s[:], b[:],
                                                            invw[:]))
        vdrain()
        p.op("vector", lambda e, s=s2: e.tensor_add(s[:], s[:], pbf[:]))
        vdrain()
        p.op("vector", lambda e, s=s2: e.tensor_scalar(
            s[:], s[:], 1.0 / TWO_PI, ROUND_OFS, ALU.mult, ALU.add))
    p.op("vector", lambda e: e.tensor_copy(bvb[:], bvf[:]), sV, 1)

    for wi, (wsrc, wdst) in enumerate(((wqT, wqTb), (wkT, wkTb), (wvT, wvTb),
                                       (woT, woTb))):
        p.wait("sync", sV, sV.n)
        for kc in range(FC):
            if wi < 3:
                src = wsrc[kc * 128:(kc + 1) * 128, :]
            else:
                src = wsrc[:, kc * 128:(kc + 1) * 128]
            p.op("sync", lambda e, s=src, kc=kc: e.dma_start(
                out=wstage[:, kc * 128:(kc + 1) * 128], in_=s), dW, 16)
        p.wait("vector", dW, dW.n)
        p.op("vector", lambda e, d=wdst: e.tensor_copy(d[:], wstage[:]), sV, 1)
    wcast_v = sV.n

    p.wait("tensor", sV, wcast_v)
    for wsb, col in ((wqTb, sqc), (wkTb, skc), (wvTb, svc)):
        for kc in range(FC):
            p.op("tensor", lambda e, w=wsb, kc=kc: e.matmul(
                psMean[:, 0:1], w[:, kc * 128:(kc + 1) * 128], ones_b[:, 0:1],
                start=(kc == 0), stop=(kc == FC - 1)),
                sT if kc == FC - 1 else None, 1)
        p.wait("vector", sT, sT.n)
        p.op("vector", lambda e, c=col: e.tensor_copy(c[:], psMean[:, 0:1]),
             sV, 1)
        p.wait("tensor", sV, sV.n)
    for dc in range(FC):
        p.op("tensor", lambda e, dc=dc: e.matmul(
            psSq[:, dc:dc + 1], woTb[:, dc * 128:(dc + 1) * 128], bvb[:, 0:1],
            start=True, stop=True), sT if dc == FC - 1 else None, 1)
    p.wait("vector", sT, sT.n)
    p.op("vector", lambda e: e.tensor_add(boP[:], psSq[:, 0:FC], bo8[:]),
         sV, 1)
    p.op("vector", lambda e: e.tensor_copy(r_hi[:], r_sb[:]))
    vdrain()
    p.op("vector", lambda e: e.tensor_sub(r_lo[:], r_sb[:], r_hi[:]), sV, 1)
    p.wait("tensor", sV, sV.n)
    for pas, rp in ((0, r_hi), (1, r_lo)):
        for kc in range(FC):
            p.op("tensor", lambda e, kc=kc, rp=rp, pas=pas: e.matmul(
                psMean[:, 0:8], ones_b[:], rp[:, kc * 8:(kc + 1) * 8],
                start=(kc == 0 and pas == 0),
                stop=(kc == FC - 1 and pas == 1)),
                sT if (kc == FC - 1 and pas == 1) else None, 1)
    p.wait("vector", sT, sT.n)
    p.op("vector", lambda e: e.tensor_copy(sr_rep[:], psMean[:, 0:8]), sV, 1)
    prol_v = sV.n

    x_loads = [0] * NCH
    cast_v = [0] * NCH
    cons_v = [0] * NCH
    qkv_t = [0] * NCH
    stats_used_v = 0

    for c in range(NCH):
        par = c % 2
        if c >= 2:
            p.wait("sync", sV, cons_v[c - 2])
        for fc in range(FC):
            p.op("sync", lambda e, c=c, fc=fc, par=par: e.dma_start(
                out=xfA[par][:, fc * CH:(fc + 1) * CH],
                in_=x_t[fc * 128:(fc + 1) * 128, c * CH:(c + 1) * CH]),
                dW, 16)
        x_loads[c] = dW.n

        p.wait("vector", dW, x_loads[c])
        if c >= 2:
            p.wait("vector", sT, qkv_t[c - 2])
        p.op("vector", lambda e, par=par: e.tensor_copy(xbA[par][:],
                                                        xfA[par][:]))
        p.op("vector", lambda e, par=par: e.tensor_mul(
            xsqA[par][:], xbA[par][:], xbA[par][:]), sV, 1)
        cast_v[c] = sV.n

        p.wait("tensor", sV, cast_v[c])
        p.wait("tensor", sV, stats_used_v if c > 0 else prol_v)
        for kc in range(FC):
            p.op("tensor", lambda e, par=par, kc=kc: e.matmul(
                psMean[:], ones_b[:], xbA[par][:, kc * CH:(kc + 1) * CH],
                start=(kc == 0), stop=(kc == FC - 1)))
        for kc in range(FC):
            p.op("tensor", lambda e, par=par, kc=kc: e.matmul(
                psSq[:], ones_b[:], xsqA[par][:, kc * CH:(kc + 1) * CH],
                start=(kc == 0), stop=(kc == FC - 1)),
                sT if kc == FC - 1 else None, 1, fast=True)
        stats_t = sT.n

        if c >= 2:
            p.wait("tensor", sV, cons_v[c - 2])
        for wsb, dst, last in ((wqTb, psQ[par], False), (wkTb, psK[par], False),
                               (wvTb, psV[par], True)):
            for kc in range(FC):
                p.op("tensor", lambda e, w=wsb, d=dst, par=par, kc=kc:
                     e.matmul(
                         d[:], w[:, kc * 128:(kc + 1) * 128],
                         xbA[par][:, kc * CH:(kc + 1) * CH],
                         start=(kc == 0), stop=(kc == FC - 1)),
                     sT if (kc == FC - 1 and last) else None, 1, fast=True)
        qkv_t[c] = sT.n

        p.wait("vector", sT, stats_t)
        p.op("vector", lambda e: e.tensor_scalar(stat1[:], psMean[:], 1.0 / D,
                                                 None, ALU.mult))
        p.op("vector", lambda e: e.tensor_scalar(stat2[:], psSq[:], 1.0 / D,
                                                 None, ALU.mult))
        p.op("vector", lambda e: e.scalar_tensor_tensor(
            ga[:], stat1[:], -1.0, stat1[:], ALU.mult, ALU.mult))
        p.op("vector", lambda e: e.scalar_tensor_tensor(
            gb[:], stat2[:], 1e-5, ga[:], ALU.add, ALU.add), sV, 1)
        stats_used_v = sV.n
        p.wait("scalar", sV, stats_used_v)
        p.op("scalar", lambda e: e.activation(ga[:], gb[:], AF.Ln), sA, 1)
        p.wait("vector", sA, sA.n)
        p.op("vector", lambda e: e.tensor_scalar(
            gb[:], ga[:], -0.5, None, ALU.mult), sV, 1)
        p.wait("scalar", sV, sV.n)
        p.op("scalar", lambda e: e.activation(rrep[:], gb[:], AF.Exp), sA, 1)
        p.wait("vector", sA, sA.n)
        p.op("vector", lambda e: e.scalar_tensor_tensor(
            nmr[:], stat1[:], -1.0, rrep[:], ALU.mult, ALU.mult))

        p.wait("vector", sT, qkv_t[c])
        sl = slice(c * CH, (c + 1) * CH)
        for psX, s2x, sxc, us_f, uc_f in ((psQ[par], s2q, sqc, usq, ucq),
                                          (psK[par], s2k, skc, usk, uck)):
            p.op("vector", lambda e, psX=psX: e.tensor_mul(ga[:], psX[:],
                                                           rrep[:]))
            p.op("vector", lambda e, sxc=sxc: e.scalar_tensor_tensor(
                gb[:], nmr[:], sxc[:], ga[:], ALU.mult, ALU.add))
            p.op("vector", lambda e, s2x=s2x: e.tensor_scalar(
                ga[:], gb[:], s1c[:], s2x[:], ALU.mult, ALU.add))
            p.op("vector", lambda e: e.tensor_copy(gi[:], ga[:]))
            p.op("vector", lambda e: e.tensor_copy(gb[:], gi[:]))
            p.op("vector", lambda e: e.tensor_sub(ga[:], ga[:], gb[:]))
            p.op("vector", lambda e: e.tensor_scalar(gb[:], ga[:], 0.5, None,
                                                     ALU.is_gt))
            p.op("vector", lambda e: e.scalar_tensor_tensor(
                gc[:], gb[:], -1.0, ga[:], ALU.mult, ALU.add))
            p.op("vector", lambda e, u=us_f, sl=sl: e.tensor_scalar(
                u[:, sl], gc[:], TWO_PI, None, ALU.mult))
            p.op("vector", lambda e: e.tensor_scalar(
                ga[:], gc[:], 0.25, None, ALU.add))
            p.op("vector", lambda e: e.tensor_scalar(gb[:], ga[:], 0.5, None,
                                                     ALU.is_gt))
            p.op("vector", lambda e: e.scalar_tensor_tensor(
                gc[:], gb[:], -1.0, ga[:], ALU.mult, ALU.add))
            p.op("vector", lambda e, u=uc_f, sl=sl: e.tensor_scalar(
                u[:, sl], gc[:], TWO_PI, None, ALU.mult), sV, 1)
        p.op("vector", lambda e, par=par: e.tensor_mul(ga[:], psV[par][:],
                                                       rrep[:]))
        p.op("vector", lambda e, sl=sl: e.scalar_tensor_tensor(
            vf[:, sl], nmr[:], svc[:], ga[:], ALU.mult, ALU.add), sV, 1)
        cons_v[c] = sV.n

    p.wait("scalar", sV, cons_v[NCH - 1])
    for src, dst in ((usq, sinq), (ucq, cosq), (usk, sink), (uck, cosk)):
        p.op("scalar", lambda e, s=src, d=dst: e.activation(
            d[:], s[:], AF.Sin), sA, 1)
    p.wait("vector", sA, sA.n)
    for t in (cosq, sinq):
        p.op("vector", lambda e, t=t: e.tensor_scalar(
            t[:], t[:], INV_SQRT_H, None, ALU.mult), sV, 1)
    sinscale_v = sV.n
    trig_a = sA.n
    if dbg:
        p.wait("gpsimd", sA, trig_a)
        p.wait("gpsimd", sV, sinscale_v)
        for n, t in (("cosq", cosq), ("sinq", sinq), ("cosk", cosk),
                     ("sink", sink), ("vf", vf)):
            p.op("gpsimd", lambda e, n=n, t=t: e.dma_start(
                out=dbg_t[n][:], in_=t[:]), dG, 16)
        for n, t in (("usq", usq), ("ucq", ucq)):
            p.op("gpsimd", lambda e, n=n, t=t: e.dma_start(
                out=dbg_us[n][:], in_=t[:]), dG, 16)
        p.wait("gpsimd", dG, dG.n)

    with nc.Block() as blk:
        p.flush(blk)
    esA.close()

    esB = contextlib.ExitStack()
    v_tm = esB.enter_context(nc.sbuf_tensor("v_tm", [128, S], BF16))

    esB0 = contextlib.ExitStack()
    psTrV = [esB0.enter_context(nc.psum_tensor(f"psTrV{i}", [128, 128], BF16))
             for i in range(2)]
    tr_cons = [0, 0]
    for kt in range(S // 128):
        par = kt % 2
        if kt >= 2:
            p.wait("tensor", sV, tr_cons[par])
        p.op("tensor", lambda e, kt=kt, par=par: e.transpose(
            psTrV[par][:], vf[:, kt * 128:(kt + 1) * 128], eyeb[:]), sT, 1)
        p.wait("vector", sT, sT.n)
        p.op("vector", lambda e, kt=kt, par=par: e.tensor_copy(
            v_tm[:, kt * 128:(kt + 1) * 128], psTrV[par][:]), sV, 1)
        tr_cons[par] = sV.n
    with nc.Block() as blk:
        p.flush(blk)
    esB0.close()

    esB1 = contextlib.ExitStack()

    def SBb(name, shape, dt):
        return esB1.enter_context(nc.sbuf_tensor(name, shape, dt))

    def PSb(name, shape, dt=F32):
        return esB1.enter_context(nc.psum_tensor(name, shape, dt))

    QC = 512
    NQ = S // QC
    NKT = S // 128
    xfB = [SBb(f"xfB{i}", [128, FC * QC], F32) for i in range(2)]
    ptb = [SBb(f"ptb{i}", [128, 1024], BF16) for i in range(2)]
    attnT = [SBb(f"attnT{i}", [128, QC], BF16) for i in range(2)]
    lnS = [SBb(f"lnS{i}", [128, QC], F32) for i in range(2)]
    rcp = [SBb(f"rcp{i}", [128, QC], F32) for i in range(2)]
    psSc = [PSb(f"psSc{i}", [128, 1024]) for i in range(2)]
    psPV = [PSb(f"psPV{i}", [128, QC]) for i in range(2)]
    psSm = [PSb(f"psSm{i}", [128, QC]) for i in range(2)]

    xB_loads = [0] * NQ
    x2_st = [0] * NQ
    wo_ev_all = [0] * NQ
    prev_exp = {0: 0, 1: 0}
    prev_spv = {0: 0, 1: 0}
    attn_overwrite_t = {0: 0, 1: 0}

    for q in range(NQ):
        qpar = q % 2
        qs = slice(q * QC, (q + 1) * QC)
        if q >= 2:
            p.wait("gpsimd", dG, x2_st[q - 2])
        for fc in range(FC):
            p.op("gpsimd", lambda e, q=q, fc=fc, qpar=qpar: e.dma_start(
                out=xfB[qpar][:, fc * QC:(fc + 1) * QC],
                in_=x_t[fc * 128:(fc + 1) * 128, q * QC:(q + 1) * QC]),
                dG, 16)
        xB_loads[q] = dG.n

        if q >= 1:
            p.wait("tensor", sV, wo_ev_all[q - 1])
        exp_a = [0] * (NKT // 2)
        spv_t = [0] * (NKT // 2)
        for pr in range(NKT // 2):
            par = pr % 2
            p.wait("tensor", sA, exp_a[pr - 2] if pr >= 2 else prev_exp[par])
            for half in range(2):
                kt = pr * 2 + half
                ks = slice(kt * 128, (kt + 1) * 128)
                hs = slice(half * 512, (half + 1) * 512)
                p.op("tensor", lambda e, ks=ks, hs=hs, par=par, qs=qs:
                     e.matmul(psSc[par][:, hs], cosk[:, ks], cosq[:, qs],
                              start=True, stop=False))
                p.op("tensor", lambda e, ks=ks, hs=hs, par=par, qs=qs:
                     e.matmul(psSc[par][:, hs], sink[:, ks], sinq[:, qs],
                              start=False, stop=True),
                     sT if half == 1 else None, 1, fast=True)
            sc_t = sT.n
            p.wait("scalar", sT, sc_t)
            p.wait("scalar", sT, spv_t[pr - 2] if pr >= 2 else prev_spv[par])
            p.op("scalar", lambda e, par=par: e.activation(
                ptb[par][:], psSc[par][:], AF.Exp), sA, 1)
            exp_a[pr] = sA.n
            p.wait("tensor", sA, exp_a[pr])
            for half in range(2):
                kt = pr * 2 + half
                hs = slice(half * 512, (half + 1) * 512)
                p.op("tensor", lambda e, par=par, hs=hs, qpar=qpar, kt=kt:
                     e.matmul(psSm[qpar][:], ones_b[:], ptb[par][:, hs],
                              start=(kt == 0), stop=(kt == NKT - 1)))
                p.op("tensor", lambda e, par=par, hs=hs, qpar=qpar, kt=kt:
                     e.matmul(psPV[qpar][:], v_tm[:, kt * 128:(kt + 1) * 128],
                              ptb[par][:, hs],
                              start=(kt == 0), stop=(kt == NKT - 1)),
                     sT if kt == NKT - 1 else None, 1, fast=True)
            spv_t[pr] = sT.n
        prev_exp = {0: exp_a[NKT // 2 - 2], 1: exp_a[NKT // 2 - 1]}
        prev_spv = {0: spv_t[NKT // 2 - 2], 1: spv_t[NKT // 2 - 1]}
        pv_t = sT.n

        p.wait("scalar", sT, pv_t)
        p.op("scalar", lambda e, qpar=qpar: e.activation(
            lnS[qpar][:], psSm[qpar][:], AF.Ln), sA, 1)
        p.wait("vector", sA, sA.n)
        p.op("vector", lambda e, qpar=qpar: e.tensor_scalar(
            lnS[qpar][:], lnS[qpar][:], -1.0, None, ALU.mult), sV, 1)
        p.wait("scalar", sV, sV.n)
        p.op("scalar", lambda e, qpar=qpar: e.activation(
            rcp[qpar][:], lnS[qpar][:], AF.Exp), sA, 1)
        rcp_a = sA.n
        p.wait("vector", sA, rcp_a)
        p.wait("vector", sT, attn_overwrite_t[qpar])
        p.op("vector", lambda e, qpar=qpar: e.tensor_mul(
            attnT[qpar][:], psPV[qpar][:], rcp[qpar][:]), sV, 1)
        attn_v = sV.n

        p.wait("tensor", sV, attn_v)
        p.wait("vector", dG, xB_loads[q])
        wo_ev = [0] * FC
        for dc in range(FC):
            wo_ps = psPV[1 - qpar] if dc % 2 == 0 else psSm[qpar]
            if dc >= 2:
                p.wait("tensor", sV, wo_ev[dc - 2])
            p.op("tensor", lambda e, dc=dc, ps_=wo_ps, qpar=qpar: e.matmul(
                ps_[:], woTb[:, dc * 128:(dc + 1) * 128], attnT[qpar][:],
                start=True, stop=True), sT, 1, fast=True)
            mm_t = sT.n
            p.wait("vector", sT, mm_t)
            p.op("vector", lambda e, dc=dc, ps_=wo_ps, qpar=qpar:
                 e.scalar_tensor_tensor(
                     xfB[qpar][:, dc * QC:(dc + 1) * QC], ps_[:],
                     boP[:, dc:dc + 1],
                     xfB[qpar][:, dc * QC:(dc + 1) * QC],
                     ALU.add, ALU.add), sV, 1)
            wo_ev[dc] = sV.n
            if dc == FC - 1:
                attn_overwrite_t[qpar] = mm_t
        wo_ev_all[q] = sV.n
        p.wait("gpsimd", sV, wo_ev_all[q])
        for fc in range(FC):
            p.op("gpsimd", lambda e, q=q, fc=fc, qpar=qpar: e.dma_start(
                out=x2_hbm[fc * 128:(fc + 1) * 128, q * QC:(q + 1) * QC],
                in_=xfB[qpar][:, fc * QC:(fc + 1) * QC]), dG, 16)
        x2_st[q] = dG.n

    with nc.Block() as blk:
        p.flush(blk)
    esB1.close()
    esB.close()
    esAB.close()

    esC = contextlib.ExitStack()

    def SBc(name, shape, dt):
        return esC.enter_context(nc.sbuf_tensor(name, shape, dt))

    def PSc(name, shape, dt=F32):
        return esC.enter_context(nc.psum_tensor(name, shape, dt))

    CC = 256
    NCC = SUB // CC
    htil = SBc("htil", [128, NE * SUB * 4], BF16)
    ln2b = SBc("ln2b", [128, FC * SUB], BF16)
    x2f = [SBc(f"x2f{i}", [128, FC * CC], F32) for i in range(2)]
    xb2 = SBc("xb2", [128, FC * CC], BF16)
    x2lo = SBc("x2lo", [128, FC * CC], BF16)
    xsq2 = SBc("xsq2", [128, FC * CC], BF16)
    st1C = SBc("st1C", [128, CC], F32)
    st2C = SBc("st2C", [128, CC], F32)
    rrepC = SBc("rrepC", [128, CC], F32)
    nmrC = SBc("nmrC", [128, CC], F32)
    gaC = SBc("gaC", [128, CC], F32)
    gbC = SBc("gbC", [128, CC], F32)
    coefT = SBc("coefT", [8, SUB], BF16)
    sel8 = SBc("sel8", [8, 8 * 128], BF16)
    c8m = SBc("c8m", [128, 8], F32)
    c8q = SBc("c8q", [128, 8], F32)
    c8a = SBc("c8a", [128, 8], F32)
    c8b = SBc("c8b", [128, 8], F32)
    c8c = SBc("c8c", [128, 8], F32)
    rstd8 = SBc("rstd8", [128, 8], F32)
    nmr8 = SBc("nmr8", [128, 8], F32)
    r64a = SBc("r64a", [128, 64], F32)
    r64b = SBc("r64b", [128, 64], F32)
    coefb64 = SBc("coefb64", [128, 64], BF16)
    wslot = [SBc(f"wslot{i}", [128, FC * DE], BF16) for i in range(2)]
    wstgC = SBc("wstgC", [128, 1536], F32)
    dslot = [SBc(f"dslot{i}", [128, NE * 4 * 128], BF16) for i in range(2)]
    gsall = SBc("gsall", [128, 8 * 512], BF16)
    t2b = [SBc("t2b0", [128, 512], F32)] * 2
    x2r = [SBc(f"x2r{i}", [128, 512], F32) for i in range(2)]


    psG = [PSc(f"psG{i}", [128, 512]) for i in range(2)]
    psU = [PSc(f"psU{i}", [128, 512]) for i in range(2)]
    psD = [PSc(f"psD{i}", [128, 512]) for i in range(2)]
    psM = PSc("psM", [128, 512])
    psTr = PSc("psTr", [8, 128], BF16)

    wstg_free_v = 0
    slot_free_t = [0, 0, 0]
    slot_scr_dG = [0, 0, 0]
    dslot_free_t = [0, 0]
    dslot_scr_dG = [0, 0]
    x2f_cons = [0, 0]
    x2f_cons_v = [0, 0]
    x2r_free = [0, 0]
    ln2b_free_t = 0
    psM_free_v = 0
    psD_free_v = 0
    gu_prev_silu = {0: 0, 1: 0}
    up_prev_ev = {0: 0, 1: 0}
    prev_up_ev_all = {i: 0 for i in range(8)}

    def fetch_piece(dst_sb, src_f32, src_bf, sub, length, slot_free_t,
                    scr_done_dG, src_off=0):
        nonlocal wstg_free_v
        if sub == 0:
            nh = (length + 1535) // 1536
            for hh in range(nh):
                a = hh * 1536
                b = min(length, a + 1536)
                p.wait("sync", sV, wstg_free_v)
                p.op("sync", lambda e, s=src_f32, a=a, b=b, o=src_off:
                     e.dma_start(out=wstgC[:, 0:b - a],
                                 in_=s[:, o + a:o + b]), dW, 16)
                ld = dW.n
                p.wait("vector", dW, ld)
                if hh == 0:
                    p.wait("vector", sT, slot_free_t)
                    p.wait("vector", dG, scr_done_dG)
                p.op("vector", lambda e, d=dst_sb, a=a, b=b: e.tensor_copy(
                    d[:, a:b], wstgC[:, 0:b - a]), sV, 1)
                wstg_free_v = sV.n
            p.wait("gpsimd", sV, wstg_free_v)
            p.op("gpsimd", lambda e, d=dst_sb, s=src_bf, L=length,
                 o=src_off: e.dma_start(out=s[:, o:o + L], in_=d[:, 0:L]),
                 dG, 16)
            return sV, wstg_free_v, dG.n
        p.wait("sync", sT, slot_free_t)
        p.wait("sync", dG, scr_done_dG)
        p.op("sync", lambda e, d=dst_sb, s=src_bf, L=length, o=src_off:
             e.dma_start(out=d[:, 0:L], in_=s[:, o:o + L]), dW, 16)
        return dW, dW.n, scr_done_dG

    p.op("sync", lambda e: e.dma_start(out=sel8[:], in_=sel_in[:]), dW, 16)
    p.wait("tensor", dW, dW.n)
    p.op("vector", lambda e: e.tensor_copy(r_hi[:], r_sb[:]))
    vdrain()
    p.op("vector", lambda e: e.tensor_sub(r_lo[:], r_sb[:], r_hi[:]), sV, 1)
    p.wait("tensor", sV, sV.n)

    for sub in range(NSUB):
        for h in range(NCC):
            ci = sub * NCC + h
            par = h % 2
            t0 = sub * SUB + h * CC
            p.wait("gpsimd", sT, x2f_cons[par])
            p.wait("gpsimd", sV, x2f_cons_v[par])
            for fc in range(FC):
                p.op("gpsimd", lambda e, fc=fc, t0=t0, par=par: e.dma_start(
                    out=x2f[par][:, fc * CC:(fc + 1) * CC],
                    in_=x2_hbm[fc * 128:(fc + 1) * 128, t0:t0 + CC]), dG, 16)
            x2_ld = dG.n
            p.wait("vector", dG, x2_ld)
            p.wait("vector", sT, chunk_last_t if ci > 0 else 0)
            p.op("vector", lambda e, par=par: e.tensor_copy(xb2[:],
                                                            x2f[par][:]))
            p.op("vector", lambda e, par=par: e.tensor_sub(
                x2lo[:], x2f[par][:], xb2[:]))
            p.op("vector", lambda e: e.tensor_mul(xsq2[:], xb2[:], xb2[:]),
                 sV, 1)
            cast2_v = sV.n
            p.wait("tensor", sV, cast2_v)
            p.wait("tensor", sV, psD_free_v)
            for kc in range(FC):
                p.op("tensor", lambda e, kc=kc: e.matmul(
                    psD[0][:, 0:CC], ones_b[:], xb2[:, kc * CC:(kc + 1) * CC],
                    start=(kc == 0), stop=(kc == FC - 1)))
            for kc in range(FC):
                p.op("tensor", lambda e, kc=kc: e.matmul(
                    psD[1][:, 0:CC], ones_b[:], xsq2[:, kc * CC:(kc + 1) * CC],
                    start=(kc == 0), stop=(kc == FC - 1)),
                    sT if kc == FC - 1 else None, 1, fast=True)
            stats_mm_done = sT.n
            p.wait("vector", sT, stats_mm_done)
            p.op("vector", lambda e: e.tensor_scalar(
                st1C[:], psD[0][:, 0:CC], 1.0 / D, None, ALU.mult))
            p.op("vector", lambda e: e.tensor_scalar(
                st2C[:], psD[1][:, 0:CC], 1.0 / D, None, ALU.mult))
            p.op("vector", lambda e: e.scalar_tensor_tensor(
                gaC[:], st1C[:], -1.0, st1C[:], ALU.mult, ALU.mult))
            p.op("vector", lambda e: e.scalar_tensor_tensor(
                gbC[:], st2C[:], 1e-5, gaC[:], ALU.add, ALU.add), sV, 1)
            psD_free_v = sV.n
            p.wait("scalar", sV, psD_free_v)
            p.op("scalar", lambda e: e.activation(gaC[:], gbC[:], AF.Ln),
                 sA, 1)
            p.wait("vector", sA, sA.n)
            p.op("vector", lambda e: e.tensor_scalar(
                gbC[:], gaC[:], -0.5, None, ALU.mult), sV, 1)
            p.wait("scalar", sV, sV.n)
            p.op("scalar", lambda e: e.activation(rrepC[:], gbC[:], AF.Exp),
                 sA, 1)
            p.wait("vector", sA, sA.n)
            p.op("vector", lambda e: e.scalar_tensor_tensor(
                nmrC[:], st1C[:], -1.0, rrepC[:], ALU.mult, ALU.mult))
            p.wait("vector", sT, ln2b_free_t if (sub > 0 and h == 0) else 0)
            for fc in range(FC):
                p.op("vector", lambda e, fc=fc, par=par: e.tensor_mul(
                    gaC[:], x2f[par][:, fc * CC:(fc + 1) * CC], rrepC[:]))
                p.op("vector", lambda e, fc=fc, h=h: e.tensor_add(
                    ln2b[:, fc * SUB + h * CC:fc * SUB + (h + 1) * CC],
                    gaC[:], nmrC[:]),
                    sV if fc == FC - 1 else None, 1)
            ln2b_chunk_v = sV.n

            for tt in range(CC // 128):
                ti = h * 2 + tt
                tsl = slice(tt * 128, (tt + 1) * 128)
                if ti == 0:
                    p.wait("tensor", sV, psM_free_v)
                for kc in range(FC):
                    p.op("tensor", lambda e, kc=kc, tsl=tsl, ti=ti: e.matmul(
                        psM[:, 64 + ti:65 + ti],
                        xb2[:, kc * CC:(kc + 1) * CC][:, tsl],
                        ones_b[:, 0:1],
                        start=(kc == 0), stop=(kc == FC - 1)))
                for kc in range(FC):
                    p.op("tensor", lambda e, kc=kc, tsl=tsl, ti=ti: e.matmul(
                        psM[:, 72 + ti:73 + ti],
                        xsq2[:, kc * CC:(kc + 1) * CC][:, tsl],
                        ones_b[:, 0:1],
                        start=(kc == 0), stop=(kc == FC - 1)))
                for pas, xs, rp in ((0, xb2, r_hi), (1, xb2, r_lo),
                                    (2, x2lo, r_hi)):
                    for kc in range(FC):
                        p.op("tensor", lambda e, kc=kc, tsl=tsl, xs=xs,
                             rp=rp, pas=pas, ti=ti: e.matmul(
                                 psM[:, ti * 8:(ti + 1) * 8],
                                 xs[:, kc * CC:(kc + 1) * CC][:, tsl],
                                 rp[:, kc * 8:(kc + 1) * 8],
                                 start=(kc == 0 and pas == 0),
                                 stop=(kc == FC - 1 and pas == 2)),
                             sT if (kc == FC - 1 and pas == 2) else None, 1)
            x2f_cons[par] = sT.n
            x2f_cons_v[par] = ln2b_chunk_v
            chunk_last_t = sT.n

        def d3(h_):
            return bass.AP(h_, 0, [[64, 128], [8, 8], [1, 8]])

        def segb(h_):
            return bass.AP(h_, 0, [[8, 128], [1, 8], [0, 8]])

        def tileb(h_):
            return bass.AP(h_, 0, [[8, 128], [0, 8], [1, 8]])

        p.wait("vector", sT, chunk_last_t)
        p.op("vector", lambda e: e.tensor_scalar(
            c8m[:], psM[:, 64:72], 1.0 / D, None, ALU.mult))
        p.op("vector", lambda e: e.tensor_scalar(
            c8q[:], psM[:, 72:80], 1.0 / D, None, ALU.mult))
        vdrain()
        p.op("vector", lambda e: e.scalar_tensor_tensor(
            c8a[:], c8m[:], -1.0, c8m[:], ALU.mult, ALU.mult))
        vdrain()
        p.op("vector", lambda e: e.scalar_tensor_tensor(
            c8b[:], c8q[:], 1e-5, c8a[:], ALU.add, ALU.add), sV, 1)
        p.wait("scalar", sV, sV.n)
        p.op("scalar", lambda e: e.activation(c8a[:], c8b[:], AF.Ln), sA, 1)
        p.wait("vector", sA, sA.n)
        p.op("vector", lambda e: e.tensor_scalar(
            c8b[:], c8a[:], -0.5, None, ALU.mult), sV, 1)
        p.wait("scalar", sV, sV.n)
        p.op("scalar", lambda e: e.activation(rstd8[:], c8b[:], AF.Exp),
             sA, 1)
        p.wait("vector", sA, sA.n)
        p.op("vector", lambda e: e.scalar_tensor_tensor(
            nmr8[:], c8m[:], -1.0, rstd8[:], ALU.mult, ALU.mult))
        p.op("vector", lambda e: e.tensor_copy(r64a[:], psM[:, 0:64]), sV, 1)
        psM_free_v = sV.n
        vdrain()
        p.op("vector", lambda e: e.tensor_tensor(
            d3(r64a), d3(r64a), segb(rstd8), ALU.mult))
        p.op("vector", lambda e: e.tensor_tensor(
            d3(r64b), tileb(sr_rep), segb(nmr8), ALU.mult))
        vdrain()
        p.op("vector", lambda e: e.tensor_add(r64a[:], r64a[:], r64b[:]))
        vdrain()
        p.op("vector", lambda e: e.reduce_max(c8a[:], d3(r64a), axis=AX))
        vdrain()
        p.op("vector", lambda e: e.tensor_tensor(
            d3(r64b), d3(r64a), segb(c8a), ALU.subtract), sV, 1)
        p.wait("scalar", sV, sV.n)
        p.op("scalar", lambda e: e.activation(r64a[:], r64b[:], AF.Exp),
             sA, 1)
        p.wait("vector", sA, sA.n)
        p.op("vector", lambda e: e.reduce_sum(c8b[:], d3(r64a), axis=AX))
        vdrain()
        p.op("vector", lambda e: e.reciprocal(c8c[:], c8b[:]))
        vdrain()
        p.op("vector", lambda e: e.tensor_tensor(
            d3(r64b), d3(r64a), segb(c8c), ALU.mult))
        vdrain()
        p.op("vector", lambda e: e.reduce_max(c8a[:], d3(r64b), axis=AX))
        vdrain()
        p.op("vector", lambda e: e.tensor_tensor(
            d3(r64a), d3(r64b), segb(c8a), ALU.is_equal))
        vdrain()
        p.op("vector", lambda e: e.scalar_tensor_tensor(
            r64a[:], r64a[:], -2.0, r64b[:], ALU.mult, ALU.add))
        vdrain()
        p.op("vector", lambda e: e.reduce_max(c8b[:], d3(r64a), axis=AX))
        vdrain()
        p.op("vector", lambda e: e.tensor_tensor(
            d3(r64a), d3(r64b), segb(c8b), ALU.is_ge))
        vdrain()
        p.op("vector", lambda e: e.tensor_mul(r64a[:], r64a[:], r64b[:]))
        vdrain()
        p.op("vector", lambda e: e.tensor_copy(coefb64[:], r64a[:]), sV, 1)
        coef_cast_v = sV.n
        p.wait("tensor", sV, coef_cast_v)
        for ti in range(8):
            p.op("tensor", lambda e, ti=ti: e.transpose(
                psTr[:], coefb64[:, ti * 8:(ti + 1) * 8], eyeb[:]), sT, 1)
            p.wait("vector", sT, sT.n)
            p.op("vector", lambda e, ti=ti: e.tensor_copy(
                coefT[:, ti * 128:(ti + 1) * 128], psTr[:]), sV, 1)
            if ti < 7:
                p.wait("tensor", sV, sV.n)
        coef_ready_v = sV.n
        if dbg:
            p.wait("gpsimd", sV, coef_ready_v)
            p.op("gpsimd", lambda e, sub=sub: e.dma_start(
                out=dbg_coef[:, sub * SUB:(sub + 1) * SUB], in_=coefT[:]),
                dG, 16)
            if sub == 0:
                p.op("gpsimd", lambda e: e.dma_start(
                    out=dbg_ln2b[:], in_=ln2b[:]), dG, 16)

        for pe in range(NE):
            gslot = 0
            uslot = 1
            gsem, gval, slot_scr_dG[gslot] = fetch_piece(
                wslot[gslot], gup[pe, 0], gup_sc[pe, 0], sub, FC * DE,
                slot_free_t[gslot], slot_scr_dG[gslot])
            usem, uval, slot_scr_dG[uslot] = fetch_piece(
                wslot[uslot], gup[pe, 1], gup_sc[pe, 1], sub, FC * DE,
                slot_free_t[uslot], slot_scr_dG[uslot])
            p.wait("tensor", gsem, gval)
            p.wait("tensor", sV, coef_ready_v)
            silu_a = {}
            for dcq in range(8):
                qc, dc = dcq // 4, dcq % 4
                par = dcq % 2
                p.wait("tensor", sA,
                       silu_a[dcq - 2] if dcq >= 2 else gu_prev_silu[par])
                for kc in range(FC):
                    p.op("tensor", lambda e, kc=kc, dc=dc, qc=qc, par=par,
                         gslot=gslot: e.matmul(
                             psG[par][:],
                             wslot[gslot][:, kc * DE + dc * 128:
                                          kc * DE + (dc + 1) * 128],
                             ln2b[:, kc * SUB + qc * 512:
                                  kc * SUB + (qc + 1) * 512],
                             start=(kc == 0), stop=(kc == FC - 1)),
                         sT if kc == FC - 1 else None, 1, fast=True)
                g_t = sT.n
                p.wait("scalar", sT, g_t)
                p.wait("scalar", sV, prev_up_ev_all[dcq])
                p.op("scalar", lambda e, dcq=dcq, par=par: e.activation(
                    gsall[:, dcq * 512:(dcq + 1) * 512], psG[par][:],
                    AF.Silu), sA, 1)
                silu_a[dcq] = sA.n
            slot_free_t[gslot] = sT.n
            gu_prev_silu = {0: silu_a[6], 1: silu_a[7]}

            p.wait("tensor", usem, uval)
            up_ev = {}
            crep_t = 0
            for dcq in range(8):
                qc, dc = dcq // 4, dcq % 4
                par = dcq % 2
                if pe < 8 and dc == 0:
                    p.wait("tensor", sV, psM_free_v)
                    p.op("tensor", lambda e, pe=pe, qc=qc: e.matmul(
                        psM[:], sel8[:, pe * 128:(pe + 1) * 128],
                        coefT[:, qc * 512:(qc + 1) * 512],
                        start=True, stop=True), sT, 1)
                    crep_t = sT.n
                p.wait("tensor", sV,
                       up_ev[dcq - 2] if dcq >= 2 else up_prev_ev[par])
                for kc in range(FC):
                    p.op("tensor", lambda e, kc=kc, dc=dc, qc=qc, par=par,
                         uslot=uslot: e.matmul(
                             psU[par][:],
                             wslot[uslot][:, kc * DE + dc * 128:
                                          kc * DE + (dc + 1) * 128],
                             ln2b[:, kc * SUB + qc * 512:
                                  kc * SUB + (qc + 1) * 512],
                             start=(kc == 0), stop=(kc == FC - 1)),
                         sT if kc == FC - 1 else None, 1, fast=True)
                u_t = sT.n
                p.wait("vector", sT, u_t)
                p.wait("vector", sA, silu_a[dcq])
                hpos = pe * 4 * SUB + dc * SUB + qc * 512
                if pe < 8:
                    p.op("vector", lambda e, dcq=dcq, par=par: e.tensor_mul(
                        t2b[par][:], psU[par][:],
                        gsall[:, dcq * 512:(dcq + 1) * 512]))
                    p.op("vector", lambda e, hpos=hpos, par=par:
                         e.tensor_mul(htil[:, hpos:hpos + 512], t2b[par][:],
                                      psM[:]), sV, 1)
                else:
                    p.op("vector", lambda e, dcq=dcq, par=par, hpos=hpos:
                         e.tensor_mul(htil[:, hpos:hpos + 512], psU[par][:],
                                      gsall[:, dcq * 512:(dcq + 1) * 512]),
                         sV, 1)
                up_ev[dcq] = sV.n
                if pe < 8 and dc == 3:
                    psM_free_v = sV.n
            slot_free_t[uslot] = sT.n
            up_prev_ev = {0: up_ev[6], 1: up_ev[7]}
            prev_up_ev_all = dict(up_ev)
        htil_v = sV.n

        if dbg and sub == 0:
            p.wait("gpsimd", sV, htil_v)
            p.op("gpsimd", lambda e: e.dma_start(
                out=dbg_htil[:], in_=htil[:, 0:2 * SUB * 4]), dG, 16)
        p.wait("tensor", sV, htil_v)
        dev_v = {}
        for dc in range(FC):
            dpar = dc % 2
            dsem, dval, dslot_scr_dG[dpar] = fetch_piece(
                dslot[dpar], dwn[dc], dwn_sc[dc], sub, NE * 4 * 128,
                dslot_free_t[dpar], dslot_scr_dG[dpar])
            p.wait("tensor", dsem, dval)
            for qc in range(2):
                idx = dc * 2 + qc
                par = idx % 2
                p.wait("gpsimd", yst, x2r_free[par])
                p.op("gpsimd", lambda e, dc=dc, qc=qc, sub=sub, par=par:
                     e.dma_start(
                         out=x2r[par][:],
                         in_=x2_hbm[dc * 128:(dc + 1) * 128,
                                    sub * SUB + qc * 512:
                                    sub * SUB + (qc + 1) * 512]), dG, 16)
                x2r_ld = dG.n
                if idx >= 2:
                    p.wait("tensor", sV, dev_v[idx - 2])
                for pe in range(NE):
                    for cc in range(4):
                        blk = pe * 4 + cc
                        first = (pe == 0 and cc == 0)
                        last = (pe == NE - 1 and cc == 3)
                        p.op("tensor",
                             lambda e, blk=blk, pe=pe, cc=cc, qc=qc, par=par,
                             dpar=dpar, first=first, last=last: e.matmul(
                                 psD[par][:],
                                 dslot[dpar][:, blk * 128:(blk + 1) * 128],
                                 htil[:, pe * 4 * SUB + cc * SUB + qc * 512:
                                      pe * 4 * SUB + cc * SUB +
                                      (qc + 1) * 512],
                                 start=first, stop=last),
                             sT if last else None, 1, fast=True)
                if qc == 1:
                    dslot_free_t[dpar] = sT.n
                dmm_t = sT.n
                p.wait("vector", sT, dmm_t)
                p.wait("vector", dG, x2r_ld)
                p.op("vector", lambda e, par=par: e.tensor_add(
                    x2r[par][:], psD[par][:], x2r[par][:]), sV, 1)
                dev_v[idx] = sV.n
                p.wait("gpsimd", sV, dev_v[idx])
                p.op("gpsimd", lambda e, dc=dc, qc=qc, sub=sub, par=par:
                     e.dma_start(
                         out=y[dc * 128:(dc + 1) * 128,
                               sub * SUB + qc * 512:
                               sub * SUB + (qc + 1) * 512],
                         in_=x2r[par][:]), yst, 16)
                x2r_free[par] = yst.n
        psD_free_v = dev_v[2 * FC - 1]
        ln2b_free_t = sT.n
        stats_mm_done = sT.n

    p.wait("gpsimd", yst, yst.n)
    p.wait("gpsimd", dG, dG.n)

    with nc.Block() as blk:
        p.flush(blk)
    esC.close()
    es.close()
    return nc


_NC_CACHE = {}


def _get_nc():
    if "nc" not in _NC_CACHE:
        _NC_CACHE["nc"] = build_nc()
    return _NC_CACHE["nc"]


def _mk_sel():
    s = np.zeros((8, 8 * 128), np.float32)
    for e in range(8):
        s[e, e * 128:(e + 1) * 128] = 1.0
    return s.astype(ml_dtypes.bfloat16)


def _prep_inputs(inputs):
    bf = ml_dtypes.bfloat16
    x = np.asarray(inputs["L0_attn_x"])

    def tile_gu(w):
        return np.ascontiguousarray(
            w.reshape(8, 128, DE).transpose(1, 0, 2).reshape(128, 8 * DE))

    eg = np.asarray(inputs["L0_ffn_experts_gate"])
    eu = np.asarray(inputs["L0_ffn_experts_up"])
    ed = np.asarray(inputs["L0_ffn_experts_down"])
    sg = np.asarray(inputs["L0_ffn_shared_gate"])
    su = np.asarray(inputs["L0_ffn_shared_up"])
    sd = np.asarray(inputs["L0_ffn_shared_down"])
    gup = np.zeros((NE, 2, 128, FC * DE), np.float32)
    for e in range(8):
        gup[e, 0] = tile_gu(eg[e])
        gup[e, 1] = tile_gu(eu[e])
    gup[8, 0] = tile_gu(sg[:, 0:DE])
    gup[8, 1] = tile_gu(su[:, 0:DE])
    gup[9, 0] = tile_gu(sg[:, DE:2 * DE])
    gup[9, 1] = tile_gu(su[:, DE:2 * DE])

    downs = [ed[e] for e in range(8)] + [sd[0:DE, :], sd[DE:2 * DE, :]]
    dwn = np.zeros((FC, 128, NE * 4 * 128), np.float32)
    for dc in range(FC):
        blocks = []
        for pe in range(NE):
            w = downs[pe]
            for cc in range(4):
                blocks.append(w[cc * 128:(cc + 1) * 128,
                                dc * 128:(dc + 1) * 128])
        dwn[dc] = np.concatenate(blocks, axis=1)

    common = dict(
        wqT=np.ascontiguousarray(np.asarray(inputs["L0_attn_Wq"]).T),
        wkT=np.ascontiguousarray(np.asarray(inputs["L0_attn_Wk"]).T),
        wvT=np.ascontiguousarray(np.asarray(inputs["L0_attn_Wv"]).T),
        woT=np.ascontiguousarray(np.asarray(inputs["L0_attn_Wo"]).T),
        wavel=np.ascontiguousarray(np.asarray(inputs["L0_attn_wavelengths"])),
        pbias=np.ascontiguousarray(np.asarray(inputs["L0_attn_phase_bias"])),
        bq=np.ascontiguousarray(np.asarray(inputs["L0_attn_bq"])),
        bk=np.ascontiguousarray(np.asarray(inputs["L0_attn_bk"])),
        bv=np.ascontiguousarray(np.asarray(inputs["L0_attn_bv"])),
        bo=np.ascontiguousarray(np.asarray(inputs["L0_attn_bo"])),
        router=np.ascontiguousarray(np.asarray(inputs["L0_ffn_router"])),
        gup=gup, dwn=dwn,
        eye_in=np.eye(128, dtype=np.float32).astype(bf),
        sel_in=_mk_sel(),
    )
    per_core = []
    for b in range(8):
        m = dict(common)
        m["x_t"] = np.ascontiguousarray(np.asarray(x[b]).T)
        per_core.append(m)
    return per_core


def kernel(**inputs):
    nc = _get_nc()
    in_maps = _prep_inputs(inputs)
    res = run_bass_kernel_spmd(nc, in_maps, core_ids=list(range(8)))
    out = np.empty((8, S, D), np.float32)
    for b in range(8):
        out[b] = np.asarray(res.results[b]["y"]).T
    return out


# version 07
# speedup vs baseline: 1.2385x; 1.1093x over previous
import contextlib
import math

import numpy as np
import ml_dtypes

import concourse.bass as bass
import concourse.mybir as mybir
from concourse.bass_utils import run_bass_kernel_spmd

F32 = mybir.dt.float32
F16 = mybir.dt.float16
BF16 = mybir.dt.bfloat16
I32 = mybir.dt.int32
ALU = mybir.AluOpType
AF = mybir.ActivationFunctionType
AX = mybir.AxisListType.X

D = 1024
S = 4096
H = 128
FC = D // 128
NE = 10
DE = 512
NSUB = 4
SUB = S // NSUB

TWO_PI = float(2.0 * math.pi)
INV_SQRT_H = 1.0 / math.sqrt(H)
EXP_SHIFT = -float(math.sqrt(H))
ROUND_OFS = 128.0


class Sem:
    def __init__(self, h, is_dma=False):
        self.h = h
        self.is_dma = is_dma
        self.n = 0

    def bump(self, k):
        self.n += k
        return self.n


class Prog:
    ENGS = ("sync", "vector", "scalar", "tensor", "gpsimd")

    def __init__(self):
        self.q = {e: [] for e in self.ENGS}

    def op(self, eng, build, sem=None, k=1, fast=False):
        if sem is not None:
            v = sem.bump(k)
            if sem.is_dma or fast:
                self.q[eng].append(
                    lambda e, b=build, s=sem, kk=k: b(e).then_inc(s.h, kk))
            else:
                def _f(e, b=build, s=sem, kk=k):
                    b(e)
                    e.drain(fusable=False).then_inc(s.h, kk)
                self.q[eng].append(_f)
            return v
        self.q[eng].append(build)
        return None

    def wait(self, eng, sem, val):
        if val:
            self.q[eng].append(lambda e, s=sem, v=val: e.wait_ge(s.h, v))

    def flush(self, block):
        for eng in self.ENGS:
            ops = self.q[eng]
            if not ops:
                continue

            def body(engine, ops=ops):
                for f in ops:
                    f(engine)

            getattr(block, eng)(body)
        self.q = {e: [] for e in self.ENGS}


def build_nc(dbg=False):
    nc = bass.Bass("TRN2", target_bir_lowering=False,
                   detect_race_conditions=False)

    x_t = nc.dram_tensor("x_t", [D, S], F32, kind="ExternalInput")
    wqT = nc.dram_tensor("wqT", [D, H], F32, kind="ExternalInput")
    wkT = nc.dram_tensor("wkT", [D, H], F32, kind="ExternalInput")
    wvT = nc.dram_tensor("wvT", [D, H], F32, kind="ExternalInput")
    woT = nc.dram_tensor("woT", [H, D], F32, kind="ExternalInput")
    wavel = nc.dram_tensor("wavel", [H], F32, kind="ExternalInput")
    pbias = nc.dram_tensor("pbias", [H], F32, kind="ExternalInput")
    bq_d = nc.dram_tensor("bq", [H], F32, kind="ExternalInput")
    bk_d = nc.dram_tensor("bk", [H], F32, kind="ExternalInput")
    bv_d = nc.dram_tensor("bv", [H], F32, kind="ExternalInput")
    bo_d = nc.dram_tensor("bo", [D], F32, kind="ExternalInput")
    router = nc.dram_tensor("router", [D, 8], F32, kind="ExternalInput")
    gup = nc.dram_tensor("gup", [NE, 2, 128, FC * DE], F32,
                         kind="ExternalInput")
    dwn = nc.dram_tensor("dwn", [FC, 128, NE * 4 * 128], F32,
                         kind="ExternalInput")
    eye_in = nc.dram_tensor("eye_in", [128, 128], BF16, kind="ExternalInput")
    sel_in = nc.dram_tensor("sel_in", [8, 8 * 128], BF16, kind="ExternalInput")

    y = nc.dram_tensor("y", [D, S], F32, kind="ExternalOutput")

    x2_hbm = nc.dram_tensor("x2_hbm", [D, S], F32,
                            kind="ExternalOutput" if dbg else "Internal")
    if dbg:
        dbg_t = {n: nc.dram_tensor("dbg_" + n, [128, S], F16,
                                   kind="ExternalOutput")
                 for n in ("cosq", "sinq", "cosk", "sink")}
        dbg_t["vf"] = nc.dram_tensor("dbg_vf", [128, S], BF16,
                                     kind="ExternalOutput")
        dbg_us = {n: nc.dram_tensor("dbg_" + n, [128, S], F16,
                                    kind="ExternalOutput")
                  for n in ("usq", "ucq")}
        dbg_coef = nc.dram_tensor("dbg_coef", [8, S], BF16,
                                  kind="ExternalOutput")
        dbg_htil = nc.dram_tensor("dbg_htil", [128, 2 * SUB * 4], BF16,
                                  kind="ExternalOutput")
        dbg_ln2b = nc.dram_tensor("dbg_ln2b", [128, FC * SUB], BF16,
                                  kind="ExternalOutput")
        dbg_rt = nc.dram_tensor("dbg_rt", [128, 64], F32,
                                kind="ExternalOutput")
        dbg_rt2 = nc.dram_tensor("dbg_rt2", [128, 40 * 8], F32,
                                 kind="ExternalOutput")
    gup_sc = nc.dram_tensor("gup_sc", [NE, 2, 128, FC * DE], BF16)
    dwn_sc = nc.dram_tensor("dwn_sc", [FC, 128, NE * 4 * 128], BF16)

    es = contextlib.ExitStack()

    def SB(name, shape, dt):
        return es.enter_context(nc.sbuf_tensor(name, shape, dt))

    def SEM(name, is_dma=False):
        return Sem(es.enter_context(nc.semaphore(name)), is_dma)

    eyeb = SB("eyeb", [128, 128], BF16)
    ones_b = SB("ones_b", [128, 128], BF16)
    ones_f = SB("ones_f", [128, 128], F32)
    c2pi = SB("c2pi", [128, 1], F32)
    ceps = SB("ceps", [128, 1], F32)
    cnhalf = SB("cnhalf", [128, 1], F32)
    cnone = SB("cnone", [128, 1], F32)
    cinvsq = SB("cinvsq", [128, 1], F32)
    cshift = SB("cshift", [128, 1], F32)
    invw = SB("invw", [128, 1], F32)
    s1c = SB("s1c", [128, 1], F32)
    s2q = SB("s2q", [128, 1], F32)
    s2k = SB("s2k", [128, 1], F32)
    sqc = SB("sqc", [128, 1], F32)
    skc = SB("skc", [128, 1], F32)
    svc = SB("svc", [128, 1], F32)
    bvb = SB("bvb", [128, 1], BF16)
    bqf = SB("bqf", [128, 1], F32)
    bkf = SB("bkf", [128, 1], F32)
    bvf = SB("bvf", [128, 1], F32)
    wlf = SB("wlf", [128, 1], F32)
    pbf = SB("pbf", [128, 1], F32)
    bo8 = SB("bo8", [128, FC], F32)
    boP = SB("boP", [128, FC], F32)
    tcol = SB("tcol", [128, 1], F32)
    wqTb = SB("wqTb", [128, FC * 128], BF16)
    wkTb = SB("wkTb", [128, FC * 128], BF16)
    wvTb = SB("wvTb", [128, FC * 128], BF16)
    woTb = SB("woTb", [128, FC * 128], BF16)
    r_sb = SB("r_sb", [128, FC * 8], F32)
    r_hi = SB("r_hi", [128, FC * 8], BF16)
    r_lo = SB("r_lo", [128, FC * 8], BF16)
    sr_rep = SB("sr_rep", [128, 8], F32)

    dW = SEM("dW", is_dma=True)
    dG = SEM("dG", is_dma=True)
    yst = SEM("yst", is_dma=True)
    sT = SEM("sT")
    sV = SEM("sV")
    sA = SEM("sA")

    p = Prog()

    def vdrain():
        p.op("vector", lambda e: e.drain(fusable=False))

    esAB = contextlib.ExitStack()

    def SBab(name, shape, dt):
        return esAB.enter_context(nc.sbuf_tensor(name, shape, dt))

    cosq = SBab("cosq", [128, S], F16)
    sinq = SBab("sinq", [128, S], F16)
    cosk = SBab("cosk", [128, S], F16)
    sink = SBab("sink", [128, S], F16)
    vf = SBab("vf", [128, S], BF16)

    esA = contextlib.ExitStack()

    def SBa(name, shape, dt):
        return esA.enter_context(nc.sbuf_tensor(name, shape, dt))

    def PSa(name, shape, dt=F32):
        return esA.enter_context(nc.psum_tensor(name, shape, dt))

    CH = 512
    NCH = S // CH
    xfA = [SBa(f"xfA{i}", [128, FC * CH], F32) for i in range(2)]
    xbA = [SBa(f"xbA{i}", [128, FC * CH], BF16) for i in range(2)]
    xsqA = [SBa(f"xsqA{i}", [128, FC * CH], BF16) for i in range(2)]
    stat1 = SBa("stat1", [128, CH], F32)
    stat2 = SBa("stat2", [128, CH], F32)
    rrep = SBa("rrep", [128, CH], F32)
    nmr = SBa("nmr", [128, CH], F32)
    ga = SBa("ga", [128, CH], F32)
    gb = SBa("gb", [128, CH], F32)
    gc = SBa("gc", [128, CH], F32)
    gi = SBa("gi", [128, CH], I32)
    usq = SBa("usq", [128, S], F16)
    ucq = SBa("ucq", [128, S], F16)
    usk = SBa("usk", [128, S], F16)
    uck = SBa("uck", [128, S], F16)
    wstage = SBa("wstage", [128, FC * 128], F32)

    psMean = PSa("psMean", [128, CH])
    psSq = PSa("psSq", [128, CH])
    psQ = [PSa(f"psQ{i}", [128, CH]) for i in range(2)]
    psK = [PSa(f"psK{i}", [128, CH]) for i in range(2)]
    psV = [PSa(f"psV{i}", [128, CH]) for i in range(2)]

    p.op("sync", lambda e: e.dma_start(out=eyeb[:], in_=eye_in[:]), dW, 16)
    for col, src in ((wlf, wavel), (pbf, pbias), (bqf, bq_d), (bkf, bk_d),
                     (bvf, bv_d)):
        p.op("sync", lambda e, c=col, s=src: e.dma_start(
            out=c[:], in_=bass.AP(s, 0, [[1, 128], [1, 1]])), dW, 16)
    for dc in range(FC):
        p.op("sync", lambda e, dc=dc: e.dma_start(
            out=bo8[:, dc:dc + 1],
            in_=bass.AP(bo_d, dc * 128, [[1, 128], [1, 1]])), dW, 16)
    for kc in range(FC):
        p.op("sync", lambda e, kc=kc: e.dma_start(
            out=r_sb[:, kc * 8:(kc + 1) * 8],
            in_=router[kc * 128:(kc + 1) * 128, :]), dW, 16)
    small_loads = dW.n

    p.op("vector", lambda e: e.memset(ones_b[:], 1.0))
    p.op("vector", lambda e: e.memset(ones_f[:], 1.0))
    p.op("vector", lambda e: e.memset(c2pi[:], TWO_PI))
    p.op("vector", lambda e: e.memset(ceps[:], 1e-5))
    p.op("vector", lambda e: e.memset(cnhalf[:], -0.5))
    p.op("vector", lambda e: e.memset(cnone[:], -1.0))
    p.op("vector", lambda e: e.memset(cinvsq[:], INV_SQRT_H))
    p.op("vector", lambda e: e.memset(cshift[:], EXP_SHIFT))
    p.wait("vector", dW, small_loads)
    p.op("vector", lambda e: e.tensor_scalar(tcol[:], wlf[:], 1e-8, None,
                                             ALU.add))
    vdrain()
    p.op("vector", lambda e: e.reciprocal(invw[:], tcol[:]))
    vdrain()
    p.op("vector", lambda e: e.tensor_scalar(s1c[:], invw[:], 1.0 / TWO_PI,
                                             None, ALU.mult))
    for bcol, s2 in ((bqf, s2q), (bkf, s2k)):
        p.op("vector", lambda e, b=bcol, s=s2: e.tensor_mul(s[:], b[:],
                                                            invw[:]))
        vdrain()
        p.op("vector", lambda e, s=s2: e.tensor_add(s[:], s[:], pbf[:]))
        vdrain()
        p.op("vector", lambda e, s=s2: e.tensor_scalar(
            s[:], s[:], 1.0 / TWO_PI, ROUND_OFS, ALU.mult, ALU.add))
    p.op("vector", lambda e: e.tensor_copy(bvb[:], bvf[:]), sV, 1)

    for wi, (wsrc, wdst) in enumerate(((wqT, wqTb), (wkT, wkTb), (wvT, wvTb),
                                       (woT, woTb))):
        p.wait("sync", sV, sV.n)
        for kc in range(FC):
            if wi < 3:
                src = wsrc[kc * 128:(kc + 1) * 128, :]
            else:
                src = wsrc[:, kc * 128:(kc + 1) * 128]
            p.op("sync", lambda e, s=src, kc=kc: e.dma_start(
                out=wstage[:, kc * 128:(kc + 1) * 128], in_=s), dW, 16)
        p.wait("vector", dW, dW.n)
        p.op("vector", lambda e, d=wdst: e.tensor_copy(d[:], wstage[:]), sV, 1)
    wcast_v = sV.n

    p.wait("tensor", sV, wcast_v)
    for wsb, col in ((wqTb, sqc), (wkTb, skc), (wvTb, svc)):
        for kc in range(FC):
            p.op("tensor", lambda e, w=wsb, kc=kc: e.matmul(
                psMean[:, 0:1], w[:, kc * 128:(kc + 1) * 128], ones_b[:, 0:1],
                start=(kc == 0), stop=(kc == FC - 1)),
                sT if kc == FC - 1 else None, 1)
        p.wait("vector", sT, sT.n)
        p.op("vector", lambda e, c=col: e.tensor_copy(c[:], psMean[:, 0:1]),
             sV, 1)
        p.wait("tensor", sV, sV.n)
    for dc in range(FC):
        p.op("tensor", lambda e, dc=dc: e.matmul(
            psSq[:, dc:dc + 1], woTb[:, dc * 128:(dc + 1) * 128], bvb[:, 0:1],
            start=True, stop=True), sT if dc == FC - 1 else None, 1)
    p.wait("vector", sT, sT.n)
    p.op("vector", lambda e: e.tensor_add(boP[:], psSq[:, 0:FC], bo8[:]),
         sV, 1)
    p.op("vector", lambda e: e.tensor_copy(r_hi[:], r_sb[:]))
    vdrain()
    p.op("vector", lambda e: e.tensor_sub(r_lo[:], r_sb[:], r_hi[:]), sV, 1)
    p.wait("tensor", sV, sV.n)
    for pas, rp in ((0, r_hi), (1, r_lo)):
        for kc in range(FC):
            p.op("tensor", lambda e, kc=kc, rp=rp, pas=pas: e.matmul(
                psMean[:, 0:8], ones_b[:], rp[:, kc * 8:(kc + 1) * 8],
                start=(kc == 0 and pas == 0),
                stop=(kc == FC - 1 and pas == 1)),
                sT if (kc == FC - 1 and pas == 1) else None, 1)
    p.wait("vector", sT, sT.n)
    p.op("vector", lambda e: e.tensor_copy(sr_rep[:], psMean[:, 0:8]), sV, 1)
    prol_v = sV.n

    x_loads = [0] * NCH
    cast_v = [0] * NCH
    cons_v = [0] * NCH
    qkv_t = [0] * NCH
    stats_used_v = 0

    for c in range(NCH):
        par = c % 2
        if c >= 2:
            p.wait("sync", sV, cons_v[c - 2])
        for fc in range(FC):
            p.op("sync", lambda e, c=c, fc=fc, par=par: e.dma_start(
                out=xfA[par][:, fc * CH:(fc + 1) * CH],
                in_=x_t[fc * 128:(fc + 1) * 128, c * CH:(c + 1) * CH]),
                dW, 16)
        x_loads[c] = dW.n

        p.wait("vector", dW, x_loads[c])
        if c >= 2:
            p.wait("vector", sT, qkv_t[c - 2])
        p.op("vector", lambda e, par=par: e.tensor_copy(xbA[par][:],
                                                        xfA[par][:]))
        p.op("vector", lambda e, par=par: e.tensor_mul(
            xsqA[par][:], xbA[par][:], xbA[par][:]), sV, 1)
        cast_v[c] = sV.n

        p.wait("tensor", sV, cast_v[c])
        p.wait("tensor", sV, stats_used_v if c > 0 else prol_v)
        for kc in range(FC):
            p.op("tensor", lambda e, par=par, kc=kc: e.matmul(
                psMean[:], ones_b[:], xbA[par][:, kc * CH:(kc + 1) * CH],
                start=(kc == 0), stop=(kc == FC - 1)))
        for kc in range(FC):
            p.op("tensor", lambda e, par=par, kc=kc: e.matmul(
                psSq[:], ones_b[:], xsqA[par][:, kc * CH:(kc + 1) * CH],
                start=(kc == 0), stop=(kc == FC - 1)),
                sT if kc == FC - 1 else None, 1, fast=True)
        stats_t = sT.n

        if c >= 2:
            p.wait("tensor", sV, cons_v[c - 2])
        for wsb, dst, last in ((wqTb, psQ[par], False), (wkTb, psK[par], False),
                               (wvTb, psV[par], True)):
            for kc in range(FC):
                p.op("tensor", lambda e, w=wsb, d=dst, par=par, kc=kc:
                     e.matmul(
                         d[:], w[:, kc * 128:(kc + 1) * 128],
                         xbA[par][:, kc * CH:(kc + 1) * CH],
                         start=(kc == 0), stop=(kc == FC - 1)),
                     sT if (kc == FC - 1 and last) else None, 1, fast=True)
        qkv_t[c] = sT.n

        p.wait("vector", sT, stats_t)
        p.op("vector", lambda e: e.tensor_scalar(stat1[:], psMean[:], 1.0 / D,
                                                 None, ALU.mult))
        p.op("vector", lambda e: e.tensor_scalar(stat2[:], psSq[:], 1.0 / D,
                                                 None, ALU.mult))
        p.op("vector", lambda e: e.scalar_tensor_tensor(
            ga[:], stat1[:], -1.0, stat1[:], ALU.mult, ALU.mult))
        p.op("vector", lambda e: e.scalar_tensor_tensor(
            gb[:], stat2[:], 1e-5, ga[:], ALU.add, ALU.add), sV, 1)
        stats_used_v = sV.n
        p.wait("scalar", sV, stats_used_v)
        p.op("scalar", lambda e: e.activation(ga[:], gb[:], AF.Ln), sA, 1)
        p.wait("vector", sA, sA.n)
        p.op("vector", lambda e: e.tensor_scalar(
            gb[:], ga[:], -0.5, None, ALU.mult), sV, 1)
        p.wait("scalar", sV, sV.n)
        p.op("scalar", lambda e: e.activation(rrep[:], gb[:], AF.Exp), sA, 1)
        p.wait("vector", sA, sA.n)
        p.op("vector", lambda e: e.scalar_tensor_tensor(
            nmr[:], stat1[:], -1.0, rrep[:], ALU.mult, ALU.mult))

        p.wait("vector", sT, qkv_t[c])
        sl = slice(c * CH, (c + 1) * CH)
        for psX, s2x, sxc, us_f, uc_f in ((psQ[par], s2q, sqc, usq, ucq),
                                          (psK[par], s2k, skc, usk, uck)):
            p.op("vector", lambda e, psX=psX: e.tensor_mul(ga[:], psX[:],
                                                           rrep[:]))
            p.op("vector", lambda e, sxc=sxc: e.scalar_tensor_tensor(
                gb[:], nmr[:], sxc[:], ga[:], ALU.mult, ALU.add))
            p.op("vector", lambda e, s2x=s2x: e.tensor_scalar(
                ga[:], gb[:], s1c[:], s2x[:], ALU.mult, ALU.add))
            p.op("vector", lambda e: e.tensor_copy(gi[:], ga[:]))
            p.op("vector", lambda e: e.tensor_copy(gb[:], gi[:]))
            p.op("vector", lambda e: e.tensor_sub(ga[:], ga[:], gb[:]))
            p.op("vector", lambda e: e.tensor_scalar(gb[:], ga[:], 0.5, None,
                                                     ALU.is_gt))
            p.op("vector", lambda e: e.scalar_tensor_tensor(
                gc[:], gb[:], -1.0, ga[:], ALU.mult, ALU.add))
            p.op("vector", lambda e, u=us_f, sl=sl: e.tensor_scalar(
                u[:, sl], gc[:], TWO_PI, None, ALU.mult))
            p.op("vector", lambda e: e.tensor_scalar(
                ga[:], gc[:], 0.25, None, ALU.add))
            p.op("vector", lambda e: e.tensor_scalar(gb[:], ga[:], 0.5, None,
                                                     ALU.is_gt))
            p.op("vector", lambda e: e.scalar_tensor_tensor(
                gc[:], gb[:], -1.0, ga[:], ALU.mult, ALU.add))
            p.op("vector", lambda e, u=uc_f, sl=sl: e.tensor_scalar(
                u[:, sl], gc[:], TWO_PI, None, ALU.mult), sV, 1)
        p.op("vector", lambda e, par=par: e.tensor_mul(ga[:], psV[par][:],
                                                       rrep[:]))
        p.op("vector", lambda e, sl=sl: e.scalar_tensor_tensor(
            vf[:, sl], nmr[:], svc[:], ga[:], ALU.mult, ALU.add), sV, 1)
        cons_v[c] = sV.n

    p.wait("scalar", sV, cons_v[NCH - 1])
    for src, dst in ((usq, sinq), (ucq, cosq), (usk, sink), (uck, cosk)):
        p.op("scalar", lambda e, s=src, d=dst: e.activation(
            d[:], s[:], AF.Sin), sA, 1)
    p.wait("vector", sA, sA.n)
    for t in (cosq, sinq):
        p.op("vector", lambda e, t=t: e.tensor_scalar(
            t[:], t[:], INV_SQRT_H, None, ALU.mult), sV, 1)
    sinscale_v = sV.n
    trig_a = sA.n
    if dbg:
        p.wait("gpsimd", sA, trig_a)
        p.wait("gpsimd", sV, sinscale_v)
        for n, t in (("cosq", cosq), ("sinq", sinq), ("cosk", cosk),
                     ("sink", sink), ("vf", vf)):
            p.op("gpsimd", lambda e, n=n, t=t: e.dma_start(
                out=dbg_t[n][:], in_=t[:]), dG, 16)
        for n, t in (("usq", usq), ("ucq", ucq)):
            p.op("gpsimd", lambda e, n=n, t=t: e.dma_start(
                out=dbg_us[n][:], in_=t[:]), dG, 16)
        p.wait("gpsimd", dG, dG.n)

    with nc.Block() as blk:
        p.flush(blk)
    esA.close()

    esB = contextlib.ExitStack()
    v_tm = esB.enter_context(nc.sbuf_tensor("v_tm", [128, S], BF16))

    esB0 = contextlib.ExitStack()
    psTrV = [esB0.enter_context(nc.psum_tensor(f"psTrV{i}", [128, 128], BF16))
             for i in range(2)]
    tr_cons = [0, 0]
    for kt in range(S // 128):
        par = kt % 2
        if kt >= 2:
            p.wait("tensor", sV, tr_cons[par])
        p.op("tensor", lambda e, kt=kt, par=par: e.transpose(
            psTrV[par][:], vf[:, kt * 128:(kt + 1) * 128], eyeb[:]), sT, 1)
        p.wait("vector", sT, sT.n)
        p.op("vector", lambda e, kt=kt, par=par: e.tensor_copy(
            v_tm[:, kt * 128:(kt + 1) * 128], psTrV[par][:]), sV, 1)
        tr_cons[par] = sV.n
    with nc.Block() as blk:
        p.flush(blk)
    esB0.close()

    esB1 = contextlib.ExitStack()

    def SBb(name, shape, dt):
        return esB1.enter_context(nc.sbuf_tensor(name, shape, dt))

    def PSb(name, shape, dt=F32):
        return esB1.enter_context(nc.psum_tensor(name, shape, dt))

    QC = 512
    NQ = S // QC
    NKT = S // 128
    xfB = [SBb(f"xfB{i}", [128, FC * QC], F32) for i in range(2)]
    wstgB = [SBb(f"wstgB{i}", [128, 5120], F32) for i in range(2)]
    wcastB = [SBb(f"wcastB{i}", [128, 5120], BF16) for i in range(2)]
    ptb = [SBb(f"ptb{i}", [128, 1024], BF16) for i in range(2)]
    attnT = [SBb(f"attnT{i}", [128, QC], BF16) for i in range(2)]
    lnS = [SBb(f"lnS{i}", [128, QC], F32) for i in range(2)]
    rcp = [SBb(f"rcp{i}", [128, QC], F32) for i in range(2)]
    psSc = [PSb(f"psSc{i}", [128, 1024]) for i in range(2)]
    psPV = [PSb(f"psPV{i}", [128, QC]) for i in range(2)]
    psSm = [PSb(f"psSm{i}", [128, QC]) for i in range(2)]

    wpieces = []
    for pe in range(NE):
        for gi_ in range(2):
            wpieces.append((gup[pe, gi_], gup_sc[pe, gi_], FC * DE))
    for dc in range(FC):
        wpieces.append((dwn[dc], dwn_sc[dc], NE * 4 * 128))
    wp_i = [0]
    wcast_free = [0, 0]
    wcast_wr = [0, 0]

    def emit_wcast_round(nmax=4):
        for _ in range(nmax):
            if wp_i[0] >= len(wpieces):
                return
            i = wp_i[0]
            wp_i[0] += 1
            par = i % 2
            srcf, dstb, ln = wpieces[i]
            p.wait("sync", sV, wcast_free[par])
            p.op("sync", lambda e, s=srcf, L=ln, par=par: e.dma_start(
                out=wstgB[par][:, 0:L], in_=s[:, 0:L]), dW, 16)
            ld = dW.n
            p.wait("vector", dW, ld)
            p.wait("vector", dG, wcast_wr[par])
            p.op("vector", lambda e, L=ln, par=par: e.tensor_copy(
                wcastB[par][:, 0:L], wstgB[par][:, 0:L]), sV, 1)
            wcast_free[par] = sV.n
            p.wait("gpsimd", sV, wcast_free[par])
            p.op("gpsimd", lambda e, d=dstb, L=ln, par=par: e.dma_start(
                out=d[:, 0:L], in_=wcastB[par][:, 0:L]), dG, 16)
            wcast_wr[par] = dG.n

    xB_loads = [0] * NQ
    x2_st = [0] * NQ
    wo_ev_all = [0] * NQ
    prev_exp = {0: 0, 1: 0}
    prev_spv = {0: 0, 1: 0}
    attn_overwrite_t = {0: 0, 1: 0}

    for q in range(NQ):
        qpar = q % 2
        qs = slice(q * QC, (q + 1) * QC)
        if q >= 2:
            p.wait("gpsimd", dG, x2_st[q - 2])
        for fc in range(FC):
            p.op("gpsimd", lambda e, q=q, fc=fc, qpar=qpar: e.dma_start(
                out=xfB[qpar][:, fc * QC:(fc + 1) * QC],
                in_=x_t[fc * 128:(fc + 1) * 128, q * QC:(q + 1) * QC]),
                dG, 16)
        xB_loads[q] = dG.n
        emit_wcast_round(4)

        if q >= 1:
            p.wait("tensor", sV, wo_ev_all[q - 1])
        exp_a = [0] * (NKT // 2)
        spv_t = [0] * (NKT // 2)
        for pr in range(NKT // 2):
            par = pr % 2
            p.wait("tensor", sA, exp_a[pr - 2] if pr >= 2 else prev_exp[par])
            for half in range(2):
                kt = pr * 2 + half
                ks = slice(kt * 128, (kt + 1) * 128)
                hs = slice(half * 512, (half + 1) * 512)
                p.op("tensor", lambda e, ks=ks, hs=hs, par=par, qs=qs:
                     e.matmul(psSc[par][:, hs], cosk[:, ks], cosq[:, qs],
                              start=True, stop=False))
                p.op("tensor", lambda e, ks=ks, hs=hs, par=par, qs=qs:
                     e.matmul(psSc[par][:, hs], sink[:, ks], sinq[:, qs],
                              start=False, stop=True),
                     sT if half == 1 else None, 1, fast=True)
            sc_t = sT.n
            p.wait("scalar", sT, sc_t)
            p.wait("scalar", sT, spv_t[pr - 2] if pr >= 2 else prev_spv[par])
            p.op("scalar", lambda e, par=par: e.activation(
                ptb[par][:], psSc[par][:], AF.Exp), sA, 1)
            exp_a[pr] = sA.n
            p.wait("tensor", sA, exp_a[pr])
            for half in range(2):
                kt = pr * 2 + half
                hs = slice(half * 512, (half + 1) * 512)
                p.op("tensor", lambda e, par=par, hs=hs, qpar=qpar, kt=kt:
                     e.matmul(psSm[qpar][:], ones_b[:], ptb[par][:, hs],
                              start=(kt == 0), stop=(kt == NKT - 1)))
                p.op("tensor", lambda e, par=par, hs=hs, qpar=qpar, kt=kt:
                     e.matmul(psPV[qpar][:], v_tm[:, kt * 128:(kt + 1) * 128],
                              ptb[par][:, hs],
                              start=(kt == 0), stop=(kt == NKT - 1)),
                     sT if kt == NKT - 1 else None, 1, fast=True)
            spv_t[pr] = sT.n
        prev_exp = {0: exp_a[NKT // 2 - 2], 1: exp_a[NKT // 2 - 1]}
        prev_spv = {0: spv_t[NKT // 2 - 2], 1: spv_t[NKT // 2 - 1]}
        pv_t = sT.n

        p.wait("scalar", sT, pv_t)
        p.op("scalar", lambda e, qpar=qpar: e.activation(
            lnS[qpar][:], psSm[qpar][:], AF.Ln), sA, 1)
        p.wait("vector", sA, sA.n)
        p.op("vector", lambda e, qpar=qpar: e.tensor_scalar(
            lnS[qpar][:], lnS[qpar][:], -1.0, None, ALU.mult), sV, 1)
        p.wait("scalar", sV, sV.n)
        p.op("scalar", lambda e, qpar=qpar: e.activation(
            rcp[qpar][:], lnS[qpar][:], AF.Exp), sA, 1)
        rcp_a = sA.n
        p.wait("vector", sA, rcp_a)
        p.wait("vector", sT, attn_overwrite_t[qpar])
        p.op("vector", lambda e, qpar=qpar: e.tensor_mul(
            attnT[qpar][:], psPV[qpar][:], rcp[qpar][:]), sV, 1)
        attn_v = sV.n

        p.wait("tensor", sV, attn_v)
        p.wait("vector", dG, xB_loads[q])
        wo_ev = [0] * FC
        for dc in range(FC):
            wo_ps = psPV[1 - qpar] if dc % 2 == 0 else psSm[qpar]
            if dc >= 2:
                p.wait("tensor", sV, wo_ev[dc - 2])
            p.op("tensor", lambda e, dc=dc, ps_=wo_ps, qpar=qpar: e.matmul(
                ps_[:], woTb[:, dc * 128:(dc + 1) * 128], attnT[qpar][:],
                start=True, stop=True), sT, 1, fast=True)
            mm_t = sT.n
            p.wait("vector", sT, mm_t)
            p.op("vector", lambda e, dc=dc, ps_=wo_ps, qpar=qpar:
                 e.scalar_tensor_tensor(
                     xfB[qpar][:, dc * QC:(dc + 1) * QC], ps_[:],
                     boP[:, dc:dc + 1],
                     xfB[qpar][:, dc * QC:(dc + 1) * QC],
                     ALU.add, ALU.add), sV, 1)
            wo_ev[dc] = sV.n
            if dc == FC - 1:
                attn_overwrite_t[qpar] = mm_t
        wo_ev_all[q] = sV.n
        p.wait("gpsimd", sV, wo_ev_all[q])
        for fc in range(FC):
            p.op("gpsimd", lambda e, q=q, fc=fc, qpar=qpar: e.dma_start(
                out=x2_hbm[fc * 128:(fc + 1) * 128, q * QC:(q + 1) * QC],
                in_=xfB[qpar][:, fc * QC:(fc + 1) * QC]), dG, 16)
        x2_st[q] = dG.n

    emit_wcast_round(99)
    all_scratch_dG = dG.n
    with nc.Block() as blk:
        p.flush(blk)
    esB1.close()
    esB.close()
    esAB.close()

    esC = contextlib.ExitStack()

    def SBc(name, shape, dt):
        return esC.enter_context(nc.sbuf_tensor(name, shape, dt))

    def PSc(name, shape, dt=F32):
        return esC.enter_context(nc.psum_tensor(name, shape, dt))

    CC = 256
    NCC = SUB // CC
    htil = SBc("htil", [128, NE * SUB * 4], BF16)
    ln2b = SBc("ln2b", [128, FC * SUB], BF16)
    x2f = [SBc(f"x2f{i}", [128, FC * CC], F32) for i in range(2)]
    xb2 = SBc("xb2", [128, FC * CC], BF16)
    x2lo = SBc("x2lo", [128, FC * CC], BF16)
    xsq2 = SBc("xsq2", [128, FC * CC], BF16)
    st1C = SBc("st1C", [128, CC], F32)
    st2C = SBc("st2C", [128, CC], F32)
    rrepC = SBc("rrepC", [128, CC], F32)
    nmrC = SBc("nmrC", [128, CC], F32)
    gaC = SBc("gaC", [128, CC], F32)
    gbC = SBc("gbC", [128, CC], F32)
    coefT = SBc("coefT", [8, SUB], BF16)
    sel8 = SBc("sel8", [8, 8 * 128], BF16)
    c8m = SBc("c8m", [128, 8], F32)
    c8q = SBc("c8q", [128, 8], F32)
    c8a = SBc("c8a", [128, 8], F32)
    c8b = SBc("c8b", [128, 8], F32)
    c8c = SBc("c8c", [128, 8], F32)
    rstd8 = SBc("rstd8", [128, 8], F32)
    nmr8 = SBc("nmr8", [128, 8], F32)
    r64a = SBc("r64a", [128, 64], F32)
    r64b = SBc("r64b", [128, 64], F32)
    coefb64 = SBc("coefb64", [128, 64], BF16)
    wslot = [SBc(f"wslot{i}", [128, FC * DE], BF16) for i in range(2)]
    wstgC = SBc("wstgC", [128, 1536], F32)
    dslot = [SBc(f"dslot{i}", [128, NE * 4 * 128], BF16) for i in range(2)]
    gsall = SBc("gsall", [128, 8 * 512], BF16)
    t2b = [SBc("t2b0", [128, 512], F32)] * 2
    x2r = [SBc(f"x2r{i}", [128, 512], F32) for i in range(2)]


    psG = [PSc(f"psG{i}", [128, 512]) for i in range(2)]
    psU = [PSc(f"psU{i}", [128, 512]) for i in range(2)]
    psD = [PSc(f"psD{i}", [128, 512]) for i in range(2)]
    psM = PSc("psM", [128, 512])
    psTr = PSc("psTr", [8, 128], BF16)

    wstg_free_v = 0
    slot_free_t = [0, 0, 0]
    slot_scr_dG = [all_scratch_dG] * 3
    dslot_free_t = [0, 0]
    dslot_scr_dG = [all_scratch_dG, all_scratch_dG]
    x2f_cons = [0, 0]
    x2f_cons_v = [0, 0]
    x2r_free = [0, 0]
    ln2b_free_t = 0
    psM_free_v = 0
    psD_free_v = 0
    gu_prev_silu = {0: 0, 1: 0}
    up_prev_ev = {0: 0, 1: 0}
    prev_up_ev_all = {i: 0 for i in range(8)}

    def fetch_piece(dst_sb, src_f32, src_bf, sub, length, slot_free_t,
                    scr_done_dG, src_off=0):
        nonlocal wstg_free_v
        if sub == 0:
            nh = (length + 1535) // 1536
            for hh in range(nh):
                a = hh * 1536
                b = min(length, a + 1536)
                p.wait("sync", sV, wstg_free_v)
                p.op("sync", lambda e, s=src_f32, a=a, b=b, o=src_off:
                     e.dma_start(out=wstgC[:, 0:b - a],
                                 in_=s[:, o + a:o + b]), dW, 16)
                ld = dW.n
                p.wait("vector", dW, ld)
                if hh == 0:
                    p.wait("vector", sT, slot_free_t)
                    p.wait("vector", dG, scr_done_dG)
                p.op("vector", lambda e, d=dst_sb, a=a, b=b: e.tensor_copy(
                    d[:, a:b], wstgC[:, 0:b - a]), sV, 1)
                wstg_free_v = sV.n
            p.wait("gpsimd", sV, wstg_free_v)
            p.op("gpsimd", lambda e, d=dst_sb, s=src_bf, L=length,
                 o=src_off: e.dma_start(out=s[:, o:o + L], in_=d[:, 0:L]),
                 dG, 16)
            return sV, wstg_free_v, dG.n
        p.wait("sync", sT, slot_free_t)
        p.wait("sync", dG, scr_done_dG)
        p.op("sync", lambda e, d=dst_sb, s=src_bf, L=length, o=src_off:
             e.dma_start(out=d[:, 0:L], in_=s[:, o:o + L]), dW, 16)
        return dW, dW.n, scr_done_dG

    p.op("sync", lambda e: e.dma_start(out=sel8[:], in_=sel_in[:]), dW, 16)
    p.wait("tensor", dW, dW.n)
    p.op("vector", lambda e: e.tensor_copy(r_hi[:], r_sb[:]))
    vdrain()
    p.op("vector", lambda e: e.tensor_sub(r_lo[:], r_sb[:], r_hi[:]), sV, 1)
    p.wait("tensor", sV, sV.n)

    for sub in range(NSUB):
        for h in range(NCC):
            ci = sub * NCC + h
            par = h % 2
            t0 = sub * SUB + h * CC
            p.wait("gpsimd", sT, x2f_cons[par])
            p.wait("gpsimd", sV, x2f_cons_v[par])
            for fc in range(FC):
                p.op("gpsimd", lambda e, fc=fc, t0=t0, par=par: e.dma_start(
                    out=x2f[par][:, fc * CC:(fc + 1) * CC],
                    in_=x2_hbm[fc * 128:(fc + 1) * 128, t0:t0 + CC]), dG, 16)
            x2_ld = dG.n
            p.wait("vector", dG, x2_ld)
            p.wait("vector", sT, chunk_last_t if ci > 0 else 0)
            p.op("vector", lambda e, par=par: e.tensor_copy(xb2[:],
                                                            x2f[par][:]))
            p.op("vector", lambda e, par=par: e.tensor_sub(
                x2lo[:], x2f[par][:], xb2[:]))
            p.op("vector", lambda e: e.tensor_mul(xsq2[:], xb2[:], xb2[:]),
                 sV, 1)
            cast2_v = sV.n
            p.wait("tensor", sV, cast2_v)
            p.wait("tensor", sV, psD_free_v)
            for kc in range(FC):
                p.op("tensor", lambda e, kc=kc: e.matmul(
                    psD[0][:, 0:CC], ones_b[:], xb2[:, kc * CC:(kc + 1) * CC],
                    start=(kc == 0), stop=(kc == FC - 1)))
            for kc in range(FC):
                p.op("tensor", lambda e, kc=kc: e.matmul(
                    psD[1][:, 0:CC], ones_b[:], xsq2[:, kc * CC:(kc + 1) * CC],
                    start=(kc == 0), stop=(kc == FC - 1)),
                    sT if kc == FC - 1 else None, 1, fast=True)
            stats_mm_done = sT.n
            p.wait("vector", sT, stats_mm_done)
            p.op("vector", lambda e: e.tensor_scalar(
                st1C[:], psD[0][:, 0:CC], 1.0 / D, None, ALU.mult))
            p.op("vector", lambda e: e.tensor_scalar(
                st2C[:], psD[1][:, 0:CC], 1.0 / D, None, ALU.mult))
            p.op("vector", lambda e: e.scalar_tensor_tensor(
                gaC[:], st1C[:], -1.0, st1C[:], ALU.mult, ALU.mult))
            p.op("vector", lambda e: e.scalar_tensor_tensor(
                gbC[:], st2C[:], 1e-5, gaC[:], ALU.add, ALU.add), sV, 1)
            psD_free_v = sV.n
            p.wait("scalar", sV, psD_free_v)
            p.op("scalar", lambda e: e.activation(gaC[:], gbC[:], AF.Ln),
                 sA, 1)
            p.wait("vector", sA, sA.n)
            p.op("vector", lambda e: e.tensor_scalar(
                gbC[:], gaC[:], -0.5, None, ALU.mult), sV, 1)
            p.wait("scalar", sV, sV.n)
            p.op("scalar", lambda e: e.activation(rrepC[:], gbC[:], AF.Exp),
                 sA, 1)
            p.wait("vector", sA, sA.n)
            p.op("vector", lambda e: e.scalar_tensor_tensor(
                nmrC[:], st1C[:], -1.0, rrepC[:], ALU.mult, ALU.mult))
            p.wait("vector", sT, ln2b_free_t if (sub > 0 and h == 0) else 0)
            for fc in range(FC):
                p.op("vector", lambda e, fc=fc, par=par: e.tensor_mul(
                    gaC[:], x2f[par][:, fc * CC:(fc + 1) * CC], rrepC[:]))
                p.op("vector", lambda e, fc=fc, h=h: e.tensor_add(
                    ln2b[:, fc * SUB + h * CC:fc * SUB + (h + 1) * CC],
                    gaC[:], nmrC[:]),
                    sV if fc == FC - 1 else None, 1)
            ln2b_chunk_v = sV.n
            ln2b_sub_v = ln2b_chunk_v

            for tt in range(CC // 128):
                ti = h * 2 + tt
                tsl = slice(tt * 128, (tt + 1) * 128)
                if ti == 0:
                    p.wait("tensor", sV, psM_free_v)
                for kc in range(FC):
                    p.op("tensor", lambda e, kc=kc, tsl=tsl, ti=ti: e.matmul(
                        psM[:, 64 + ti:65 + ti],
                        xb2[:, kc * CC:(kc + 1) * CC][:, tsl],
                        ones_b[:, 0:1],
                        start=(kc == 0), stop=(kc == FC - 1)))
                for kc in range(FC):
                    p.op("tensor", lambda e, kc=kc, tsl=tsl, ti=ti: e.matmul(
                        psM[:, 72 + ti:73 + ti],
                        xsq2[:, kc * CC:(kc + 1) * CC][:, tsl],
                        ones_b[:, 0:1],
                        start=(kc == 0), stop=(kc == FC - 1)))
                for pas, xs, rp in ((0, xb2, r_hi), (1, xb2, r_lo),
                                    (2, x2lo, r_hi)):
                    for kc in range(FC):
                        p.op("tensor", lambda e, kc=kc, tsl=tsl, xs=xs,
                             rp=rp, pas=pas, ti=ti: e.matmul(
                                 psM[:, ti * 8:(ti + 1) * 8],
                                 xs[:, kc * CC:(kc + 1) * CC][:, tsl],
                                 rp[:, kc * 8:(kc + 1) * 8],
                                 start=(kc == 0 and pas == 0),
                                 stop=(kc == FC - 1 and pas == 2)),
                             sT if (kc == FC - 1 and pas == 2) else None, 1)
            x2f_cons[par] = sT.n
            x2f_cons_v[par] = ln2b_chunk_v
            chunk_last_t = sT.n

        def d3(h_):
            return bass.AP(h_, 0, [[64, 128], [8, 8], [1, 8]])

        def segb(h_):
            return bass.AP(h_, 0, [[8, 128], [1, 8], [0, 8]])

        def tileb(h_):
            return bass.AP(h_, 0, [[8, 128], [0, 8], [1, 8]])

        p.wait("vector", sT, chunk_last_t)
        p.op("vector", lambda e: e.tensor_scalar(
            c8m[:], psM[:, 64:72], 1.0 / D, None, ALU.mult))
        p.op("vector", lambda e: e.tensor_scalar(
            c8q[:], psM[:, 72:80], 1.0 / D, None, ALU.mult))
        vdrain()
        p.op("vector", lambda e: e.scalar_tensor_tensor(
            c8a[:], c8m[:], -1.0, c8m[:], ALU.mult, ALU.mult))
        vdrain()
        p.op("vector", lambda e: e.scalar_tensor_tensor(
            c8b[:], c8q[:], 1e-5, c8a[:], ALU.add, ALU.add), sV, 1)
        p.wait("scalar", sV, sV.n)
        p.op("scalar", lambda e: e.activation(c8a[:], c8b[:], AF.Ln), sA, 1)
        p.wait("vector", sA, sA.n)
        p.op("vector", lambda e: e.tensor_scalar(
            c8b[:], c8a[:], -0.5, None, ALU.mult), sV, 1)
        p.wait("scalar", sV, sV.n)
        p.op("scalar", lambda e: e.activation(rstd8[:], c8b[:], AF.Exp),
             sA, 1)
        p.wait("vector", sA, sA.n)
        p.op("vector", lambda e: e.scalar_tensor_tensor(
            nmr8[:], c8m[:], -1.0, rstd8[:], ALU.mult, ALU.mult))
        p.op("vector", lambda e: e.tensor_copy(r64a[:], psM[:, 0:64]), sV, 1)
        psM_free_v = sV.n
        vdrain()
        p.op("vector", lambda e: e.tensor_tensor(
            d3(r64a), d3(r64a), segb(rstd8), ALU.mult))
        p.op("vector", lambda e: e.tensor_tensor(
            d3(r64b), tileb(sr_rep), segb(nmr8), ALU.mult))
        vdrain()
        p.op("vector", lambda e: e.tensor_add(r64a[:], r64a[:], r64b[:]))
        vdrain()
        p.op("vector", lambda e: e.reduce_max(c8a[:], d3(r64a), axis=AX))
        vdrain()
        p.op("vector", lambda e: e.tensor_tensor(
            d3(r64b), d3(r64a), segb(c8a), ALU.subtract), sV, 1)
        p.wait("scalar", sV, sV.n)
        p.op("scalar", lambda e: e.activation(r64a[:], r64b[:], AF.Exp),
             sA, 1)
        p.wait("vector", sA, sA.n)
        p.op("vector", lambda e: e.reduce_sum(c8b[:], d3(r64a), axis=AX))
        vdrain()
        p.op("vector", lambda e: e.reciprocal(c8c[:], c8b[:]))
        vdrain()
        p.op("vector", lambda e: e.tensor_tensor(
            d3(r64b), d3(r64a), segb(c8c), ALU.mult))
        vdrain()
        p.op("vector", lambda e: e.reduce_max(c8a[:], d3(r64b), axis=AX))
        vdrain()
        p.op("vector", lambda e: e.tensor_tensor(
            d3(r64a), d3(r64b), segb(c8a), ALU.is_equal))
        vdrain()
        p.op("vector", lambda e: e.scalar_tensor_tensor(
            r64a[:], r64a[:], -2.0, r64b[:], ALU.mult, ALU.add))
        vdrain()
        p.op("vector", lambda e: e.reduce_max(c8b[:], d3(r64a), axis=AX))
        vdrain()
        p.op("vector", lambda e: e.tensor_tensor(
            d3(r64a), d3(r64b), segb(c8b), ALU.is_ge))
        vdrain()
        p.op("vector", lambda e: e.tensor_mul(r64a[:], r64a[:], r64b[:]))
        vdrain()
        p.op("vector", lambda e: e.tensor_copy(coefb64[:], r64a[:]), sV, 1)
        coef_cast_v = sV.n
        p.wait("tensor", sV, coef_cast_v)
        for ti in range(8):
            p.op("tensor", lambda e, ti=ti: e.transpose(
                psTr[:], coefb64[:, ti * 8:(ti + 1) * 8], eyeb[:]), sT, 1)
            p.wait("vector", sT, sT.n)
            p.op("vector", lambda e, ti=ti: e.tensor_copy(
                coefT[:, ti * 128:(ti + 1) * 128], psTr[:]), sV, 1)
            if ti < 7:
                p.wait("tensor", sV, sV.n)
        coef_ready_v = sV.n
        if dbg:
            p.wait("gpsimd", sV, coef_ready_v)
            p.op("gpsimd", lambda e, sub=sub: e.dma_start(
                out=dbg_coef[:, sub * SUB:(sub + 1) * SUB], in_=coefT[:]),
                dG, 16)
            if sub == 0:
                p.op("gpsimd", lambda e: e.dma_start(
                    out=dbg_ln2b[:], in_=ln2b[:]), dG, 16)

        for pe in range(NE):
            gslot = 0
            uslot = 1
            gsem, gval, slot_scr_dG[gslot] = fetch_piece(
                wslot[gslot], gup[pe, 0], gup_sc[pe, 0], 1, FC * DE,
                slot_free_t[gslot], slot_scr_dG[gslot])
            usem, uval, slot_scr_dG[uslot] = fetch_piece(
                wslot[uslot], gup[pe, 1], gup_sc[pe, 1], 1, FC * DE,
                slot_free_t[uslot], slot_scr_dG[uslot])
            p.wait("tensor", gsem, gval)
            if pe == 0:
                p.wait("tensor", sV, ln2b_sub_v)
            silu_a = {}
            for dcq in range(8):
                qc, dc = dcq // 4, dcq % 4
                par = dcq % 2
                p.wait("tensor", sA,
                       silu_a[dcq - 2] if dcq >= 2 else gu_prev_silu[par])
                for kc in range(FC):
                    p.op("tensor", lambda e, kc=kc, dc=dc, qc=qc, par=par,
                         gslot=gslot: e.matmul(
                             psG[par][:],
                             wslot[gslot][:, kc * DE + dc * 128:
                                          kc * DE + (dc + 1) * 128],
                             ln2b[:, kc * SUB + qc * 512:
                                  kc * SUB + (qc + 1) * 512],
                             start=(kc == 0), stop=(kc == FC - 1)),
                         sT if kc == FC - 1 else None, 1, fast=True)
                g_t = sT.n
                p.wait("scalar", sT, g_t)
                p.wait("scalar", sV, prev_up_ev_all[dcq])
                p.op("scalar", lambda e, dcq=dcq, par=par: e.activation(
                    gsall[:, dcq * 512:(dcq + 1) * 512], psG[par][:],
                    AF.Silu), sA, 1)
                silu_a[dcq] = sA.n
            slot_free_t[gslot] = sT.n
            gu_prev_silu = {0: silu_a[6], 1: silu_a[7]}

            p.wait("tensor", usem, uval)
            up_ev = {}
            crep_t = 0
            for dcq in range(8):
                qc, dc = dcq // 4, dcq % 4
                par = dcq % 2
                if pe < 8 and dc == 0:
                    p.wait("tensor", sV, psM_free_v)
                    p.wait("tensor", sV, coef_ready_v)
                    p.op("tensor", lambda e, pe=pe, qc=qc: e.matmul(
                        psM[:], sel8[:, pe * 128:(pe + 1) * 128],
                        coefT[:, qc * 512:(qc + 1) * 512],
                        start=True, stop=True), sT, 1)
                    crep_t = sT.n
                p.wait("tensor", sV,
                       up_ev[dcq - 2] if dcq >= 2 else up_prev_ev[par])
                for kc in range(FC):
                    p.op("tensor", lambda e, kc=kc, dc=dc, qc=qc, par=par,
                         uslot=uslot: e.matmul(
                             psU[par][:],
                             wslot[uslot][:, kc * DE + dc * 128:
                                          kc * DE + (dc + 1) * 128],
                             ln2b[:, kc * SUB + qc * 512:
                                  kc * SUB + (qc + 1) * 512],
                             start=(kc == 0), stop=(kc == FC - 1)),
                         sT if kc == FC - 1 else None, 1, fast=True)
                u_t = sT.n
                p.wait("vector", sT, u_t)
                p.wait("vector", sA, silu_a[dcq])
                hpos = pe * 4 * SUB + dc * SUB + qc * 512
                if pe < 8:
                    p.op("vector", lambda e, dcq=dcq, par=par: e.tensor_mul(
                        t2b[par][:], psU[par][:],
                        gsall[:, dcq * 512:(dcq + 1) * 512]))
                    p.op("vector", lambda e, hpos=hpos, par=par:
                         e.tensor_mul(htil[:, hpos:hpos + 512], t2b[par][:],
                                      psM[:]), sV, 1)
                else:
                    p.op("vector", lambda e, dcq=dcq, par=par, hpos=hpos:
                         e.tensor_mul(htil[:, hpos:hpos + 512], psU[par][:],
                                      gsall[:, dcq * 512:(dcq + 1) * 512]),
                         sV, 1)
                up_ev[dcq] = sV.n
                if pe < 8 and dc == 3:
                    psM_free_v = sV.n
            slot_free_t[uslot] = sT.n
            up_prev_ev = {0: up_ev[6], 1: up_ev[7]}
            prev_up_ev_all = dict(up_ev)
        htil_v = sV.n

        if dbg and sub == 0:
            p.wait("gpsimd", sV, htil_v)
            p.op("gpsimd", lambda e: e.dma_start(
                out=dbg_htil[:], in_=htil[:, 0:2 * SUB * 4]), dG, 16)
        p.wait("tensor", sV, htil_v)
        dev_v = {}
        for dc in range(FC):
            dpar = dc % 2
            dsem, dval, dslot_scr_dG[dpar] = fetch_piece(
                dslot[dpar], dwn[dc], dwn_sc[dc], 1, NE * 4 * 128,
                dslot_free_t[dpar], dslot_scr_dG[dpar])
            p.wait("tensor", dsem, dval)
            for qc in range(2):
                idx = dc * 2 + qc
                par = idx % 2
                p.wait("gpsimd", yst, x2r_free[par])
                p.op("gpsimd", lambda e, dc=dc, qc=qc, sub=sub, par=par:
                     e.dma_start(
                         out=x2r[par][:],
                         in_=x2_hbm[dc * 128:(dc + 1) * 128,
                                    sub * SUB + qc * 512:
                                    sub * SUB + (qc + 1) * 512]), dG, 16)
                x2r_ld = dG.n
                if idx >= 2:
                    p.wait("tensor", sV, dev_v[idx - 2])
                for pe in range(NE):
                    for cc in range(4):
                        blk = pe * 4 + cc
                        first = (pe == 0 and cc == 0)
                        last = (pe == NE - 1 and cc == 3)
                        p.op("tensor",
                             lambda e, blk=blk, pe=pe, cc=cc, qc=qc, par=par,
                             dpar=dpar, first=first, last=last: e.matmul(
                                 psD[par][:],
                                 dslot[dpar][:, blk * 128:(blk + 1) * 128],
                                 htil[:, pe * 4 * SUB + cc * SUB + qc * 512:
                                      pe * 4 * SUB + cc * SUB +
                                      (qc + 1) * 512],
                                 start=first, stop=last),
                             sT if last else None, 1, fast=True)
                if qc == 1:
                    dslot_free_t[dpar] = sT.n
                dmm_t = sT.n
                p.wait("vector", sT, dmm_t)
                p.wait("vector", dG, x2r_ld)
                p.op("vector", lambda e, par=par: e.tensor_add(
                    x2r[par][:], psD[par][:], x2r[par][:]), sV, 1)
                dev_v[idx] = sV.n
                p.wait("gpsimd", sV, dev_v[idx])
                p.op("gpsimd", lambda e, dc=dc, qc=qc, sub=sub, par=par:
                     e.dma_start(
                         out=y[dc * 128:(dc + 1) * 128,
                               sub * SUB + qc * 512:
                               sub * SUB + (qc + 1) * 512],
                         in_=x2r[par][:]), yst, 16)
                x2r_free[par] = yst.n
        psD_free_v = dev_v[2 * FC - 1]
        ln2b_free_t = sT.n
        stats_mm_done = sT.n

    p.wait("gpsimd", yst, yst.n)
    p.wait("gpsimd", dG, dG.n)

    with nc.Block() as blk:
        p.flush(blk)
    esC.close()
    es.close()
    return nc


_NC_CACHE = {}


def _get_nc():
    if "nc" not in _NC_CACHE:
        _NC_CACHE["nc"] = build_nc()
    return _NC_CACHE["nc"]


def _mk_sel():
    s = np.zeros((8, 8 * 128), np.float32)
    for e in range(8):
        s[e, e * 128:(e + 1) * 128] = 1.0
    return s.astype(ml_dtypes.bfloat16)


def _prep_inputs(inputs):
    bf = ml_dtypes.bfloat16
    x = np.asarray(inputs["L0_attn_x"])

    def tile_gu(w):
        return np.ascontiguousarray(
            w.reshape(8, 128, DE).transpose(1, 0, 2).reshape(128, 8 * DE))

    eg = np.asarray(inputs["L0_ffn_experts_gate"])
    eu = np.asarray(inputs["L0_ffn_experts_up"])
    ed = np.asarray(inputs["L0_ffn_experts_down"])
    sg = np.asarray(inputs["L0_ffn_shared_gate"])
    su = np.asarray(inputs["L0_ffn_shared_up"])
    sd = np.asarray(inputs["L0_ffn_shared_down"])
    gup = np.zeros((NE, 2, 128, FC * DE), np.float32)
    for e in range(8):
        gup[e, 0] = tile_gu(eg[e])
        gup[e, 1] = tile_gu(eu[e])
    gup[8, 0] = tile_gu(sg[:, 0:DE])
    gup[8, 1] = tile_gu(su[:, 0:DE])
    gup[9, 0] = tile_gu(sg[:, DE:2 * DE])
    gup[9, 1] = tile_gu(su[:, DE:2 * DE])

    downs = [ed[e] for e in range(8)] + [sd[0:DE, :], sd[DE:2 * DE, :]]
    dwn = np.zeros((FC, 128, NE * 4 * 128), np.float32)
    for dc in range(FC):
        blocks = []
        for pe in range(NE):
            w = downs[pe]
            for cc in range(4):
                blocks.append(w[cc * 128:(cc + 1) * 128,
                                dc * 128:(dc + 1) * 128])
        dwn[dc] = np.concatenate(blocks, axis=1)

    common = dict(
        wqT=np.ascontiguousarray(np.asarray(inputs["L0_attn_Wq"]).T),
        wkT=np.ascontiguousarray(np.asarray(inputs["L0_attn_Wk"]).T),
        wvT=np.ascontiguousarray(np.asarray(inputs["L0_attn_Wv"]).T),
        woT=np.ascontiguousarray(np.asarray(inputs["L0_attn_Wo"]).T),
        wavel=np.ascontiguousarray(np.asarray(inputs["L0_attn_wavelengths"])),
        pbias=np.ascontiguousarray(np.asarray(inputs["L0_attn_phase_bias"])),
        bq=np.ascontiguousarray(np.asarray(inputs["L0_attn_bq"])),
        bk=np.ascontiguousarray(np.asarray(inputs["L0_attn_bk"])),
        bv=np.ascontiguousarray(np.asarray(inputs["L0_attn_bv"])),
        bo=np.ascontiguousarray(np.asarray(inputs["L0_attn_bo"])),
        router=np.ascontiguousarray(np.asarray(inputs["L0_ffn_router"])),
        gup=gup, dwn=dwn,
        eye_in=np.eye(128, dtype=np.float32).astype(bf),
        sel_in=_mk_sel(),
    )
    per_core = []
    for b in range(8):
        m = dict(common)
        m["x_t"] = np.ascontiguousarray(np.asarray(x[b]).T)
        per_core.append(m)
    return per_core


def kernel(**inputs):
    nc = _get_nc()
    in_maps = _prep_inputs(inputs)
    res = run_bass_kernel_spmd(nc, in_maps, core_ids=list(range(8)))
    out = np.empty((8, S, D), np.float32)
    for b in range(8):
        out[b] = np.asarray(res.results[b]["y"]).T
    return out


# version 08
# speedup vs baseline: 1.2500x; 1.0094x over previous
import contextlib
import math

import numpy as np
import ml_dtypes

import concourse.bass as bass
import concourse.mybir as mybir
from concourse.bass_utils import run_bass_kernel_spmd

F32 = mybir.dt.float32
F16 = mybir.dt.float16
BF16 = mybir.dt.bfloat16
I32 = mybir.dt.int32
ALU = mybir.AluOpType
AF = mybir.ActivationFunctionType
AX = mybir.AxisListType.X

D = 1024
S = 4096
H = 128
FC = D // 128
NE = 10
DE = 512
NSUB = 4
SUB = S // NSUB

TWO_PI = float(2.0 * math.pi)
INV_SQRT_H = 1.0 / math.sqrt(H)
EXP_SHIFT = -float(math.sqrt(H))
ROUND_OFS = 128.0


class Sem:
    def __init__(self, h, is_dma=False):
        self.h = h
        self.is_dma = is_dma
        self.n = 0

    def bump(self, k):
        self.n += k
        return self.n


class Prog:
    ENGS = ("sync", "vector", "scalar", "tensor", "gpsimd")

    def __init__(self):
        self.q = {e: [] for e in self.ENGS}

    def op(self, eng, build, sem=None, k=1, fast=False):
        if sem is not None:
            v = sem.bump(k)
            if sem.is_dma or fast:
                self.q[eng].append(
                    lambda e, b=build, s=sem, kk=k: b(e).then_inc(s.h, kk))
            else:
                def _f(e, b=build, s=sem, kk=k):
                    b(e)
                    e.drain(fusable=False).then_inc(s.h, kk)
                self.q[eng].append(_f)
            return v
        self.q[eng].append(build)
        return None

    def wait(self, eng, sem, val):
        if val:
            self.q[eng].append(lambda e, s=sem, v=val: e.wait_ge(s.h, v))

    def flush(self, block):
        for eng in self.ENGS:
            ops = self.q[eng]
            if not ops:
                continue

            def body(engine, ops=ops):
                for f in ops:
                    f(engine)

            getattr(block, eng)(body)
        self.q = {e: [] for e in self.ENGS}


def build_nc(dbg=False):
    nc = bass.Bass("TRN2", target_bir_lowering=False,
                   detect_race_conditions=False)

    x_t = nc.dram_tensor("x_t", [D, S], F32, kind="ExternalInput")
    wqT = nc.dram_tensor("wqT", [D, H], F32, kind="ExternalInput")
    wkT = nc.dram_tensor("wkT", [D, H], F32, kind="ExternalInput")
    wvT = nc.dram_tensor("wvT", [D, H], F32, kind="ExternalInput")
    woT = nc.dram_tensor("woT", [H, D], F32, kind="ExternalInput")
    wavel = nc.dram_tensor("wavel", [H], F32, kind="ExternalInput")
    pbias = nc.dram_tensor("pbias", [H], F32, kind="ExternalInput")
    bq_d = nc.dram_tensor("bq", [H], F32, kind="ExternalInput")
    bk_d = nc.dram_tensor("bk", [H], F32, kind="ExternalInput")
    bv_d = nc.dram_tensor("bv", [H], F32, kind="ExternalInput")
    bo_d = nc.dram_tensor("bo", [D], F32, kind="ExternalInput")
    router = nc.dram_tensor("router", [D, 8], F32, kind="ExternalInput")
    gup = nc.dram_tensor("gup", [NE, 2, 128, FC * DE], F32,
                         kind="ExternalInput")
    dwn = nc.dram_tensor("dwn", [FC, 128, NE * 4 * 128], F32,
                         kind="ExternalInput")
    eye_in = nc.dram_tensor("eye_in", [128, 128], BF16, kind="ExternalInput")
    sel_in = nc.dram_tensor("sel_in", [8, 8 * 128], BF16, kind="ExternalInput")

    y = nc.dram_tensor("y", [D, S], F32, kind="ExternalOutput")

    x2_hbm = nc.dram_tensor("x2_hbm", [D, S], F32,
                            kind="ExternalOutput" if dbg else "Internal")
    if dbg:
        dbg_t = {n: nc.dram_tensor("dbg_" + n, [128, S], F16,
                                   kind="ExternalOutput")
                 for n in ("cosq", "sinq", "cosk", "sink")}
        dbg_t["vf"] = nc.dram_tensor("dbg_vf", [128, S], BF16,
                                     kind="ExternalOutput")
        dbg_us = {n: nc.dram_tensor("dbg_" + n, [128, S], F16,
                                    kind="ExternalOutput")
                  for n in ("usq", "ucq")}
        dbg_coef = nc.dram_tensor("dbg_coef", [8, S], BF16,
                                  kind="ExternalOutput")
        dbg_htil = nc.dram_tensor("dbg_htil", [128, 2 * SUB * 4], BF16,
                                  kind="ExternalOutput")
        dbg_ln2b = nc.dram_tensor("dbg_ln2b", [128, FC * SUB], BF16,
                                  kind="ExternalOutput")
        dbg_rt = nc.dram_tensor("dbg_rt", [128, 64], F32,
                                kind="ExternalOutput")
        dbg_rt2 = nc.dram_tensor("dbg_rt2", [128, 40 * 8], F32,
                                 kind="ExternalOutput")
    gup_sc = nc.dram_tensor("gup_sc", [NE, 2, 128, FC * DE], BF16)
    dwn_sc = nc.dram_tensor("dwn_sc", [FC, 128, NE * 4 * 128], BF16)

    es = contextlib.ExitStack()

    def SB(name, shape, dt):
        return es.enter_context(nc.sbuf_tensor(name, shape, dt))

    def SEM(name, is_dma=False):
        return Sem(es.enter_context(nc.semaphore(name)), is_dma)

    eyeb = SB("eyeb", [128, 128], BF16)
    ones_b = SB("ones_b", [128, 128], BF16)
    ones_f = SB("ones_f", [128, 128], F32)
    c2pi = SB("c2pi", [128, 1], F32)
    ceps = SB("ceps", [128, 1], F32)
    cnhalf = SB("cnhalf", [128, 1], F32)
    cnone = SB("cnone", [128, 1], F32)
    cinvsq = SB("cinvsq", [128, 1], F32)
    cshift = SB("cshift", [128, 1], F32)
    invw = SB("invw", [128, 1], F32)
    s1c = SB("s1c", [128, 1], F32)
    s2q = SB("s2q", [128, 1], F32)
    s2k = SB("s2k", [128, 1], F32)
    sqc = SB("sqc", [128, 1], F32)
    skc = SB("skc", [128, 1], F32)
    svc = SB("svc", [128, 1], F32)
    bvb = SB("bvb", [128, 1], BF16)
    bqf = SB("bqf", [128, 1], F32)
    bkf = SB("bkf", [128, 1], F32)
    bvf = SB("bvf", [128, 1], F32)
    wlf = SB("wlf", [128, 1], F32)
    pbf = SB("pbf", [128, 1], F32)
    bo8 = SB("bo8", [128, FC], F32)
    boP = SB("boP", [128, FC], F32)
    tcol = SB("tcol", [128, 1], F32)
    wqTb = SB("wqTb", [128, FC * 128], BF16)
    wkTb = SB("wkTb", [128, FC * 128], BF16)
    wvTb = SB("wvTb", [128, FC * 128], BF16)
    woTb = SB("woTb", [128, FC * 128], BF16)
    r_sb = SB("r_sb", [128, FC * 8], F32)
    r_hi = SB("r_hi", [128, FC * 8], BF16)
    r_lo = SB("r_lo", [128, FC * 8], BF16)
    sr_rep = SB("sr_rep", [128, 8], F32)

    dW = SEM("dW", is_dma=True)
    dG = SEM("dG", is_dma=True)
    yst = SEM("yst", is_dma=True)
    sT = SEM("sT")
    sV = SEM("sV")
    sA = SEM("sA")

    p = Prog()

    def vdrain():
        p.op("vector", lambda e: e.drain(fusable=False))

    esAB = contextlib.ExitStack()

    def SBab(name, shape, dt):
        return esAB.enter_context(nc.sbuf_tensor(name, shape, dt))

    cosq = SBab("cosq", [128, S], F16)
    sinq = SBab("sinq", [128, S], F16)
    cosk = SBab("cosk", [128, S], F16)
    sink = SBab("sink", [128, S], F16)
    vf = SBab("vf", [128, S], BF16)

    esA = contextlib.ExitStack()

    def SBa(name, shape, dt):
        return esA.enter_context(nc.sbuf_tensor(name, shape, dt))

    def PSa(name, shape, dt=F32):
        return esA.enter_context(nc.psum_tensor(name, shape, dt))

    CH = 512
    NCH = S // CH
    xfA = [SBa(f"xfA{i}", [128, FC * CH], F32) for i in range(2)]
    xbA = [SBa(f"xbA{i}", [128, FC * CH], BF16) for i in range(2)]
    xsqA = [SBa(f"xsqA{i}", [128, FC * CH], BF16) for i in range(2)]
    stat1 = SBa("stat1", [128, CH], F32)
    stat2 = SBa("stat2", [128, CH], F32)
    rrep = SBa("rrep", [128, CH], F32)
    nmr = SBa("nmr", [128, CH], F32)
    ga = SBa("ga", [128, CH], F32)
    gb = SBa("gb", [128, CH], F32)
    gc = SBa("gc", [128, CH], F32)
    gi = SBa("gi", [128, CH], I32)
    usq = SBa("usq", [128, S], F16)
    ucq = SBa("ucq", [128, S], F16)
    usk = SBa("usk", [128, S], F16)
    uck = SBa("uck", [128, S], F16)
    wstage = SBa("wstage", [128, FC * 128], F32)

    psMean = PSa("psMean", [128, CH])
    psSq = PSa("psSq", [128, CH])
    psQ = [PSa(f"psQ{i}", [128, CH]) for i in range(2)]
    psK = [PSa(f"psK{i}", [128, CH]) for i in range(2)]
    psV = [PSa(f"psV{i}", [128, CH]) for i in range(2)]

    p.op("sync", lambda e: e.dma_start(out=eyeb[:], in_=eye_in[:]), dW, 16)
    for col, src in ((wlf, wavel), (pbf, pbias), (bqf, bq_d), (bkf, bk_d),
                     (bvf, bv_d)):
        p.op("sync", lambda e, c=col, s=src: e.dma_start(
            out=c[:], in_=bass.AP(s, 0, [[1, 128], [1, 1]])), dW, 16)
    for dc in range(FC):
        p.op("sync", lambda e, dc=dc: e.dma_start(
            out=bo8[:, dc:dc + 1],
            in_=bass.AP(bo_d, dc * 128, [[1, 128], [1, 1]])), dW, 16)
    for kc in range(FC):
        p.op("sync", lambda e, kc=kc: e.dma_start(
            out=r_sb[:, kc * 8:(kc + 1) * 8],
            in_=router[kc * 128:(kc + 1) * 128, :]), dW, 16)
    small_loads = dW.n

    p.op("vector", lambda e: e.memset(ones_b[:], 1.0))
    p.op("vector", lambda e: e.memset(ones_f[:], 1.0))
    p.op("vector", lambda e: e.memset(c2pi[:], TWO_PI))
    p.op("vector", lambda e: e.memset(ceps[:], 1e-5))
    p.op("vector", lambda e: e.memset(cnhalf[:], -0.5))
    p.op("vector", lambda e: e.memset(cnone[:], -1.0))
    p.op("vector", lambda e: e.memset(cinvsq[:], INV_SQRT_H))
    p.op("vector", lambda e: e.memset(cshift[:], EXP_SHIFT))
    p.wait("vector", dW, small_loads)
    p.op("vector", lambda e: e.tensor_scalar(tcol[:], wlf[:], 1e-8, None,
                                             ALU.add))
    vdrain()
    p.op("vector", lambda e: e.reciprocal(invw[:], tcol[:]))
    vdrain()
    p.op("vector", lambda e: e.tensor_scalar(s1c[:], invw[:], 1.0 / TWO_PI,
                                             None, ALU.mult))
    for bcol, s2 in ((bqf, s2q), (bkf, s2k)):
        p.op("vector", lambda e, b=bcol, s=s2: e.tensor_mul(s[:], b[:],
                                                            invw[:]))
        vdrain()
        p.op("vector", lambda e, s=s2: e.tensor_add(s[:], s[:], pbf[:]))
        vdrain()
        p.op("vector", lambda e, s=s2: e.tensor_scalar(
            s[:], s[:], 1.0 / TWO_PI, ROUND_OFS, ALU.mult, ALU.add))
    p.op("vector", lambda e: e.tensor_copy(bvb[:], bvf[:]), sV, 1)

    for wi, (wsrc, wdst) in enumerate(((wqT, wqTb), (wkT, wkTb), (wvT, wvTb),
                                       (woT, woTb))):
        p.wait("sync", sV, sV.n)
        for kc in range(FC):
            if wi < 3:
                src = wsrc[kc * 128:(kc + 1) * 128, :]
            else:
                src = wsrc[:, kc * 128:(kc + 1) * 128]
            p.op("sync", lambda e, s=src, kc=kc: e.dma_start(
                out=wstage[:, kc * 128:(kc + 1) * 128], in_=s), dW, 16)
        p.wait("vector", dW, dW.n)
        p.op("vector", lambda e, d=wdst: e.tensor_copy(d[:], wstage[:]), sV, 1)
    wcast_v = sV.n

    p.wait("tensor", sV, wcast_v)
    for wsb, col in ((wqTb, sqc), (wkTb, skc), (wvTb, svc)):
        for kc in range(FC):
            p.op("tensor", lambda e, w=wsb, kc=kc: e.matmul(
                psMean[:, 0:1], w[:, kc * 128:(kc + 1) * 128], ones_b[:, 0:1],
                start=(kc == 0), stop=(kc == FC - 1)),
                sT if kc == FC - 1 else None, 1)
        p.wait("vector", sT, sT.n)
        p.op("vector", lambda e, c=col: e.tensor_copy(c[:], psMean[:, 0:1]),
             sV, 1)
        p.wait("tensor", sV, sV.n)
    for dc in range(FC):
        p.op("tensor", lambda e, dc=dc: e.matmul(
            psSq[:, dc:dc + 1], woTb[:, dc * 128:(dc + 1) * 128], bvb[:, 0:1],
            start=True, stop=True), sT if dc == FC - 1 else None, 1)
    p.wait("vector", sT, sT.n)
    p.op("vector", lambda e: e.tensor_add(boP[:], psSq[:, 0:FC], bo8[:]),
         sV, 1)
    p.op("vector", lambda e: e.tensor_copy(r_hi[:], r_sb[:]))
    vdrain()
    p.op("vector", lambda e: e.tensor_sub(r_lo[:], r_sb[:], r_hi[:]), sV, 1)
    p.wait("tensor", sV, sV.n)
    for pas, rp in ((0, r_hi), (1, r_lo)):
        for kc in range(FC):
            p.op("tensor", lambda e, kc=kc, rp=rp, pas=pas: e.matmul(
                psMean[:, 0:8], ones_b[:], rp[:, kc * 8:(kc + 1) * 8],
                start=(kc == 0 and pas == 0),
                stop=(kc == FC - 1 and pas == 1)),
                sT if (kc == FC - 1 and pas == 1) else None, 1)
    p.wait("vector", sT, sT.n)
    p.op("vector", lambda e: e.tensor_copy(sr_rep[:], psMean[:, 0:8]), sV, 1)
    prol_v = sV.n

    x_loads = [0] * NCH
    cast_v = [0] * NCH
    cons_v = [0] * NCH
    qkv_t = [0] * NCH
    stats_used_v = 0

    for c in range(NCH):
        par = c % 2
        if c >= 2:
            p.wait("sync", sV, cons_v[c - 2])
        for fc in range(FC):
            p.op("sync", lambda e, c=c, fc=fc, par=par: e.dma_start(
                out=xfA[par][:, fc * CH:(fc + 1) * CH],
                in_=x_t[fc * 128:(fc + 1) * 128, c * CH:(c + 1) * CH]),
                dW, 16)
        x_loads[c] = dW.n

        p.wait("vector", dW, x_loads[c])
        if c >= 2:
            p.wait("vector", sT, qkv_t[c - 2])
        p.op("vector", lambda e, par=par: e.tensor_copy(xbA[par][:],
                                                        xfA[par][:]))
        p.op("vector", lambda e, par=par: e.tensor_mul(
            xsqA[par][:], xbA[par][:], xbA[par][:]), sV, 1)
        cast_v[c] = sV.n

        p.wait("tensor", sV, cast_v[c])
        p.wait("tensor", sV, stats_used_v if c > 0 else prol_v)
        for kc in range(FC):
            p.op("tensor", lambda e, par=par, kc=kc: e.matmul(
                psMean[:], ones_b[:], xbA[par][:, kc * CH:(kc + 1) * CH],
                start=(kc == 0), stop=(kc == FC - 1)))
        for kc in range(FC):
            p.op("tensor", lambda e, par=par, kc=kc: e.matmul(
                psSq[:], ones_b[:], xsqA[par][:, kc * CH:(kc + 1) * CH],
                start=(kc == 0), stop=(kc == FC - 1)),
                sT if kc == FC - 1 else None, 1, fast=True)
        stats_t = sT.n

        if c >= 2:
            p.wait("tensor", sV, cons_v[c - 2])
        for wsb, dst, last in ((wqTb, psQ[par], False), (wkTb, psK[par], False),
                               (wvTb, psV[par], True)):
            for kc in range(FC):
                p.op("tensor", lambda e, w=wsb, d=dst, par=par, kc=kc:
                     e.matmul(
                         d[:], w[:, kc * 128:(kc + 1) * 128],
                         xbA[par][:, kc * CH:(kc + 1) * CH],
                         start=(kc == 0), stop=(kc == FC - 1)),
                     sT if (kc == FC - 1 and last) else None, 1, fast=True)
        qkv_t[c] = sT.n

        p.wait("vector", sT, stats_t)
        p.op("vector", lambda e: e.tensor_scalar(stat1[:], psMean[:], 1.0 / D,
                                                 None, ALU.mult))
        p.op("vector", lambda e: e.tensor_scalar(stat2[:], psSq[:], 1.0 / D,
                                                 None, ALU.mult))
        p.op("vector", lambda e: e.scalar_tensor_tensor(
            ga[:], stat1[:], -1.0, stat1[:], ALU.mult, ALU.mult))
        p.op("vector", lambda e: e.scalar_tensor_tensor(
            gb[:], stat2[:], 1e-5, ga[:], ALU.add, ALU.add), sV, 1)
        stats_used_v = sV.n
        p.wait("scalar", sV, stats_used_v)
        p.op("scalar", lambda e: e.activation(ga[:], gb[:], AF.Ln), sA, 1)
        p.wait("vector", sA, sA.n)
        p.op("vector", lambda e: e.tensor_scalar(
            gb[:], ga[:], -0.5, None, ALU.mult), sV, 1)
        p.wait("scalar", sV, sV.n)
        p.op("scalar", lambda e: e.activation(rrep[:], gb[:], AF.Exp), sA, 1)
        p.wait("vector", sA, sA.n)
        p.op("vector", lambda e: e.scalar_tensor_tensor(
            nmr[:], stat1[:], -1.0, rrep[:], ALU.mult, ALU.mult))

        p.wait("vector", sT, qkv_t[c])
        sl = slice(c * CH, (c + 1) * CH)
        for psX, s2x, sxc, us_f, uc_f in ((psQ[par], s2q, sqc, usq, ucq),
                                          (psK[par], s2k, skc, usk, uck)):
            p.op("vector", lambda e, psX=psX: e.tensor_mul(ga[:], psX[:],
                                                           rrep[:]))
            p.op("vector", lambda e, sxc=sxc: e.scalar_tensor_tensor(
                gb[:], nmr[:], sxc[:], ga[:], ALU.mult, ALU.add))
            p.op("vector", lambda e, s2x=s2x: e.tensor_scalar(
                ga[:], gb[:], s1c[:], s2x[:], ALU.mult, ALU.add))
            p.op("vector", lambda e: e.tensor_copy(gi[:], ga[:]))
            p.op("vector", lambda e: e.tensor_copy(gb[:], gi[:]))
            p.op("vector", lambda e: e.tensor_sub(ga[:], ga[:], gb[:]))
            p.op("vector", lambda e: e.tensor_scalar(gb[:], ga[:], 0.5, None,
                                                     ALU.is_gt))
            p.op("vector", lambda e: e.scalar_tensor_tensor(
                gc[:], gb[:], -1.0, ga[:], ALU.mult, ALU.add))
            p.op("vector", lambda e, u=us_f, sl=sl: e.tensor_scalar(
                u[:, sl], gc[:], TWO_PI, None, ALU.mult))
            p.op("vector", lambda e: e.tensor_scalar(
                ga[:], gc[:], 0.25, None, ALU.add))
            p.op("vector", lambda e: e.tensor_scalar(gb[:], ga[:], 0.5, None,
                                                     ALU.is_gt))
            p.op("vector", lambda e: e.scalar_tensor_tensor(
                gc[:], gb[:], -1.0, ga[:], ALU.mult, ALU.add))
            p.op("vector", lambda e, u=uc_f, sl=sl: e.tensor_scalar(
                u[:, sl], gc[:], TWO_PI, None, ALU.mult), sV, 1)
        p.op("vector", lambda e, par=par: e.tensor_mul(ga[:], psV[par][:],
                                                       rrep[:]))
        p.op("vector", lambda e, sl=sl: e.scalar_tensor_tensor(
            vf[:, sl], nmr[:], svc[:], ga[:], ALU.mult, ALU.add), sV, 1)
        cons_v[c] = sV.n

    p.wait("scalar", sV, cons_v[NCH - 1])
    for src, dst in ((usq, sinq), (ucq, cosq), (usk, sink), (uck, cosk)):
        p.op("scalar", lambda e, s=src, d=dst: e.activation(
            d[:], s[:], AF.Sin), sA, 1)
    p.wait("vector", sA, sA.n)
    for t in (cosq, sinq):
        p.op("vector", lambda e, t=t: e.tensor_scalar(
            t[:], t[:], INV_SQRT_H, None, ALU.mult), sV, 1)
    sinscale_v = sV.n
    trig_a = sA.n
    if dbg:
        p.wait("gpsimd", sA, trig_a)
        p.wait("gpsimd", sV, sinscale_v)
        for n, t in (("cosq", cosq), ("sinq", sinq), ("cosk", cosk),
                     ("sink", sink), ("vf", vf)):
            p.op("gpsimd", lambda e, n=n, t=t: e.dma_start(
                out=dbg_t[n][:], in_=t[:]), dG, 16)
        for n, t in (("usq", usq), ("ucq", ucq)):
            p.op("gpsimd", lambda e, n=n, t=t: e.dma_start(
                out=dbg_us[n][:], in_=t[:]), dG, 16)
        p.wait("gpsimd", dG, dG.n)

    with nc.Block() as blk:
        p.flush(blk)
    esA.close()

    esB = contextlib.ExitStack()
    v_tm = esB.enter_context(nc.sbuf_tensor("v_tm", [128, S], BF16))

    esB0 = contextlib.ExitStack()
    psTrV = [esB0.enter_context(nc.psum_tensor(f"psTrV{i}", [128, 128], BF16))
             for i in range(2)]
    tr_cons = [0, 0]
    for kt in range(S // 128):
        par = kt % 2
        if kt >= 2:
            p.wait("tensor", sV, tr_cons[par])
        p.op("tensor", lambda e, kt=kt, par=par: e.transpose(
            psTrV[par][:], vf[:, kt * 128:(kt + 1) * 128], eyeb[:]), sT, 1)
        p.wait("vector", sT, sT.n)
        p.op("vector", lambda e, kt=kt, par=par: e.tensor_copy(
            v_tm[:, kt * 128:(kt + 1) * 128], psTrV[par][:]), sV, 1)
        tr_cons[par] = sV.n
    with nc.Block() as blk:
        p.flush(blk)
    esB0.close()

    esB1 = contextlib.ExitStack()

    def SBb(name, shape, dt):
        return esB1.enter_context(nc.sbuf_tensor(name, shape, dt))

    def PSb(name, shape, dt=F32):
        return esB1.enter_context(nc.psum_tensor(name, shape, dt))

    QC = 512
    NQ = S // QC
    NKT = S // 128
    xfB = [SBb(f"xfB{i}", [128, FC * QC], F32) for i in range(2)]
    wstgB = [SBb(f"wstgB{i}", [128, 5120], F32) for i in range(2)]
    wcastB = [SBb(f"wcastB{i}", [128, 5120], BF16) for i in range(2)]
    ptb = [SBb(f"ptb{i}", [128, 1024], BF16) for i in range(2)]
    attnT = [SBb(f"attnT{i}", [128, QC], BF16) for i in range(2)]
    lnS = [SBb(f"lnS{i}", [128, QC], F32) for i in range(2)]
    rcp = [SBb(f"rcp{i}", [128, QC], F32) for i in range(2)]
    psSc = [PSb(f"psSc{i}", [128, 1024]) for i in range(2)]
    psPV = [PSb(f"psPV{i}", [128, QC]) for i in range(2)]
    psSm = [PSb(f"psSm{i}", [128, QC]) for i in range(2)]

    wpieces = []
    for pe in range(NE):
        for gi_ in range(2):
            wpieces.append((gup[pe, gi_], gup_sc[pe, gi_], FC * DE))
    for dc in range(FC):
        wpieces.append((dwn[dc], dwn_sc[dc], NE * 4 * 128))
    wp_i = [0]
    wcast_free = [0, 0]
    wcast_wr = [0, 0]

    def emit_wcast_round(nmax=4):
        for _ in range(nmax):
            if wp_i[0] >= len(wpieces):
                return
            i = wp_i[0]
            wp_i[0] += 1
            par = i % 2
            srcf, dstb, ln = wpieces[i]
            p.wait("sync", sV, wcast_free[par])
            p.op("sync", lambda e, s=srcf, L=ln, par=par: e.dma_start(
                out=wstgB[par][:, 0:L], in_=s[:, 0:L]), dW, 16)
            ld = dW.n
            p.wait("vector", dW, ld)
            p.wait("vector", dG, wcast_wr[par])
            p.op("vector", lambda e, L=ln, par=par: e.tensor_copy(
                wcastB[par][:, 0:L], wstgB[par][:, 0:L]), sV, 1)
            wcast_free[par] = sV.n
            p.wait("gpsimd", sV, wcast_free[par])
            p.op("gpsimd", lambda e, d=dstb, L=ln, par=par: e.dma_start(
                out=d[:, 0:L], in_=wcastB[par][:, 0:L]), dG, 16)
            wcast_wr[par] = dG.n

    xB_loads = [0] * NQ
    x2_st = [0] * NQ
    wo_ev_all = [0] * NQ
    prev_exp = {0: 0, 1: 0}
    prev_spv = {0: 0, 1: 0}
    attn_overwrite_t = {0: 0, 1: 0}

    for q in range(NQ):
        qpar = q % 2
        qs = slice(q * QC, (q + 1) * QC)
        if q >= 2:
            p.wait("gpsimd", dG, x2_st[q - 2])
        for fc in range(FC):
            p.op("gpsimd", lambda e, q=q, fc=fc, qpar=qpar: e.dma_start(
                out=xfB[qpar][:, fc * QC:(fc + 1) * QC],
                in_=x_t[fc * 128:(fc + 1) * 128, q * QC:(q + 1) * QC]),
                dG, 16)
        xB_loads[q] = dG.n
        emit_wcast_round(4)

        if q >= 1:
            p.wait("tensor", sV, wo_ev_all[q - 1])
        exp_a = [0] * (NKT // 2)
        spv_t = [0] * (NKT // 2)
        for pr in range(NKT // 2):
            par = pr % 2
            p.wait("tensor", sA, exp_a[pr - 2] if pr >= 2 else prev_exp[par])
            for half in range(2):
                kt = pr * 2 + half
                ks = slice(kt * 128, (kt + 1) * 128)
                hs = slice(half * 512, (half + 1) * 512)
                p.op("tensor", lambda e, ks=ks, hs=hs, par=par, qs=qs:
                     e.matmul(psSc[par][:, hs], cosk[:, ks], cosq[:, qs],
                              start=True, stop=False))
                p.op("tensor", lambda e, ks=ks, hs=hs, par=par, qs=qs:
                     e.matmul(psSc[par][:, hs], sink[:, ks], sinq[:, qs],
                              start=False, stop=True),
                     sT if half == 1 else None, 1, fast=True)
            sc_t = sT.n
            p.wait("scalar", sT, sc_t)
            p.wait("scalar", sT, spv_t[pr - 2] if pr >= 2 else prev_spv[par])
            p.op("scalar", lambda e, par=par: e.activation(
                ptb[par][:], psSc[par][:], AF.Exp), sA, 1)
            exp_a[pr] = sA.n
            p.wait("tensor", sA, exp_a[pr])
            for half in range(2):
                kt = pr * 2 + half
                hs = slice(half * 512, (half + 1) * 512)
                p.op("tensor", lambda e, par=par, hs=hs, qpar=qpar, kt=kt:
                     e.matmul(psSm[qpar][:], ones_b[:], ptb[par][:, hs],
                              start=(kt == 0), stop=(kt == NKT - 1)))
                p.op("tensor", lambda e, par=par, hs=hs, qpar=qpar, kt=kt:
                     e.matmul(psPV[qpar][:], v_tm[:, kt * 128:(kt + 1) * 128],
                              ptb[par][:, hs],
                              start=(kt == 0), stop=(kt == NKT - 1)),
                     sT if kt == NKT - 1 else None, 1, fast=True)
            spv_t[pr] = sT.n
        prev_exp = {0: exp_a[NKT // 2 - 2], 1: exp_a[NKT // 2 - 1]}
        prev_spv = {0: spv_t[NKT // 2 - 2], 1: spv_t[NKT // 2 - 1]}
        pv_t = sT.n

        p.wait("scalar", sT, pv_t)
        p.op("scalar", lambda e, qpar=qpar: e.activation(
            lnS[qpar][:], psSm[qpar][:], AF.Ln), sA, 1)
        p.wait("vector", sA, sA.n)
        p.op("vector", lambda e, qpar=qpar: e.tensor_scalar(
            lnS[qpar][:], lnS[qpar][:], -1.0, None, ALU.mult), sV, 1)
        p.wait("scalar", sV, sV.n)
        p.op("scalar", lambda e, qpar=qpar: e.activation(
            rcp[qpar][:], lnS[qpar][:], AF.Exp), sA, 1)
        rcp_a = sA.n
        p.wait("vector", sA, rcp_a)
        p.wait("vector", sT, attn_overwrite_t[qpar])
        p.op("vector", lambda e, qpar=qpar: e.tensor_mul(
            attnT[qpar][:], psPV[qpar][:], rcp[qpar][:]), sV, 1)
        attn_v = sV.n

        p.wait("tensor", sV, attn_v)
        p.wait("vector", dG, xB_loads[q])
        wo_ev = [0] * FC
        for dc in range(FC):
            wo_ps = psPV[1 - qpar] if dc % 2 == 0 else psSm[qpar]
            if dc >= 2:
                p.wait("tensor", sV, wo_ev[dc - 2])
            p.op("tensor", lambda e, dc=dc, ps_=wo_ps, qpar=qpar: e.matmul(
                ps_[:], woTb[:, dc * 128:(dc + 1) * 128], attnT[qpar][:],
                start=True, stop=True), sT, 1, fast=True)
            mm_t = sT.n
            p.wait("vector", sT, mm_t)
            p.op("vector", lambda e, dc=dc, ps_=wo_ps, qpar=qpar:
                 e.scalar_tensor_tensor(
                     xfB[qpar][:, dc * QC:(dc + 1) * QC], ps_[:],
                     boP[:, dc:dc + 1],
                     xfB[qpar][:, dc * QC:(dc + 1) * QC],
                     ALU.add, ALU.add), sV, 1)
            wo_ev[dc] = sV.n
            if dc == FC - 1:
                attn_overwrite_t[qpar] = mm_t
        wo_ev_all[q] = sV.n
        p.wait("gpsimd", sV, wo_ev_all[q])
        for fc in range(FC):
            p.op("gpsimd", lambda e, q=q, fc=fc, qpar=qpar: e.dma_start(
                out=x2_hbm[fc * 128:(fc + 1) * 128, q * QC:(q + 1) * QC],
                in_=xfB[qpar][:, fc * QC:(fc + 1) * QC]), dG, 16)
        x2_st[q] = dG.n

    emit_wcast_round(99)
    all_scratch_dG = dG.n
    with nc.Block() as blk:
        p.flush(blk)
    esB1.close()
    esB.close()
    esAB.close()

    esC = contextlib.ExitStack()

    def SBc(name, shape, dt):
        return esC.enter_context(nc.sbuf_tensor(name, shape, dt))

    def PSc(name, shape, dt=F32):
        return esC.enter_context(nc.psum_tensor(name, shape, dt))

    CC = 256
    NCC = SUB // CC
    htil = SBc("htil", [128, NE * SUB * 4], BF16)
    ln2b = SBc("ln2b", [128, FC * SUB], BF16)
    x2f = [SBc(f"x2f{i}", [128, FC * CC], F32) for i in range(2)]
    xb2 = SBc("xb2", [128, FC * CC], BF16)
    x2lo = SBc("x2lo", [128, FC * CC], BF16)
    xsq2 = SBc("xsq2", [128, FC * CC], BF16)
    st1C = SBc("st1C", [128, CC], F32)
    st2C = SBc("st2C", [128, CC], F32)
    rrepC = SBc("rrepC", [128, CC], F32)
    nmrC = SBc("nmrC", [128, CC], F32)
    gaC = SBc("gaC", [128, CC], F32)
    gbC = SBc("gbC", [128, CC], F32)
    coefT = SBc("coefT", [8, SUB], BF16)
    sel8 = SBc("sel8", [8, 8 * 128], BF16)
    c8m = SBc("c8m", [128, 8], F32)
    c8q = SBc("c8q", [128, 8], F32)
    c8a = SBc("c8a", [128, 8], F32)
    c8b = SBc("c8b", [128, 8], F32)
    c8c = SBc("c8c", [128, 8], F32)
    rstd8 = SBc("rstd8", [128, 8], F32)
    nmr8 = SBc("nmr8", [128, 8], F32)
    r64a = SBc("r64a", [128, 64], F32)
    r64b = SBc("r64b", [128, 64], F32)
    coefb64 = SBc("coefb64", [128, 64], BF16)
    wslot = [SBc(f"wslot{i}", [128, FC * DE], BF16) for i in range(2)]
    wstgC = SBc("wstgC", [128, 1536], F32)
    dslot = [SBc(f"dslot{i}", [128, NE * 4 * 128], BF16) for i in range(2)]
    gsall = SBc("gsall", [128, 8 * 512], BF16)
    t2b = [SBc("t2b0", [128, 512], F32)] * 2
    x2r = [SBc(f"x2r{i}", [128, 512], F32) for i in range(2)]


    psG = [PSc(f"psG{i}", [128, 512]) for i in range(2)]
    psU = [PSc(f"psU{i}", [128, 512]) for i in range(2)]
    psD = [PSc(f"psD{i}", [128, 512]) for i in range(2)]
    psM = PSc("psM", [128, 512])
    psTr = PSc("psTr", [8, 128], BF16)

    wstg_free_v = 0
    slot_free_t = [0, 0, 0]
    slot_scr_dG = [all_scratch_dG] * 3
    dslot_free_t = [0, 0]
    dslot_scr_dG = [all_scratch_dG, all_scratch_dG]
    x2f_cons = [0, 0]
    x2f_cons_v = [0, 0]
    x2r_free = [0, 0]
    ln2b_free_t = 0
    psM_free_v = 0
    psD_free_v = 0
    gu_prev_silu = {0: 0, 1: 0}
    up_prev_ev = {0: 0, 1: 0}
    prev_up_ev_all = {i: 0 for i in range(8)}

    def fetch_piece(dst_sb, src_f32, src_bf, sub, length, slot_free_t,
                    scr_done_dG, src_off=0):
        nonlocal wstg_free_v
        if sub == 0:
            nh = (length + 1535) // 1536
            for hh in range(nh):
                a = hh * 1536
                b = min(length, a + 1536)
                p.wait("sync", sV, wstg_free_v)
                p.op("sync", lambda e, s=src_f32, a=a, b=b, o=src_off:
                     e.dma_start(out=wstgC[:, 0:b - a],
                                 in_=s[:, o + a:o + b]), dW, 16)
                ld = dW.n
                p.wait("vector", dW, ld)
                if hh == 0:
                    p.wait("vector", sT, slot_free_t)
                    p.wait("vector", dG, scr_done_dG)
                p.op("vector", lambda e, d=dst_sb, a=a, b=b: e.tensor_copy(
                    d[:, a:b], wstgC[:, 0:b - a]), sV, 1)
                wstg_free_v = sV.n
            p.wait("gpsimd", sV, wstg_free_v)
            p.op("gpsimd", lambda e, d=dst_sb, s=src_bf, L=length,
                 o=src_off: e.dma_start(out=s[:, o:o + L], in_=d[:, 0:L]),
                 dG, 16)
            return sV, wstg_free_v, dG.n
        p.wait("sync", sT, slot_free_t)
        p.wait("sync", dG, scr_done_dG)
        p.op("sync", lambda e, d=dst_sb, s=src_bf, L=length, o=src_off:
             e.dma_start(out=d[:, 0:L], in_=s[:, o:o + L]), dW, 16)
        return dW, dW.n, scr_done_dG

    p.op("sync", lambda e: e.dma_start(out=sel8[:], in_=sel_in[:]), dW, 16)
    p.wait("tensor", dW, dW.n)
    p.op("vector", lambda e: e.tensor_copy(r_hi[:], r_sb[:]))
    vdrain()
    p.op("vector", lambda e: e.tensor_sub(r_lo[:], r_sb[:], r_hi[:]), sV, 1)
    p.wait("tensor", sV, sV.n)

    for sub in range(NSUB):
        for h in range(NCC):
            ci = sub * NCC + h
            par = h % 2
            t0 = sub * SUB + h * CC
            p.wait("gpsimd", sT, x2f_cons[par])
            p.wait("gpsimd", sV, x2f_cons_v[par])
            for fc in range(FC):
                p.op("gpsimd", lambda e, fc=fc, t0=t0, par=par: e.dma_start(
                    out=x2f[par][:, fc * CC:(fc + 1) * CC],
                    in_=x2_hbm[fc * 128:(fc + 1) * 128, t0:t0 + CC]), dG, 16)
            x2_ld = dG.n
            p.wait("vector", dG, x2_ld)
            p.wait("vector", sT, chunk_last_t if ci > 0 else 0)
            p.op("vector", lambda e, par=par: e.tensor_copy(xb2[:],
                                                            x2f[par][:]))
            p.op("vector", lambda e, par=par: e.tensor_sub(
                x2lo[:], x2f[par][:], xb2[:]))
            p.op("vector", lambda e: e.tensor_mul(xsq2[:], xb2[:], xb2[:]),
                 sV, 1)
            cast2_v = sV.n
            p.wait("tensor", sV, cast2_v)
            p.wait("tensor", sV, psD_free_v)
            for kc in range(FC):
                p.op("tensor", lambda e, kc=kc: e.matmul(
                    psD[0][:, 0:CC], ones_b[:], xb2[:, kc * CC:(kc + 1) * CC],
                    start=(kc == 0), stop=(kc == FC - 1)))
            for kc in range(FC):
                p.op("tensor", lambda e, kc=kc: e.matmul(
                    psD[1][:, 0:CC], ones_b[:], xsq2[:, kc * CC:(kc + 1) * CC],
                    start=(kc == 0), stop=(kc == FC - 1)),
                    sT if kc == FC - 1 else None, 1, fast=True)
            stats_mm_done = sT.n
            p.wait("vector", sT, stats_mm_done)
            p.op("vector", lambda e: e.tensor_scalar(
                st1C[:], psD[0][:, 0:CC], 1.0 / D, None, ALU.mult))
            p.op("vector", lambda e: e.tensor_scalar(
                st2C[:], psD[1][:, 0:CC], 1.0 / D, None, ALU.mult))
            p.op("vector", lambda e: e.scalar_tensor_tensor(
                gaC[:], st1C[:], -1.0, st1C[:], ALU.mult, ALU.mult))
            p.op("vector", lambda e: e.scalar_tensor_tensor(
                gbC[:], st2C[:], 1e-5, gaC[:], ALU.add, ALU.add), sV, 1)
            psD_free_v = sV.n
            p.wait("scalar", sV, psD_free_v)
            p.op("scalar", lambda e: e.activation(gaC[:], gbC[:], AF.Ln),
                 sA, 1)
            p.wait("vector", sA, sA.n)
            p.op("vector", lambda e: e.tensor_scalar(
                gbC[:], gaC[:], -0.5, None, ALU.mult), sV, 1)
            p.wait("scalar", sV, sV.n)
            p.op("scalar", lambda e: e.activation(rrepC[:], gbC[:], AF.Exp),
                 sA, 1)
            p.wait("vector", sA, sA.n)
            p.op("vector", lambda e: e.scalar_tensor_tensor(
                nmrC[:], st1C[:], -1.0, rrepC[:], ALU.mult, ALU.mult))
            p.wait("vector", sT, ln2b_free_t if (sub > 0 and h == 0) else 0)
            for fc in range(FC):
                p.op("vector", lambda e, fc=fc, par=par: e.tensor_mul(
                    gaC[:], x2f[par][:, fc * CC:(fc + 1) * CC], rrepC[:]))
                p.op("vector", lambda e, fc=fc, h=h: e.tensor_add(
                    ln2b[:, fc * SUB + h * CC:fc * SUB + (h + 1) * CC],
                    gaC[:], nmrC[:]),
                    sV if fc == FC - 1 else None, 1)
            ln2b_chunk_v = sV.n
            ln2b_sub_v = ln2b_chunk_v

            for tt in range(CC // 128):
                ti = h * 2 + tt
                tsl = slice(tt * 128, (tt + 1) * 128)
                if ti == 0:
                    p.wait("tensor", sV, psM_free_v)
                for kc in range(FC):
                    p.op("tensor", lambda e, kc=kc, tsl=tsl, ti=ti: e.matmul(
                        psM[:, 64 + ti:65 + ti],
                        xb2[:, kc * CC:(kc + 1) * CC][:, tsl],
                        ones_b[:, 0:1],
                        start=(kc == 0), stop=(kc == FC - 1)))
                for kc in range(FC):
                    p.op("tensor", lambda e, kc=kc, tsl=tsl, ti=ti: e.matmul(
                        psM[:, 72 + ti:73 + ti],
                        xsq2[:, kc * CC:(kc + 1) * CC][:, tsl],
                        ones_b[:, 0:1],
                        start=(kc == 0), stop=(kc == FC - 1)))
                for pas, xs, rp in ((0, xb2, r_hi), (1, xb2, r_lo),
                                    (2, x2lo, r_hi)):
                    for kc in range(FC):
                        p.op("tensor", lambda e, kc=kc, tsl=tsl, xs=xs,
                             rp=rp, pas=pas, ti=ti: e.matmul(
                                 psM[:, ti * 8:(ti + 1) * 8],
                                 xs[:, kc * CC:(kc + 1) * CC][:, tsl],
                                 rp[:, kc * 8:(kc + 1) * 8],
                                 start=(kc == 0 and pas == 0),
                                 stop=(kc == FC - 1 and pas == 2)),
                             sT if (kc == FC - 1 and pas == 2) else None, 1,
                             fast=True)
            x2f_cons[par] = sT.n
            x2f_cons_v[par] = ln2b_chunk_v
            chunk_last_t = sT.n

        def d3(h_):
            return bass.AP(h_, 0, [[64, 128], [8, 8], [1, 8]])

        def segb(h_):
            return bass.AP(h_, 0, [[8, 128], [1, 8], [0, 8]])

        def tileb(h_):
            return bass.AP(h_, 0, [[8, 128], [0, 8], [1, 8]])

        p.op("tensor", lambda e: e.drain(fusable=False), sT, 1, fast=True)
        psm_drain_t = sT.n
        p.wait("vector", sT, psm_drain_t)
        p.op("vector", lambda e: e.tensor_scalar(
            c8m[:], psM[:, 64:72], 1.0 / D, None, ALU.mult))
        p.op("vector", lambda e: e.tensor_scalar(
            c8q[:], psM[:, 72:80], 1.0 / D, None, ALU.mult))
        vdrain()
        p.op("vector", lambda e: e.scalar_tensor_tensor(
            c8a[:], c8m[:], -1.0, c8m[:], ALU.mult, ALU.mult))
        vdrain()
        p.op("vector", lambda e: e.scalar_tensor_tensor(
            c8b[:], c8q[:], 1e-5, c8a[:], ALU.add, ALU.add), sV, 1)
        p.wait("scalar", sV, sV.n)
        p.op("scalar", lambda e: e.activation(c8a[:], c8b[:], AF.Ln), sA, 1)
        p.wait("vector", sA, sA.n)
        p.op("vector", lambda e: e.tensor_scalar(
            c8b[:], c8a[:], -0.5, None, ALU.mult), sV, 1)
        p.wait("scalar", sV, sV.n)
        p.op("scalar", lambda e: e.activation(rstd8[:], c8b[:], AF.Exp),
             sA, 1)
        p.wait("vector", sA, sA.n)
        p.op("vector", lambda e: e.scalar_tensor_tensor(
            nmr8[:], c8m[:], -1.0, rstd8[:], ALU.mult, ALU.mult))
        p.op("vector", lambda e: e.tensor_copy(r64a[:], psM[:, 0:64]), sV, 1)
        psM_free_v = sV.n
        vdrain()
        p.op("vector", lambda e: e.tensor_tensor(
            d3(r64a), d3(r64a), segb(rstd8), ALU.mult))
        p.op("vector", lambda e: e.tensor_tensor(
            d3(r64b), tileb(sr_rep), segb(nmr8), ALU.mult))
        vdrain()
        p.op("vector", lambda e: e.tensor_add(r64a[:], r64a[:], r64b[:]))
        vdrain()
        p.op("vector", lambda e: e.reduce_max(c8a[:], d3(r64a), axis=AX))
        vdrain()
        p.op("vector", lambda e: e.tensor_tensor(
            d3(r64b), d3(r64a), segb(c8a), ALU.subtract), sV, 1)
        p.wait("scalar", sV, sV.n)
        p.op("scalar", lambda e: e.activation(r64a[:], r64b[:], AF.Exp),
             sA, 1)
        p.wait("vector", sA, sA.n)
        p.op("vector", lambda e: e.reduce_sum(c8b[:], d3(r64a), axis=AX))
        vdrain()
        p.op("vector", lambda e: e.reciprocal(c8c[:], c8b[:]))
        vdrain()
        p.op("vector", lambda e: e.tensor_tensor(
            d3(r64b), d3(r64a), segb(c8c), ALU.mult))
        vdrain()
        p.op("vector", lambda e: e.reduce_max(c8a[:], d3(r64b), axis=AX))
        vdrain()
        p.op("vector", lambda e: e.tensor_tensor(
            d3(r64a), d3(r64b), segb(c8a), ALU.is_equal))
        vdrain()
        p.op("vector", lambda e: e.scalar_tensor_tensor(
            r64a[:], r64a[:], -2.0, r64b[:], ALU.mult, ALU.add))
        vdrain()
        p.op("vector", lambda e: e.reduce_max(c8b[:], d3(r64a), axis=AX))
        vdrain()
        p.op("vector", lambda e: e.tensor_tensor(
            d3(r64a), d3(r64b), segb(c8b), ALU.is_ge))
        vdrain()
        p.op("vector", lambda e: e.tensor_mul(r64a[:], r64a[:], r64b[:]))
        vdrain()
        p.op("vector", lambda e: e.tensor_copy(coefb64[:], r64a[:]), sV, 1)
        coef_cast_v = sV.n
        p.wait("tensor", sV, coef_cast_v)
        for ti in range(8):
            p.op("tensor", lambda e, ti=ti: e.transpose(
                psTr[:], coefb64[:, ti * 8:(ti + 1) * 8], eyeb[:]), sT, 1)
            p.wait("vector", sT, sT.n)
            p.op("vector", lambda e, ti=ti: e.tensor_copy(
                coefT[:, ti * 128:(ti + 1) * 128], psTr[:]), sV, 1)
            if ti < 7:
                p.wait("tensor", sV, sV.n)
        coef_ready_v = sV.n
        if dbg:
            p.wait("gpsimd", sV, coef_ready_v)
            p.op("gpsimd", lambda e, sub=sub: e.dma_start(
                out=dbg_coef[:, sub * SUB:(sub + 1) * SUB], in_=coefT[:]),
                dG, 16)
            if sub == 0:
                p.op("gpsimd", lambda e: e.dma_start(
                    out=dbg_ln2b[:], in_=ln2b[:]), dG, 16)

        for pe in range(NE):
            gslot = 0
            uslot = 1
            gsem, gval, slot_scr_dG[gslot] = fetch_piece(
                wslot[gslot], gup[pe, 0], gup_sc[pe, 0], 1, FC * DE,
                slot_free_t[gslot], slot_scr_dG[gslot])
            usem, uval, slot_scr_dG[uslot] = fetch_piece(
                wslot[uslot], gup[pe, 1], gup_sc[pe, 1], 1, FC * DE,
                slot_free_t[uslot], slot_scr_dG[uslot])
            p.wait("tensor", gsem, gval)
            if pe == 0:
                p.wait("tensor", sV, ln2b_sub_v)
            silu_a = {}
            for dcq in range(8):
                qc, dc = dcq // 4, dcq % 4
                par = dcq % 2
                p.wait("tensor", sA,
                       silu_a[dcq - 2] if dcq >= 2 else gu_prev_silu[par])
                for kc in range(FC):
                    p.op("tensor", lambda e, kc=kc, dc=dc, qc=qc, par=par,
                         gslot=gslot: e.matmul(
                             psG[par][:],
                             wslot[gslot][:, kc * DE + dc * 128:
                                          kc * DE + (dc + 1) * 128],
                             ln2b[:, kc * SUB + qc * 512:
                                  kc * SUB + (qc + 1) * 512],
                             start=(kc == 0), stop=(kc == FC - 1)),
                         sT if kc == FC - 1 else None, 1, fast=True)
                g_t = sT.n
                p.wait("scalar", sT, g_t)
                p.wait("scalar", sV, prev_up_ev_all[dcq])
                p.op("scalar", lambda e, dcq=dcq, par=par: e.activation(
                    gsall[:, dcq * 512:(dcq + 1) * 512], psG[par][:],
                    AF.Silu), sA, 1)
                silu_a[dcq] = sA.n
            slot_free_t[gslot] = sT.n
            gu_prev_silu = {0: silu_a[6], 1: silu_a[7]}

            p.wait("tensor", usem, uval)
            up_ev = {}
            for dcq in range(8):
                qc, dc = dcq // 4, dcq % 4
                par = dcq % 2
                if pe < 8 and dc == 0:
                    p.wait("tensor", sV, psM_free_v)
                    p.wait("tensor", sV, coef_ready_v)
                    p.op("tensor", lambda e, pe=pe, qc=qc: e.matmul(
                        psM[:], sel8[:, pe * 128:(pe + 1) * 128],
                        coefT[:, qc * 512:(qc + 1) * 512],
                        start=True, stop=True))
                p.wait("tensor", sV,
                       up_ev[dcq - 2] if dcq >= 2 else up_prev_ev[par])
                for kc in range(FC):
                    p.op("tensor", lambda e, kc=kc, dc=dc, qc=qc, par=par,
                         uslot=uslot: e.matmul(
                             psU[par][:],
                             wslot[uslot][:, kc * DE + dc * 128:
                                          kc * DE + (dc + 1) * 128],
                             ln2b[:, kc * SUB + qc * 512:
                                  kc * SUB + (qc + 1) * 512],
                             start=(kc == 0), stop=(kc == FC - 1)),
                         sT if kc == FC - 1 else None, 1, fast=True)
                u_t = sT.n
                p.wait("vector", sT, u_t)
                p.wait("vector", sA, silu_a[dcq])
                hpos = pe * 4 * SUB + dc * SUB + qc * 512
                if pe < 8:
                    p.op("vector", lambda e, dcq=dcq, par=par: e.tensor_mul(
                        t2b[par][:], psU[par][:],
                        gsall[:, dcq * 512:(dcq + 1) * 512]))
                    p.op("vector", lambda e, hpos=hpos, par=par:
                         e.tensor_mul(htil[:, hpos:hpos + 512], t2b[par][:],
                                      psM[:]), sV, 1)
                else:
                    p.op("vector", lambda e, dcq=dcq, par=par, hpos=hpos:
                         e.tensor_mul(htil[:, hpos:hpos + 512], psU[par][:],
                                      gsall[:, dcq * 512:(dcq + 1) * 512]),
                         sV, 1)
                up_ev[dcq] = sV.n
                if pe < 8 and dc == 3:
                    psM_free_v = sV.n
            slot_free_t[uslot] = sT.n
            up_prev_ev = {0: up_ev[6], 1: up_ev[7]}
            prev_up_ev_all = dict(up_ev)
        htil_v = sV.n

        if dbg and sub == 0:
            p.wait("gpsimd", sV, htil_v)
            p.op("gpsimd", lambda e: e.dma_start(
                out=dbg_htil[:], in_=htil[:, 0:2 * SUB * 4]), dG, 16)
        p.wait("tensor", sV, htil_v)
        dev_v = {}
        for dc in range(FC):
            dpar = dc % 2
            dsem, dval, dslot_scr_dG[dpar] = fetch_piece(
                dslot[dpar], dwn[dc], dwn_sc[dc], 1, NE * 4 * 128,
                dslot_free_t[dpar], dslot_scr_dG[dpar])
            p.wait("tensor", dsem, dval)
            for qc in range(2):
                idx = dc * 2 + qc
                par = idx % 2
                p.wait("gpsimd", yst, x2r_free[par])
                p.op("gpsimd", lambda e, dc=dc, qc=qc, sub=sub, par=par:
                     e.dma_start(
                         out=x2r[par][:],
                         in_=x2_hbm[dc * 128:(dc + 1) * 128,
                                    sub * SUB + qc * 512:
                                    sub * SUB + (qc + 1) * 512]), dG, 16)
                x2r_ld = dG.n
                if idx >= 2:
                    p.wait("tensor", sV, dev_v[idx - 2])
                for pe in range(NE):
                    for cc in range(4):
                        blk = pe * 4 + cc
                        first = (pe == 0 and cc == 0)
                        last = (pe == NE - 1 and cc == 3)
                        p.op("tensor",
                             lambda e, blk=blk, pe=pe, cc=cc, qc=qc, par=par,
                             dpar=dpar, first=first, last=last: e.matmul(
                                 psD[par][:],
                                 dslot[dpar][:, blk * 128:(blk + 1) * 128],
                                 htil[:, pe * 4 * SUB + cc * SUB + qc * 512:
                                      pe * 4 * SUB + cc * SUB +
                                      (qc + 1) * 512],
                                 start=first, stop=last),
                             sT if last else None, 1, fast=True)
                if qc == 1:
                    dslot_free_t[dpar] = sT.n
                dmm_t = sT.n
                p.wait("vector", sT, dmm_t)
                p.wait("vector", dG, x2r_ld)
                p.op("vector", lambda e, par=par: e.tensor_add(
                    x2r[par][:], psD[par][:], x2r[par][:]), sV, 1)
                dev_v[idx] = sV.n
                p.wait("gpsimd", sV, dev_v[idx])
                p.op("gpsimd", lambda e, dc=dc, qc=qc, sub=sub, par=par:
                     e.dma_start(
                         out=y[dc * 128:(dc + 1) * 128,
                               sub * SUB + qc * 512:
                               sub * SUB + (qc + 1) * 512],
                         in_=x2r[par][:]), yst, 16)
                x2r_free[par] = yst.n
        psD_free_v = dev_v[2 * FC - 1]
        ln2b_free_t = sT.n
        stats_mm_done = sT.n

    p.wait("gpsimd", yst, yst.n)
    p.wait("gpsimd", dG, dG.n)

    with nc.Block() as blk:
        p.flush(blk)
    esC.close()
    es.close()
    return nc


_NC_CACHE = {}


def _get_nc():
    if "nc" not in _NC_CACHE:
        _NC_CACHE["nc"] = build_nc()
    return _NC_CACHE["nc"]


def _mk_sel():
    s = np.zeros((8, 8 * 128), np.float32)
    for e in range(8):
        s[e, e * 128:(e + 1) * 128] = 1.0
    return s.astype(ml_dtypes.bfloat16)


def _prep_inputs(inputs):
    bf = ml_dtypes.bfloat16
    x = np.asarray(inputs["L0_attn_x"])

    def tile_gu(w):
        return np.ascontiguousarray(
            w.reshape(8, 128, DE).transpose(1, 0, 2).reshape(128, 8 * DE))

    eg = np.asarray(inputs["L0_ffn_experts_gate"])
    eu = np.asarray(inputs["L0_ffn_experts_up"])
    ed = np.asarray(inputs["L0_ffn_experts_down"])
    sg = np.asarray(inputs["L0_ffn_shared_gate"])
    su = np.asarray(inputs["L0_ffn_shared_up"])
    sd = np.asarray(inputs["L0_ffn_shared_down"])
    gup = np.zeros((NE, 2, 128, FC * DE), np.float32)
    for e in range(8):
        gup[e, 0] = tile_gu(eg[e])
        gup[e, 1] = tile_gu(eu[e])
    gup[8, 0] = tile_gu(sg[:, 0:DE])
    gup[8, 1] = tile_gu(su[:, 0:DE])
    gup[9, 0] = tile_gu(sg[:, DE:2 * DE])
    gup[9, 1] = tile_gu(su[:, DE:2 * DE])

    downs = [ed[e] for e in range(8)] + [sd[0:DE, :], sd[DE:2 * DE, :]]
    dwn = np.zeros((FC, 128, NE * 4 * 128), np.float32)
    for dc in range(FC):
        blocks = []
        for pe in range(NE):
            w = downs[pe]
            for cc in range(4):
                blocks.append(w[cc * 128:(cc + 1) * 128,
                                dc * 128:(dc + 1) * 128])
        dwn[dc] = np.concatenate(blocks, axis=1)

    common = dict(
        wqT=np.ascontiguousarray(np.asarray(inputs["L0_attn_Wq"]).T),
        wkT=np.ascontiguousarray(np.asarray(inputs["L0_attn_Wk"]).T),
        wvT=np.ascontiguousarray(np.asarray(inputs["L0_attn_Wv"]).T),
        woT=np.ascontiguousarray(np.asarray(inputs["L0_attn_Wo"]).T),
        wavel=np.ascontiguousarray(np.asarray(inputs["L0_attn_wavelengths"])),
        pbias=np.ascontiguousarray(np.asarray(inputs["L0_attn_phase_bias"])),
        bq=np.ascontiguousarray(np.asarray(inputs["L0_attn_bq"])),
        bk=np.ascontiguousarray(np.asarray(inputs["L0_attn_bk"])),
        bv=np.ascontiguousarray(np.asarray(inputs["L0_attn_bv"])),
        bo=np.ascontiguousarray(np.asarray(inputs["L0_attn_bo"])),
        router=np.ascontiguousarray(np.asarray(inputs["L0_ffn_router"])),
        gup=gup, dwn=dwn,
        eye_in=np.eye(128, dtype=np.float32).astype(bf),
        sel_in=_mk_sel(),
    )
    per_core = []
    for b in range(8):
        m = dict(common)
        m["x_t"] = np.ascontiguousarray(np.asarray(x[b]).T)
        per_core.append(m)
    return per_core


def kernel(**inputs):
    nc = _get_nc()
    in_maps = _prep_inputs(inputs)
    res = run_bass_kernel_spmd(nc, in_maps, core_ids=list(range(8)))
    out = np.empty((8, S, D), np.float32)
    for b in range(8):
        out[b] = np.asarray(res.results[b]["y"]).T
    return out


# version 09
# speedup vs baseline: 1.2652x; 1.0121x over previous
import contextlib
import math

import numpy as np
import ml_dtypes

import concourse.bass as bass
import concourse.mybir as mybir
from concourse.bass_utils import run_bass_kernel_spmd

F32 = mybir.dt.float32
F16 = mybir.dt.float16
BF16 = mybir.dt.bfloat16
I32 = mybir.dt.int32
ALU = mybir.AluOpType
AF = mybir.ActivationFunctionType
AX = mybir.AxisListType.X

D = 1024
S = 4096
H = 128
FC = D // 128
NE = 10
DE = 512
NSUB = 4
SUB = S // NSUB

TWO_PI = float(2.0 * math.pi)
INV_SQRT_H = 1.0 / math.sqrt(H)
EXP_SHIFT = -float(math.sqrt(H))
ROUND_OFS = 128.0


class Sem:
    def __init__(self, h, is_dma=False):
        self.h = h
        self.is_dma = is_dma
        self.n = 0

    def bump(self, k):
        self.n += k
        return self.n


class Prog:
    ENGS = ("sync", "vector", "scalar", "tensor", "gpsimd")

    def __init__(self):
        self.q = {e: [] for e in self.ENGS}

    def op(self, eng, build, sem=None, k=1, fast=False):
        if sem is not None:
            v = sem.bump(k)
            if sem.is_dma or fast:
                self.q[eng].append(
                    lambda e, b=build, s=sem, kk=k: b(e).then_inc(s.h, kk))
            else:
                def _f(e, b=build, s=sem, kk=k):
                    b(e)
                    e.drain(fusable=False).then_inc(s.h, kk)
                self.q[eng].append(_f)
            return v
        self.q[eng].append(build)
        return None

    def wait(self, eng, sem, val):
        if val:
            self.q[eng].append(lambda e, s=sem, v=val: e.wait_ge(s.h, v))

    def flush(self, block):
        for eng in self.ENGS:
            ops = self.q[eng]
            if not ops:
                continue

            def body(engine, ops=ops):
                for f in ops:
                    f(engine)

            getattr(block, eng)(body)
        self.q = {e: [] for e in self.ENGS}


def build_nc(dbg=False):
    nc = bass.Bass("TRN2", target_bir_lowering=False,
                   detect_race_conditions=False)

    x_t = nc.dram_tensor("x_t", [D, S], F32, kind="ExternalInput")
    wqT = nc.dram_tensor("wqT", [D, H], F32, kind="ExternalInput")
    wkT = nc.dram_tensor("wkT", [D, H], F32, kind="ExternalInput")
    wvT = nc.dram_tensor("wvT", [D, H], F32, kind="ExternalInput")
    woT = nc.dram_tensor("woT", [H, D], F32, kind="ExternalInput")
    wavel = nc.dram_tensor("wavel", [H], F32, kind="ExternalInput")
    pbias = nc.dram_tensor("pbias", [H], F32, kind="ExternalInput")
    bq_d = nc.dram_tensor("bq", [H], F32, kind="ExternalInput")
    bk_d = nc.dram_tensor("bk", [H], F32, kind="ExternalInput")
    bv_d = nc.dram_tensor("bv", [H], F32, kind="ExternalInput")
    bo_d = nc.dram_tensor("bo", [D], F32, kind="ExternalInput")
    router = nc.dram_tensor("router", [D, 8], F32, kind="ExternalInput")
    gup = nc.dram_tensor("gup", [NE, 2, 128, FC * DE], F32,
                         kind="ExternalInput")
    dwn = nc.dram_tensor("dwn", [FC, 128, NE * 4 * 128], F32,
                         kind="ExternalInput")
    eye_in = nc.dram_tensor("eye_in", [128, 128], BF16, kind="ExternalInput")
    sel_in = nc.dram_tensor("sel_in", [8, 8 * 128], BF16, kind="ExternalInput")

    y = nc.dram_tensor("y", [D, S], F32, kind="ExternalOutput")

    x2_hbm = nc.dram_tensor("x2_hbm", [D, S], F32,
                            kind="ExternalOutput" if dbg else "Internal")
    if dbg:
        dbg_t = {n: nc.dram_tensor("dbg_" + n, [128, S], F16,
                                   kind="ExternalOutput")
                 for n in ("cosq", "sinq", "cosk", "sink")}
        dbg_t["vf"] = nc.dram_tensor("dbg_vf", [128, S], BF16,
                                     kind="ExternalOutput")
        dbg_us = {n: nc.dram_tensor("dbg_" + n, [128, S], F16,
                                    kind="ExternalOutput")
                  for n in ("usq", "ucq")}
        dbg_coef = nc.dram_tensor("dbg_coef", [8, S], BF16,
                                  kind="ExternalOutput")
        dbg_htil = nc.dram_tensor("dbg_htil", [128, 2 * SUB * 4], BF16,
                                  kind="ExternalOutput")
        dbg_ln2b = nc.dram_tensor("dbg_ln2b", [128, FC * SUB], BF16,
                                  kind="ExternalOutput")
        dbg_rt = nc.dram_tensor("dbg_rt", [128, 64], F32,
                                kind="ExternalOutput")
        dbg_rt2 = nc.dram_tensor("dbg_rt2", [128, 40 * 8], F32,
                                 kind="ExternalOutput")
    gup_sc = nc.dram_tensor("gup_sc", [NE, 2, 128, FC * DE], BF16)
    dwn_sc = nc.dram_tensor("dwn_sc", [FC, 128, NE * 4 * 128], BF16)

    es = contextlib.ExitStack()

    def SB(name, shape, dt):
        return es.enter_context(nc.sbuf_tensor(name, shape, dt))

    def SEM(name, is_dma=False):
        return Sem(es.enter_context(nc.semaphore(name)), is_dma)

    eyeb = SB("eyeb", [128, 128], BF16)
    ones_b = SB("ones_b", [128, 128], BF16)
    ones_f = SB("ones_f", [128, 128], F32)
    c2pi = SB("c2pi", [128, 1], F32)
    ceps = SB("ceps", [128, 1], F32)
    cnhalf = SB("cnhalf", [128, 1], F32)
    cnone = SB("cnone", [128, 1], F32)
    cinvsq = SB("cinvsq", [128, 1], F32)
    cshift = SB("cshift", [128, 1], F32)
    invw = SB("invw", [128, 1], F32)
    s1c = SB("s1c", [128, 1], F32)
    s2q = SB("s2q", [128, 1], F32)
    s2k = SB("s2k", [128, 1], F32)
    sqc = SB("sqc", [128, 1], F32)
    skc = SB("skc", [128, 1], F32)
    svc = SB("svc", [128, 1], F32)
    bvb = SB("bvb", [128, 1], BF16)
    bqf = SB("bqf", [128, 1], F32)
    bkf = SB("bkf", [128, 1], F32)
    bvf = SB("bvf", [128, 1], F32)
    wlf = SB("wlf", [128, 1], F32)
    pbf = SB("pbf", [128, 1], F32)
    bo8 = SB("bo8", [128, FC], F32)
    boP = SB("boP", [128, FC], F32)
    tcol = SB("tcol", [128, 1], F32)
    wqTb = SB("wqTb", [128, FC * 128], BF16)
    wkTb = SB("wkTb", [128, FC * 128], BF16)
    wvTb = SB("wvTb", [128, FC * 128], BF16)
    woTb = SB("woTb", [128, FC * 128], BF16)
    r_sb = SB("r_sb", [128, FC * 8], F32)
    r_hi = SB("r_hi", [128, FC * 8], BF16)
    r_lo = SB("r_lo", [128, FC * 8], BF16)
    sr_rep = SB("sr_rep", [128, 8], F32)

    dW = SEM("dW", is_dma=True)
    dG = SEM("dG", is_dma=True)
    yst = SEM("yst", is_dma=True)
    sT = SEM("sT")
    sV = SEM("sV")
    sA = SEM("sA")

    p = Prog()

    def vdrain():
        p.op("vector", lambda e: e.drain(fusable=False))

    esAB = contextlib.ExitStack()

    def SBab(name, shape, dt):
        return esAB.enter_context(nc.sbuf_tensor(name, shape, dt))

    cosq = SBab("cosq", [128, S], F16)
    sinq = SBab("sinq", [128, S], F16)
    cosk = SBab("cosk", [128, S], F16)
    sink = SBab("sink", [128, S], F16)
    vf = SBab("vf", [128, S], BF16)

    esA = contextlib.ExitStack()

    def SBa(name, shape, dt):
        return esA.enter_context(nc.sbuf_tensor(name, shape, dt))

    def PSa(name, shape, dt=F32):
        return esA.enter_context(nc.psum_tensor(name, shape, dt))

    CH = 512
    NCH = S // CH
    xfA = [SBa(f"xfA{i}", [128, FC * CH], F32) for i in range(2)]
    xbA = [SBa(f"xbA{i}", [128, FC * CH], BF16) for i in range(2)]
    xsqA = [SBa(f"xsqA{i}", [128, FC * CH], BF16) for i in range(2)]
    stat1 = SBa("stat1", [128, CH], F32)
    stat2 = SBa("stat2", [128, CH], F32)
    rrep = SBa("rrep", [128, CH], F32)
    nmr = SBa("nmr", [128, CH], F32)
    ga = SBa("ga", [128, CH], F32)
    gb = SBa("gb", [128, CH], F32)
    gc = SBa("gc", [128, CH], F32)
    gi = SBa("gi", [128, CH], I32)
    usq = SBa("usq", [128, S], F16)
    ucq = SBa("ucq", [128, S], F16)
    usk = SBa("usk", [128, S], F16)
    uck = SBa("uck", [128, S], F16)
    wstage = SBa("wstage", [128, FC * 128], F32)

    psMean = PSa("psMean", [128, CH])
    psSq = PSa("psSq", [128, CH])
    psQ = [PSa(f"psQ{i}", [128, CH]) for i in range(2)]
    psK = [PSa(f"psK{i}", [128, CH]) for i in range(2)]
    psV = [PSa(f"psV{i}", [128, CH]) for i in range(2)]

    p.op("sync", lambda e: e.dma_start(out=eyeb[:], in_=eye_in[:]), dW, 16)
    for col, src in ((wlf, wavel), (pbf, pbias), (bqf, bq_d), (bkf, bk_d),
                     (bvf, bv_d)):
        p.op("sync", lambda e, c=col, s=src: e.dma_start(
            out=c[:], in_=bass.AP(s, 0, [[1, 128], [1, 1]])), dW, 16)
    for dc in range(FC):
        p.op("sync", lambda e, dc=dc: e.dma_start(
            out=bo8[:, dc:dc + 1],
            in_=bass.AP(bo_d, dc * 128, [[1, 128], [1, 1]])), dW, 16)
    for kc in range(FC):
        p.op("sync", lambda e, kc=kc: e.dma_start(
            out=r_sb[:, kc * 8:(kc + 1) * 8],
            in_=router[kc * 128:(kc + 1) * 128, :]), dW, 16)
    small_loads = dW.n

    p.op("vector", lambda e: e.memset(ones_b[:], 1.0))
    p.op("vector", lambda e: e.memset(ones_f[:], 1.0))
    p.op("vector", lambda e: e.memset(c2pi[:], TWO_PI))
    p.op("vector", lambda e: e.memset(ceps[:], 1e-5))
    p.op("vector", lambda e: e.memset(cnhalf[:], -0.5))
    p.op("vector", lambda e: e.memset(cnone[:], -1.0))
    p.op("vector", lambda e: e.memset(cinvsq[:], INV_SQRT_H))
    p.op("vector", lambda e: e.memset(cshift[:], EXP_SHIFT))
    p.wait("vector", dW, small_loads)
    p.op("vector", lambda e: e.tensor_scalar(tcol[:], wlf[:], 1e-8, None,
                                             ALU.add))
    vdrain()
    p.op("vector", lambda e: e.reciprocal(invw[:], tcol[:]))
    vdrain()
    p.op("vector", lambda e: e.tensor_scalar(s1c[:], invw[:], 1.0 / TWO_PI,
                                             None, ALU.mult))
    for bcol, s2 in ((bqf, s2q), (bkf, s2k)):
        p.op("vector", lambda e, b=bcol, s=s2: e.tensor_mul(s[:], b[:],
                                                            invw[:]))
        vdrain()
        p.op("vector", lambda e, s=s2: e.tensor_add(s[:], s[:], pbf[:]))
        vdrain()
        p.op("vector", lambda e, s=s2: e.tensor_scalar(
            s[:], s[:], 1.0 / TWO_PI, ROUND_OFS, ALU.mult, ALU.add))
    p.op("vector", lambda e: e.tensor_copy(bvb[:], bvf[:]), sV, 1)

    for wi, (wsrc, wdst) in enumerate(((wqT, wqTb), (wkT, wkTb), (wvT, wvTb),
                                       (woT, woTb))):
        p.wait("sync", sV, sV.n)
        for kc in range(FC):
            if wi < 3:
                src = wsrc[kc * 128:(kc + 1) * 128, :]
            else:
                src = wsrc[:, kc * 128:(kc + 1) * 128]
            p.op("sync", lambda e, s=src, kc=kc: e.dma_start(
                out=wstage[:, kc * 128:(kc + 1) * 128], in_=s), dW, 16)
        p.wait("vector", dW, dW.n)
        p.op("vector", lambda e, d=wdst: e.tensor_copy(d[:], wstage[:]), sV, 1)
    wcast_v = sV.n

    p.wait("tensor", sV, wcast_v)
    for wsb, col in ((wqTb, sqc), (wkTb, skc), (wvTb, svc)):
        for kc in range(FC):
            p.op("tensor", lambda e, w=wsb, kc=kc: e.matmul(
                psMean[:, 0:1], w[:, kc * 128:(kc + 1) * 128], ones_b[:, 0:1],
                start=(kc == 0), stop=(kc == FC - 1)),
                sT if kc == FC - 1 else None, 1)
        p.wait("vector", sT, sT.n)
        p.op("vector", lambda e, c=col: e.tensor_copy(c[:], psMean[:, 0:1]),
             sV, 1)
        p.wait("tensor", sV, sV.n)
    for dc in range(FC):
        p.op("tensor", lambda e, dc=dc: e.matmul(
            psSq[:, dc:dc + 1], woTb[:, dc * 128:(dc + 1) * 128], bvb[:, 0:1],
            start=True, stop=True), sT if dc == FC - 1 else None, 1)
    p.wait("vector", sT, sT.n)
    p.op("vector", lambda e: e.tensor_add(boP[:], psSq[:, 0:FC], bo8[:]),
         sV, 1)
    p.op("vector", lambda e: e.tensor_copy(r_hi[:], r_sb[:]))
    vdrain()
    p.op("vector", lambda e: e.tensor_sub(r_lo[:], r_sb[:], r_hi[:]), sV, 1)
    p.wait("tensor", sV, sV.n)
    for pas, rp in ((0, r_hi), (1, r_lo)):
        for kc in range(FC):
            p.op("tensor", lambda e, kc=kc, rp=rp, pas=pas: e.matmul(
                psMean[:, 0:8], ones_b[:], rp[:, kc * 8:(kc + 1) * 8],
                start=(kc == 0 and pas == 0),
                stop=(kc == FC - 1 and pas == 1)),
                sT if (kc == FC - 1 and pas == 1) else None, 1)
    p.wait("vector", sT, sT.n)
    p.op("vector", lambda e: e.tensor_copy(sr_rep[:], psMean[:, 0:8]), sV, 1)
    prol_v = sV.n

    x_loads = [0] * NCH
    cast_v = [0] * NCH
    cons_v = [0] * NCH
    qkv_t = [0] * NCH
    stats_used_v = 0

    for c in range(NCH):
        par = c % 2
        if c >= 2:
            p.wait("sync", sV, cons_v[c - 2])
        for fc in range(FC):
            p.op("sync", lambda e, c=c, fc=fc, par=par: e.dma_start(
                out=xfA[par][:, fc * CH:(fc + 1) * CH],
                in_=x_t[fc * 128:(fc + 1) * 128, c * CH:(c + 1) * CH]),
                dW, 16)
        x_loads[c] = dW.n

        p.wait("vector", dW, x_loads[c])
        if c >= 2:
            p.wait("vector", sT, qkv_t[c - 2])
        p.op("vector", lambda e, par=par: e.tensor_copy(xbA[par][:],
                                                        xfA[par][:]))
        p.op("vector", lambda e, par=par: e.tensor_mul(
            xsqA[par][:], xbA[par][:], xbA[par][:]), sV, 1)
        cast_v[c] = sV.n

        p.wait("tensor", sV, cast_v[c])
        p.wait("tensor", sV, stats_used_v if c > 0 else prol_v)
        for kc in range(FC):
            p.op("tensor", lambda e, par=par, kc=kc: e.matmul(
                psMean[:], ones_b[:], xbA[par][:, kc * CH:(kc + 1) * CH],
                start=(kc == 0), stop=(kc == FC - 1)))
        for kc in range(FC):
            p.op("tensor", lambda e, par=par, kc=kc: e.matmul(
                psSq[:], ones_b[:], xsqA[par][:, kc * CH:(kc + 1) * CH],
                start=(kc == 0), stop=(kc == FC - 1)),
                sT if kc == FC - 1 else None, 1, fast=True)
        stats_t = sT.n

        if c >= 2:
            p.wait("tensor", sV, cons_v[c - 2])
        for wsb, dst, last in ((wqTb, psQ[par], False), (wkTb, psK[par], False),
                               (wvTb, psV[par], True)):
            for kc in range(FC):
                p.op("tensor", lambda e, w=wsb, d=dst, par=par, kc=kc:
                     e.matmul(
                         d[:], w[:, kc * 128:(kc + 1) * 128],
                         xbA[par][:, kc * CH:(kc + 1) * CH],
                         start=(kc == 0), stop=(kc == FC - 1)),
                     sT if (kc == FC - 1 and last) else None, 1, fast=True)
        qkv_t[c] = sT.n

        p.wait("vector", sT, stats_t)
        p.op("vector", lambda e: e.tensor_scalar(stat1[:], psMean[:], 1.0 / D,
                                                 None, ALU.mult))
        p.op("vector", lambda e: e.tensor_scalar(stat2[:], psSq[:], 1.0 / D,
                                                 None, ALU.mult))
        p.op("vector", lambda e: e.scalar_tensor_tensor(
            ga[:], stat1[:], -1.0, stat1[:], ALU.mult, ALU.mult))
        p.op("vector", lambda e: e.scalar_tensor_tensor(
            gb[:], stat2[:], 1e-5, ga[:], ALU.add, ALU.add), sV, 1)
        stats_used_v = sV.n
        p.wait("scalar", sV, stats_used_v)
        p.op("scalar", lambda e: e.activation(ga[:], gb[:], AF.Ln), sA, 1)
        p.wait("vector", sA, sA.n)
        p.op("vector", lambda e: e.tensor_scalar(
            gb[:], ga[:], -0.5, None, ALU.mult), sV, 1)
        p.wait("scalar", sV, sV.n)
        p.op("scalar", lambda e: e.activation(rrep[:], gb[:], AF.Exp), sA, 1)
        p.wait("vector", sA, sA.n)
        p.op("vector", lambda e: e.scalar_tensor_tensor(
            nmr[:], stat1[:], -1.0, rrep[:], ALU.mult, ALU.mult))

        p.wait("vector", sT, qkv_t[c])
        sl = slice(c * CH, (c + 1) * CH)
        for psX, s2x, sxc, us_f, uc_f in ((psQ[par], s2q, sqc, usq, ucq),
                                          (psK[par], s2k, skc, usk, uck)):
            p.op("vector", lambda e, psX=psX: e.tensor_mul(ga[:], psX[:],
                                                           rrep[:]))
            p.op("vector", lambda e, sxc=sxc: e.scalar_tensor_tensor(
                gb[:], nmr[:], sxc[:], ga[:], ALU.mult, ALU.add))
            p.op("vector", lambda e, s2x=s2x: e.tensor_scalar(
                ga[:], gb[:], s1c[:], s2x[:], ALU.mult, ALU.add))
            p.op("vector", lambda e: e.tensor_copy(gi[:], ga[:]))
            p.op("vector", lambda e: e.tensor_copy(gb[:], gi[:]))
            p.op("vector", lambda e: e.tensor_sub(ga[:], ga[:], gb[:]))
            p.op("vector", lambda e: e.tensor_scalar(gb[:], ga[:], 0.5, None,
                                                     ALU.is_gt))
            p.op("vector", lambda e: e.scalar_tensor_tensor(
                gc[:], gb[:], -1.0, ga[:], ALU.mult, ALU.add))
            p.op("vector", lambda e, u=us_f, sl=sl: e.tensor_scalar(
                u[:, sl], gc[:], TWO_PI, None, ALU.mult))
            p.op("vector", lambda e: e.tensor_scalar(
                ga[:], gc[:], 0.25, None, ALU.add))
            p.op("vector", lambda e: e.tensor_scalar(gb[:], ga[:], 0.5, None,
                                                     ALU.is_gt))
            p.op("vector", lambda e: e.scalar_tensor_tensor(
                gc[:], gb[:], -1.0, ga[:], ALU.mult, ALU.add))
            p.op("vector", lambda e, u=uc_f, sl=sl: e.tensor_scalar(
                u[:, sl], gc[:], TWO_PI, None, ALU.mult), sV, 1)
        p.op("vector", lambda e, par=par: e.tensor_mul(ga[:], psV[par][:],
                                                       rrep[:]))
        p.op("vector", lambda e, sl=sl: e.scalar_tensor_tensor(
            vf[:, sl], nmr[:], svc[:], ga[:], ALU.mult, ALU.add), sV, 1)
        cons_v[c] = sV.n

    p.wait("scalar", sV, cons_v[NCH - 1])
    for src, dst in ((usq, sinq), (ucq, cosq), (usk, sink), (uck, cosk)):
        p.op("scalar", lambda e, s=src, d=dst: e.activation(
            d[:], s[:], AF.Sin), sA, 1)
    p.wait("vector", sA, sA.n)
    for t in (cosq, sinq):
        p.op("vector", lambda e, t=t: e.tensor_scalar(
            t[:], t[:], INV_SQRT_H, None, ALU.mult), sV, 1)
    sinscale_v = sV.n
    trig_a = sA.n
    if dbg:
        p.wait("gpsimd", sA, trig_a)
        p.wait("gpsimd", sV, sinscale_v)
        for n, t in (("cosq", cosq), ("sinq", sinq), ("cosk", cosk),
                     ("sink", sink), ("vf", vf)):
            p.op("gpsimd", lambda e, n=n, t=t: e.dma_start(
                out=dbg_t[n][:], in_=t[:]), dG, 16)
        for n, t in (("usq", usq), ("ucq", ucq)):
            p.op("gpsimd", lambda e, n=n, t=t: e.dma_start(
                out=dbg_us[n][:], in_=t[:]), dG, 16)
        p.wait("gpsimd", dG, dG.n)

    with nc.Block() as blk:
        p.flush(blk)
    esA.close()

    esB = contextlib.ExitStack()
    v_tm = esB.enter_context(nc.sbuf_tensor("v_tm", [128, S], BF16))

    esB0 = contextlib.ExitStack()
    psTrV = [esB0.enter_context(nc.psum_tensor(f"psTrV{i}", [128, 128], BF16))
             for i in range(2)]
    tr_cons = [0, 0]
    for kt in range(S // 128):
        par = kt % 2
        if kt >= 2:
            p.wait("tensor", sV, tr_cons[par])
        p.op("tensor", lambda e, kt=kt, par=par: e.transpose(
            psTrV[par][:], vf[:, kt * 128:(kt + 1) * 128], eyeb[:]), sT, 1)
        p.wait("vector", sT, sT.n)
        p.op("vector", lambda e, kt=kt, par=par: e.tensor_copy(
            v_tm[:, kt * 128:(kt + 1) * 128], psTrV[par][:]), sV, 1)
        tr_cons[par] = sV.n
    with nc.Block() as blk:
        p.flush(blk)
    esB0.close()

    esB1 = contextlib.ExitStack()

    def SBb(name, shape, dt):
        return esB1.enter_context(nc.sbuf_tensor(name, shape, dt))

    def PSb(name, shape, dt=F32):
        return esB1.enter_context(nc.psum_tensor(name, shape, dt))

    QC = 512
    NQ = S // QC
    NKT = S // 128
    xfB = [SBb(f"xfB{i}", [128, FC * QC], F32) for i in range(2)]
    wstgB = [SBb(f"wstgB{i}", [128, 5120], F32) for i in range(2)]
    wcastB = [SBb(f"wcastB{i}", [128, 5120], BF16) for i in range(2)]
    ptb = [SBb(f"ptb{i}", [128, 1024], BF16) for i in range(2)]
    attnT = [SBb(f"attnT{i}", [128, QC], BF16) for i in range(2)]
    lnS = [SBb(f"lnS{i}", [128, QC], F32) for i in range(2)]
    rcp = [SBb(f"rcp{i}", [128, QC], F32) for i in range(2)]
    psSc = [PSb(f"psSc{i}", [128, 1024]) for i in range(2)]
    psPV = [PSb(f"psPV{i}", [128, QC]) for i in range(2)]
    psSm = [PSb(f"psSm{i}", [128, QC]) for i in range(2)]

    wpieces = []
    for pe in range(NE):
        for gi_ in range(2):
            wpieces.append((gup[pe, gi_], gup_sc[pe, gi_], FC * DE))
    for dc in range(FC):
        wpieces.append((dwn[dc], dwn_sc[dc], NE * 4 * 128))
    wp_i = [0]
    wcast_free = [0, 0]
    wcast_wr = [0, 0]

    def emit_wcast_round(nmax=4):
        for _ in range(nmax):
            if wp_i[0] >= len(wpieces):
                return
            i = wp_i[0]
            wp_i[0] += 1
            par = i % 2
            srcf, dstb, ln = wpieces[i]
            p.wait("sync", sV, wcast_free[par])
            p.op("sync", lambda e, s=srcf, L=ln, par=par: e.dma_start(
                out=wstgB[par][:, 0:L], in_=s[:, 0:L]), dW, 16)
            ld = dW.n
            p.wait("vector", dW, ld)
            p.wait("vector", dG, wcast_wr[par])
            p.op("vector", lambda e, L=ln, par=par: e.tensor_copy(
                wcastB[par][:, 0:L], wstgB[par][:, 0:L]), sV, 1)
            wcast_free[par] = sV.n
            p.wait("gpsimd", sV, wcast_free[par])
            p.op("gpsimd", lambda e, d=dstb, L=ln, par=par: e.dma_start(
                out=d[:, 0:L], in_=wcastB[par][:, 0:L]), dG, 16)
            wcast_wr[par] = dG.n

    xB_loads = [0] * NQ
    x2_st = [0] * NQ
    wo_ev_all = [0] * NQ
    prev_exp = {0: 0, 1: 0}
    prev_spv = {0: 0, 1: 0}
    attn_overwrite_t = {0: 0, 1: 0}

    for q in range(NQ):
        qpar = q % 2
        qs = slice(q * QC, (q + 1) * QC)
        if q >= 2:
            p.wait("gpsimd", dG, x2_st[q - 2])
        for fc in range(FC):
            p.op("gpsimd", lambda e, q=q, fc=fc, qpar=qpar: e.dma_start(
                out=xfB[qpar][:, fc * QC:(fc + 1) * QC],
                in_=x_t[fc * 128:(fc + 1) * 128, q * QC:(q + 1) * QC]),
                dG, 16)
        xB_loads[q] = dG.n
        emit_wcast_round(4)

        if q >= 1:
            p.wait("tensor", sV, wo_ev_all[q - 1])
        exp_a = [0] * (NKT // 2)
        spv_t = [0] * (NKT // 2)
        for pr in range(NKT // 2):
            par = pr % 2
            p.wait("tensor", sA, exp_a[pr - 2] if pr >= 2 else prev_exp[par])
            for half in range(2):
                kt = pr * 2 + half
                ks = slice(kt * 128, (kt + 1) * 128)
                hs = slice(half * 512, (half + 1) * 512)
                p.op("tensor", lambda e, ks=ks, hs=hs, par=par, qs=qs:
                     e.matmul(psSc[par][:, hs], cosk[:, ks], cosq[:, qs],
                              start=True, stop=False))
                p.op("tensor", lambda e, ks=ks, hs=hs, par=par, qs=qs:
                     e.matmul(psSc[par][:, hs], sink[:, ks], sinq[:, qs],
                              start=False, stop=True),
                     sT if half == 1 else None, 1, fast=True)
            sc_t = sT.n
            p.wait("scalar", sT, sc_t)
            p.wait("scalar", sT, spv_t[pr - 2] if pr >= 2 else prev_spv[par])
            p.op("scalar", lambda e, par=par: e.activation(
                ptb[par][:], psSc[par][:], AF.Exp), sA, 1)
            exp_a[pr] = sA.n
            p.wait("tensor", sA, exp_a[pr])
            for half in range(2):
                kt = pr * 2 + half
                hs = slice(half * 512, (half + 1) * 512)
                p.op("tensor", lambda e, par=par, hs=hs, qpar=qpar, kt=kt:
                     e.matmul(psSm[qpar][:], ones_b[:], ptb[par][:, hs],
                              start=(kt == 0), stop=(kt == NKT - 1)))
                p.op("tensor", lambda e, par=par, hs=hs, qpar=qpar, kt=kt:
                     e.matmul(psPV[qpar][:], v_tm[:, kt * 128:(kt + 1) * 128],
                              ptb[par][:, hs],
                              start=(kt == 0), stop=(kt == NKT - 1)),
                     sT if kt == NKT - 1 else None, 1, fast=True)
            spv_t[pr] = sT.n
        prev_exp = {0: exp_a[NKT // 2 - 2], 1: exp_a[NKT // 2 - 1]}
        prev_spv = {0: spv_t[NKT // 2 - 2], 1: spv_t[NKT // 2 - 1]}
        pv_t = sT.n

        p.wait("scalar", sT, pv_t)
        p.op("scalar", lambda e, qpar=qpar: e.activation(
            lnS[qpar][:], psSm[qpar][:], AF.Ln), sA, 1)
        p.wait("vector", sA, sA.n)
        p.op("vector", lambda e, qpar=qpar: e.tensor_scalar(
            lnS[qpar][:], lnS[qpar][:], -1.0, None, ALU.mult), sV, 1)
        p.wait("scalar", sV, sV.n)
        p.op("scalar", lambda e, qpar=qpar: e.activation(
            rcp[qpar][:], lnS[qpar][:], AF.Exp), sA, 1)
        rcp_a = sA.n
        p.wait("vector", sA, rcp_a)
        p.wait("vector", sT, attn_overwrite_t[qpar])
        p.op("vector", lambda e, qpar=qpar: e.tensor_mul(
            attnT[qpar][:], psPV[qpar][:], rcp[qpar][:]), sV, 1)
        attn_v = sV.n

        p.wait("tensor", sV, attn_v)
        p.wait("vector", dG, xB_loads[q])
        wo_ev = [0] * FC
        for dc in range(FC):
            wo_ps = psPV[1 - qpar] if dc % 2 == 0 else psSm[qpar]
            if dc >= 2:
                p.wait("tensor", sV, wo_ev[dc - 2])
            p.op("tensor", lambda e, dc=dc, ps_=wo_ps, qpar=qpar: e.matmul(
                ps_[:], woTb[:, dc * 128:(dc + 1) * 128], attnT[qpar][:],
                start=True, stop=True), sT, 1, fast=True)
            mm_t = sT.n
            p.wait("vector", sT, mm_t)
            p.op("vector", lambda e, dc=dc, ps_=wo_ps, qpar=qpar:
                 e.scalar_tensor_tensor(
                     xfB[qpar][:, dc * QC:(dc + 1) * QC], ps_[:],
                     boP[:, dc:dc + 1],
                     xfB[qpar][:, dc * QC:(dc + 1) * QC],
                     ALU.add, ALU.add), sV, 1)
            wo_ev[dc] = sV.n
            if dc == FC - 1:
                attn_overwrite_t[qpar] = mm_t
        wo_ev_all[q] = sV.n
        p.wait("gpsimd", sV, wo_ev_all[q])
        for fc in range(FC):
            p.op("gpsimd", lambda e, q=q, fc=fc, qpar=qpar: e.dma_start(
                out=x2_hbm[fc * 128:(fc + 1) * 128, q * QC:(q + 1) * QC],
                in_=xfB[qpar][:, fc * QC:(fc + 1) * QC]), dG, 16)
        x2_st[q] = dG.n

    emit_wcast_round(99)
    all_scratch_dG = dG.n
    with nc.Block() as blk:
        p.flush(blk)
    esB1.close()
    esB.close()
    esAB.close()

    esC = contextlib.ExitStack()

    def SBc(name, shape, dt):
        return esC.enter_context(nc.sbuf_tensor(name, shape, dt))

    def PSc(name, shape, dt=F32):
        return esC.enter_context(nc.psum_tensor(name, shape, dt))

    CC = 256
    NCC = SUB // CC
    htil = SBc("htil", [128, NE * SUB * 4], BF16)
    ln2b = SBc("ln2b", [128, FC * SUB], BF16)
    x2f = [SBc(f"x2f{i}", [128, FC * CC], F32) for i in range(2)]
    xb2 = SBc("xb2", [128, FC * CC], BF16)
    x2lo = SBc("x2lo", [128, FC * CC], BF16)
    xsq2 = SBc("xsq2", [128, FC * CC], BF16)
    st1C = SBc("st1C", [128, CC], F32)
    st2C = SBc("st2C", [128, CC], F32)
    rrepC = SBc("rrepC", [128, CC], F32)
    nmrC = SBc("nmrC", [128, CC], F32)
    gaC = SBc("gaC", [128, CC], F32)
    gbC = SBc("gbC", [128, CC], F32)
    coefT = SBc("coefT", [8, SUB], BF16)
    sel8 = SBc("sel8", [8, 8 * 128], BF16)
    c8m = SBc("c8m", [128, 8], F32)
    c8q = SBc("c8q", [128, 8], F32)
    c8a = SBc("c8a", [128, 8], F32)
    c8b = SBc("c8b", [128, 8], F32)
    c8c = SBc("c8c", [128, 8], F32)
    rstd8 = SBc("rstd8", [128, 8], F32)
    nmr8 = SBc("nmr8", [128, 8], F32)
    r64a = SBc("r64a", [128, 64], F32)
    r64b = SBc("r64b", [128, 64], F32)
    coefb64 = SBc("coefb64", [128, 64], BF16)
    wslot = [SBc(f"wslot{i}", [128, FC * DE], BF16) for i in range(2)]
    wstgC = SBc("wstgC", [128, 1536], F32)
    dslot = [SBc(f"dslot{i}", [128, NE * 4 * 128], BF16) for i in range(2)]
    gsall = SBc("gsall", [128, 8 * 512], BF16)
    t2b = [SBc("t2b0", [128, 512], F32)] * 2
    x2r = [SBc(f"x2r{i}", [128, 512], F32) for i in range(2)]


    psG = [PSc(f"psG{i}", [128, 512]) for i in range(2)]
    psU = [PSc(f"psU{i}", [128, 512]) for i in range(2)]
    psD = [PSc(f"psD{i}", [128, 512]) for i in range(2)]
    psM = PSc("psM", [128, 512])
    psTr = PSc("psTr", [8, 128], BF16)

    wstg_free_v = 0
    slot_free_t = [0, 0, 0]
    slot_scr_dG = [all_scratch_dG] * 3
    dslot_free_t = [0, 0]
    dslot_scr_dG = [all_scratch_dG, all_scratch_dG]
    x2f_cons = [0, 0]
    x2f_cons_v = [0, 0]
    x2r_free = [0, 0]
    ln2b_free_t = 0
    psM_free_v = 0
    psD_free_v = 0
    gu_prev_silu = {0: 0, 1: 0}
    up_prev_ev = {0: 0, 1: 0}
    prev_up_ev_all = {i: 0 for i in range(8)}

    def fetch_piece(dst_sb, src_f32, src_bf, sub, length, slot_free_t,
                    scr_done_dG, src_off=0):
        nonlocal wstg_free_v
        if sub == 0:
            nh = (length + 1535) // 1536
            for hh in range(nh):
                a = hh * 1536
                b = min(length, a + 1536)
                p.wait("sync", sV, wstg_free_v)
                p.op("sync", lambda e, s=src_f32, a=a, b=b, o=src_off:
                     e.dma_start(out=wstgC[:, 0:b - a],
                                 in_=s[:, o + a:o + b]), dW, 16)
                ld = dW.n
                p.wait("vector", dW, ld)
                if hh == 0:
                    p.wait("vector", sT, slot_free_t)
                    p.wait("vector", dG, scr_done_dG)
                p.op("vector", lambda e, d=dst_sb, a=a, b=b: e.tensor_copy(
                    d[:, a:b], wstgC[:, 0:b - a]), sV, 1)
                wstg_free_v = sV.n
            p.wait("gpsimd", sV, wstg_free_v)
            p.op("gpsimd", lambda e, d=dst_sb, s=src_bf, L=length,
                 o=src_off: e.dma_start(out=s[:, o:o + L], in_=d[:, 0:L]),
                 dG, 16)
            return sV, wstg_free_v, dG.n
        p.wait("sync", sT, slot_free_t)
        p.wait("sync", dG, scr_done_dG)
        p.op("sync", lambda e, d=dst_sb, s=src_bf, L=length, o=src_off:
             e.dma_start(out=d[:, 0:L], in_=s[:, o:o + L]), dW, 16)
        return dW, dW.n, scr_done_dG

    p.op("sync", lambda e: e.dma_start(out=sel8[:], in_=sel_in[:]), dW, 16)
    p.wait("tensor", dW, dW.n)
    p.op("vector", lambda e: e.tensor_copy(r_hi[:], r_sb[:]))
    vdrain()
    p.op("vector", lambda e: e.tensor_sub(r_lo[:], r_sb[:], r_hi[:]), sV, 1)
    p.wait("tensor", sV, sV.n)

    def d3(h_):
        return bass.AP(h_, 0, [[64, 128], [8, 8], [1, 8]])

    def segb(h_):
        return bass.AP(h_, 0, [[8, 128], [1, 8], [0, 8]])

    def tileb(h_):
        return bass.AP(h_, 0, [[8, 128], [0, 8], [1, 8]])


    chunk_last_t = 0
    stats_read_v = 0
    ln2b_sub_v = 0
    coef_cast_v = 0
    coef_ready_v = 0
    psm_drain_t = 0
    pend_g = None
    pend_u = None

    def emit_stats_part1(sub, h):
        nonlocal chunk_last_t, stats_read_v
        par = h % 2
        t0 = sub * SUB + h * CC
        p.wait("gpsimd", sT, x2f_cons[par])
        p.wait("gpsimd", sV, x2f_cons_v[par])
        for fc in range(FC):
            p.op("gpsimd", lambda e, fc=fc, t0=t0, par=par: e.dma_start(
                out=x2f[par][:, fc * CC:(fc + 1) * CC],
                in_=x2_hbm[fc * 128:(fc + 1) * 128, t0:t0 + CC]), dG, 16)
        x2_ld = dG.n
        p.wait("vector", dG, x2_ld)
        p.wait("vector", sT, chunk_last_t)
        p.op("vector", lambda e, par=par: e.tensor_copy(xb2[:],
                                                        x2f[par][:]))
        p.op("vector", lambda e, par=par: e.tensor_sub(
            x2lo[:], x2f[par][:], xb2[:]))
        p.op("vector", lambda e: e.tensor_mul(xsq2[:], xb2[:], xb2[:]),
             sV, 1)
        cast2_v = sV.n
        p.wait("tensor", sV, cast2_v)
        p.wait("tensor", sV, stats_read_v)
        p.wait("tensor", sA, max(gu_prev_silu.values()))
        for kc in range(FC):
            p.op("tensor", lambda e, kc=kc: e.matmul(
                psG[0][:, 0:CC], ones_b[:], xb2[:, kc * CC:(kc + 1) * CC],
                start=(kc == 0), stop=(kc == FC - 1)))
        for kc in range(FC):
            p.op("tensor", lambda e, kc=kc: e.matmul(
                psG[1][:, 0:CC], ones_b[:], xsq2[:, kc * CC:(kc + 1) * CC],
                start=(kc == 0), stop=(kc == FC - 1)),
                sT if kc == FC - 1 else None, 1, fast=True)
        stats_mm_done = sT.n
        p.wait("vector", sT, stats_mm_done)
        p.op("vector", lambda e: e.tensor_scalar(
            st1C[:], psG[0][:, 0:CC], 1.0 / D, None, ALU.mult))
        p.op("vector", lambda e: e.tensor_scalar(
            st2C[:], psG[1][:, 0:CC], 1.0 / D, None, ALU.mult))
        p.op("vector", lambda e: e.scalar_tensor_tensor(
            gaC[:], st1C[:], -1.0, st1C[:], ALU.mult, ALU.mult))
        p.op("vector", lambda e: e.scalar_tensor_tensor(
            gbC[:], st2C[:], 1e-5, gaC[:], ALU.add, ALU.add), sV, 1)
        stats_read_v = sV.n
        p.wait("scalar", sV, stats_read_v)
        p.op("scalar", lambda e: e.activation(gaC[:], gbC[:], AF.Ln),
             sA, 1)
        p.wait("vector", sA, sA.n)
        p.op("vector", lambda e: e.tensor_scalar(
            gbC[:], gaC[:], -0.5, None, ALU.mult), sV, 1)
        p.wait("scalar", sV, sV.n)
        p.op("scalar", lambda e: e.activation(rrepC[:], gbC[:], AF.Exp),
             sA, 1)
        p.wait("vector", sA, sA.n)
        p.op("vector", lambda e: e.scalar_tensor_tensor(
            nmrC[:], st1C[:], -1.0, rrepC[:], ALU.mult, ALU.mult))

    def emit_stats_part2(sub, h):
        nonlocal chunk_last_t, ln2b_sub_v
        par = h % 2
        p.wait("vector", sT, ln2b_free_t if (sub > 0 and h == 0) else 0)
        for fc in range(FC):
            p.op("vector", lambda e, fc=fc, par=par: e.tensor_mul(
                gaC[:], x2f[par][:, fc * CC:(fc + 1) * CC], rrepC[:]))
            p.op("vector", lambda e, fc=fc, h=h: e.tensor_add(
                ln2b[:, fc * SUB + h * CC:fc * SUB + (h + 1) * CC],
                gaC[:], nmrC[:]),
                sV if fc == FC - 1 else None, 1)
        ln2b_chunk_v = sV.n
        ln2b_sub_v = ln2b_chunk_v

        for tt in range(CC // 128):
            ti = h * 2 + tt
            tsl = slice(tt * 128, (tt + 1) * 128)
            if ti == 0:
                p.wait("tensor", sV, psM_free_v)
            for kc in range(FC):
                p.op("tensor", lambda e, kc=kc, tsl=tsl, ti=ti: e.matmul(
                    psM[:, 64 + ti:65 + ti],
                    xb2[:, kc * CC:(kc + 1) * CC][:, tsl],
                    ones_b[:, 0:1],
                    start=(kc == 0), stop=(kc == FC - 1)))
            for kc in range(FC):
                p.op("tensor", lambda e, kc=kc, tsl=tsl, ti=ti: e.matmul(
                    psM[:, 72 + ti:73 + ti],
                    xsq2[:, kc * CC:(kc + 1) * CC][:, tsl],
                    ones_b[:, 0:1],
                    start=(kc == 0), stop=(kc == FC - 1)))
            for pas, xs, rp in ((0, xb2, r_hi), (1, xb2, r_lo),
                                (2, x2lo, r_hi)):
                for kc in range(FC):
                    p.op("tensor", lambda e, kc=kc, tsl=tsl, xs=xs,
                         rp=rp, pas=pas, ti=ti: e.matmul(
                             psM[:, ti * 8:(ti + 1) * 8],
                             xs[:, kc * CC:(kc + 1) * CC][:, tsl],
                             rp[:, kc * 8:(kc + 1) * 8],
                             start=(kc == 0 and pas == 0),
                             stop=(kc == FC - 1 and pas == 2)),
                         sT if (kc == FC - 1 and pas == 2) else None, 1,
                         fast=True)
        x2f_cons[par] = sT.n
        x2f_cons_v[par] = ln2b_chunk_v
        chunk_last_t = sT.n

    def emit_router_chain(sub):
        nonlocal psm_drain_t, psM_free_v, coef_cast_v
        p.op("tensor", lambda e: e.drain(fusable=False), sT, 1, fast=True)
        psm_drain_t = sT.n
        p.wait("vector", sT, psm_drain_t)
        p.op("vector", lambda e: e.tensor_scalar(
            c8m[:], psM[:, 64:72], 1.0 / D, None, ALU.mult))
        p.op("vector", lambda e: e.tensor_scalar(
            c8q[:], psM[:, 72:80], 1.0 / D, None, ALU.mult))
        vdrain()
        p.op("vector", lambda e: e.scalar_tensor_tensor(
            c8a[:], c8m[:], -1.0, c8m[:], ALU.mult, ALU.mult))
        vdrain()
        p.op("vector", lambda e: e.scalar_tensor_tensor(
            c8b[:], c8q[:], 1e-5, c8a[:], ALU.add, ALU.add), sV, 1)
        p.wait("scalar", sV, sV.n)
        p.op("scalar", lambda e: e.activation(c8a[:], c8b[:], AF.Ln), sA, 1)
        p.wait("vector", sA, sA.n)
        p.op("vector", lambda e: e.tensor_scalar(
            c8b[:], c8a[:], -0.5, None, ALU.mult), sV, 1)
        p.wait("scalar", sV, sV.n)
        p.op("scalar", lambda e: e.activation(rstd8[:], c8b[:], AF.Exp),
             sA, 1)
        p.wait("vector", sA, sA.n)
        p.op("vector", lambda e: e.scalar_tensor_tensor(
            nmr8[:], c8m[:], -1.0, rstd8[:], ALU.mult, ALU.mult))
        p.op("vector", lambda e: e.tensor_copy(r64a[:], psM[:, 0:64]), sV, 1)
        psM_free_v = sV.n
        vdrain()
        p.op("vector", lambda e: e.tensor_tensor(
            d3(r64a), d3(r64a), segb(rstd8), ALU.mult))
        p.op("vector", lambda e: e.tensor_tensor(
            d3(r64b), tileb(sr_rep), segb(nmr8), ALU.mult))
        vdrain()
        p.op("vector", lambda e: e.tensor_add(r64a[:], r64a[:], r64b[:]))
        vdrain()
        p.op("vector", lambda e: e.reduce_max(c8a[:], d3(r64a), axis=AX))
        vdrain()
        p.op("vector", lambda e: e.tensor_tensor(
            d3(r64b), d3(r64a), segb(c8a), ALU.subtract), sV, 1)
        p.wait("scalar", sV, sV.n)
        p.op("scalar", lambda e: e.activation(r64a[:], r64b[:], AF.Exp),
             sA, 1)
        p.wait("vector", sA, sA.n)
        p.op("vector", lambda e: e.reduce_sum(c8b[:], d3(r64a), axis=AX))
        vdrain()
        p.op("vector", lambda e: e.reciprocal(c8c[:], c8b[:]))
        vdrain()
        p.op("vector", lambda e: e.tensor_tensor(
            d3(r64b), d3(r64a), segb(c8c), ALU.mult))
        vdrain()
        p.op("vector", lambda e: e.reduce_max(c8a[:], d3(r64b), axis=AX))
        vdrain()
        p.op("vector", lambda e: e.tensor_tensor(
            d3(r64a), d3(r64b), segb(c8a), ALU.is_equal))
        vdrain()
        p.op("vector", lambda e: e.scalar_tensor_tensor(
            r64a[:], r64a[:], -2.0, r64b[:], ALU.mult, ALU.add))
        vdrain()
        p.op("vector", lambda e: e.reduce_max(c8b[:], d3(r64a), axis=AX))
        vdrain()
        p.op("vector", lambda e: e.tensor_tensor(
            d3(r64a), d3(r64b), segb(c8b), ALU.is_ge))
        vdrain()
        p.op("vector", lambda e: e.tensor_mul(r64a[:], r64a[:], r64b[:]))
        vdrain()
        p.op("vector", lambda e: e.tensor_copy(coefb64[:], r64a[:]), sV, 1)
        coef_cast_v = sV.n

    def emit_coef_transpose(sub):
        nonlocal coef_ready_v
        p.wait("tensor", sV, coef_cast_v)
        for ti in range(8):
            p.op("tensor", lambda e, ti=ti: e.transpose(
                psTr[:], coefb64[:, ti * 8:(ti + 1) * 8], eyeb[:]), sT, 1)
            p.wait("vector", sT, sT.n)
            p.op("vector", lambda e, ti=ti: e.tensor_copy(
                coefT[:, ti * 128:(ti + 1) * 128], psTr[:]), sV, 1)
            if ti < 7:
                p.wait("tensor", sV, sV.n)
        coef_ready_v = sV.n
        if dbg:
            p.wait("gpsimd", sV, coef_ready_v)
            p.op("gpsimd", lambda e, sub=sub: e.dma_start(
                out=dbg_coef[:, sub * SUB:(sub + 1) * SUB], in_=coefT[:]),
                dG, 16)
            if sub == 0:
                p.op("gpsimd", lambda e: e.dma_start(
                    out=dbg_ln2b[:], in_=ln2b[:]), dG, 16)

    for sub in range(NSUB):
        if sub == 0:
            for h in range(NCC):
                emit_stats_part1(0, h)
                emit_stats_part2(0, h)
            emit_router_chain(0)
            emit_coef_transpose(0)


        for pe in range(NE):
            gslot = 0
            uslot = 1
            if pe == 0 and pend_g is not None:
                gsem, gval = pend_g
                usem, uval = pend_u
                pend_g = pend_u = None
            else:
                gsem, gval, slot_scr_dG[gslot] = fetch_piece(
                    wslot[gslot], gup[pe, 0], gup_sc[pe, 0], 1, FC * DE,
                    slot_free_t[gslot], slot_scr_dG[gslot])
                usem, uval, slot_scr_dG[uslot] = fetch_piece(
                    wslot[uslot], gup[pe, 1], gup_sc[pe, 1], 1, FC * DE,
                    slot_free_t[uslot], slot_scr_dG[uslot])
            p.wait("tensor", gsem, gval)
            if pe == 0:
                p.wait("tensor", sV, ln2b_sub_v)
            silu_a = {}
            for dcq in range(8):
                qc, dc = dcq // 4, dcq % 4
                par = dcq % 2
                p.wait("tensor", sA,
                       silu_a[dcq - 2] if dcq >= 2 else gu_prev_silu[par])
                for kc in range(FC):
                    p.op("tensor", lambda e, kc=kc, dc=dc, qc=qc, par=par,
                         gslot=gslot: e.matmul(
                             psG[par][:],
                             wslot[gslot][:, kc * DE + dc * 128:
                                          kc * DE + (dc + 1) * 128],
                             ln2b[:, kc * SUB + qc * 512:
                                  kc * SUB + (qc + 1) * 512],
                             start=(kc == 0), stop=(kc == FC - 1)),
                         sT if kc == FC - 1 else None, 1, fast=True)
                g_t = sT.n
                p.wait("scalar", sT, g_t)
                p.wait("scalar", sV, prev_up_ev_all[dcq])
                p.op("scalar", lambda e, dcq=dcq, par=par: e.activation(
                    gsall[:, dcq * 512:(dcq + 1) * 512], psG[par][:],
                    AF.Silu), sA, 1)
                silu_a[dcq] = sA.n
            slot_free_t[gslot] = sT.n
            gu_prev_silu = {0: silu_a[6], 1: silu_a[7]}

            p.wait("tensor", usem, uval)
            up_ev = {}
            for dcq in range(8):
                qc, dc = dcq // 4, dcq % 4
                par = dcq % 2
                if pe < 8 and dc == 0:
                    p.wait("tensor", sV, psM_free_v)
                    p.wait("tensor", sV, coef_ready_v)
                    p.op("tensor", lambda e, pe=pe, qc=qc: e.matmul(
                        psM[:], sel8[:, pe * 128:(pe + 1) * 128],
                        coefT[:, qc * 512:(qc + 1) * 512],
                        start=True, stop=True))
                p.wait("tensor", sV,
                       up_ev[dcq - 2] if dcq >= 2 else up_prev_ev[par])
                for kc in range(FC):
                    p.op("tensor", lambda e, kc=kc, dc=dc, qc=qc, par=par,
                         uslot=uslot: e.matmul(
                             psU[par][:],
                             wslot[uslot][:, kc * DE + dc * 128:
                                          kc * DE + (dc + 1) * 128],
                             ln2b[:, kc * SUB + qc * 512:
                                  kc * SUB + (qc + 1) * 512],
                             start=(kc == 0), stop=(kc == FC - 1)),
                         sT if kc == FC - 1 else None, 1, fast=True)
                u_t = sT.n
                p.wait("vector", sT, u_t)
                p.wait("vector", sA, silu_a[dcq])
                hpos = pe * 4 * SUB + dc * SUB + qc * 512
                if pe < 8:
                    p.op("vector", lambda e, dcq=dcq, par=par: e.tensor_mul(
                        t2b[par][:], psU[par][:],
                        gsall[:, dcq * 512:(dcq + 1) * 512]))
                    p.op("vector", lambda e, hpos=hpos, par=par:
                         e.tensor_mul(htil[:, hpos:hpos + 512], t2b[par][:],
                                      psM[:]), sV, 1)
                else:
                    p.op("vector", lambda e, dcq=dcq, par=par, hpos=hpos:
                         e.tensor_mul(htil[:, hpos:hpos + 512], psU[par][:],
                                      gsall[:, dcq * 512:(dcq + 1) * 512]),
                         sV, 1)
                up_ev[dcq] = sV.n
                if pe < 8 and dc == 3:
                    psM_free_v = sV.n
            slot_free_t[uslot] = sT.n
            up_prev_ev = {0: up_ev[6], 1: up_ev[7]}
            prev_up_ev_all = dict(up_ev)
        htil_v = sV.n

        if dbg and sub == 0:
            p.wait("gpsimd", sV, htil_v)
            p.op("gpsimd", lambda e: e.dma_start(
                out=dbg_htil[:], in_=htil[:, 0:2 * SUB * 4]), dG, 16)
        ln2b_free_t = sT.n
        p.wait("tensor", sV, htil_v)
        dev_v = {}
        nxt = sub + 1
        for dc in range(FC):
            dpar = dc % 2
            dsem, dval, dslot_scr_dG[dpar] = fetch_piece(
                dslot[dpar], dwn[dc], dwn_sc[dc], 1, NE * 4 * 128,
                dslot_free_t[dpar], dslot_scr_dG[dpar])
            p.wait("tensor", dsem, dval)
            for qc in range(2):
                idx = dc * 2 + qc
                par = idx % 2
                if nxt < NSUB:
                    if idx in (0, 4, 8, 12):
                        emit_stats_part1(nxt, idx // 4)
                    elif idx in (2, 6, 10):
                        emit_stats_part2(nxt, idx // 4)
                    elif idx == 13:
                        emit_stats_part2(nxt, 3)
                    elif idx == 5:
                        pg0, pg1, slot_scr_dG[0] = fetch_piece(
                            wslot[0], gup[0, 0], gup_sc[0, 0], 1, FC * DE,
                            slot_free_t[0], slot_scr_dG[0])
                        pend_g = (pg0, pg1)
                    elif idx == 11:
                        pu0, pu1, slot_scr_dG[1] = fetch_piece(
                            wslot[1], gup[0, 1], gup_sc[0, 1], 1, FC * DE,
                            slot_free_t[1], slot_scr_dG[1])
                        pend_u = (pu0, pu1)
                    elif idx == 14:
                        emit_router_chain(nxt)
                p.wait("gpsimd", yst, x2r_free[par])
                p.op("gpsimd", lambda e, dc=dc, qc=qc, sub=sub, par=par:
                     e.dma_start(
                         out=x2r[par][:],
                         in_=x2_hbm[dc * 128:(dc + 1) * 128,
                                    sub * SUB + qc * 512:
                                    sub * SUB + (qc + 1) * 512]), dG, 16)
                x2r_ld = dG.n
                if idx >= 2:
                    p.wait("tensor", sV, dev_v[idx - 2])
                for pe in range(NE):
                    for cc in range(4):
                        blk = pe * 4 + cc
                        first = (pe == 0 and cc == 0)
                        last = (pe == NE - 1 and cc == 3)
                        p.op("tensor",
                             lambda e, blk=blk, pe=pe, cc=cc, qc=qc, par=par,
                             dpar=dpar, first=first, last=last: e.matmul(
                                 psD[par][:],
                                 dslot[dpar][:, blk * 128:(blk + 1) * 128],
                                 htil[:, pe * 4 * SUB + cc * SUB + qc * 512:
                                      pe * 4 * SUB + cc * SUB +
                                      (qc + 1) * 512],
                                 start=first, stop=last),
                             sT if last else None, 1, fast=True)
                if qc == 1:
                    dslot_free_t[dpar] = sT.n
                dmm_t = sT.n
                p.wait("vector", sT, dmm_t)
                p.wait("vector", dG, x2r_ld)
                p.op("vector", lambda e, par=par: e.tensor_add(
                    x2r[par][:], psD[par][:], x2r[par][:]), sV, 1)
                dev_v[idx] = sV.n
                p.wait("gpsimd", sV, dev_v[idx])
                p.op("gpsimd", lambda e, dc=dc, qc=qc, sub=sub, par=par:
                     e.dma_start(
                         out=y[dc * 128:(dc + 1) * 128,
                               sub * SUB + qc * 512:
                               sub * SUB + (qc + 1) * 512],
                         in_=x2r[par][:]), yst, 16)
                x2r_free[par] = yst.n
        if nxt < NSUB:
            emit_coef_transpose(nxt)

    p.wait("gpsimd", yst, yst.n)
    p.wait("gpsimd", dG, dG.n)

    with nc.Block() as blk:
        p.flush(blk)
    esC.close()
    es.close()
    return nc


_NC_CACHE = {}


def _get_nc():
    if "nc" not in _NC_CACHE:
        _NC_CACHE["nc"] = build_nc()
    return _NC_CACHE["nc"]


def _mk_sel():
    s = np.zeros((8, 8 * 128), np.float32)
    for e in range(8):
        s[e, e * 128:(e + 1) * 128] = 1.0
    return s.astype(ml_dtypes.bfloat16)


def _prep_inputs(inputs):
    bf = ml_dtypes.bfloat16
    x = np.asarray(inputs["L0_attn_x"])

    def tile_gu(w):
        return np.ascontiguousarray(
            w.reshape(8, 128, DE).transpose(1, 0, 2).reshape(128, 8 * DE))

    eg = np.asarray(inputs["L0_ffn_experts_gate"])
    eu = np.asarray(inputs["L0_ffn_experts_up"])
    ed = np.asarray(inputs["L0_ffn_experts_down"])
    sg = np.asarray(inputs["L0_ffn_shared_gate"])
    su = np.asarray(inputs["L0_ffn_shared_up"])
    sd = np.asarray(inputs["L0_ffn_shared_down"])
    gup = np.zeros((NE, 2, 128, FC * DE), np.float32)
    for e in range(8):
        gup[e, 0] = tile_gu(eg[e])
        gup[e, 1] = tile_gu(eu[e])
    gup[8, 0] = tile_gu(sg[:, 0:DE])
    gup[8, 1] = tile_gu(su[:, 0:DE])
    gup[9, 0] = tile_gu(sg[:, DE:2 * DE])
    gup[9, 1] = tile_gu(su[:, DE:2 * DE])

    downs = [ed[e] for e in range(8)] + [sd[0:DE, :], sd[DE:2 * DE, :]]
    dwn = np.zeros((FC, 128, NE * 4 * 128), np.float32)
    for dc in range(FC):
        blocks = []
        for pe in range(NE):
            w = downs[pe]
            for cc in range(4):
                blocks.append(w[cc * 128:(cc + 1) * 128,
                                dc * 128:(dc + 1) * 128])
        dwn[dc] = np.concatenate(blocks, axis=1)

    common = dict(
        wqT=np.ascontiguousarray(np.asarray(inputs["L0_attn_Wq"]).T),
        wkT=np.ascontiguousarray(np.asarray(inputs["L0_attn_Wk"]).T),
        wvT=np.ascontiguousarray(np.asarray(inputs["L0_attn_Wv"]).T),
        woT=np.ascontiguousarray(np.asarray(inputs["L0_attn_Wo"]).T),
        wavel=np.ascontiguousarray(np.asarray(inputs["L0_attn_wavelengths"])),
        pbias=np.ascontiguousarray(np.asarray(inputs["L0_attn_phase_bias"])),
        bq=np.ascontiguousarray(np.asarray(inputs["L0_attn_bq"])),
        bk=np.ascontiguousarray(np.asarray(inputs["L0_attn_bk"])),
        bv=np.ascontiguousarray(np.asarray(inputs["L0_attn_bv"])),
        bo=np.ascontiguousarray(np.asarray(inputs["L0_attn_bo"])),
        router=np.ascontiguousarray(np.asarray(inputs["L0_ffn_router"])),
        gup=gup, dwn=dwn,
        eye_in=np.eye(128, dtype=np.float32).astype(bf),
        sel_in=_mk_sel(),
    )
    per_core = []
    for b in range(8):
        m = dict(common)
        m["x_t"] = np.ascontiguousarray(np.asarray(x[b]).T)
        per_core.append(m)
    return per_core


def kernel(**inputs):
    nc = _get_nc()
    in_maps = _prep_inputs(inputs)
    res = run_bass_kernel_spmd(nc, in_maps, core_ids=list(range(8)))
    out = np.empty((8, S, D), np.float32)
    for b in range(8):
        out[b] = np.asarray(res.results[b]["y"]).T
    return out


# version 10
# speedup vs baseline: 1.2858x; 1.0163x over previous
import contextlib
import math

import numpy as np
import ml_dtypes

import concourse.bass as bass
import concourse.mybir as mybir
from concourse.bass_utils import run_bass_kernel_spmd

F32 = mybir.dt.float32
F16 = mybir.dt.float16
BF16 = mybir.dt.bfloat16
I32 = mybir.dt.int32
ALU = mybir.AluOpType
AF = mybir.ActivationFunctionType
AX = mybir.AxisListType.X

D = 1024
S = 4096
H = 128
FC = D // 128
NE = 10
DE = 512
NSUB = 4
SUB = S // NSUB

TWO_PI = float(2.0 * math.pi)
INV_SQRT_H = 1.0 / math.sqrt(H)
EXP_SHIFT = -float(math.sqrt(H))
ROUND_OFS = 128.0


class Sem:
    def __init__(self, h, is_dma=False):
        self.h = h
        self.is_dma = is_dma
        self.n = 0

    def bump(self, k):
        self.n += k
        return self.n


class Prog:
    ENGS = ("sync", "vector", "scalar", "tensor", "gpsimd")

    def __init__(self):
        self.q = {e: [] for e in self.ENGS}

    def op(self, eng, build, sem=None, k=1, fast=False):
        if sem is not None:
            v = sem.bump(k)
            if sem.is_dma or fast:
                self.q[eng].append(
                    lambda e, b=build, s=sem, kk=k: b(e).then_inc(s.h, kk))
            else:
                def _f(e, b=build, s=sem, kk=k):
                    b(e)
                    e.drain(fusable=False).then_inc(s.h, kk)
                self.q[eng].append(_f)
            return v
        self.q[eng].append(build)
        return None

    def wait(self, eng, sem, val):
        if val:
            self.q[eng].append(lambda e, s=sem, v=val: e.wait_ge(s.h, v))

    def flush(self, block):
        for eng in self.ENGS:
            ops = self.q[eng]
            if not ops:
                continue

            def body(engine, ops=ops):
                for f in ops:
                    f(engine)

            getattr(block, eng)(body)
        self.q = {e: [] for e in self.ENGS}


def build_nc(dbg=False):
    nc = bass.Bass("TRN2", target_bir_lowering=False,
                   detect_race_conditions=False)

    x_t = nc.dram_tensor("x_t", [D, S], F32, kind="ExternalInput")
    wqT = nc.dram_tensor("wqT", [D, H], F32, kind="ExternalInput")
    wkT = nc.dram_tensor("wkT", [D, H], F32, kind="ExternalInput")
    wvT = nc.dram_tensor("wvT", [D, H], F32, kind="ExternalInput")
    woT = nc.dram_tensor("woT", [H, D], F32, kind="ExternalInput")
    wavel = nc.dram_tensor("wavel", [H], F32, kind="ExternalInput")
    pbias = nc.dram_tensor("pbias", [H], F32, kind="ExternalInput")
    bq_d = nc.dram_tensor("bq", [H], F32, kind="ExternalInput")
    bk_d = nc.dram_tensor("bk", [H], F32, kind="ExternalInput")
    bv_d = nc.dram_tensor("bv", [H], F32, kind="ExternalInput")
    bo_d = nc.dram_tensor("bo", [D], F32, kind="ExternalInput")
    router = nc.dram_tensor("router", [D, 8], F32, kind="ExternalInput")
    gup = nc.dram_tensor("gup", [NE, 2, 128, FC * DE], F32,
                         kind="ExternalInput")
    dwn = nc.dram_tensor("dwn", [FC, 128, NE * 4 * 128], F32,
                         kind="ExternalInput")
    eye_in = nc.dram_tensor("eye_in", [128, 128], BF16, kind="ExternalInput")
    sel_in = nc.dram_tensor("sel_in", [8, 8 * 128], BF16, kind="ExternalInput")

    y = nc.dram_tensor("y", [D, S], F32, kind="ExternalOutput")

    x2_hbm = nc.dram_tensor("x2_hbm", [D, S], F32,
                            kind="ExternalOutput" if dbg else "Internal")
    if dbg:
        dbg_t = {n: nc.dram_tensor("dbg_" + n, [128, S], F16,
                                   kind="ExternalOutput")
                 for n in ("cosq", "sinq", "cosk", "sink")}
        dbg_t["vf"] = nc.dram_tensor("dbg_vf", [128, S], BF16,
                                     kind="ExternalOutput")
        dbg_us = {n: nc.dram_tensor("dbg_" + n, [128, S], F16,
                                    kind="ExternalOutput")
                  for n in ("usq", "ucq")}
        dbg_coef = nc.dram_tensor("dbg_coef", [8, S], BF16,
                                  kind="ExternalOutput")
        dbg_htil = nc.dram_tensor("dbg_htil", [128, 2 * SUB * 4], BF16,
                                  kind="ExternalOutput")
        dbg_ln2b = nc.dram_tensor("dbg_ln2b", [128, FC * SUB], BF16,
                                  kind="ExternalOutput")
        dbg_rt = nc.dram_tensor("dbg_rt", [128, 64], F32,
                                kind="ExternalOutput")
        dbg_rt2 = nc.dram_tensor("dbg_rt2", [128, 40 * 8], F32,
                                 kind="ExternalOutput")
    gup_sc = nc.dram_tensor("gup_sc", [NE, 2, 128, FC * DE], BF16)
    dwn_sc = nc.dram_tensor("dwn_sc", [FC, 128, NE * 4 * 128], BF16)

    es = contextlib.ExitStack()

    def SB(name, shape, dt):
        return es.enter_context(nc.sbuf_tensor(name, shape, dt))

    def SEM(name, is_dma=False):
        return Sem(es.enter_context(nc.semaphore(name)), is_dma)

    eyeb = SB("eyeb", [128, 128], BF16)
    ones_b = SB("ones_b", [128, 128], BF16)
    ones_f = SB("ones_f", [128, 128], F32)
    c2pi = SB("c2pi", [128, 1], F32)
    ceps = SB("ceps", [128, 1], F32)
    cnhalf = SB("cnhalf", [128, 1], F32)
    cnone = SB("cnone", [128, 1], F32)
    cinvsq = SB("cinvsq", [128, 1], F32)
    cshift = SB("cshift", [128, 1], F32)
    invw = SB("invw", [128, 1], F32)
    s1c = SB("s1c", [128, 1], F32)
    s2q = SB("s2q", [128, 1], F32)
    s2k = SB("s2k", [128, 1], F32)
    sqc = SB("sqc", [128, 1], F32)
    skc = SB("skc", [128, 1], F32)
    svc = SB("svc", [128, 1], F32)
    bvb = SB("bvb", [128, 1], BF16)
    bqf = SB("bqf", [128, 1], F32)
    bkf = SB("bkf", [128, 1], F32)
    bvf = SB("bvf", [128, 1], F32)
    wlf = SB("wlf", [128, 1], F32)
    pbf = SB("pbf", [128, 1], F32)
    bo8 = SB("bo8", [128, FC], F32)
    boP = SB("boP", [128, FC], F32)
    tcol = SB("tcol", [128, 1], F32)
    wqTb = SB("wqTb", [128, FC * 128], BF16)
    wkTb = SB("wkTb", [128, FC * 128], BF16)
    wvTb = SB("wvTb", [128, FC * 128], BF16)
    woTb = SB("woTb", [128, FC * 128], BF16)
    r_sb = SB("r_sb", [128, FC * 8], F32)
    r_hi = SB("r_hi", [128, FC * 8], BF16)
    r_lo = SB("r_lo", [128, FC * 8], BF16)
    sr_rep = SB("sr_rep", [128, 8], F32)

    dW = SEM("dW", is_dma=True)
    dG = SEM("dG", is_dma=True)
    yst = SEM("yst", is_dma=True)
    sT = SEM("sT")
    sV = SEM("sV")
    sA = SEM("sA")

    p = Prog()

    def vdrain():
        p.op("vector", lambda e: e.drain(fusable=False))

    esAB = contextlib.ExitStack()

    def SBab(name, shape, dt):
        return esAB.enter_context(nc.sbuf_tensor(name, shape, dt))

    cosq = SBab("cosq", [128, S], F16)
    sinq = SBab("sinq", [128, S], F16)
    cosk = SBab("cosk", [128, S], F16)
    sink = SBab("sink", [128, S], F16)
    vf = SBab("vf", [128, S], BF16)

    esA = contextlib.ExitStack()

    def SBa(name, shape, dt):
        return esA.enter_context(nc.sbuf_tensor(name, shape, dt))

    def PSa(name, shape, dt=F32):
        return esA.enter_context(nc.psum_tensor(name, shape, dt))

    CH = 512
    NCH = S // CH
    xfA = [SBa(f"xfA{i}", [128, FC * CH], F32) for i in range(2)]
    xbA = [SBa(f"xbA{i}", [128, FC * CH], BF16) for i in range(2)]
    xsqA = [SBa(f"xsqA{i}", [128, FC * CH], BF16) for i in range(2)]
    stat1 = SBa("stat1", [128, CH], F32)
    stat2 = SBa("stat2", [128, CH], F32)
    rrep = SBa("rrep", [128, CH], F32)
    nmr = SBa("nmr", [128, CH], F32)
    ga = SBa("ga", [128, CH], F32)
    gb = SBa("gb", [128, CH], F32)
    gc = SBa("gc", [128, CH], F32)
    gi = SBa("gi", [128, CH], I32)
    usq = SBa("usq", [128, S], F16)
    ucq = SBa("ucq", [128, S], F16)
    usk = SBa("usk", [128, S], F16)
    uck = SBa("uck", [128, S], F16)
    wstage = SBa("wstage", [128, FC * 128], F32)

    psMean = PSa("psMean", [128, CH])
    psSq = PSa("psSq", [128, CH])
    psQ = [PSa(f"psQ{i}", [128, CH]) for i in range(2)]
    psK = [PSa(f"psK{i}", [128, CH]) for i in range(2)]
    psV = [PSa(f"psV{i}", [128, CH]) for i in range(2)]

    p.op("sync", lambda e: e.dma_start(out=eyeb[:], in_=eye_in[:]), dW, 16)
    for col, src in ((wlf, wavel), (pbf, pbias), (bqf, bq_d), (bkf, bk_d),
                     (bvf, bv_d)):
        p.op("sync", lambda e, c=col, s=src: e.dma_start(
            out=c[:], in_=bass.AP(s, 0, [[1, 128], [1, 1]])), dW, 16)
    for dc in range(FC):
        p.op("sync", lambda e, dc=dc: e.dma_start(
            out=bo8[:, dc:dc + 1],
            in_=bass.AP(bo_d, dc * 128, [[1, 128], [1, 1]])), dW, 16)
    for kc in range(FC):
        p.op("sync", lambda e, kc=kc: e.dma_start(
            out=r_sb[:, kc * 8:(kc + 1) * 8],
            in_=router[kc * 128:(kc + 1) * 128, :]), dW, 16)
    small_loads = dW.n

    p.op("vector", lambda e: e.memset(ones_b[:], 1.0))
    p.op("vector", lambda e: e.memset(ones_f[:], 1.0))
    p.op("vector", lambda e: e.memset(c2pi[:], TWO_PI))
    p.op("vector", lambda e: e.memset(ceps[:], 1e-5))
    p.op("vector", lambda e: e.memset(cnhalf[:], -0.5))
    p.op("vector", lambda e: e.memset(cnone[:], -1.0))
    p.op("vector", lambda e: e.memset(cinvsq[:], INV_SQRT_H))
    p.op("vector", lambda e: e.memset(cshift[:], EXP_SHIFT))
    p.wait("vector", dW, small_loads)
    p.op("vector", lambda e: e.tensor_scalar(tcol[:], wlf[:], 1e-8, None,
                                             ALU.add))
    vdrain()
    p.op("vector", lambda e: e.reciprocal(invw[:], tcol[:]))
    vdrain()
    p.op("vector", lambda e: e.tensor_scalar(s1c[:], invw[:], 1.0 / TWO_PI,
                                             None, ALU.mult))
    for bcol, s2 in ((bqf, s2q), (bkf, s2k)):
        p.op("vector", lambda e, b=bcol, s=s2: e.tensor_mul(s[:], b[:],
                                                            invw[:]))
        vdrain()
        p.op("vector", lambda e, s=s2: e.tensor_add(s[:], s[:], pbf[:]))
        vdrain()
        p.op("vector", lambda e, s=s2: e.tensor_scalar(
            s[:], s[:], 1.0 / TWO_PI, ROUND_OFS, ALU.mult, ALU.add))
    p.op("vector", lambda e: e.tensor_copy(bvb[:], bvf[:]), sV, 1)

    for wi, (wsrc, wdst) in enumerate(((wqT, wqTb), (wkT, wkTb), (wvT, wvTb),
                                       (woT, woTb))):
        p.wait("sync", sV, sV.n)
        for kc in range(FC):
            if wi < 3:
                src = wsrc[kc * 128:(kc + 1) * 128, :]
            else:
                src = wsrc[:, kc * 128:(kc + 1) * 128]
            p.op("sync", lambda e, s=src, kc=kc: e.dma_start(
                out=wstage[:, kc * 128:(kc + 1) * 128], in_=s), dW, 16)
        p.wait("vector", dW, dW.n)
        p.op("vector", lambda e, d=wdst: e.tensor_copy(d[:], wstage[:]), sV, 1)
    wcast_v = sV.n

    p.wait("tensor", sV, wcast_v)
    for wsb, col in ((wqTb, sqc), (wkTb, skc), (wvTb, svc)):
        for kc in range(FC):
            p.op("tensor", lambda e, w=wsb, kc=kc: e.matmul(
                psMean[:, 0:1], w[:, kc * 128:(kc + 1) * 128], ones_b[:, 0:1],
                start=(kc == 0), stop=(kc == FC - 1)),
                sT if kc == FC - 1 else None, 1)
        p.wait("vector", sT, sT.n)
        p.op("vector", lambda e, c=col: e.tensor_copy(c[:], psMean[:, 0:1]),
             sV, 1)
        p.wait("tensor", sV, sV.n)
    for dc in range(FC):
        p.op("tensor", lambda e, dc=dc: e.matmul(
            psSq[:, dc:dc + 1], woTb[:, dc * 128:(dc + 1) * 128], bvb[:, 0:1],
            start=True, stop=True), sT if dc == FC - 1 else None, 1)
    p.wait("vector", sT, sT.n)
    p.op("vector", lambda e: e.tensor_add(boP[:], psSq[:, 0:FC], bo8[:]),
         sV, 1)
    p.op("vector", lambda e: e.tensor_copy(r_hi[:], r_sb[:]))
    vdrain()
    p.op("vector", lambda e: e.tensor_sub(r_lo[:], r_sb[:], r_hi[:]), sV, 1)
    p.wait("tensor", sV, sV.n)
    for pas, rp in ((0, r_hi), (1, r_lo)):
        for kc in range(FC):
            p.op("tensor", lambda e, kc=kc, rp=rp, pas=pas: e.matmul(
                psMean[:, 0:8], ones_b[:], rp[:, kc * 8:(kc + 1) * 8],
                start=(kc == 0 and pas == 0),
                stop=(kc == FC - 1 and pas == 1)),
                sT if (kc == FC - 1 and pas == 1) else None, 1)
    p.wait("vector", sT, sT.n)
    p.op("vector", lambda e: e.tensor_copy(sr_rep[:], psMean[:, 0:8]), sV, 1)
    prol_v = sV.n

    x_loads = [0] * NCH
    cast_v = [0] * NCH
    cons_v = [0] * NCH
    qkv_t = [0] * NCH
    stats_used_v = 0

    for c in range(NCH):
        par = c % 2
        if c >= 2:
            p.wait("sync", sV, cons_v[c - 2])
        for fc in range(FC):
            p.op("sync", lambda e, c=c, fc=fc, par=par: e.dma_start(
                out=xfA[par][:, fc * CH:(fc + 1) * CH],
                in_=x_t[fc * 128:(fc + 1) * 128, c * CH:(c + 1) * CH]),
                dW, 16)
        x_loads[c] = dW.n

        p.wait("vector", dW, x_loads[c])
        if c >= 2:
            p.wait("vector", sT, qkv_t[c - 2])
        p.op("vector", lambda e, par=par: e.tensor_copy(xbA[par][:],
                                                        xfA[par][:]))
        p.op("vector", lambda e, par=par: e.tensor_mul(
            xsqA[par][:], xbA[par][:], xbA[par][:]), sV, 1)
        cast_v[c] = sV.n

        p.wait("tensor", sV, cast_v[c])
        p.wait("tensor", sV, stats_used_v if c > 0 else prol_v)
        for kc in range(FC):
            p.op("tensor", lambda e, par=par, kc=kc: e.matmul(
                psMean[:], ones_b[:], xbA[par][:, kc * CH:(kc + 1) * CH],
                start=(kc == 0), stop=(kc == FC - 1)))
        for kc in range(FC):
            p.op("tensor", lambda e, par=par, kc=kc: e.matmul(
                psSq[:], ones_b[:], xsqA[par][:, kc * CH:(kc + 1) * CH],
                start=(kc == 0), stop=(kc == FC - 1)),
                sT if kc == FC - 1 else None, 1, fast=True)
        stats_t = sT.n

        if c >= 2:
            p.wait("tensor", sV, cons_v[c - 2])
        for wsb, dst, last in ((wqTb, psQ[par], False), (wkTb, psK[par], False),
                               (wvTb, psV[par], True)):
            for kc in range(FC):
                p.op("tensor", lambda e, w=wsb, d=dst, par=par, kc=kc:
                     e.matmul(
                         d[:], w[:, kc * 128:(kc + 1) * 128],
                         xbA[par][:, kc * CH:(kc + 1) * CH],
                         start=(kc == 0), stop=(kc == FC - 1)),
                     sT if (kc == FC - 1 and last) else None, 1, fast=True)
        qkv_t[c] = sT.n

        p.wait("vector", sT, stats_t)
        p.op("vector", lambda e: e.tensor_scalar(stat1[:], psMean[:], 1.0 / D,
                                                 None, ALU.mult))
        p.op("vector", lambda e: e.tensor_scalar(stat2[:], psSq[:], 1.0 / D,
                                                 None, ALU.mult))
        p.op("vector", lambda e: e.scalar_tensor_tensor(
            ga[:], stat1[:], -1.0, stat1[:], ALU.mult, ALU.mult))
        p.op("vector", lambda e: e.scalar_tensor_tensor(
            gb[:], stat2[:], 1e-5, ga[:], ALU.add, ALU.add), sV, 1)
        stats_used_v = sV.n
        p.wait("scalar", sV, stats_used_v)
        p.op("scalar", lambda e: e.activation(ga[:], gb[:], AF.Ln), sA, 1)
        p.wait("vector", sA, sA.n)
        p.op("vector", lambda e: e.tensor_scalar(
            gb[:], ga[:], -0.5, None, ALU.mult), sV, 1)
        p.wait("scalar", sV, sV.n)
        p.op("scalar", lambda e: e.activation(rrep[:], gb[:], AF.Exp), sA, 1)
        p.wait("vector", sA, sA.n)
        p.op("vector", lambda e: e.scalar_tensor_tensor(
            nmr[:], stat1[:], -1.0, rrep[:], ALU.mult, ALU.mult))

        p.wait("vector", sT, qkv_t[c])
        sl = slice(c * CH, (c + 1) * CH)
        for psX, s2x, sxc, us_f, uc_f in ((psQ[par], s2q, sqc, usq, ucq),
                                          (psK[par], s2k, skc, usk, uck)):
            p.op("vector", lambda e, psX=psX: e.tensor_mul(ga[:], psX[:],
                                                           rrep[:]))
            p.op("vector", lambda e, sxc=sxc: e.scalar_tensor_tensor(
                gb[:], nmr[:], sxc[:], ga[:], ALU.mult, ALU.add))
            p.op("vector", lambda e, s2x=s2x: e.tensor_scalar(
                ga[:], gb[:], s1c[:], s2x[:], ALU.mult, ALU.add))
            p.op("vector", lambda e: e.tensor_copy(gi[:], ga[:]))
            p.op("vector", lambda e: e.tensor_copy(gb[:], gi[:]))
            p.op("vector", lambda e: e.tensor_sub(ga[:], ga[:], gb[:]))
            p.op("vector", lambda e: e.tensor_scalar(gb[:], ga[:], 0.5, None,
                                                     ALU.is_gt))
            p.op("vector", lambda e: e.scalar_tensor_tensor(
                gc[:], gb[:], -1.0, ga[:], ALU.mult, ALU.add))
            p.op("vector", lambda e, u=us_f, sl=sl: e.tensor_scalar(
                u[:, sl], gc[:], TWO_PI, None, ALU.mult))
            p.op("vector", lambda e: e.tensor_scalar(
                ga[:], gc[:], 0.25, None, ALU.add))
            p.op("vector", lambda e: e.tensor_scalar(gb[:], ga[:], 0.5, None,
                                                     ALU.is_gt))
            p.op("vector", lambda e: e.scalar_tensor_tensor(
                gc[:], gb[:], -1.0, ga[:], ALU.mult, ALU.add))
            p.op("vector", lambda e, u=uc_f, sl=sl: e.tensor_scalar(
                u[:, sl], gc[:], TWO_PI, None, ALU.mult), sV, 1)
        p.op("vector", lambda e, par=par: e.tensor_mul(ga[:], psV[par][:],
                                                       rrep[:]))
        p.op("vector", lambda e, sl=sl: e.scalar_tensor_tensor(
            vf[:, sl], nmr[:], svc[:], ga[:], ALU.mult, ALU.add), sV, 1)
        cons_v[c] = sV.n

    p.wait("scalar", sV, cons_v[NCH - 1])
    for src, dst in ((usq, sinq), (ucq, cosq), (usk, sink), (uck, cosk)):
        p.op("scalar", lambda e, s=src, d=dst: e.activation(
            d[:], s[:], AF.Sin), sA, 1)
    p.wait("vector", sA, sA.n)
    for t in (cosq, sinq):
        p.op("vector", lambda e, t=t: e.tensor_scalar(
            t[:], t[:], INV_SQRT_H, None, ALU.mult), sV, 1)
    sinscale_v = sV.n
    trig_a = sA.n
    if dbg:
        p.wait("gpsimd", sA, trig_a)
        p.wait("gpsimd", sV, sinscale_v)
        for n, t in (("cosq", cosq), ("sinq", sinq), ("cosk", cosk),
                     ("sink", sink), ("vf", vf)):
            p.op("gpsimd", lambda e, n=n, t=t: e.dma_start(
                out=dbg_t[n][:], in_=t[:]), dG, 16)
        for n, t in (("usq", usq), ("ucq", ucq)):
            p.op("gpsimd", lambda e, n=n, t=t: e.dma_start(
                out=dbg_us[n][:], in_=t[:]), dG, 16)
        p.wait("gpsimd", dG, dG.n)

    with nc.Block() as blk:
        p.flush(blk)
    esA.close()

    esB = contextlib.ExitStack()
    v_tm = esB.enter_context(nc.sbuf_tensor("v_tm", [128, S], BF16))

    esB0 = contextlib.ExitStack()
    psTrV = [esB0.enter_context(nc.psum_tensor(f"psTrV{i}", [128, 128], BF16))
             for i in range(2)]
    tr_cons = [0, 0]
    for kt in range(S // 128):
        par = kt % 2
        if kt >= 2:
            p.wait("tensor", sV, tr_cons[par])
        p.op("tensor", lambda e, kt=kt, par=par: e.transpose(
            psTrV[par][:], vf[:, kt * 128:(kt + 1) * 128], eyeb[:]), sT, 1)
        p.wait("vector", sT, sT.n)
        p.op("vector", lambda e, kt=kt, par=par: e.tensor_copy(
            v_tm[:, kt * 128:(kt + 1) * 128], psTrV[par][:]), sV, 1)
        tr_cons[par] = sV.n
    with nc.Block() as blk:
        p.flush(blk)
    esB0.close()

    esB1 = contextlib.ExitStack()

    def SBb(name, shape, dt):
        return esB1.enter_context(nc.sbuf_tensor(name, shape, dt))

    def PSb(name, shape, dt=F32):
        return esB1.enter_context(nc.psum_tensor(name, shape, dt))

    QC = 512
    NQ = S // QC
    NKT = S // 128
    xfB = [SBb(f"xfB{i}", [128, FC * QC], F32) for i in range(2)]
    wstgB = [SBb(f"wstgB{i}", [128, 5120], F32) for i in range(2)]
    wcastB = [SBb(f"wcastB{i}", [128, 5120], BF16) for i in range(2)]
    ptb = [SBb(f"ptb{i}", [128, 1024], BF16) for i in range(2)]
    attnT = [SBb(f"attnT{i}", [128, QC], BF16) for i in range(2)]
    lnS = [SBb(f"lnS{i}", [128, QC], F32) for i in range(2)]
    rcp = [SBb(f"rcp{i}", [128, QC], F32) for i in range(2)]
    psSc = [PSb(f"psSc{i}", [128, 1024]) for i in range(2)]
    psPV = [PSb(f"psPV{i}", [128, QC]) for i in range(2)]
    psSm = [PSb(f"psSm{i}", [128, QC]) for i in range(2)]

    wpieces = []
    for pe in range(NE):
        for gi_ in range(2):
            wpieces.append((gup[pe, gi_], gup_sc[pe, gi_], FC * DE))
    for dc in range(FC):
        wpieces.append((dwn[dc], dwn_sc[dc], NE * 4 * 128))
    wp_i = [0]
    wcast_free = [0, 0]
    wcast_wr = [0, 0]

    def emit_wcast_round(nmax=4):
        for _ in range(nmax):
            if wp_i[0] >= len(wpieces):
                return
            i = wp_i[0]
            wp_i[0] += 1
            par = i % 2
            srcf, dstb, ln = wpieces[i]
            p.wait("sync", sV, wcast_free[par])
            p.op("sync", lambda e, s=srcf, L=ln, par=par: e.dma_start(
                out=wstgB[par][:, 0:L], in_=s[:, 0:L]), dW, 16)
            ld = dW.n
            p.wait("vector", dW, ld)
            p.wait("vector", dG, wcast_wr[par])
            p.op("vector", lambda e, L=ln, par=par: e.tensor_copy(
                wcastB[par][:, 0:L], wstgB[par][:, 0:L]), sV, 1)
            wcast_free[par] = sV.n
            p.wait("gpsimd", sV, wcast_free[par])
            p.op("gpsimd", lambda e, d=dstb, L=ln, par=par: e.dma_start(
                out=d[:, 0:L], in_=wcastB[par][:, 0:L]), dG, 16)
            wcast_wr[par] = dG.n

    xB_loads = [0] * NQ
    x2_st = [0] * NQ
    wo_ev_all = [0] * NQ
    prev_exp = {0: 0, 1: 0}
    prev_spv = {0: 0, 1: 0}
    attn_overwrite_t = {0: 0, 1: 0}

    for q in range(NQ):
        qpar = q % 2
        qs = slice(q * QC, (q + 1) * QC)
        if q >= 2:
            p.wait("gpsimd", dG, x2_st[q - 2])
        for fc in range(FC):
            p.op("gpsimd", lambda e, q=q, fc=fc, qpar=qpar: e.dma_start(
                out=xfB[qpar][:, fc * QC:(fc + 1) * QC],
                in_=x_t[fc * 128:(fc + 1) * 128, q * QC:(q + 1) * QC]),
                dG, 16)
        xB_loads[q] = dG.n
        emit_wcast_round(4)

        if q >= 1:
            p.wait("tensor", sV, wo_ev_all[q - 1])
        exp_a = [0] * (NKT // 2)
        spv_t = [0] * (NKT // 2)
        for pr in range(NKT // 2):
            par = pr % 2
            p.wait("tensor", sA, exp_a[pr - 2] if pr >= 2 else prev_exp[par])
            for half in range(2):
                kt = pr * 2 + half
                ks = slice(kt * 128, (kt + 1) * 128)
                hs = slice(half * 512, (half + 1) * 512)
                p.op("tensor", lambda e, ks=ks, hs=hs, par=par, qs=qs:
                     e.matmul(psSc[par][:, hs], cosk[:, ks], cosq[:, qs],
                              start=True, stop=False))
                p.op("tensor", lambda e, ks=ks, hs=hs, par=par, qs=qs:
                     e.matmul(psSc[par][:, hs], sink[:, ks], sinq[:, qs],
                              start=False, stop=True),
                     sT if half == 1 else None, 1, fast=True)
            sc_t = sT.n
            p.wait("scalar", sT, sc_t)
            p.wait("scalar", sT, spv_t[pr - 2] if pr >= 2 else prev_spv[par])
            p.op("scalar", lambda e, par=par: e.activation(
                ptb[par][:], psSc[par][:], AF.Exp), sA, 1)
            exp_a[pr] = sA.n
            p.wait("tensor", sA, exp_a[pr])
            for half in range(2):
                kt = pr * 2 + half
                hs = slice(half * 512, (half + 1) * 512)
                p.op("tensor", lambda e, par=par, hs=hs, qpar=qpar, kt=kt:
                     e.matmul(psSm[qpar][:], ones_b[:], ptb[par][:, hs],
                              start=(kt == 0), stop=(kt == NKT - 1)))
                p.op("tensor", lambda e, par=par, hs=hs, qpar=qpar, kt=kt:
                     e.matmul(psPV[qpar][:], v_tm[:, kt * 128:(kt + 1) * 128],
                              ptb[par][:, hs],
                              start=(kt == 0), stop=(kt == NKT - 1)),
                     sT if kt == NKT - 1 else None, 1, fast=True)
            spv_t[pr] = sT.n
        prev_exp = {0: exp_a[NKT // 2 - 2], 1: exp_a[NKT // 2 - 1]}
        prev_spv = {0: spv_t[NKT // 2 - 2], 1: spv_t[NKT // 2 - 1]}
        pv_t = sT.n

        p.wait("scalar", sT, pv_t)
        p.op("scalar", lambda e, qpar=qpar: e.activation(
            lnS[qpar][:], psSm[qpar][:], AF.Ln), sA, 1)
        p.wait("vector", sA, sA.n)
        p.op("vector", lambda e, qpar=qpar: e.tensor_scalar(
            lnS[qpar][:], lnS[qpar][:], -1.0, None, ALU.mult), sV, 1)
        p.wait("scalar", sV, sV.n)
        p.op("scalar", lambda e, qpar=qpar: e.activation(
            rcp[qpar][:], lnS[qpar][:], AF.Exp), sA, 1)
        rcp_a = sA.n
        p.wait("vector", sA, rcp_a)
        p.wait("vector", sT, attn_overwrite_t[qpar])
        p.op("vector", lambda e, qpar=qpar: e.tensor_mul(
            attnT[qpar][:], psPV[qpar][:], rcp[qpar][:]), sV, 1)
        attn_v = sV.n

        p.wait("tensor", sV, attn_v)
        p.wait("vector", dG, xB_loads[q])
        wo_ev = [0] * FC
        for dc in range(FC):
            wo_ps = psPV[1 - qpar] if dc % 2 == 0 else psSm[qpar]
            if dc >= 2:
                p.wait("tensor", sV, wo_ev[dc - 2])
            p.op("tensor", lambda e, dc=dc, ps_=wo_ps, qpar=qpar: e.matmul(
                ps_[:], woTb[:, dc * 128:(dc + 1) * 128], attnT[qpar][:],
                start=True, stop=True), sT, 1, fast=True)
            mm_t = sT.n
            p.wait("vector", sT, mm_t)
            p.op("vector", lambda e, dc=dc, ps_=wo_ps, qpar=qpar:
                 e.scalar_tensor_tensor(
                     xfB[qpar][:, dc * QC:(dc + 1) * QC], ps_[:],
                     boP[:, dc:dc + 1],
                     xfB[qpar][:, dc * QC:(dc + 1) * QC],
                     ALU.add, ALU.add), sV, 1)
            wo_ev[dc] = sV.n
            if dc == FC - 1:
                attn_overwrite_t[qpar] = mm_t
        wo_ev_all[q] = sV.n
        p.wait("gpsimd", sV, wo_ev_all[q])
        for fc in range(FC):
            p.op("gpsimd", lambda e, q=q, fc=fc, qpar=qpar: e.dma_start(
                out=x2_hbm[fc * 128:(fc + 1) * 128, q * QC:(q + 1) * QC],
                in_=xfB[qpar][:, fc * QC:(fc + 1) * QC]), dG, 16)
        x2_st[q] = dG.n

    emit_wcast_round(99)
    all_scratch_dG = dG.n
    with nc.Block() as blk:
        p.flush(blk)
    esB1.close()
    esB.close()
    esAB.close()

    esC = contextlib.ExitStack()

    def SBc(name, shape, dt):
        return esC.enter_context(nc.sbuf_tensor(name, shape, dt))

    def PSc(name, shape, dt=F32):
        return esC.enter_context(nc.psum_tensor(name, shape, dt))

    CC = 256
    NCC = SUB // CC
    htil = SBc("htil", [128, NE * SUB * 4], BF16)
    ln2b = SBc("ln2b", [128, FC * SUB], BF16)
    x2f = [SBc(f"x2f{i}", [128, FC * CC], F32) for i in range(2)]
    xb2 = SBc("xb2", [128, FC * CC], BF16)
    x2lo = SBc("x2lo", [128, FC * CC], BF16)
    xsq2 = SBc("xsq2", [128, FC * CC], BF16)
    st1C = SBc("st1C", [128, CC], F32)
    st2C = SBc("st2C", [128, CC], F32)
    rrepC = SBc("rrepC", [128, CC], F32)
    nmrC = SBc("nmrC", [128, CC], F32)
    gaC = SBc("gaC", [128, CC], F32)
    gbC = SBc("gbC", [128, CC], F32)
    coefT = SBc("coefT", [8, SUB], BF16)
    sel8 = SBc("sel8", [8, 8 * 128], BF16)
    c8m = SBc("c8m", [128, 8], F32)
    c8q = SBc("c8q", [128, 8], F32)
    c8a = SBc("c8a", [128, 8], F32)
    c8b = SBc("c8b", [128, 8], F32)
    c8c = SBc("c8c", [128, 8], F32)
    rstd8 = SBc("rstd8", [128, 8], F32)
    nmr8 = SBc("nmr8", [128, 8], F32)
    r64a = SBc("r64a", [128, 64], F32)
    r64b = SBc("r64b", [128, 64], F32)
    coefb64 = SBc("coefb64", [128, 64], BF16)
    wslot = [SBc(f"wslot{i}", [128, FC * DE], BF16) for i in range(2)]
    wstgC = SBc("wstgC", [128, 1536], F32)
    dslot = [SBc(f"dslot{i}", [128, NE * 4 * 128], BF16) for i in range(2)]
    gsall = SBc("gsall", [128, 8 * 512], BF16)
    t2b = [SBc("t2b0", [128, 512], F32)] * 2
    x2r = [SBc(f"x2r{i}", [128, 512], F32) for i in range(2)]


    psG = [PSc(f"psG{i}", [128, 512]) for i in range(2)]
    psU = [PSc(f"psU{i}", [128, 512]) for i in range(2)]
    psD = [PSc(f"psD{i}", [128, 512]) for i in range(2)]
    psM = PSc("psM", [128, 512])
    psTr = PSc("psTr", [8, 128], BF16)

    wstg_free_v = 0
    slot_free_t = [0, 0, 0]
    slot_scr_dG = [all_scratch_dG] * 3
    dslot_free_t = [0, 0]
    dslot_scr_dG = [all_scratch_dG, all_scratch_dG]
    x2f_cons = [0, 0]
    x2f_cons_v = [0, 0]
    x2r_free = [0, 0]
    ln2b_free_t = 0
    psM_free_v = 0
    psD_free_v = 0
    gu_prev_silu = {0: 0, 1: 0}
    up_prev_ev = {0: 0, 1: 0}
    prev_up_ev_all = {i: 0 for i in range(8)}

    def fetch_piece(dst_sb, src_f32, src_bf, sub, length, slot_free_t,
                    scr_done_dG, src_off=0):
        nonlocal wstg_free_v
        if sub == 0:
            nh = (length + 1535) // 1536
            for hh in range(nh):
                a = hh * 1536
                b = min(length, a + 1536)
                p.wait("sync", sV, wstg_free_v)
                p.op("sync", lambda e, s=src_f32, a=a, b=b, o=src_off:
                     e.dma_start(out=wstgC[:, 0:b - a],
                                 in_=s[:, o + a:o + b]), dW, 16)
                ld = dW.n
                p.wait("vector", dW, ld)
                if hh == 0:
                    p.wait("vector", sT, slot_free_t)
                    p.wait("vector", dG, scr_done_dG)
                p.op("vector", lambda e, d=dst_sb, a=a, b=b: e.tensor_copy(
                    d[:, a:b], wstgC[:, 0:b - a]), sV, 1)
                wstg_free_v = sV.n
            p.wait("gpsimd", sV, wstg_free_v)
            p.op("gpsimd", lambda e, d=dst_sb, s=src_bf, L=length,
                 o=src_off: e.dma_start(out=s[:, o:o + L], in_=d[:, 0:L]),
                 dG, 16)
            return sV, wstg_free_v, dG.n
        p.wait("sync", sT, slot_free_t)
        p.wait("sync", dG, scr_done_dG)
        p.op("sync", lambda e, d=dst_sb, s=src_bf, L=length, o=src_off:
             e.dma_start(out=d[:, 0:L], in_=s[:, o:o + L]), dW, 16)
        return dW, dW.n, scr_done_dG

    p.op("sync", lambda e: e.dma_start(out=sel8[:], in_=sel_in[:]), dW, 16)
    p.wait("tensor", dW, dW.n)
    p.op("vector", lambda e: e.tensor_copy(r_hi[:], r_sb[:]))
    vdrain()
    p.op("vector", lambda e: e.tensor_sub(r_lo[:], r_sb[:], r_hi[:]), sV, 1)
    p.wait("tensor", sV, sV.n)

    def d3(h_):
        return bass.AP(h_, 0, [[64, 128], [8, 8], [1, 8]])

    def segb(h_):
        return bass.AP(h_, 0, [[8, 128], [1, 8], [0, 8]])

    def tileb(h_):
        return bass.AP(h_, 0, [[8, 128], [0, 8], [1, 8]])


    chunk_last_t = 0
    stats_read_v = 0
    cast2_v = 0
    prev_down_last_v = 0
    ln2b_sub_v = 0
    coef_cast_v = 0
    coef_ready_v = 0
    psm_drain_t = 0
    pend_g = None
    pend_u = None

    def emit_casts(sub, h):
        nonlocal cast2_v
        par = h % 2
        t0 = sub * SUB + h * CC
        p.wait("gpsimd", sT, x2f_cons[par])
        p.wait("gpsimd", sV, x2f_cons_v[par])
        for fc in range(FC):
            p.op("gpsimd", lambda e, fc=fc, t0=t0, par=par: e.dma_start(
                out=x2f[par][:, fc * CC:(fc + 1) * CC],
                in_=x2_hbm[fc * 128:(fc + 1) * 128, t0:t0 + CC]), dG, 16)
        x2_ld = dG.n
        p.wait("vector", dG, x2_ld)
        p.wait("vector", sT, chunk_last_t)
        p.op("vector", lambda e, par=par: e.tensor_copy(xb2[:],
                                                        x2f[par][:]))
        p.op("vector", lambda e, par=par: e.tensor_sub(
            x2lo[:], x2f[par][:], xb2[:]))
        p.op("vector", lambda e: e.tensor_mul(xsq2[:], xb2[:], xb2[:]),
             sV, 1)
        cast2_v = sV.n

    def emit_stats_rest(sub, h, psX):
        nonlocal stats_read_v
        p.wait("tensor", sV, cast2_v)
        p.wait("tensor", sV, stats_read_v)
        if psX is psD:
            p.wait("tensor", sV, prev_down_last_v)
        else:
            p.wait("tensor", sA, max(gu_prev_silu.values()))
        for kc in range(FC):
            p.op("tensor", lambda e, kc=kc, psX=psX: e.matmul(
                psX[0][:, 0:CC], ones_b[:], xb2[:, kc * CC:(kc + 1) * CC],
                start=(kc == 0), stop=(kc == FC - 1)))
        for kc in range(FC):
            p.op("tensor", lambda e, kc=kc, psX=psX: e.matmul(
                psX[1][:, 0:CC], ones_b[:], xsq2[:, kc * CC:(kc + 1) * CC],
                start=(kc == 0), stop=(kc == FC - 1)),
                sT if kc == FC - 1 else None, 1, fast=True)
        stats_mm_done = sT.n
        p.wait("vector", sT, stats_mm_done)
        p.op("vector", lambda e, psX=psX: e.tensor_scalar(
            st1C[:], psX[0][:, 0:CC], 1.0 / D, None, ALU.mult))
        p.op("vector", lambda e, psX=psX: e.tensor_scalar(
            st2C[:], psX[1][:, 0:CC], 1.0 / D, None, ALU.mult))
        p.op("vector", lambda e: e.scalar_tensor_tensor(
            gaC[:], st1C[:], -1.0, st1C[:], ALU.mult, ALU.mult))
        p.op("vector", lambda e: e.scalar_tensor_tensor(
            gbC[:], st2C[:], 1e-5, gaC[:], ALU.add, ALU.add), sV, 1)
        stats_read_v = sV.n
        p.wait("scalar", sV, stats_read_v)
        p.op("scalar", lambda e: e.activation(gaC[:], gbC[:], AF.Ln),
             sA, 1)
        p.wait("vector", sA, sA.n)
        p.op("vector", lambda e: e.tensor_scalar(
            gbC[:], gaC[:], -0.5, None, ALU.mult), sV, 1)
        p.wait("scalar", sV, sV.n)
        p.op("scalar", lambda e: e.activation(rrepC[:], gbC[:], AF.Exp),
             sA, 1)
        p.wait("vector", sA, sA.n)
        p.op("vector", lambda e: e.scalar_tensor_tensor(
            nmrC[:], st1C[:], -1.0, rrepC[:], ALU.mult, ALU.mult))

    def emit_ln2b(sub, h):
        nonlocal ln2b_sub_v
        par = h % 2
        p.wait("vector", sT, ln2b_free_t if (sub > 0 and h == 0) else 0)
        for fc in range(FC):
            p.op("vector", lambda e, fc=fc, par=par: e.tensor_mul(
                gaC[:], x2f[par][:, fc * CC:(fc + 1) * CC], rrepC[:]))
            p.op("vector", lambda e, fc=fc, h=h: e.tensor_add(
                ln2b[:, fc * SUB + h * CC:fc * SUB + (h + 1) * CC],
                gaC[:], nmrC[:]),
                sV if fc == FC - 1 else None, 1)
        ln2b_sub_v = sV.n
        x2f_cons_v[par] = sV.n

    def emit_router_mms(sub, h):
        nonlocal chunk_last_t
        par = h % 2
        for tt in range(CC // 128):
            ti = h * 2 + tt
            tsl = slice(tt * 128, (tt + 1) * 128)
            if ti == 0:
                p.wait("tensor", sV, psM_free_v)
            for kc in range(FC):
                p.op("tensor", lambda e, kc=kc, tsl=tsl, ti=ti: e.matmul(
                    psM[:, 64 + ti:65 + ti],
                    xb2[:, kc * CC:(kc + 1) * CC][:, tsl],
                    ones_b[:, 0:1],
                    start=(kc == 0), stop=(kc == FC - 1)))
            for kc in range(FC):
                p.op("tensor", lambda e, kc=kc, tsl=tsl, ti=ti: e.matmul(
                    psM[:, 72 + ti:73 + ti],
                    xsq2[:, kc * CC:(kc + 1) * CC][:, tsl],
                    ones_b[:, 0:1],
                    start=(kc == 0), stop=(kc == FC - 1)))
            for pas, xs, rp in ((0, xb2, r_hi), (1, xb2, r_lo),
                                (2, x2lo, r_hi)):
                for kc in range(FC):
                    p.op("tensor", lambda e, kc=kc, tsl=tsl, xs=xs,
                         rp=rp, pas=pas, ti=ti: e.matmul(
                             psM[:, ti * 8:(ti + 1) * 8],
                             xs[:, kc * CC:(kc + 1) * CC][:, tsl],
                             rp[:, kc * 8:(kc + 1) * 8],
                             start=(kc == 0 and pas == 0),
                             stop=(kc == FC - 1 and pas == 2)),
                         sT if (kc == FC - 1 and pas == 2) else None, 1,
                         fast=True)
        x2f_cons[par] = sT.n
        chunk_last_t = sT.n

    def emit_router_chain(sub):
        nonlocal psm_drain_t, psM_free_v, coef_cast_v
        p.op("tensor", lambda e: e.drain(fusable=False), sT, 1, fast=True)
        psm_drain_t = sT.n
        p.wait("vector", sT, psm_drain_t)
        p.op("vector", lambda e: e.tensor_scalar(
            c8m[:], psM[:, 64:72], 1.0 / D, None, ALU.mult))
        p.op("vector", lambda e: e.tensor_scalar(
            c8q[:], psM[:, 72:80], 1.0 / D, None, ALU.mult))
        vdrain()
        p.op("vector", lambda e: e.scalar_tensor_tensor(
            c8a[:], c8m[:], -1.0, c8m[:], ALU.mult, ALU.mult))
        vdrain()
        p.op("vector", lambda e: e.scalar_tensor_tensor(
            c8b[:], c8q[:], 1e-5, c8a[:], ALU.add, ALU.add), sV, 1)
        p.wait("scalar", sV, sV.n)
        p.op("scalar", lambda e: e.activation(c8a[:], c8b[:], AF.Ln), sA, 1)
        p.wait("vector", sA, sA.n)
        p.op("vector", lambda e: e.tensor_scalar(
            c8b[:], c8a[:], -0.5, None, ALU.mult), sV, 1)
        p.wait("scalar", sV, sV.n)
        p.op("scalar", lambda e: e.activation(rstd8[:], c8b[:], AF.Exp),
             sA, 1)
        p.wait("vector", sA, sA.n)
        p.op("vector", lambda e: e.scalar_tensor_tensor(
            nmr8[:], c8m[:], -1.0, rstd8[:], ALU.mult, ALU.mult))
        p.op("vector", lambda e: e.tensor_copy(r64a[:], psM[:, 0:64]), sV, 1)
        psM_free_v = sV.n
        vdrain()
        p.op("vector", lambda e: e.tensor_tensor(
            d3(r64a), d3(r64a), segb(rstd8), ALU.mult))
        p.op("vector", lambda e: e.tensor_tensor(
            d3(r64b), tileb(sr_rep), segb(nmr8), ALU.mult))
        vdrain()
        p.op("vector", lambda e: e.tensor_add(r64a[:], r64a[:], r64b[:]))
        vdrain()
        p.op("vector", lambda e: e.reduce_max(c8a[:], d3(r64a), axis=AX))
        vdrain()
        p.op("vector", lambda e: e.tensor_tensor(
            d3(r64b), d3(r64a), segb(c8a), ALU.subtract), sV, 1)
        p.wait("scalar", sV, sV.n)
        p.op("scalar", lambda e: e.activation(r64a[:], r64b[:], AF.Exp),
             sA, 1)
        p.wait("vector", sA, sA.n)
        p.op("vector", lambda e: e.reduce_sum(c8b[:], d3(r64a), axis=AX))
        vdrain()
        p.op("vector", lambda e: e.reciprocal(c8c[:], c8b[:]))
        vdrain()
        p.op("vector", lambda e: e.tensor_tensor(
            d3(r64b), d3(r64a), segb(c8c), ALU.mult))
        vdrain()
        p.op("vector", lambda e: e.reduce_max(c8a[:], d3(r64b), axis=AX))
        vdrain()
        p.op("vector", lambda e: e.tensor_tensor(
            d3(r64a), d3(r64b), segb(c8a), ALU.is_equal))
        vdrain()
        p.op("vector", lambda e: e.scalar_tensor_tensor(
            r64a[:], r64a[:], -2.0, r64b[:], ALU.mult, ALU.add))
        vdrain()
        p.op("vector", lambda e: e.reduce_max(c8b[:], d3(r64a), axis=AX))
        vdrain()
        p.op("vector", lambda e: e.tensor_tensor(
            d3(r64a), d3(r64b), segb(c8b), ALU.is_ge))
        vdrain()
        p.op("vector", lambda e: e.tensor_mul(r64a[:], r64a[:], r64b[:]))
        vdrain()
        p.op("vector", lambda e: e.tensor_copy(coefb64[:], r64a[:]), sV, 1)
        coef_cast_v = sV.n

    def emit_coef_transpose(sub):
        nonlocal coef_ready_v
        p.wait("tensor", sV, coef_cast_v)
        for ti in range(8):
            p.op("tensor", lambda e, ti=ti: e.transpose(
                psTr[:], coefb64[:, ti * 8:(ti + 1) * 8], eyeb[:]), sT, 1)
            p.wait("vector", sT, sT.n)
            p.op("vector", lambda e, ti=ti: e.tensor_copy(
                coefT[:, ti * 128:(ti + 1) * 128], psTr[:]), sV, 1)
            if ti < 7:
                p.wait("tensor", sV, sV.n)
        coef_ready_v = sV.n
        if dbg:
            p.wait("gpsimd", sV, coef_ready_v)
            p.op("gpsimd", lambda e, sub=sub: e.dma_start(
                out=dbg_coef[:, sub * SUB:(sub + 1) * SUB], in_=coefT[:]),
                dG, 16)
            if sub == 0:
                p.op("gpsimd", lambda e: e.dma_start(
                    out=dbg_ln2b[:], in_=ln2b[:]), dG, 16)

    for sub in range(NSUB):
        if sub == 0:
            for h in range(NCC):
                emit_casts(0, h)
                emit_stats_rest(0, h, psD if h == 0 else psG)
                emit_router_mms(0, h)
                emit_ln2b(0, h)
            emit_router_chain(0)
            emit_coef_transpose(0)


        for pe in range(NE):
            if sub + 1 < NSUB and pe == 2:
                emit_casts(sub + 1, 0)
            if sub + 1 < NSUB and pe == 8:
                emit_stats_rest(sub + 1, 0, psD)
                c0_read_v = stats_read_v
            if sub + 1 < NSUB and pe == 9:
                emit_router_mms(sub + 1, 0)
                emit_casts(sub + 1, 1)
            gslot = 0
            uslot = 1
            if pe == 0 and pend_g is not None:
                gsem, gval = pend_g
                usem, uval = pend_u
                pend_g = pend_u = None
            else:
                gsem, gval, slot_scr_dG[gslot] = fetch_piece(
                    wslot[gslot], gup[pe, 0], gup_sc[pe, 0], 1, FC * DE,
                    slot_free_t[gslot], slot_scr_dG[gslot])
                usem, uval, slot_scr_dG[uslot] = fetch_piece(
                    wslot[uslot], gup[pe, 1], gup_sc[pe, 1], 1, FC * DE,
                    slot_free_t[uslot], slot_scr_dG[uslot])
            p.wait("tensor", gsem, gval)
            if pe == 0:
                p.wait("tensor", sV, ln2b_sub_v)
            silu_a = {}
            for dcq in range(8):
                qc, dc = dcq // 4, dcq % 4
                par = dcq % 2
                p.wait("tensor", sA,
                       silu_a[dcq - 2] if dcq >= 2 else gu_prev_silu[par])
                for kc in range(FC):
                    p.op("tensor", lambda e, kc=kc, dc=dc, qc=qc, par=par,
                         gslot=gslot: e.matmul(
                             psG[par][:],
                             wslot[gslot][:, kc * DE + dc * 128:
                                          kc * DE + (dc + 1) * 128],
                             ln2b[:, kc * SUB + qc * 512:
                                  kc * SUB + (qc + 1) * 512],
                             start=(kc == 0), stop=(kc == FC - 1)),
                         sT if kc == FC - 1 else None, 1, fast=True)
                g_t = sT.n
                p.wait("scalar", sT, g_t)
                p.wait("scalar", sV, prev_up_ev_all[dcq])
                p.op("scalar", lambda e, dcq=dcq, par=par: e.activation(
                    gsall[:, dcq * 512:(dcq + 1) * 512], psG[par][:],
                    AF.Silu), sA, 1)
                silu_a[dcq] = sA.n
            slot_free_t[gslot] = sT.n
            gu_prev_silu = {0: silu_a[6], 1: silu_a[7]}

            p.wait("tensor", usem, uval)
            up_ev = {}
            for dcq in range(8):
                qc, dc = dcq // 4, dcq % 4
                par = dcq % 2
                if pe < 8 and dc == 0:
                    p.wait("tensor", sV, psM_free_v)
                    p.wait("tensor", sV, coef_ready_v)
                    p.op("tensor", lambda e, pe=pe, qc=qc: e.matmul(
                        psM[:], sel8[:, pe * 128:(pe + 1) * 128],
                        coefT[:, qc * 512:(qc + 1) * 512],
                        start=True, stop=True))
                p.wait("tensor", sV,
                       up_ev[dcq - 2] if dcq >= 2 else up_prev_ev[par])
                for kc in range(FC):
                    p.op("tensor", lambda e, kc=kc, dc=dc, qc=qc, par=par,
                         uslot=uslot: e.matmul(
                             psU[par][:],
                             wslot[uslot][:, kc * DE + dc * 128:
                                          kc * DE + (dc + 1) * 128],
                             ln2b[:, kc * SUB + qc * 512:
                                  kc * SUB + (qc + 1) * 512],
                             start=(kc == 0), stop=(kc == FC - 1)),
                         sT if kc == FC - 1 else None, 1, fast=True)
                u_t = sT.n
                p.wait("vector", sT, u_t)
                p.wait("vector", sA, silu_a[dcq])
                hpos = pe * 4 * SUB + dc * SUB + qc * 512
                if pe < 8:
                    p.op("vector", lambda e, dcq=dcq, par=par: e.tensor_mul(
                        t2b[par][:], psU[par][:],
                        gsall[:, dcq * 512:(dcq + 1) * 512]))
                    p.op("vector", lambda e, hpos=hpos, par=par:
                         e.tensor_mul(htil[:, hpos:hpos + 512], t2b[par][:],
                                      psM[:]), sV, 1)
                else:
                    p.op("vector", lambda e, dcq=dcq, par=par, hpos=hpos:
                         e.tensor_mul(htil[:, hpos:hpos + 512], psU[par][:],
                                      gsall[:, dcq * 512:(dcq + 1) * 512]),
                         sV, 1)
                up_ev[dcq] = sV.n
                if pe < 8 and dc == 3:
                    psM_free_v = sV.n
            slot_free_t[uslot] = sT.n
            up_prev_ev = {0: up_ev[6], 1: up_ev[7]}
            prev_up_ev_all = dict(up_ev)
        htil_v = sV.n

        if dbg and sub == 0:
            p.wait("gpsimd", sV, htil_v)
            p.op("gpsimd", lambda e: e.dma_start(
                out=dbg_htil[:], in_=htil[:, 0:2 * SUB * 4]), dG, 16)
        ln2b_free_t = sT.n
        p.wait("tensor", sV, htil_v)
        dev_v = {}
        nxt = sub + 1
        for dc in range(FC):
            dpar = dc % 2
            dsem, dval, dslot_scr_dG[dpar] = fetch_piece(
                dslot[dpar], dwn[dc], dwn_sc[dc], 1, NE * 4 * 128,
                dslot_free_t[dpar], dslot_scr_dG[dpar])
            p.wait("tensor", dsem, dval)
            for qc in range(2):
                idx = dc * 2 + qc
                par = idx % 2
                if nxt < NSUB:
                    if idx == 0:
                        emit_ln2b(nxt, 0)
                        emit_stats_rest(nxt, 1, psG)
                    elif idx == 2:
                        emit_router_mms(nxt, 1)
                        emit_ln2b(nxt, 1)
                    elif idx == 4:
                        emit_casts(nxt, 2)
                        emit_stats_rest(nxt, 2, psG)
                    elif idx == 6:
                        emit_router_mms(nxt, 2)
                        emit_ln2b(nxt, 2)
                    elif idx == 8:
                        emit_casts(nxt, 3)
                        emit_stats_rest(nxt, 3, psG)
                    elif idx == 10:
                        emit_router_mms(nxt, 3)
                        emit_ln2b(nxt, 3)
                    elif idx == 12:
                        emit_router_chain(nxt)
                    elif idx == 5:
                        pg0, pg1, slot_scr_dG[0] = fetch_piece(
                            wslot[0], gup[0, 0], gup_sc[0, 0], 1, FC * DE,
                            slot_free_t[0], slot_scr_dG[0])
                        pend_g = (pg0, pg1)
                    elif idx == 11:
                        pu0, pu1, slot_scr_dG[1] = fetch_piece(
                            wslot[1], gup[0, 1], gup_sc[0, 1], 1, FC * DE,
                            slot_free_t[1], slot_scr_dG[1])
                        pend_u = (pu0, pu1)
                p.wait("gpsimd", yst, x2r_free[par])
                p.op("gpsimd", lambda e, dc=dc, qc=qc, sub=sub, par=par:
                     e.dma_start(
                         out=x2r[par][:],
                         in_=x2_hbm[dc * 128:(dc + 1) * 128,
                                    sub * SUB + qc * 512:
                                    sub * SUB + (qc + 1) * 512]), dG, 16)
                x2r_ld = dG.n
                if idx >= 2:
                    p.wait("tensor", sV, dev_v[idx - 2])
                elif nxt < NSUB:
                    p.wait("tensor", sV, c0_read_v)
                for pe in range(NE):
                    for cc in range(4):
                        blk = pe * 4 + cc
                        first = (pe == 0 and cc == 0)
                        last = (pe == NE - 1 and cc == 3)
                        p.op("tensor",
                             lambda e, blk=blk, pe=pe, cc=cc, qc=qc, par=par,
                             dpar=dpar, first=first, last=last: e.matmul(
                                 psD[par][:],
                                 dslot[dpar][:, blk * 128:(blk + 1) * 128],
                                 htil[:, pe * 4 * SUB + cc * SUB + qc * 512:
                                      pe * 4 * SUB + cc * SUB +
                                      (qc + 1) * 512],
                                 start=first, stop=last),
                             sT if last else None, 1, fast=True)
                if qc == 1:
                    dslot_free_t[dpar] = sT.n
                dmm_t = sT.n
                p.wait("vector", sT, dmm_t)
                p.wait("vector", dG, x2r_ld)
                p.op("vector", lambda e, par=par: e.tensor_add(
                    x2r[par][:], psD[par][:], x2r[par][:]), sV, 1)
                dev_v[idx] = sV.n
                p.wait("gpsimd", sV, dev_v[idx])
                p.op("gpsimd", lambda e, dc=dc, qc=qc, sub=sub, par=par:
                     e.dma_start(
                         out=y[dc * 128:(dc + 1) * 128,
                               sub * SUB + qc * 512:
                               sub * SUB + (qc + 1) * 512],
                         in_=x2r[par][:]), yst, 16)
                x2r_free[par] = yst.n
        prev_down_last_v = dev_v[2 * FC - 1]
        if nxt < NSUB:
            emit_coef_transpose(nxt)

    p.wait("gpsimd", yst, yst.n)
    p.wait("gpsimd", dG, dG.n)

    with nc.Block() as blk:
        p.flush(blk)
    esC.close()
    es.close()
    return nc


_NC_CACHE = {}


def _get_nc():
    if "nc" not in _NC_CACHE:
        _NC_CACHE["nc"] = build_nc()
    return _NC_CACHE["nc"]


def _mk_sel():
    s = np.zeros((8, 8 * 128), np.float32)
    for e in range(8):
        s[e, e * 128:(e + 1) * 128] = 1.0
    return s.astype(ml_dtypes.bfloat16)


def _prep_inputs(inputs):
    bf = ml_dtypes.bfloat16
    x = np.asarray(inputs["L0_attn_x"])

    def tile_gu(w):
        return np.ascontiguousarray(
            w.reshape(8, 128, DE).transpose(1, 0, 2).reshape(128, 8 * DE))

    eg = np.asarray(inputs["L0_ffn_experts_gate"])
    eu = np.asarray(inputs["L0_ffn_experts_up"])
    ed = np.asarray(inputs["L0_ffn_experts_down"])
    sg = np.asarray(inputs["L0_ffn_shared_gate"])
    su = np.asarray(inputs["L0_ffn_shared_up"])
    sd = np.asarray(inputs["L0_ffn_shared_down"])
    gup = np.zeros((NE, 2, 128, FC * DE), np.float32)
    for e in range(8):
        gup[e, 0] = tile_gu(eg[e])
        gup[e, 1] = tile_gu(eu[e])
    gup[8, 0] = tile_gu(sg[:, 0:DE])
    gup[8, 1] = tile_gu(su[:, 0:DE])
    gup[9, 0] = tile_gu(sg[:, DE:2 * DE])
    gup[9, 1] = tile_gu(su[:, DE:2 * DE])

    downs = [ed[e] for e in range(8)] + [sd[0:DE, :], sd[DE:2 * DE, :]]
    dwn = np.zeros((FC, 128, NE * 4 * 128), np.float32)
    for dc in range(FC):
        blocks = []
        for pe in range(NE):
            w = downs[pe]
            for cc in range(4):
                blocks.append(w[cc * 128:(cc + 1) * 128,
                                dc * 128:(dc + 1) * 128])
        dwn[dc] = np.concatenate(blocks, axis=1)

    common = dict(
        wqT=np.ascontiguousarray(np.asarray(inputs["L0_attn_Wq"]).T),
        wkT=np.ascontiguousarray(np.asarray(inputs["L0_attn_Wk"]).T),
        wvT=np.ascontiguousarray(np.asarray(inputs["L0_attn_Wv"]).T),
        woT=np.ascontiguousarray(np.asarray(inputs["L0_attn_Wo"]).T),
        wavel=np.ascontiguousarray(np.asarray(inputs["L0_attn_wavelengths"])),
        pbias=np.ascontiguousarray(np.asarray(inputs["L0_attn_phase_bias"])),
        bq=np.ascontiguousarray(np.asarray(inputs["L0_attn_bq"])),
        bk=np.ascontiguousarray(np.asarray(inputs["L0_attn_bk"])),
        bv=np.ascontiguousarray(np.asarray(inputs["L0_attn_bv"])),
        bo=np.ascontiguousarray(np.asarray(inputs["L0_attn_bo"])),
        router=np.ascontiguousarray(np.asarray(inputs["L0_ffn_router"])),
        gup=gup, dwn=dwn,
        eye_in=np.eye(128, dtype=np.float32).astype(bf),
        sel_in=_mk_sel(),
    )
    per_core = []
    for b in range(8):
        m = dict(common)
        m["x_t"] = np.ascontiguousarray(np.asarray(x[b]).T)
        per_core.append(m)
    return per_core


def kernel(**inputs):
    nc = _get_nc()
    in_maps = _prep_inputs(inputs)
    res = run_bass_kernel_spmd(nc, in_maps, core_ids=list(range(8)))
    out = np.empty((8, S, D), np.float32)
    for b in range(8):
        out[b] = np.asarray(res.results[b]["y"]).T
    return out


# version 11
# speedup vs baseline: 1.2977x; 1.0093x over previous
import contextlib
import math

import numpy as np
import ml_dtypes

import concourse.bass as bass
import concourse.mybir as mybir
from concourse.bass_utils import run_bass_kernel_spmd

F32 = mybir.dt.float32
F16 = mybir.dt.float16
BF16 = mybir.dt.bfloat16
I32 = mybir.dt.int32
ALU = mybir.AluOpType
AF = mybir.ActivationFunctionType
AX = mybir.AxisListType.X

D = 1024
S = 4096
H = 128
FC = D // 128
NE = 10
DE = 512
NSUB = 4
SUB = S // NSUB

TWO_PI = float(2.0 * math.pi)
INV_SQRT_H = 1.0 / math.sqrt(H)
EXP_SHIFT = -float(math.sqrt(H))
ROUND_OFS = 128.0


class Sem:
    def __init__(self, h, is_dma=False):
        self.h = h
        self.is_dma = is_dma
        self.n = 0

    def bump(self, k):
        self.n += k
        return self.n


class Prog:
    ENGS = ("sync", "vector", "scalar", "tensor", "gpsimd")

    def __init__(self):
        self.q = {e: [] for e in self.ENGS}

    def op(self, eng, build, sem=None, k=1, fast=False):
        if sem is not None:
            v = sem.bump(k)
            if sem.is_dma or fast:
                self.q[eng].append(
                    lambda e, b=build, s=sem, kk=k: b(e).then_inc(s.h, kk))
            else:
                def _f(e, b=build, s=sem, kk=k):
                    b(e)
                    e.drain(fusable=False).then_inc(s.h, kk)
                self.q[eng].append(_f)
            return v
        self.q[eng].append(build)
        return None

    def wait(self, eng, sem, val):
        if val:
            self.q[eng].append(lambda e, s=sem, v=val: e.wait_ge(s.h, v))

    def flush(self, block):
        for eng in self.ENGS:
            ops = self.q[eng]
            if not ops:
                continue

            def body(engine, ops=ops):
                for f in ops:
                    f(engine)

            getattr(block, eng)(body)
        self.q = {e: [] for e in self.ENGS}


def build_nc(dbg=False):
    nc = bass.Bass("TRN2", target_bir_lowering=False,
                   detect_race_conditions=False)

    x_t = nc.dram_tensor("x_t", [D, S], F32, kind="ExternalInput")
    wqT = nc.dram_tensor("wqT", [D, H], F32, kind="ExternalInput")
    wkT = nc.dram_tensor("wkT", [D, H], F32, kind="ExternalInput")
    wvT = nc.dram_tensor("wvT", [D, H], F32, kind="ExternalInput")
    woT = nc.dram_tensor("woT", [H, D], F32, kind="ExternalInput")
    wavel = nc.dram_tensor("wavel", [H], F32, kind="ExternalInput")
    pbias = nc.dram_tensor("pbias", [H], F32, kind="ExternalInput")
    bq_d = nc.dram_tensor("bq", [H], F32, kind="ExternalInput")
    bk_d = nc.dram_tensor("bk", [H], F32, kind="ExternalInput")
    bv_d = nc.dram_tensor("bv", [H], F32, kind="ExternalInput")
    bo_d = nc.dram_tensor("bo", [D], F32, kind="ExternalInput")
    router = nc.dram_tensor("router", [D, 8], F32, kind="ExternalInput")
    gup = nc.dram_tensor("gup", [NE, 2, 128, FC * DE], F32,
                         kind="ExternalInput")
    dwn = nc.dram_tensor("dwn", [FC, 128, NE * 4 * 128], F32,
                         kind="ExternalInput")
    eye_in = nc.dram_tensor("eye_in", [128, 128], BF16, kind="ExternalInput")
    sel_in = nc.dram_tensor("sel_in", [8, 8 * 128], BF16, kind="ExternalInput")

    y = nc.dram_tensor("y", [D, S], F32, kind="ExternalOutput")

    x2_hbm = nc.dram_tensor("x2_hbm", [D, S], F32,
                            kind="ExternalOutput" if dbg else "Internal")
    if dbg:
        dbg_t = {n: nc.dram_tensor("dbg_" + n, [128, S], F16,
                                   kind="ExternalOutput")
                 for n in ("cosq", "sinq", "cosk", "sink")}
        dbg_t["vf"] = nc.dram_tensor("dbg_vf", [128, S], BF16,
                                     kind="ExternalOutput")
        dbg_us = {n: nc.dram_tensor("dbg_" + n, [128, S], F16,
                                    kind="ExternalOutput")
                  for n in ("usq", "ucq")}
        dbg_coef = nc.dram_tensor("dbg_coef", [8, S], BF16,
                                  kind="ExternalOutput")
        dbg_htil = nc.dram_tensor("dbg_htil", [128, 2 * SUB * 4], BF16,
                                  kind="ExternalOutput")
        dbg_ln2b = nc.dram_tensor("dbg_ln2b", [128, FC * SUB], BF16,
                                  kind="ExternalOutput")
        dbg_rt = nc.dram_tensor("dbg_rt", [128, 64], F32,
                                kind="ExternalOutput")
        dbg_rt2 = nc.dram_tensor("dbg_rt2", [128, 40 * 8], F32,
                                 kind="ExternalOutput")
    gup_sc = nc.dram_tensor("gup_sc", [NE, 2, 128, FC * DE], BF16)
    dwn_sc = nc.dram_tensor("dwn_sc", [FC, 128, NE * 4 * 128], BF16)

    es = contextlib.ExitStack()

    def SB(name, shape, dt):
        return es.enter_context(nc.sbuf_tensor(name, shape, dt))

    def SEM(name, is_dma=False):
        return Sem(es.enter_context(nc.semaphore(name)), is_dma)

    eyeb = SB("eyeb", [128, 128], BF16)
    ones_b = SB("ones_b", [128, 128], BF16)
    ones_f = SB("ones_f", [128, 128], F32)
    c2pi = SB("c2pi", [128, 1], F32)
    ceps = SB("ceps", [128, 1], F32)
    cnhalf = SB("cnhalf", [128, 1], F32)
    cnone = SB("cnone", [128, 1], F32)
    cinvsq = SB("cinvsq", [128, 1], F32)
    cshift = SB("cshift", [128, 1], F32)
    invw = SB("invw", [128, 1], F32)
    s1c = SB("s1c", [128, 1], F32)
    s2q = SB("s2q", [128, 1], F32)
    s2k = SB("s2k", [128, 1], F32)
    sqc = SB("sqc", [128, 1], F32)
    skc = SB("skc", [128, 1], F32)
    svc = SB("svc", [128, 1], F32)
    bvb = SB("bvb", [128, 1], BF16)
    bqf = SB("bqf", [128, 1], F32)
    bkf = SB("bkf", [128, 1], F32)
    bvf = SB("bvf", [128, 1], F32)
    wlf = SB("wlf", [128, 1], F32)
    pbf = SB("pbf", [128, 1], F32)
    bo8 = SB("bo8", [128, FC], F32)
    boP = SB("boP", [128, FC], F32)
    tcol = SB("tcol", [128, 1], F32)
    wqTb = SB("wqTb", [128, FC * 128], BF16)
    wkTb = SB("wkTb", [128, FC * 128], BF16)
    wvTb = SB("wvTb", [128, FC * 128], BF16)
    woTb = SB("woTb", [128, FC * 128], BF16)
    r_sb = SB("r_sb", [128, FC * 8], F32)
    r_hi = SB("r_hi", [128, FC * 8], BF16)
    r_lo = SB("r_lo", [128, FC * 8], BF16)
    sr_rep = SB("sr_rep", [128, 8], F32)

    dW = SEM("dW", is_dma=True)
    dG = SEM("dG", is_dma=True)
    yst = SEM("yst", is_dma=True)
    sT = SEM("sT")
    sV = SEM("sV")
    sA = SEM("sA")

    p = Prog()

    def vdrain():
        p.op("vector", lambda e: e.drain(fusable=False))

    esAB = contextlib.ExitStack()

    def SBab(name, shape, dt):
        return esAB.enter_context(nc.sbuf_tensor(name, shape, dt))

    cosq = SBab("cosq", [128, S], F16)
    sinq = SBab("sinq", [128, S], F16)
    cosk = SBab("cosk", [128, S], F16)
    sink = SBab("sink", [128, S], F16)
    vf = SBab("vf", [128, S], BF16)

    esA = contextlib.ExitStack()

    def SBa(name, shape, dt):
        return esA.enter_context(nc.sbuf_tensor(name, shape, dt))

    def PSa(name, shape, dt=F32):
        return esA.enter_context(nc.psum_tensor(name, shape, dt))

    CH = 512
    NCH = S // CH
    xfA = [SBa(f"xfA{i}", [128, FC * CH], F32) for i in range(2)]
    xbA = [SBa(f"xbA{i}", [128, FC * CH], BF16) for i in range(2)]
    xsqA = [SBa(f"xsqA{i}", [128, FC * CH], BF16) for i in range(2)]
    stat1 = SBa("stat1", [128, CH], F32)
    stat2 = SBa("stat2", [128, CH], F32)
    rrep = SBa("rrep", [128, CH], F32)
    nmr = SBa("nmr", [128, CH], F32)
    ga = SBa("ga", [128, CH], F32)
    gb = SBa("gb", [128, CH], F32)
    gc = SBa("gc", [128, CH], F32)
    gi = SBa("gi", [128, CH], I32)
    usq = SBa("usq", [128, S], F16)
    ucq = SBa("ucq", [128, S], F16)
    usk = SBa("usk", [128, S], F16)
    uck = SBa("uck", [128, S], F16)
    wstage = SBa("wstage", [128, FC * 128], F32)

    psMean = PSa("psMean", [128, CH])
    psSq = PSa("psSq", [128, CH])
    psQ = [PSa(f"psQ{i}", [128, CH]) for i in range(2)]
    psK = [PSa(f"psK{i}", [128, CH]) for i in range(2)]
    psV = [PSa(f"psV{i}", [128, CH]) for i in range(2)]

    p.op("sync", lambda e: e.dma_start(out=eyeb[:], in_=eye_in[:]), dW, 16)
    for col, src in ((wlf, wavel), (pbf, pbias), (bqf, bq_d), (bkf, bk_d),
                     (bvf, bv_d)):
        p.op("sync", lambda e, c=col, s=src: e.dma_start(
            out=c[:], in_=bass.AP(s, 0, [[1, 128], [1, 1]])), dW, 16)
    for dc in range(FC):
        p.op("sync", lambda e, dc=dc: e.dma_start(
            out=bo8[:, dc:dc + 1],
            in_=bass.AP(bo_d, dc * 128, [[1, 128], [1, 1]])), dW, 16)
    for kc in range(FC):
        p.op("sync", lambda e, kc=kc: e.dma_start(
            out=r_sb[:, kc * 8:(kc + 1) * 8],
            in_=router[kc * 128:(kc + 1) * 128, :]), dW, 16)
    small_loads = dW.n

    p.op("vector", lambda e: e.memset(ones_b[:], 1.0))
    p.op("vector", lambda e: e.memset(ones_f[:], 1.0))
    p.op("vector", lambda e: e.memset(c2pi[:], TWO_PI))
    p.op("vector", lambda e: e.memset(ceps[:], 1e-5))
    p.op("vector", lambda e: e.memset(cnhalf[:], -0.5))
    p.op("vector", lambda e: e.memset(cnone[:], -1.0))
    p.op("vector", lambda e: e.memset(cinvsq[:], INV_SQRT_H))
    p.op("vector", lambda e: e.memset(cshift[:], EXP_SHIFT))
    p.wait("vector", dW, small_loads)
    p.op("vector", lambda e: e.tensor_scalar(tcol[:], wlf[:], 1e-8, None,
                                             ALU.add))
    vdrain()
    p.op("vector", lambda e: e.reciprocal(invw[:], tcol[:]))
    vdrain()
    p.op("vector", lambda e: e.tensor_scalar(s1c[:], invw[:], 1.0 / TWO_PI,
                                             None, ALU.mult))
    for bcol, s2 in ((bqf, s2q), (bkf, s2k)):
        p.op("vector", lambda e, b=bcol, s=s2: e.tensor_mul(s[:], b[:],
                                                            invw[:]))
        vdrain()
        p.op("vector", lambda e, s=s2: e.tensor_add(s[:], s[:], pbf[:]))
        vdrain()
        p.op("vector", lambda e, s=s2: e.tensor_scalar(
            s[:], s[:], 1.0 / TWO_PI, ROUND_OFS, ALU.mult, ALU.add))
    p.op("vector", lambda e: e.tensor_copy(bvb[:], bvf[:]), sV, 1)

    for wi, (wsrc, wdst) in enumerate(((wqT, wqTb), (wkT, wkTb), (wvT, wvTb),
                                       (woT, woTb))):
        p.wait("sync", sV, sV.n)
        for kc in range(FC):
            if wi < 3:
                src = wsrc[kc * 128:(kc + 1) * 128, :]
            else:
                src = wsrc[:, kc * 128:(kc + 1) * 128]
            p.op("sync", lambda e, s=src, kc=kc: e.dma_start(
                out=wstage[:, kc * 128:(kc + 1) * 128], in_=s), dW, 16)
        p.wait("vector", dW, dW.n)
        p.op("vector", lambda e, d=wdst: e.tensor_copy(d[:], wstage[:]), sV, 1)
    wcast_v = sV.n

    p.wait("tensor", sV, wcast_v)
    for wsb, col in ((wqTb, sqc), (wkTb, skc), (wvTb, svc)):
        for kc in range(FC):
            p.op("tensor", lambda e, w=wsb, kc=kc: e.matmul(
                psMean[:, 0:1], w[:, kc * 128:(kc + 1) * 128], ones_b[:, 0:1],
                start=(kc == 0), stop=(kc == FC - 1)),
                sT if kc == FC - 1 else None, 1)
        p.wait("vector", sT, sT.n)
        p.op("vector", lambda e, c=col: e.tensor_copy(c[:], psMean[:, 0:1]),
             sV, 1)
        p.wait("tensor", sV, sV.n)
    for dc in range(FC):
        p.op("tensor", lambda e, dc=dc: e.matmul(
            psSq[:, dc:dc + 1], woTb[:, dc * 128:(dc + 1) * 128], bvb[:, 0:1],
            start=True, stop=True), sT if dc == FC - 1 else None, 1)
    p.wait("vector", sT, sT.n)
    p.op("vector", lambda e: e.tensor_add(boP[:], psSq[:, 0:FC], bo8[:]),
         sV, 1)
    p.op("vector", lambda e: e.tensor_copy(r_hi[:], r_sb[:]))
    vdrain()
    p.op("vector", lambda e: e.tensor_sub(r_lo[:], r_sb[:], r_hi[:]), sV, 1)
    p.wait("tensor", sV, sV.n)
    for pas, rp in ((0, r_hi), (1, r_lo)):
        for kc in range(FC):
            p.op("tensor", lambda e, kc=kc, rp=rp, pas=pas: e.matmul(
                psMean[:, 0:8], ones_b[:], rp[:, kc * 8:(kc + 1) * 8],
                start=(kc == 0 and pas == 0),
                stop=(kc == FC - 1 and pas == 1)),
                sT if (kc == FC - 1 and pas == 1) else None, 1)
    p.wait("vector", sT, sT.n)
    p.op("vector", lambda e: e.tensor_copy(sr_rep[:], psMean[:, 0:8]), sV, 1)
    prol_v = sV.n

    x_loads = [0] * NCH
    cast_v = [0] * NCH
    cons_v = [0] * NCH
    qkv_t = [0] * NCH
    stats_used_v = 0

    for c in range(NCH):
        par = c % 2
        if c >= 2:
            p.wait("sync", sV, cons_v[c - 2])
        for fc in range(FC):
            p.op("sync", lambda e, c=c, fc=fc, par=par: e.dma_start(
                out=xfA[par][:, fc * CH:(fc + 1) * CH],
                in_=x_t[fc * 128:(fc + 1) * 128, c * CH:(c + 1) * CH]),
                dW, 16)
        x_loads[c] = dW.n

        p.wait("vector", dW, x_loads[c])
        if c >= 2:
            p.wait("vector", sT, qkv_t[c - 2])
        p.op("vector", lambda e, par=par: e.tensor_copy(xbA[par][:],
                                                        xfA[par][:]))
        p.op("vector", lambda e, par=par: e.tensor_mul(
            xsqA[par][:], xbA[par][:], xbA[par][:]), sV, 1)
        cast_v[c] = sV.n

        p.wait("tensor", sV, cast_v[c])
        p.wait("tensor", sV, stats_used_v if c > 0 else prol_v)
        for kc in range(FC):
            p.op("tensor", lambda e, par=par, kc=kc: e.matmul(
                psMean[:], ones_b[:], xbA[par][:, kc * CH:(kc + 1) * CH],
                start=(kc == 0), stop=(kc == FC - 1)))
        for kc in range(FC):
            p.op("tensor", lambda e, par=par, kc=kc: e.matmul(
                psSq[:], ones_b[:], xsqA[par][:, kc * CH:(kc + 1) * CH],
                start=(kc == 0), stop=(kc == FC - 1)),
                sT if kc == FC - 1 else None, 1, fast=True)
        stats_t = sT.n

        if c >= 2:
            p.wait("tensor", sV, cons_v[c - 2])
        for wsb, dst, last in ((wqTb, psQ[par], False), (wkTb, psK[par], False),
                               (wvTb, psV[par], True)):
            for kc in range(FC):
                p.op("tensor", lambda e, w=wsb, d=dst, par=par, kc=kc:
                     e.matmul(
                         d[:], w[:, kc * 128:(kc + 1) * 128],
                         xbA[par][:, kc * CH:(kc + 1) * CH],
                         start=(kc == 0), stop=(kc == FC - 1)),
                     sT if (kc == FC - 1 and last) else None, 1, fast=True)
        qkv_t[c] = sT.n

        p.wait("vector", sT, stats_t)
        p.op("vector", lambda e: e.tensor_scalar(stat1[:], psMean[:], 1.0 / D,
                                                 None, ALU.mult))
        p.op("vector", lambda e: e.tensor_scalar(stat2[:], psSq[:], 1.0 / D,
                                                 None, ALU.mult))
        p.op("vector", lambda e: e.scalar_tensor_tensor(
            ga[:], stat1[:], -1.0, stat1[:], ALU.mult, ALU.mult))
        p.op("vector", lambda e: e.scalar_tensor_tensor(
            gb[:], stat2[:], 1e-5, ga[:], ALU.add, ALU.add), sV, 1)
        stats_used_v = sV.n
        p.wait("scalar", sV, stats_used_v)
        p.op("scalar", lambda e: e.activation(ga[:], gb[:], AF.Ln), sA, 1)
        p.wait("vector", sA, sA.n)
        p.op("vector", lambda e: e.tensor_scalar(
            gb[:], ga[:], -0.5, None, ALU.mult), sV, 1)
        p.wait("scalar", sV, sV.n)
        p.op("scalar", lambda e: e.activation(rrep[:], gb[:], AF.Exp), sA, 1)
        p.wait("vector", sA, sA.n)
        p.op("vector", lambda e: e.scalar_tensor_tensor(
            nmr[:], stat1[:], -1.0, rrep[:], ALU.mult, ALU.mult))

        p.wait("vector", sT, qkv_t[c])
        sl = slice(c * CH, (c + 1) * CH)
        for psX, s2x, sxc, us_f, uc_f in ((psQ[par], s2q, sqc, usq, ucq),
                                          (psK[par], s2k, skc, usk, uck)):
            p.op("vector", lambda e, psX=psX: e.tensor_mul(ga[:], psX[:],
                                                           rrep[:]))
            p.op("vector", lambda e, sxc=sxc: e.scalar_tensor_tensor(
                gb[:], nmr[:], sxc[:], ga[:], ALU.mult, ALU.add))
            p.op("vector", lambda e, s2x=s2x: e.tensor_scalar(
                ga[:], gb[:], s1c[:], s2x[:], ALU.mult, ALU.add))
            p.op("vector", lambda e: e.tensor_copy(gi[:], ga[:]))
            p.op("vector", lambda e: e.tensor_copy(gb[:], gi[:]))
            p.op("vector", lambda e: e.tensor_sub(ga[:], ga[:], gb[:]))
            p.op("vector", lambda e: e.tensor_scalar(gb[:], ga[:], 0.5, None,
                                                     ALU.is_gt))
            p.op("vector", lambda e: e.scalar_tensor_tensor(
                gc[:], gb[:], -1.0, ga[:], ALU.mult, ALU.add))
            p.op("vector", lambda e, u=us_f, sl=sl: e.tensor_scalar(
                u[:, sl], gc[:], TWO_PI, None, ALU.mult))
            p.op("vector", lambda e: e.tensor_scalar(
                ga[:], gc[:], 0.25, None, ALU.add))
            p.op("vector", lambda e: e.tensor_scalar(gb[:], ga[:], 0.5, None,
                                                     ALU.is_gt))
            p.op("vector", lambda e: e.scalar_tensor_tensor(
                gc[:], gb[:], -1.0, ga[:], ALU.mult, ALU.add))
            p.op("vector", lambda e, u=uc_f, sl=sl: e.tensor_scalar(
                u[:, sl], gc[:], TWO_PI, None, ALU.mult), sV, 1)
        p.op("vector", lambda e, par=par: e.tensor_mul(ga[:], psV[par][:],
                                                       rrep[:]))
        p.op("vector", lambda e, sl=sl: e.scalar_tensor_tensor(
            vf[:, sl], nmr[:], svc[:], ga[:], ALU.mult, ALU.add), sV, 1)
        cons_v[c] = sV.n

    p.wait("scalar", sV, cons_v[NCH - 1])
    for src, dst in ((usq, sinq), (ucq, cosq), (usk, sink), (uck, cosk)):
        p.op("scalar", lambda e, s=src, d=dst: e.activation(
            d[:], s[:], AF.Sin), sA, 1)
    p.wait("vector", sA, sA.n)
    for t in (cosq, sinq):
        p.op("vector", lambda e, t=t: e.tensor_scalar(
            t[:], t[:], INV_SQRT_H, None, ALU.mult), sV, 1)
    sinscale_v = sV.n
    trig_a = sA.n
    if dbg:
        p.wait("gpsimd", sA, trig_a)
        p.wait("gpsimd", sV, sinscale_v)
        for n, t in (("cosq", cosq), ("sinq", sinq), ("cosk", cosk),
                     ("sink", sink), ("vf", vf)):
            p.op("gpsimd", lambda e, n=n, t=t: e.dma_start(
                out=dbg_t[n][:], in_=t[:]), dG, 16)
        for n, t in (("usq", usq), ("ucq", ucq)):
            p.op("gpsimd", lambda e, n=n, t=t: e.dma_start(
                out=dbg_us[n][:], in_=t[:]), dG, 16)
        p.wait("gpsimd", dG, dG.n)

    with nc.Block() as blk:
        p.flush(blk)
    esA.close()

    esB = contextlib.ExitStack()
    v_tm = esB.enter_context(nc.sbuf_tensor("v_tm", [128, S], BF16))

    esB0 = contextlib.ExitStack()
    psTrV = [esB0.enter_context(nc.psum_tensor(f"psTrV{i}", [128, 128], BF16))
             for i in range(2)]
    tr_cons = [0, 0]
    for kt in range(S // 128):
        par = kt % 2
        if kt >= 2:
            p.wait("tensor", sV, tr_cons[par])
        p.op("tensor", lambda e, kt=kt, par=par: e.transpose(
            psTrV[par][:], vf[:, kt * 128:(kt + 1) * 128], eyeb[:]), sT, 1)
        p.wait("vector", sT, sT.n)
        p.op("vector", lambda e, kt=kt, par=par: e.tensor_copy(
            v_tm[:, kt * 128:(kt + 1) * 128], psTrV[par][:]), sV, 1)
        tr_cons[par] = sV.n
    with nc.Block() as blk:
        p.flush(blk)
    esB0.close()

    esB1 = contextlib.ExitStack()

    def SBb(name, shape, dt):
        return esB1.enter_context(nc.sbuf_tensor(name, shape, dt))

    def PSb(name, shape, dt=F32):
        return esB1.enter_context(nc.psum_tensor(name, shape, dt))

    QC = 512
    NQ = S // QC
    NKT = S // 128
    xfB = [SBb(f"xfB{i}", [128, FC * QC], F32) for i in range(2)]
    wstgB = [SBb(f"wstgB{i}", [128, 5120], F32) for i in range(2)]
    wcastB = [SBb(f"wcastB{i}", [128, 5120], BF16) for i in range(2)]
    ptb = [SBb(f"ptb{i}", [128, 1024], BF16) for i in range(2)]
    attnT = [SBb(f"attnT{i}", [128, QC], BF16) for i in range(2)]
    lnS = [SBb(f"lnS{i}", [128, QC], F32) for i in range(2)]
    rcp = [SBb(f"rcp{i}", [128, QC], F32) for i in range(2)]
    psSc = [PSb(f"psSc{i}", [128, 1024]) for i in range(2)]
    psPV = [PSb(f"psPV{i}", [128, QC]) for i in range(2)]
    psSm = [PSb(f"psSm{i}", [128, QC]) for i in range(2)]

    wpieces = []
    for pe in range(NE):
        for gi_ in range(2):
            wpieces.append((gup[pe, gi_], gup_sc[pe, gi_], FC * DE))
    for dc in range(FC):
        wpieces.append((dwn[dc], dwn_sc[dc], NE * 4 * 128))
    wp_i = [0]
    wcast_free = [0, 0]
    wcast_wr = [0, 0]

    def emit_wcast_round(nmax=4):
        for _ in range(nmax):
            if wp_i[0] >= len(wpieces):
                return
            i = wp_i[0]
            wp_i[0] += 1
            par = i % 2
            srcf, dstb, ln = wpieces[i]
            p.wait("sync", sV, wcast_free[par])
            p.op("sync", lambda e, s=srcf, L=ln, par=par: e.dma_start(
                out=wstgB[par][:, 0:L], in_=s[:, 0:L]), dW, 16)
            ld = dW.n
            p.wait("vector", dW, ld)
            p.wait("vector", dG, wcast_wr[par])
            p.op("vector", lambda e, L=ln, par=par: e.tensor_copy(
                wcastB[par][:, 0:L], wstgB[par][:, 0:L]), sV, 1)
            wcast_free[par] = sV.n
            p.wait("gpsimd", sV, wcast_free[par])
            p.op("gpsimd", lambda e, d=dstb, L=ln, par=par: e.dma_start(
                out=d[:, 0:L], in_=wcastB[par][:, 0:L]), dG, 16)
            wcast_wr[par] = dG.n

    xB_loads = [0] * NQ
    x2_st = [0] * NQ
    wo_ev_all = [0] * NQ
    prev_exp = {0: 0, 1: 0}
    prev_spv = {0: 0, 1: 0}
    attn_overwrite_t = {0: 0, 1: 0}

    for q in range(NQ):
        qpar = q % 2
        qs = slice(q * QC, (q + 1) * QC)
        if q >= 2:
            p.wait("gpsimd", dG, x2_st[q - 2])
        for fc in range(FC):
            p.op("gpsimd", lambda e, q=q, fc=fc, qpar=qpar: e.dma_start(
                out=xfB[qpar][:, fc * QC:(fc + 1) * QC],
                in_=x_t[fc * 128:(fc + 1) * 128, q * QC:(q + 1) * QC]),
                dG, 16)
        xB_loads[q] = dG.n
        emit_wcast_round(4)

        if q >= 1:
            p.wait("tensor", sV, wo_ev_all[q - 1])
        exp_a = [0] * (NKT // 2)
        spv_t = [0] * (NKT // 2)
        for pr in range(NKT // 2):
            par = pr % 2
            p.wait("tensor", sA, exp_a[pr - 2] if pr >= 2 else prev_exp[par])
            for half in range(2):
                kt = pr * 2 + half
                ks = slice(kt * 128, (kt + 1) * 128)
                hs = slice(half * 512, (half + 1) * 512)
                p.op("tensor", lambda e, ks=ks, hs=hs, par=par, qs=qs:
                     e.matmul(psSc[par][:, hs], cosk[:, ks], cosq[:, qs],
                              start=True, stop=False))
                p.op("tensor", lambda e, ks=ks, hs=hs, par=par, qs=qs:
                     e.matmul(psSc[par][:, hs], sink[:, ks], sinq[:, qs],
                              start=False, stop=True),
                     sT if half == 1 else None, 1, fast=True)
            sc_t = sT.n
            p.wait("scalar", sT, sc_t)
            p.wait("scalar", sT, spv_t[pr - 2] if pr >= 2 else prev_spv[par])
            p.op("scalar", lambda e, par=par: e.activation(
                ptb[par][:], psSc[par][:], AF.Exp), sA, 1)
            exp_a[pr] = sA.n
            p.wait("tensor", sA, exp_a[pr])
            for half in range(2):
                kt = pr * 2 + half
                hs = slice(half * 512, (half + 1) * 512)
                p.op("tensor", lambda e, par=par, hs=hs, qpar=qpar, kt=kt:
                     e.matmul(psSm[qpar][:], ones_b[:], ptb[par][:, hs],
                              start=(kt == 0), stop=(kt == NKT - 1)))
                p.op("tensor", lambda e, par=par, hs=hs, qpar=qpar, kt=kt:
                     e.matmul(psPV[qpar][:], v_tm[:, kt * 128:(kt + 1) * 128],
                              ptb[par][:, hs],
                              start=(kt == 0), stop=(kt == NKT - 1)),
                     sT if kt == NKT - 1 else None, 1, fast=True)
            spv_t[pr] = sT.n
        prev_exp = {0: exp_a[NKT // 2 - 2], 1: exp_a[NKT // 2 - 1]}
        prev_spv = {0: spv_t[NKT // 2 - 2], 1: spv_t[NKT // 2 - 1]}
        pv_t = sT.n

        p.wait("scalar", sT, pv_t)
        p.op("scalar", lambda e, qpar=qpar: e.activation(
            lnS[qpar][:], psSm[qpar][:], AF.Ln), sA, 1)
        p.wait("vector", sA, sA.n)
        p.op("vector", lambda e, qpar=qpar: e.tensor_scalar(
            lnS[qpar][:], lnS[qpar][:], -1.0, None, ALU.mult), sV, 1)
        p.wait("scalar", sV, sV.n)
        p.op("scalar", lambda e, qpar=qpar: e.activation(
            rcp[qpar][:], lnS[qpar][:], AF.Exp), sA, 1)
        rcp_a = sA.n
        p.wait("vector", sA, rcp_a)
        p.wait("vector", sT, attn_overwrite_t[qpar])
        p.op("vector", lambda e, qpar=qpar: e.tensor_mul(
            attnT[qpar][:], psPV[qpar][:], rcp[qpar][:]), sV, 1)
        attn_v = sV.n

        p.wait("tensor", sV, attn_v)
        p.wait("vector", dG, xB_loads[q])
        wo_ev = [0] * FC
        for dc in range(FC):
            wo_ps = psPV[1 - qpar] if dc % 2 == 0 else psSm[qpar]
            if dc >= 2:
                p.wait("tensor", sV, wo_ev[dc - 2])
            p.op("tensor", lambda e, dc=dc, ps_=wo_ps, qpar=qpar: e.matmul(
                ps_[:], woTb[:, dc * 128:(dc + 1) * 128], attnT[qpar][:],
                start=True, stop=True), sT, 1, fast=True)
            mm_t = sT.n
            p.wait("vector", sT, mm_t)
            p.op("vector", lambda e, dc=dc, ps_=wo_ps, qpar=qpar:
                 e.scalar_tensor_tensor(
                     xfB[qpar][:, dc * QC:(dc + 1) * QC], ps_[:],
                     boP[:, dc:dc + 1],
                     xfB[qpar][:, dc * QC:(dc + 1) * QC],
                     ALU.add, ALU.add), sV, 1)
            wo_ev[dc] = sV.n
            if dc == FC - 1:
                attn_overwrite_t[qpar] = mm_t
        wo_ev_all[q] = sV.n
        p.wait("gpsimd", sV, wo_ev_all[q])
        for fc in range(FC):
            p.op("gpsimd", lambda e, q=q, fc=fc, qpar=qpar: e.dma_start(
                out=x2_hbm[fc * 128:(fc + 1) * 128, q * QC:(q + 1) * QC],
                in_=xfB[qpar][:, fc * QC:(fc + 1) * QC]), dG, 16)
        x2_st[q] = dG.n

    emit_wcast_round(99)
    all_scratch_dG = dG.n
    with nc.Block() as blk:
        p.flush(blk)
    esB1.close()
    esB.close()
    esAB.close()

    esC = contextlib.ExitStack()

    def SBc(name, shape, dt):
        return esC.enter_context(nc.sbuf_tensor(name, shape, dt))

    def PSc(name, shape, dt=F32):
        return esC.enter_context(nc.psum_tensor(name, shape, dt))

    CC = 256
    NCC = SUB // CC
    htil = SBc("htil", [128, NE * SUB * 4], BF16)
    ln2b = SBc("ln2b", [128, FC * SUB], BF16)
    x2f = [SBc(f"x2f{i}", [128, FC * CC], F32) for i in range(2)]
    xb2 = SBc("xb2", [128, FC * CC], BF16)
    x2lo = SBc("x2lo", [128, FC * CC], BF16)
    xsq2 = SBc("xsq2", [128, FC * CC], BF16)
    st1C = SBc("st1C", [128, CC], F32)
    st2C = SBc("st2C", [128, CC], F32)
    rrepC = SBc("rrepC", [128, CC], F32)
    nmrC = SBc("nmrC", [128, CC], F32)
    gaC = SBc("gaC", [128, CC], F32)
    gbC = SBc("gbC", [128, CC], F32)
    coefT = SBc("coefT", [8, SUB], BF16)
    sel8 = SBc("sel8", [8, 8 * 128], BF16)
    c8m = SBc("c8m", [128, 8], F32)
    c8q = SBc("c8q", [128, 8], F32)
    c8a = SBc("c8a", [128, 8], F32)
    c8b = SBc("c8b", [128, 8], F32)
    c8c = SBc("c8c", [128, 8], F32)
    rstd8 = SBc("rstd8", [128, 8], F32)
    nmr8 = SBc("nmr8", [128, 8], F32)
    r64a = SBc("r64a", [128, 64], F32)
    r64b = SBc("r64b", [128, 64], F32)
    coefb64 = SBc("coefb64", [128, 64], BF16)
    wslot = [SBc(f"wslot{i}", [128, FC * DE], BF16) for i in range(2)]
    wstgC = SBc("wstgC", [128, 1536], F32)
    dslot = [SBc(f"dslot{i}", [128, NE * 4 * 128], BF16) for i in range(2)]
    gsall = SBc("gsall", [128, 8 * 512], BF16)
    t2b = [SBc("t2b0", [128, 512], F32)] * 2
    x2r = [SBc(f"x2r{i}", [128, 512], F32) for i in range(2)]


    psG = [PSc(f"psG{i}", [128, 512]) for i in range(2)]
    psU = [PSc(f"psU{i}", [128, 512]) for i in range(2)]
    psD = [PSc(f"psD{i}", [128, 512]) for i in range(2)]
    psM = PSc("psM", [128, 512])
    psTr = PSc("psTr", [8, 128], BF16)

    wstg_free_v = 0
    slot_free_t = [0, 0, 0]
    slot_scr_dG = [all_scratch_dG] * 3
    dslot_free_t = [0, 0]
    dslot_scr_dG = [all_scratch_dG, all_scratch_dG]
    x2f_cons = [0, 0]
    x2f_cons_v = [0, 0]
    x2r_free = [0, 0]
    ln2b_free_t = 0
    psM_free_v = 0
    psD_free_v = 0
    gu_prev_silu = {0: 0, 1: 0}
    up_prev_ev = {0: 0, 1: 0}
    prev_up_ev_all = {i: 0 for i in range(8)}

    def fetch_piece(dst_sb, src_f32, src_bf, sub, length, slot_free_t,
                    scr_done_dG, src_off=0):
        nonlocal wstg_free_v
        if sub == 0:
            nh = (length + 1535) // 1536
            for hh in range(nh):
                a = hh * 1536
                b = min(length, a + 1536)
                p.wait("sync", sV, wstg_free_v)
                p.op("sync", lambda e, s=src_f32, a=a, b=b, o=src_off:
                     e.dma_start(out=wstgC[:, 0:b - a],
                                 in_=s[:, o + a:o + b]), dW, 16)
                ld = dW.n
                p.wait("vector", dW, ld)
                if hh == 0:
                    p.wait("vector", sT, slot_free_t)
                    p.wait("vector", dG, scr_done_dG)
                p.op("vector", lambda e, d=dst_sb, a=a, b=b: e.tensor_copy(
                    d[:, a:b], wstgC[:, 0:b - a]), sV, 1)
                wstg_free_v = sV.n
            p.wait("gpsimd", sV, wstg_free_v)
            p.op("gpsimd", lambda e, d=dst_sb, s=src_bf, L=length,
                 o=src_off: e.dma_start(out=s[:, o:o + L], in_=d[:, 0:L]),
                 dG, 16)
            return sV, wstg_free_v, dG.n
        p.wait("sync", sT, slot_free_t)
        p.wait("sync", dG, scr_done_dG)
        p.op("sync", lambda e, d=dst_sb, s=src_bf, L=length, o=src_off:
             e.dma_start(out=d[:, 0:L], in_=s[:, o:o + L]), dW, 16)
        return dW, dW.n, scr_done_dG

    p.op("sync", lambda e: e.dma_start(out=sel8[:], in_=sel_in[:]), dW, 16)
    p.wait("tensor", dW, dW.n)
    p.op("vector", lambda e: e.tensor_copy(r_hi[:], r_sb[:]))
    vdrain()
    p.op("vector", lambda e: e.tensor_sub(r_lo[:], r_sb[:], r_hi[:]), sV, 1)
    p.wait("tensor", sV, sV.n)

    def d3(h_):
        return bass.AP(h_, 0, [[64, 128], [8, 8], [1, 8]])

    def segb(h_):
        return bass.AP(h_, 0, [[8, 128], [1, 8], [0, 8]])

    def tileb(h_):
        return bass.AP(h_, 0, [[8, 128], [0, 8], [1, 8]])


    chunk_last_t = 0
    stats_read_v = 0
    cast2_v = 0
    cast_a = 0
    prev_down_last_v = 0
    ln2b_sub_v = 0
    coef_cast_v = 0
    coef_ready_v = 0
    psm_drain_t = 0
    pend_g = None
    pend_u = None

    def emit_casts(sub, h):
        nonlocal cast2_v, cast_a
        par = h % 2
        t0 = sub * SUB + h * CC
        p.wait("gpsimd", sT, x2f_cons[par])
        p.wait("gpsimd", sV, x2f_cons_v[par])
        for fc in range(FC):
            p.op("gpsimd", lambda e, fc=fc, t0=t0, par=par: e.dma_start(
                out=x2f[par][:, fc * CC:(fc + 1) * CC],
                in_=x2_hbm[fc * 128:(fc + 1) * 128, t0:t0 + CC]), dG, 16)
        x2_ld = dG.n
        p.wait("scalar", dG, x2_ld)
        p.wait("scalar", sT, chunk_last_t)
        p.op("scalar", lambda e, par=par: e.activation(
            xb2[:], x2f[par][:], AF.Copy))
        p.op("scalar", lambda e: e.activation(xsq2[:], xb2[:], AF.Square),
             sA, 1)
        cast_a = sA.n
        p.wait("vector", sA, cast_a)
        p.op("vector", lambda e, par=par: e.tensor_sub(
            x2lo[:], x2f[par][:], xb2[:]), sV, 1)
        cast2_v = sV.n

    def emit_stats_rest(sub, h, psX):
        nonlocal stats_read_v
        p.wait("tensor", sA, cast_a)
        p.wait("tensor", sV, stats_read_v)
        if psX is psD:
            p.wait("tensor", sV, prev_down_last_v)
        else:
            p.wait("tensor", sA, max(gu_prev_silu.values()))
        for kc in range(FC):
            p.op("tensor", lambda e, kc=kc, psX=psX: e.matmul(
                psX[0][:, 0:CC], ones_b[:], xb2[:, kc * CC:(kc + 1) * CC],
                start=(kc == 0), stop=(kc == FC - 1)))
        for kc in range(FC):
            p.op("tensor", lambda e, kc=kc, psX=psX: e.matmul(
                psX[1][:, 0:CC], ones_b[:], xsq2[:, kc * CC:(kc + 1) * CC],
                start=(kc == 0), stop=(kc == FC - 1)),
                sT if kc == FC - 1 else None, 1, fast=True)
        stats_mm_done = sT.n
        p.wait("vector", sT, stats_mm_done)
        p.op("vector", lambda e, psX=psX: e.tensor_scalar(
            st1C[:], psX[0][:, 0:CC], 1.0 / D, None, ALU.mult))
        p.op("vector", lambda e, psX=psX: e.tensor_scalar(
            st2C[:], psX[1][:, 0:CC], 1.0 / D, None, ALU.mult))
        p.op("vector", lambda e: e.scalar_tensor_tensor(
            gaC[:], st1C[:], -1.0, st1C[:], ALU.mult, ALU.mult))
        p.op("vector", lambda e: e.scalar_tensor_tensor(
            gbC[:], st2C[:], 1e-5, gaC[:], ALU.add, ALU.add), sV, 1)
        stats_read_v = sV.n
        p.wait("scalar", sV, stats_read_v)
        p.op("scalar", lambda e: e.activation(gaC[:], gbC[:], AF.Ln),
             sA, 1)
        p.wait("vector", sA, sA.n)
        p.op("vector", lambda e: e.tensor_scalar(
            gbC[:], gaC[:], -0.5, None, ALU.mult), sV, 1)
        p.wait("scalar", sV, sV.n)
        p.op("scalar", lambda e: e.activation(rrepC[:], gbC[:], AF.Exp),
             sA, 1)
        p.wait("vector", sA, sA.n)
        p.op("vector", lambda e: e.scalar_tensor_tensor(
            nmrC[:], st1C[:], -1.0, rrepC[:], ALU.mult, ALU.mult))

    def emit_ln2b(sub, h):
        nonlocal ln2b_sub_v
        par = h % 2
        p.wait("vector", sT, ln2b_free_t if (sub > 0 and h == 0) else 0)
        for fc in range(FC):
            p.op("vector", lambda e, fc=fc, par=par: e.tensor_mul(
                gaC[:], x2f[par][:, fc * CC:(fc + 1) * CC], rrepC[:]))
            p.op("vector", lambda e, fc=fc, h=h: e.tensor_add(
                ln2b[:, fc * SUB + h * CC:fc * SUB + (h + 1) * CC],
                gaC[:], nmrC[:]),
                sV if fc == FC - 1 else None, 1)
        ln2b_sub_v = sV.n
        x2f_cons_v[par] = sV.n

    def emit_router_mms(sub, h):
        nonlocal chunk_last_t
        par = h % 2
        p.wait("tensor", sV, cast2_v)
        for tt in range(CC // 128):
            ti = h * 2 + tt
            tsl = slice(tt * 128, (tt + 1) * 128)
            if ti == 0:
                p.wait("tensor", sV, psM_free_v)
            for kc in range(FC):
                p.op("tensor", lambda e, kc=kc, tsl=tsl, ti=ti: e.matmul(
                    psM[:, 64 + ti:65 + ti],
                    xb2[:, kc * CC:(kc + 1) * CC][:, tsl],
                    ones_b[:, 0:1],
                    start=(kc == 0), stop=(kc == FC - 1)))
            for kc in range(FC):
                p.op("tensor", lambda e, kc=kc, tsl=tsl, ti=ti: e.matmul(
                    psM[:, 72 + ti:73 + ti],
                    xsq2[:, kc * CC:(kc + 1) * CC][:, tsl],
                    ones_b[:, 0:1],
                    start=(kc == 0), stop=(kc == FC - 1)))
            for pas, xs, rp in ((0, xb2, r_hi), (1, xb2, r_lo),
                                (2, x2lo, r_hi)):
                for kc in range(FC):
                    p.op("tensor", lambda e, kc=kc, tsl=tsl, xs=xs,
                         rp=rp, pas=pas, ti=ti: e.matmul(
                             psM[:, ti * 8:(ti + 1) * 8],
                             xs[:, kc * CC:(kc + 1) * CC][:, tsl],
                             rp[:, kc * 8:(kc + 1) * 8],
                             start=(kc == 0 and pas == 0),
                             stop=(kc == FC - 1 and pas == 2)),
                         sT if (kc == FC - 1 and pas == 2) else None, 1,
                         fast=True)
        x2f_cons[par] = sT.n
        chunk_last_t = sT.n

    def emit_router_chain(sub):
        nonlocal psm_drain_t, psM_free_v, coef_cast_v
        p.op("tensor", lambda e: e.drain(fusable=False), sT, 1, fast=True)
        psm_drain_t = sT.n
        p.wait("vector", sT, psm_drain_t)
        p.op("vector", lambda e: e.tensor_scalar(
            c8m[:], psM[:, 64:72], 1.0 / D, None, ALU.mult))
        p.op("vector", lambda e: e.tensor_scalar(
            c8q[:], psM[:, 72:80], 1.0 / D, None, ALU.mult))
        vdrain()
        p.op("vector", lambda e: e.scalar_tensor_tensor(
            c8a[:], c8m[:], -1.0, c8m[:], ALU.mult, ALU.mult))
        vdrain()
        p.op("vector", lambda e: e.scalar_tensor_tensor(
            c8b[:], c8q[:], 1e-5, c8a[:], ALU.add, ALU.add), sV, 1)
        p.wait("scalar", sV, sV.n)
        p.op("scalar", lambda e: e.activation(c8a[:], c8b[:], AF.Ln), sA, 1)
        p.wait("vector", sA, sA.n)
        p.op("vector", lambda e: e.tensor_scalar(
            c8b[:], c8a[:], -0.5, None, ALU.mult), sV, 1)
        p.wait("scalar", sV, sV.n)
        p.op("scalar", lambda e: e.activation(rstd8[:], c8b[:], AF.Exp),
             sA, 1)
        p.wait("vector", sA, sA.n)
        p.op("vector", lambda e: e.scalar_tensor_tensor(
            nmr8[:], c8m[:], -1.0, rstd8[:], ALU.mult, ALU.mult))
        p.op("vector", lambda e: e.tensor_copy(r64a[:], psM[:, 0:64]), sV, 1)
        psM_free_v = sV.n
        vdrain()
        p.op("vector", lambda e: e.tensor_tensor(
            d3(r64a), d3(r64a), segb(rstd8), ALU.mult))
        p.op("vector", lambda e: e.tensor_tensor(
            d3(r64b), tileb(sr_rep), segb(nmr8), ALU.mult))
        vdrain()
        p.op("vector", lambda e: e.tensor_add(r64a[:], r64a[:], r64b[:]))
        vdrain()
        p.op("vector", lambda e: e.reduce_max(c8a[:], d3(r64a), axis=AX))
        vdrain()
        p.op("vector", lambda e: e.tensor_tensor(
            d3(r64b), d3(r64a), segb(c8a), ALU.subtract), sV, 1)
        p.wait("scalar", sV, sV.n)
        p.op("scalar", lambda e: e.activation(r64a[:], r64b[:], AF.Exp),
             sA, 1)
        p.wait("vector", sA, sA.n)
        p.op("vector", lambda e: e.reduce_sum(c8b[:], d3(r64a), axis=AX))
        vdrain()
        p.op("vector", lambda e: e.reciprocal(c8c[:], c8b[:]))
        vdrain()
        p.op("vector", lambda e: e.tensor_tensor(
            d3(r64b), d3(r64a), segb(c8c), ALU.mult))
        vdrain()
        p.op("vector", lambda e: e.reduce_max(c8a[:], d3(r64b), axis=AX))
        vdrain()
        p.op("vector", lambda e: e.tensor_tensor(
            d3(r64a), d3(r64b), segb(c8a), ALU.is_equal))
        vdrain()
        p.op("vector", lambda e: e.scalar_tensor_tensor(
            r64a[:], r64a[:], -2.0, r64b[:], ALU.mult, ALU.add))
        vdrain()
        p.op("vector", lambda e: e.reduce_max(c8b[:], d3(r64a), axis=AX))
        vdrain()
        p.op("vector", lambda e: e.tensor_tensor(
            d3(r64a), d3(r64b), segb(c8b), ALU.is_ge))
        vdrain()
        p.op("vector", lambda e: e.tensor_mul(r64a[:], r64a[:], r64b[:]))
        vdrain()
        p.op("vector", lambda e: e.tensor_copy(coefb64[:], r64a[:]), sV, 1)
        coef_cast_v = sV.n

    def emit_coef_transpose(sub):
        nonlocal coef_ready_v
        p.wait("tensor", sV, coef_cast_v)
        for ti in range(8):
            p.op("tensor", lambda e, ti=ti: e.transpose(
                psTr[:], coefb64[:, ti * 8:(ti + 1) * 8], eyeb[:]), sT, 1)
            p.wait("vector", sT, sT.n)
            p.op("vector", lambda e, ti=ti: e.tensor_copy(
                coefT[:, ti * 128:(ti + 1) * 128], psTr[:]), sV, 1)
            if ti < 7:
                p.wait("tensor", sV, sV.n)
        coef_ready_v = sV.n
        if dbg:
            p.wait("gpsimd", sV, coef_ready_v)
            p.op("gpsimd", lambda e, sub=sub: e.dma_start(
                out=dbg_coef[:, sub * SUB:(sub + 1) * SUB], in_=coefT[:]),
                dG, 16)
            if sub == 0:
                p.op("gpsimd", lambda e: e.dma_start(
                    out=dbg_ln2b[:], in_=ln2b[:]), dG, 16)

    for sub in range(NSUB):
        if sub == 0:
            for h in range(NCC):
                emit_casts(0, h)
                emit_stats_rest(0, h, psD if h == 0 else psG)
                emit_router_mms(0, h)
                emit_ln2b(0, h)
            emit_router_chain(0)
            emit_coef_transpose(0)


        for pe in range(NE):
            if sub + 1 < NSUB and pe == 2:
                emit_casts(sub + 1, 0)
            if sub + 1 < NSUB and pe == 8:
                emit_stats_rest(sub + 1, 0, psD)
                c0_read_v = stats_read_v
            if sub + 1 < NSUB and pe == 9:
                emit_router_mms(sub + 1, 0)
                emit_casts(sub + 1, 1)
            gslot = 0
            uslot = 1
            if pe == 0 and pend_g is not None:
                gsem, gval = pend_g
                usem, uval = pend_u
                pend_g = pend_u = None
            else:
                gsem, gval, slot_scr_dG[gslot] = fetch_piece(
                    wslot[gslot], gup[pe, 0], gup_sc[pe, 0], 1, FC * DE,
                    slot_free_t[gslot], slot_scr_dG[gslot])
                usem, uval, slot_scr_dG[uslot] = fetch_piece(
                    wslot[uslot], gup[pe, 1], gup_sc[pe, 1], 1, FC * DE,
                    slot_free_t[uslot], slot_scr_dG[uslot])
            p.wait("tensor", gsem, gval)
            if pe == 0:
                p.wait("tensor", sV, ln2b_sub_v)
            silu_a = {}
            for dcq in range(8):
                qc, dc = dcq // 4, dcq % 4
                par = dcq % 2
                p.wait("tensor", sA,
                       silu_a[dcq - 2] if dcq >= 2 else gu_prev_silu[par])
                for kc in range(FC):
                    p.op("tensor", lambda e, kc=kc, dc=dc, qc=qc, par=par,
                         gslot=gslot: e.matmul(
                             psG[par][:],
                             wslot[gslot][:, kc * DE + dc * 128:
                                          kc * DE + (dc + 1) * 128],
                             ln2b[:, kc * SUB + qc * 512:
                                  kc * SUB + (qc + 1) * 512],
                             start=(kc == 0), stop=(kc == FC - 1)),
                         sT if kc == FC - 1 else None, 1, fast=True)
                g_t = sT.n
                p.wait("scalar", sT, g_t)
                p.wait("scalar", sV, prev_up_ev_all[dcq])
                p.op("scalar", lambda e, dcq=dcq, par=par: e.activation(
                    gsall[:, dcq * 512:(dcq + 1) * 512], psG[par][:],
                    AF.Silu), sA, 1)
                silu_a[dcq] = sA.n
            slot_free_t[gslot] = sT.n
            gu_prev_silu = {0: silu_a[6], 1: silu_a[7]}

            p.wait("tensor", usem, uval)
            up_ev = {}
            for dcq in range(8):
                qc, dc = dcq // 4, dcq % 4
                par = dcq % 2
                if pe < 8 and dc == 0:
                    p.wait("tensor", sV, psM_free_v)
                    p.wait("tensor", sV, coef_ready_v)
                    p.op("tensor", lambda e, pe=pe, qc=qc: e.matmul(
                        psM[:], sel8[:, pe * 128:(pe + 1) * 128],
                        coefT[:, qc * 512:(qc + 1) * 512],
                        start=True, stop=True))
                p.wait("tensor", sV,
                       up_ev[dcq - 2] if dcq >= 2 else up_prev_ev[par])
                for kc in range(FC):
                    p.op("tensor", lambda e, kc=kc, dc=dc, qc=qc, par=par,
                         uslot=uslot: e.matmul(
                             psU[par][:],
                             wslot[uslot][:, kc * DE + dc * 128:
                                          kc * DE + (dc + 1) * 128],
                             ln2b[:, kc * SUB + qc * 512:
                                  kc * SUB + (qc + 1) * 512],
                             start=(kc == 0), stop=(kc == FC - 1)),
                         sT if kc == FC - 1 else None, 1, fast=True)
                u_t = sT.n
                p.wait("vector", sT, u_t)
                p.wait("vector", sA, silu_a[dcq])
                hpos = pe * 4 * SUB + dc * SUB + qc * 512
                if pe < 8:
                    p.op("vector", lambda e, dcq=dcq, par=par: e.tensor_mul(
                        t2b[par][:], psU[par][:],
                        gsall[:, dcq * 512:(dcq + 1) * 512]))
                    p.op("vector", lambda e, hpos=hpos, par=par:
                         e.tensor_mul(htil[:, hpos:hpos + 512], t2b[par][:],
                                      psM[:]), sV, 1)
                else:
                    p.op("vector", lambda e, dcq=dcq, par=par, hpos=hpos:
                         e.tensor_mul(htil[:, hpos:hpos + 512], psU[par][:],
                                      gsall[:, dcq * 512:(dcq + 1) * 512]),
                         sV, 1)
                up_ev[dcq] = sV.n
                if pe < 8 and dc == 3:
                    psM_free_v = sV.n
            slot_free_t[uslot] = sT.n
            up_prev_ev = {0: up_ev[6], 1: up_ev[7]}
            prev_up_ev_all = dict(up_ev)
        htil_v = sV.n

        if dbg and sub == 0:
            p.wait("gpsimd", sV, htil_v)
            p.op("gpsimd", lambda e: e.dma_start(
                out=dbg_htil[:], in_=htil[:, 0:2 * SUB * 4]), dG, 16)
        ln2b_free_t = sT.n
        p.wait("tensor", sV, htil_v)
        dev_v = {}
        nxt = sub + 1
        for dc in range(FC):
            dpar = dc % 2
            dsem, dval, dslot_scr_dG[dpar] = fetch_piece(
                dslot[dpar], dwn[dc], dwn_sc[dc], 1, NE * 4 * 128,
                dslot_free_t[dpar], dslot_scr_dG[dpar])
            p.wait("tensor", dsem, dval)
            for qc in range(2):
                idx = dc * 2 + qc
                par = idx % 2
                if nxt < NSUB:
                    if idx == 0:
                        emit_ln2b(nxt, 0)
                        emit_stats_rest(nxt, 1, psG)
                    elif idx == 2:
                        emit_router_mms(nxt, 1)
                        emit_ln2b(nxt, 1)
                    elif idx == 4:
                        emit_casts(nxt, 2)
                        emit_stats_rest(nxt, 2, psG)
                    elif idx == 6:
                        emit_router_mms(nxt, 2)
                        emit_ln2b(nxt, 2)
                    elif idx == 8:
                        emit_casts(nxt, 3)
                        emit_stats_rest(nxt, 3, psG)
                    elif idx == 10:
                        emit_router_mms(nxt, 3)
                        emit_ln2b(nxt, 3)
                    elif idx == 12:
                        emit_router_chain(nxt)
                    elif idx == 5:
                        pg0, pg1, slot_scr_dG[0] = fetch_piece(
                            wslot[0], gup[0, 0], gup_sc[0, 0], 1, FC * DE,
                            slot_free_t[0], slot_scr_dG[0])
                        pend_g = (pg0, pg1)
                    elif idx == 11:
                        pu0, pu1, slot_scr_dG[1] = fetch_piece(
                            wslot[1], gup[0, 1], gup_sc[0, 1], 1, FC * DE,
                            slot_free_t[1], slot_scr_dG[1])
                        pend_u = (pu0, pu1)
                p.wait("gpsimd", yst, x2r_free[par])
                p.op("gpsimd", lambda e, dc=dc, qc=qc, sub=sub, par=par:
                     e.dma_start(
                         out=x2r[par][:],
                         in_=x2_hbm[dc * 128:(dc + 1) * 128,
                                    sub * SUB + qc * 512:
                                    sub * SUB + (qc + 1) * 512]), dG, 16)
                x2r_ld = dG.n
                if idx >= 2:
                    p.wait("tensor", sV, dev_v[idx - 2])
                elif nxt < NSUB:
                    p.wait("tensor", sV, c0_read_v)
                for pe in range(NE):
                    for cc in range(4):
                        blk = pe * 4 + cc
                        first = (pe == 0 and cc == 0)
                        last = (pe == NE - 1 and cc == 3)
                        p.op("tensor",
                             lambda e, blk=blk, pe=pe, cc=cc, qc=qc, par=par,
                             dpar=dpar, first=first, last=last: e.matmul(
                                 psD[par][:],
                                 dslot[dpar][:, blk * 128:(blk + 1) * 128],
                                 htil[:, pe * 4 * SUB + cc * SUB + qc * 512:
                                      pe * 4 * SUB + cc * SUB +
                                      (qc + 1) * 512],
                                 start=first, stop=last),
                             sT if last else None, 1, fast=True)
                if qc == 1:
                    dslot_free_t[dpar] = sT.n
                dmm_t = sT.n
                p.wait("vector", sT, dmm_t)
                p.wait("vector", dG, x2r_ld)
                p.op("vector", lambda e, par=par: e.tensor_add(
                    x2r[par][:], psD[par][:], x2r[par][:]), sV, 1)
                dev_v[idx] = sV.n
                p.wait("gpsimd", sV, dev_v[idx])
                p.op("gpsimd", lambda e, dc=dc, qc=qc, sub=sub, par=par:
                     e.dma_start(
                         out=y[dc * 128:(dc + 1) * 128,
                               sub * SUB + qc * 512:
                               sub * SUB + (qc + 1) * 512],
                         in_=x2r[par][:]), yst, 16)
                x2r_free[par] = yst.n
        prev_down_last_v = dev_v[2 * FC - 1]
        if nxt < NSUB:
            emit_coef_transpose(nxt)

    p.wait("gpsimd", yst, yst.n)
    p.wait("gpsimd", dG, dG.n)

    with nc.Block() as blk:
        p.flush(blk)
    esC.close()
    es.close()
    return nc


_NC_CACHE = {}


def _get_nc():
    if "nc" not in _NC_CACHE:
        _NC_CACHE["nc"] = build_nc()
    return _NC_CACHE["nc"]


def _mk_sel():
    s = np.zeros((8, 8 * 128), np.float32)
    for e in range(8):
        s[e, e * 128:(e + 1) * 128] = 1.0
    return s.astype(ml_dtypes.bfloat16)


def _prep_inputs(inputs):
    bf = ml_dtypes.bfloat16
    x = np.asarray(inputs["L0_attn_x"])

    def tile_gu(w):
        return np.ascontiguousarray(
            w.reshape(8, 128, DE).transpose(1, 0, 2).reshape(128, 8 * DE))

    eg = np.asarray(inputs["L0_ffn_experts_gate"])
    eu = np.asarray(inputs["L0_ffn_experts_up"])
    ed = np.asarray(inputs["L0_ffn_experts_down"])
    sg = np.asarray(inputs["L0_ffn_shared_gate"])
    su = np.asarray(inputs["L0_ffn_shared_up"])
    sd = np.asarray(inputs["L0_ffn_shared_down"])
    gup = np.zeros((NE, 2, 128, FC * DE), np.float32)
    for e in range(8):
        gup[e, 0] = tile_gu(eg[e])
        gup[e, 1] = tile_gu(eu[e])
    gup[8, 0] = tile_gu(sg[:, 0:DE])
    gup[8, 1] = tile_gu(su[:, 0:DE])
    gup[9, 0] = tile_gu(sg[:, DE:2 * DE])
    gup[9, 1] = tile_gu(su[:, DE:2 * DE])

    downs = [ed[e] for e in range(8)] + [sd[0:DE, :], sd[DE:2 * DE, :]]
    dwn = np.zeros((FC, 128, NE * 4 * 128), np.float32)
    for dc in range(FC):
        blocks = []
        for pe in range(NE):
            w = downs[pe]
            for cc in range(4):
                blocks.append(w[cc * 128:(cc + 1) * 128,
                                dc * 128:(dc + 1) * 128])
        dwn[dc] = np.concatenate(blocks, axis=1)

    common = dict(
        wqT=np.ascontiguousarray(np.asarray(inputs["L0_attn_Wq"]).T),
        wkT=np.ascontiguousarray(np.asarray(inputs["L0_attn_Wk"]).T),
        wvT=np.ascontiguousarray(np.asarray(inputs["L0_attn_Wv"]).T),
        woT=np.ascontiguousarray(np.asarray(inputs["L0_attn_Wo"]).T),
        wavel=np.ascontiguousarray(np.asarray(inputs["L0_attn_wavelengths"])),
        pbias=np.ascontiguousarray(np.asarray(inputs["L0_attn_phase_bias"])),
        bq=np.ascontiguousarray(np.asarray(inputs["L0_attn_bq"])),
        bk=np.ascontiguousarray(np.asarray(inputs["L0_attn_bk"])),
        bv=np.ascontiguousarray(np.asarray(inputs["L0_attn_bv"])),
        bo=np.ascontiguousarray(np.asarray(inputs["L0_attn_bo"])),
        router=np.ascontiguousarray(np.asarray(inputs["L0_ffn_router"])),
        gup=gup, dwn=dwn,
        eye_in=np.eye(128, dtype=np.float32).astype(bf),
        sel_in=_mk_sel(),
    )
    per_core = []
    for b in range(8):
        m = dict(common)
        m["x_t"] = np.ascontiguousarray(np.asarray(x[b]).T)
        per_core.append(m)
    return per_core


def kernel(**inputs):
    nc = _get_nc()
    in_maps = _prep_inputs(inputs)
    res = run_bass_kernel_spmd(nc, in_maps, core_ids=list(range(8)))
    out = np.empty((8, S, D), np.float32)
    for b in range(8):
        out[b] = np.asarray(res.results[b]["y"]).T
    return out
